# version 01
# baseline (speedup 1.0000x reference)
import os
import math
import numpy as np
import ml_dtypes
from contextlib import ExitStack

import concourse.bass as bass
import concourse.mybir as mybir
from concourse.bass_utils import run_bass_kernel_spmd

F32 = mybir.dt.float32
BF16 = mybir.dt.bfloat16
I32 = mybir.dt.int32
AF = mybir.ActivationFunctionType
AX = mybir.AxisListType
ALU = mybir.AluOpType
BF = ml_dtypes.bfloat16

D = 4096
SEQ = 2048
NT = 1024
H = 32
QL = 1536
KVL = 512
DN = 128
DR = 64
DV = 128
NEG = -1.0e9
SCALE = float(np.float32(1.0) / np.float32(np.sqrt(np.float32(DN + DR))))
EPS = 1.0e-5

NCH = (8, 16)
NW = (2, 4)
DIAG0 = (4, 12)



def _yarn_tables():
    i = np.arange(0, DR, 2, dtype=np.float32)
    freqs = (np.float32(1.0) / (np.float32(10000.0) ** (i / np.float32(DR)))).astype(np.float32)
    wavelengths = (np.float32(2.0 * np.pi) / freqs).astype(np.float32)
    ramp = np.clip((wavelengths / np.float32(4096.0) - np.float32(1.0)) / np.float32(31.0),
                   0.0, 1.0).astype(np.float32)
    scale = (np.float32(1.0) - ramp + ramp * np.float32(40.0)).astype(np.float32)
    return (freqs / scale).astype(np.float32)


def _cos_sin(pos, reps):
    inv = _yarn_tables()
    f = pos.astype(np.float32)[None, :] * inv[:, None]
    c = np.cos(f).astype(np.float32)
    s = np.sin(f).astype(np.float32)
    return np.tile(c, (reps, 1)), np.tile(s, (reps, 1))


def _perm_blocks(h):
    if h == 0:
        return [4, 5, 6, 7, 0, 1, 2, 3, 8, 9, 10, 11, 12, 13, 14, 15]
    return [0, 1, 2, 3, 4, 5, 6, 7, 12, 13, 14, 15, 8, 9, 10, 11]


def _own_rows(h):
    if h == 0:
        return np.r_[0:512, 1536:2048]
    return np.r_[512:1536]


def _maskrows(h):
    m = np.zeros((2, SEQ), np.float32)
    blk = lambda c: slice(c * 128, (c + 1) * 128)
    if h == 0:
        for c in range(0, 4):
            m[0, blk(c)] = NEG
        for c in range(8, 16):
            m[0, blk(c)] = NEG
    else:
        for c in range(8, 16):
            m[0, blk(c)] = NEG
        for c in range(8, 12):
            m[1, blk(c)] = NEG
    return m


def _tri_consts():
    out = np.zeros((128, 4 * 512), np.float32)
    st = np.arange(128)[:, None]
    for d in range(4):
        sq = np.arange(512)[None, :]
        m = np.where(sq < 128 * d, NEG, 0.0)
        inblk = (sq >= 128 * d) & (sq < 128 * (d + 1))
        m = np.where(inblk & (st > (sq - 128 * d)), NEG, m)
        out[:, d * 512:(d + 1) * 512] = m
    return out.astype(BF)


def _tile2(w, kb, mb):
    K, M = w.shape
    t = w.reshape(K // kb, kb, M // mb, mb).transpose(2, 0, 1, 3)
    return np.ascontiguousarray(t)


def _prep_shared(inp):
    d = {}
    d["wkv_t"] = np.ascontiguousarray(
        inp["L0_attn_w_kv_down"].reshape(32, 128, KVL))
    d["wkr_t"] = np.ascontiguousarray(
        inp["L0_attn_w_k_rope"].reshape(32, 128, DR))
    d["wqd_t"] = _tile2(inp["L0_attn_w_q_down"], 128, 128)
    wqun = _tile2(inp["L0_attn_w_q_up_nope"], 128, 256)
    d["wqun_t"] = wqun
    d["wqur_t"] = _tile2(inp["L0_attn_w_q_up_rope"], 128, 128)
    wukT = np.ascontiguousarray(inp["L0_attn_w_uk"].T)
    wuvT = np.ascontiguousarray(inp["L0_attn_w_uv"].T)
    d["wukT_t"] = _tile2(wukT, 128, 256)
    d["wuvT_t"] = _tile2(wuvT, 128, 256)
    d["wout_t"] = _tile2(inp["L0_attn_w_out"], 128, 128)
    q = inp["L0_ffn_qweight"].reshape(4096, 4096)
    qT = q.T
    d["q_tiled"] = _tile2(qT, 128, 128).astype(np.int32)
    d["scalesT"] = np.ascontiguousarray(
        inp["L0_ffn_scales"].reshape(4096, 32).T)[None]
    d["zT"] = np.ascontiguousarray(
        inp["L0_ffn_scaled_zeros"].reshape(4096, 32).T)
    d["bias_r"] = np.ascontiguousarray(
        inp["L0_ffn_bias"].reshape(32, 128).T)
    d["ident_bf"] = np.eye(128, dtype=BF)
    d["ident_f"] = np.eye(128, dtype=np.float32)
    d["tri"] = _tri_consts()
    g = np.zeros((32, 128, 32), BF)
    for j in range(32):
        g[j, :, j] = BF(1.0)
    d["g_ind"] = g
    cf = np.zeros((128, 8), np.float32)
    cf[:, 0] = EPS
    cf[:, 1] = SCALE
    cf[:, 2] = 0.0
    cf[:, 3] = 1.0
    cf[:, 4] = -1.0
    d["consts_f"] = cf
    qi = np.zeros((3, NT), np.float32)
    qi[1, 0:512] = 1.0
    qi[2, 512:NT] = 1.0
    d["q_rowsinit"] = qi
    return d


def _prep_core(x, b, h):
    d = {}
    perm = _perm_blocks(h)
    rows = np.concatenate([np.arange(c * 128, (c + 1) * 128) for c in perm])
    d["x_full"] = np.ascontiguousarray(x[b][rows])
    own = _own_rows(h)
    d["x_own"] = np.ascontiguousarray(x[b][own])
    pos_k = rows.astype(np.float32)
    ck, sk = _cos_sin(pos_k, 2)
    d["cos_k"], d["sin_k"] = ck, sk
    pos_q = own.astype(np.float32)
    cq, sq = _cos_sin(pos_q, 4)
    d["cos_q"], d["sin_q"] = cq, sq
    d["maskrows"] = _maskrows(h)
    return d



class _Sem:
    def __init__(self, h, name):
        self.h, self.name, self.n = h, name, 0


class _B:

    SEC = {"T": "tensor", "V": "vector", "S": "scalar", "G": "gpsimd", "Y": "sync"}

    def __init__(self, nc, blk, sems, sem_pool):
        self.nc, self.blk = nc, blk
        self.sems = sems
        self.sem_pool = list(sem_pool)
        self.dsems = {}
        self.last_w = {}
        self.readers = {}
        self.waited = {k: {} for k in self.SEC}

    def dsem(self, stream):
        if stream not in self.dsems:
            self.dsems[stream] = _Sem(self.sem_pool.pop(), f"d_{stream}")
        return self.dsems[stream]

    def _need(self, r, w):
        need = {}
        for k in r:
            for s, v in self.last_w.get(k, {}).items():
                if need.get(s, 0) < v:
                    need[s] = v
        for k in w:
            for dd in (self.last_w.get(k, {}), self.readers.get(k, {})):
                for s, v in dd.items():
                    if need.get(s, 0) < v:
                        need[s] = v
        return need

    def step(self, ename, fn, r=(), w=(), tok=True, dma=None, reg_r=None):
        sem = self.dsem(dma) if dma else self.sems[ename]
        need = self._need(r, w)
        holder = []

        def body(e):
            wt = self.waited[ename]
            for s, v in sorted(need.items(), key=lambda kv: kv[0].name):
                if wt.get(s, 0) < v:
                    if s is self.sems.get(ename) and v > s.n:
                        raise AssertionError(
                            f"own-sem deadlock {ename} wait {v} > issued {s.n}")
                    e.wait_ge(s.h, v)
                    wt[s] = v
            holder.append(fn(e))

        getattr(self.blk, self.SEC[ename])(body)
        inst = holder[0]
        if not tok:
            return None
        step_n = 16 if dma else 1
        sem.n += step_n
        inst.then_inc(sem.h, step_n)
        t = (sem, sem.n)
        for k in w:
            self.last_w[k] = {sem: sem.n}
            self.readers[k] = {}
        for k in (reg_r if reg_r is not None else r):
            d = self.readers.setdefault(k, {})
            if d.get(sem, 0) < sem.n:
                d[sem] = sem.n
        return t

    def flush(self, stream, keys):
        sem = self.dsem(stream)
        for k in keys:
            self.last_w[k] = {sem: sem.n}
            self.readers[k] = {}

    def barrier(self):
        ph = self.sems["ph"]
        target = ph.n + 5
        ph.n = target
        allsems = [self.sems[k] for k in ("T", "V", "S", "G")] + list(self.dsems.values())

        for en in self.SEC:
            def body(e, en=en):
                for s in allsems:
                    if s.n > 0 and self.waited[en].get(s, 0) < s.n:
                        e.wait_ge(s.h, s.n)
                        self.waited[en][s] = s.n
                e.nop().then_inc(ph.h, 1)
                e.wait_ge(ph.h, target)
                self.waited[en][ph] = target
            getattr(self.blk, self.SEC[en])(body)
        self.last_w = {}
        self.readers = {}


def build_nc(dbg=0):
    nc = bass.Bass("TRN2", target_bir_lowering=False)
    dt_in = {}

    def din(name, shape, dtype):
        t = nc.dram_tensor(name, list(shape), dtype, kind="ExternalInput").ap()
        dt_in[name] = t
        return t

    x_full = din("x_full", (SEQ, D), F32)
    x_own = din("x_own", (NT, D), F32)
    cos_k = din("cos_k", (64, SEQ), F32)
    sin_k = din("sin_k", (64, SEQ), F32)
    cos_q = din("cos_q", (128, NT), F32)
    sin_q = din("sin_q", (128, NT), F32)
    maskrows = din("maskrows", (2, SEQ), F32)
    q_rowsinit = din("q_rowsinit", (3, NT), F32)
    consts_f = din("consts_f", (128, 8), F32)
    ident_bf_d = din("ident_bf", (128, 128), BF16)
    ident_f_d = din("ident_f", (128, 128), F32)
    tri_d = din("tri", (128, 2048), BF16)
    g_ind_d = din("g_ind", (32, 128, 32), BF16)
    wkv_t = din("wkv_t", (32, 128, KVL), F32)
    wkr_t = din("wkr_t", (32, 128, DR), F32)
    wqd_t = din("wqd_t", (12, 32, 128, 128), F32)
    wqun_t = din("wqun_t", (16, 12, 128, 256), F32)
    wqur_t = din("wqur_t", (16, 12, 128, 128), F32)
    wukT_t = din("wukT_t", (16, 4, 128, 256), F32)
    wuvT_t = din("wuvT_t", (16, 4, 128, 256), F32)
    wout_t = din("wout_t", (32, 32, 128, 128), F32)
    q_tiled = din("q_tiled", (32, 32, 128, 128), I32)
    scalesT_d = din("scalesT", (1, 32, 4096), F32)
    zT_d = din("zT", (32, 4096), F32)
    bias_d = din("bias_r", (128, 32), F32)

    out_own = nc.dram_tensor("out_own", [NT, D], F32, kind="ExternalOutput").ap()
    dbg_t = {}
    if dbg:
        for nme, shp, dt in [
            ("dbg_xnT", (128, 32 * NT), BF16), ("dbg_cq", (128, 12 * NT), BF16),
            ("dbg_ckv", (128, 4 * SEQ), BF16), ("dbg_krx", (67, SEQ), BF16),
            ("dbg_oT", (128, 32 * NT), BF16), ("dbg_cs", (32, NT), BF16),
        ]:
            dbg_t[nme] = nc.dram_tensor(nme, list(shp), dt, kind="ExternalOutput").ap()

    xn_scr = nc.dram_tensor("xn_scr", [D, NT], BF16, kind="Internal").ap()
    sa_scr = nc.dram_tensor("sa_scr", [D, NT], BF16, kind="Internal").ap()

    with ExitStack() as top:
        e = top.enter_context
        ident_bf = e(nc.sbuf_tensor("ident_bf_s", [128, 128], BF16))
        ident_f = e(nc.sbuf_tensor("ident_f_s", [128, 128], F32))
        tri_sb = e(nc.sbuf_tensor("tri_s", [128, 2048], BF16))
        cf_sb = e(nc.sbuf_tensor("consts_f_s", [128, 8], F32))
        ones_col = e(nc.sbuf_tensor("ones_col", [128, 1], BF16))
        ones_row_f = e(nc.sbuf_tensor("ones_row_f", [1, 128], F32))
        cosq_sb = e(nc.sbuf_tensor("cosq", [128, NT], F32))
        sinq_sb = e(nc.sbuf_tensor("sinq", [128, NT], F32))
        k_ropeX = e(nc.sbuf_tensor("k_ropeX", [67, SEQ], BF16))
        colsumsT = e(nc.sbuf_tensor("colsumsT", [32, NT], BF16))
        zT_sb = e(nc.sbuf_tensor("zT_s", [32, 4096], BF16))
        bias_sb = e(nc.sbuf_tensor("bias_s", [128, 32], F32))
        g_ind_sb = e(nc.sbuf_tensor("g_ind_s", [128, 32 * 32], BF16))

        nc.const_aps.register(F32, EPS, cf_sb[:, 0:1])
        nc.const_aps.register(F32, SCALE, cf_sb[:, 1:2])
        nc.const_aps.register(F32, 0.0, cf_sb[:, 2:3])
        nc.const_aps.register(F32, 1.0, cf_sb[:, 3:4])
        nc.const_aps.register(F32, -1.0, cf_sb[:, 4:5])

        sems = {}
        for nm in ("T", "V", "S", "G", "Y", "ph"):
            sems[nm] = _Sem(e(nc.semaphore(f"sem_{nm}")), nm)
        sem_pool = [e(nc.semaphore(f"sem_d{i}")) for i in range(48)]
        blk = e(nc.Block())
        b = _B(nc, blk, sems, sem_pool)

        es_ckv = top.enter_context(ExitStack())
        c_kvT = es_ckv.enter_context(nc.sbuf_tensor("c_kvT", [128, 4 * SEQ], BF16))
        c_qT = es_ckv.enter_context(nc.sbuf_tensor("c_qT", [128, 12 * NT], BF16))
        es_kr = top.enter_context(ExitStack())
        k_rope_raw = es_kr.enter_context(
            nc.sbuf_tensor("k_rope_raw", [64, SEQ], F32))
        cosk_sb = es_kr.enter_context(nc.sbuf_tensor("cosk", [64, SEQ], F32))
        sink_sb = es_kr.enter_context(nc.sbuf_tensor("sink", [64, SEQ], F32))

        for dst, src in [(ident_bf, ident_bf_d), (ident_f, ident_f_d),
                         (tri_sb, tri_d), (cf_sb, consts_f),
                         (cosk_sb, cos_k), (sink_sb, sin_k),
                         (cosq_sb, cos_q), (sinq_sb, sin_q),
                         (bias_sb, bias_d)]:
            b.step("Y", lambda e, d=dst, s=src: e.dma_start(out=d[:], in_=s[:]),
                   w=[d_key(dst)], dma="c0")
        b.step("G", lambda e: e.dma_start(out=zT_sb[:], in_=zT_d[:]),
               w=[d_key(zT_sb)], dma="c1")
        b.step("Y", lambda e: e.dma_start(
            out=g_ind_sb[:].rearrange("p (j c) -> p j c", j=32),
            in_=g_ind_d[:].rearrange("j p c -> p j c")),
            w=[d_key(g_ind_sb)], dma="c0")
        b.flush("c0", [d_key(t) for t in (ident_bf, ident_f, tri_sb, cf_sb,
                                          cosk_sb, sink_sb, cosq_sb, sinq_sb,
                                          bias_sb, g_ind_sb)])
        b.step("V", lambda e: e.memset(ones_col[:], 1.0), w=["ones_col"])
        b.step("V", lambda e: e.memset(ones_row_f[:], 1.0), w=["ones_row_f"])
        b.barrier()

        with ExitStack() as ph:
            pe = ph.enter_context
            wkv_bf = pe(nc.sbuf_tensor("wkv_bf", [128, 32 * 512], BF16))
            wkr_bf = pe(nc.sbuf_tensor("wkr_bf", [128, 32 * 64], BF16))
            xa = [pe(nc.sbuf_tensor(f"xa{i}", [128, D], F32)) for i in range(2)]
            xnb = [pe(nc.sbuf_tensor(f"xnb{i}", [128, D], BF16)) for i in range(2)]
            xnT_tile = [pe(nc.sbuf_tensor(f"xnTt{i}", [128, D], BF16)) for i in range(2)]
            stats = pe(nc.sbuf_tensor("stats", [128, 48], F32))
            aggr = pe(nc.sbuf_tensor("aggr", [128, 2], F32))
            sd = pe(nc.sbuf_tensor("sd", [128, 1], F32))
            rstd = pe(nc.sbuf_tensor("rstd", [128, 1], F32))
            ps_tA = [pe(nc.psum_tensor(f"ps_tA{i}", [128, 1024], BF16)) for i in range(2)]
            ps_kv = pe(nc.psum_tensor("ps_kv", [128, 512], F32))
            ps_kr = pe(nc.psum_tensor("ps_kr", [64, 128], F32))

            b.step("G", lambda e: e.dma_start(
                out=wkv_bf[:].rearrange("p (k m) -> p k m", k=32),
                in_=wkv_t[:].rearrange("k p m -> p k m")),
                w=["wkv_bf"], dma="wkv")
            b.step("G", lambda e: e.dma_start(
                out=wkr_bf[:].rearrange("p (k m) -> p k m", k=32),
                in_=wkr_t[:].rearrange("k p m -> p k m")),
                w=["wkr_bf"], dma="wkr")

            def ln_tile(src_ap, t, bi, dst_tile):
                b.step("Y", lambda e: e.dma_start(
                    out=xa[bi][:], in_=src_ap[t * 128:(t + 1) * 128, :]),
                    r=[], w=[("xa", bi)], dma=f"xa{bi}")
                for u in range(8):
                    b.step("V", lambda e, u=u: e.bn_stats(
                        stats[:, u * 6:(u + 1) * 6], xa[bi][:, u * 512:(u + 1) * 512]),
                        r=[("xa", bi)], w=[("stats", u)])
                b.step("V", lambda e: e.bn_aggr(aggr[:], stats[:, 0:48]),
                       r=[("stats", u) for u in range(8)], w=["aggr"])
                b.step("S", lambda e: e.activation(sd[:], aggr[:, 1:2], AF.Sqrt, bias=EPS),
                       r=["aggr"], w=["sd"])
                b.step("V", lambda e: e.reciprocal(rstd[:], sd[:]), r=["sd"], w=["rstd"])
                b.step("V", lambda e: e.tensor_scalar(
                    out=xnb[bi][:], in0=xa[bi][:], scalar1=aggr[:, 0:1],
                    scalar2=rstd[:], op0=ALU.subtract, op1=ALU.mult),
                    r=[("xa", bi), "aggr", "rstd"], w=[("xnb", bi)])
                for q in range(4):
                    pbank = q % 2
                    for u in range(8):
                        c = q * 8 + u
                        b.step("T", lambda e, c=c, u=u: e.transpose(
                            ps_tA[pbank][:, u * 128:(u + 1) * 128],
                            xnb[bi][:, c * 128:(c + 1) * 128], ident_bf[:]),
                            r=[("xnb", bi), d_key(ident_bf)],
                            w=[("ps_tA", pbank)],
                            tok=(u == 7),
                            reg_r=[("xnb", bi), d_key(ident_bf)] if u == 7 else None)
                    b.step("V", lambda e, q=q: e.tensor_copy(
                        dst_tile[:].rearrange("p (c n) -> p c n", c=32)[:, q * 8:(q + 1) * 8, :],
                        ps_tA[pbank][:].rearrange("p (c n) -> p c n", c=8)),
                        r=[("ps_tA", pbank)], w=[("dstq", id(dst_tile), q)])

            for t in range(16):
                bi = t % 2
                ln_tile(x_full, t, bi, xnT_tile[bi])
                for m in range(4):
                    for K in range(32):
                        b.step("T", lambda e, m=m, K=K: e.matmul(
                            ps_kv[:, m * 128:(m + 1) * 128],
                            wkv_bf[:, K * 512 + m * 128:K * 512 + (m + 1) * 128],
                            xnT_tile[bi][:, K * 128:(K + 1) * 128],
                            start=(K == 0), stop=(K == 31)),
                            r=["wkv_bf", ("dstq", id(xnT_tile[bi]), K // 8)],
                            w=[("ps_kv", m)], tok=(K == 31),
                            reg_r=(["wkv_bf"] + [("dstq", id(xnT_tile[bi]), qq) for qq in range(4)]) if K == 31 else None)
                b.step("V", lambda e, t=t: e.tensor_copy(
                    c_kvT[:].rearrange("p (m n) -> p m n", m=4)[:, :, t * 128:(t + 1) * 128],
                    ps_kv[:].rearrange("p (m n) -> p m n", m=4)),
                    r=[("ps_kv", m) for m in range(4)], w=[("c_kvT", t)])
                for K in range(32):
                    b.step("T", lambda e, K=K: e.matmul(
                        ps_kr[:, :],
                        wkr_bf[:, K * 64:(K + 1) * 64],
                        xnT_tile[bi][:, K * 128:(K + 1) * 128],
                        start=(K == 0), stop=(K == 31)),
                        r=["wkr_bf", ("dstq", id(xnT_tile[bi]), K // 8)],
                        w=[("ps_kr",)], tok=(K == 31),
                        reg_r=(["wkr_bf"] + [("dstq", id(xnT_tile[bi]), qq) for qq in range(4)]) if K == 31 else None)
                b.step("V", lambda e, t=t: e.tensor_copy(
                    k_rope_raw[:, t * 128:(t + 1) * 128], ps_kr[:, :]),
                    r=[("ps_kr",)], w=[("k_rope_raw", t)])
            b.barrier()

        with ExitStack() as ph:
            pe = ph.enter_context
            rotk = pe(nc.sbuf_tensor("rotk", [64, SEQ], F32))
            pA = pe(nc.sbuf_tensor("ropeA", [64, SEQ], F32))
            pB = pe(nc.sbuf_tensor("ropeB", [64, SEQ], F32))
            b.step("V", lambda e: e.tensor_scalar_mul(
                rotk[0:32, :], k_rope_raw[32:64, :], -1.0),
                r=["k_rope_raw"], w=["rotk_a"])
            b.step("V", lambda e: e.tensor_copy(rotk[32:64, :], k_rope_raw[0:32, :]),
                   r=["k_rope_raw"], w=["rotk_b"])
            b.step("V", lambda e: e.tensor_mul(pA[:], k_rope_raw[0:64, :], cosk_sb[:]),
                   r=["k_rope_raw", d_key(cosk_sb)], w=["pA"])
            b.step("V", lambda e: e.tensor_mul(pB[:], rotk[:], sink_sb[:]),
                   r=["rotk_a", "rotk_b", d_key(sink_sb)], w=["pB"])
            b.step("V", lambda e: e.tensor_add(k_ropeX[0:64, :], pA[:], pB[:]),
                   r=["pA", "pB"], w=["krx_rope"])
            b.step("V", lambda e: e.memset(k_ropeX[64:65, :], 1.0), w=["krx_ones"])
            b.step("G", lambda e: e.dma_start(out=k_ropeX[65:67, :], in_=maskrows[:]),
                   w=["krx_mask"], dma="mk")
            if dbg:
                b.step("Y", lambda e: e.dma_start(out=dbg_t["dbg_krx"][:], in_=k_ropeX[:]),
                       r=["krx_rope", "krx_ones", "krx_mask"], dma="dbg")
            b.barrier()
        es_kr.close()

        es_xn = top.enter_context(ExitStack())
        xnT_own = es_xn.enter_context(
            nc.sbuf_tensor("xnT_own", [128, 32 * NT], BF16, side="right"))
        with ExitStack() as ph:
            pe = ph.enter_context
            xa = [pe(nc.sbuf_tensor(f"xaq{i}", [128, D], F32)) for i in range(2)]
            xnb = [pe(nc.sbuf_tensor(f"xnbq{i}", [128, D], BF16)) for i in range(2)]
            stats = pe(nc.sbuf_tensor("statsq", [128, 48], F32))
            aggr = pe(nc.sbuf_tensor("aggrq", [128, 2], F32))
            sd = pe(nc.sbuf_tensor("sdq", [128, 1], F32))
            rstd = pe(nc.sbuf_tensor("rstdq", [128, 1], F32))
            ps_tA = [pe(nc.psum_tensor(f"ps_tAq{i}", [128, 1024], BF16)) for i in range(2)]

            for t in range(8):
                bi = t % 2
                b.step("Y", lambda e, t=t: e.dma_start(
                    out=xa[bi][:], in_=x_own[t * 128:(t + 1) * 128, :]),
                    w=[("xa", bi)], dma=f"xa{bi}")
                for u in range(8):
                    b.step("V", lambda e, u=u: e.bn_stats(
                        stats[:, u * 6:(u + 1) * 6], xa[bi][:, u * 512:(u + 1) * 512]),
                        r=[("xa", bi)], w=[("stats", u)])
                b.step("V", lambda e: e.bn_aggr(aggr[:], stats[:, 0:48]),
                       r=[("stats", u) for u in range(8)], w=["aggr"])
                b.step("S", lambda e: e.activation(sd[:], aggr[:, 1:2], AF.Sqrt, bias=EPS),
                       r=["aggr"], w=["sd"])
                b.step("V", lambda e: e.reciprocal(rstd[:], sd[:]), r=["sd"], w=["rstd"])
                b.step("V", lambda e: e.tensor_scalar(
                    out=xnb[bi][:], in0=xa[bi][:], scalar1=aggr[:, 0:1],
                    scalar2=rstd[:], op0=ALU.subtract, op1=ALU.mult),
                    r=[("xa", bi), "aggr", "rstd"], w=[("xnb", bi)])
                for q in range(4):
                    pbank = q % 2
                    for u in range(8):
                        c = q * 8 + u
                        b.step("T", lambda e, c=c, u=u: e.transpose(
                            ps_tA[pbank][:, u * 128:(u + 1) * 128],
                            xnb[bi][:, c * 128:(c + 1) * 128], ident_bf[:]),
                            r=[("xnb", bi), d_key(ident_bf)],
                            w=[("ps_tA", pbank)], tok=(u == 7),
                            reg_r=[("xnb", bi), d_key(ident_bf)] if u == 7 else None)
                    b.step("V", lambda e, q=q, t=t: e.tensor_copy(
                        xnT_own[:].rearrange("p (c n) -> p c n", c=32)
                        [:, q * 8:(q + 1) * 8, t * 128:(t + 1) * 128],
                        ps_tA[pbank][:].rearrange("p (c n) -> p c n", c=8)),
                        r=[("ps_tA", pbank)], w=[("xnT_own", t, q)])
            b.barrier()

        with ExitStack() as ph:
            pe = ph.enter_context
            wqd_cb = [pe(nc.sbuf_tensor(f"wqd_cb{i}", [128, 32 * 128], BF16)) for i in range(2)]
            colsq = pe(nc.sbuf_tensor("colsq", [128, 8 * 32], BF16))
            ps_cq = [pe(nc.psum_tensor(f"ps_cq{i}", [128, 512], F32)) for i in range(4)]
            ps_cs = pe(nc.psum_tensor("ps_cs", [128, 32], F32))
            ps_csT = pe(nc.psum_tensor("ps_csT", [32, 128], BF16))

            xn_own_keys = [("xnT_own", t, q) for t in range(8) for q in range(4)]
            for m in range(12):
                b.step("G", lambda e, m=m: e.dma_start(
                    out=wqd_cb[m % 2][:].rearrange("p (k n) -> p k n", k=32),
                    in_=wqd_t[m].rearrange("k p n -> p k n")),
                    w=[("wqd_cb", m % 2)], dma=f"wqd{m % 2}")
                for t in range(2):
                    pb = (m % 2) * 2 + t
                    for K in range(32):
                        b.step("T", lambda e, m=m, t=t, K=K, pb=pb: e.matmul(
                            ps_cq[pb][:],
                            wqd_cb[m % 2][:, K * 128:(K + 1) * 128],
                            xnT_own[:, K * NT + t * 512:K * NT + t * 512 + 512],
                            start=(K == 0), stop=(K == 31)),
                            r=([("wqd_cb", m % 2)] + (xn_own_keys if K == 0 else [])),
                            w=[("ps_cq", pb)], tok=(K == 31),
                            reg_r=([("wqd_cb", m % 2)] + xn_own_keys) if K == 31 else None)
                    b.step("V", lambda e, m=m, t=t, pb=pb: e.tensor_copy(
                        c_qT[:, m * NT + t * 512:m * NT + t * 512 + 512], ps_cq[pb][:]),
                        r=[("ps_cq", pb)], w=[("c_qT", m, t)])
            for tc in range(8):
                for j in range(32):
                    b.step("T", lambda e, tc=tc, j=j: e.matmul(
                        ps_cs[:], xnT_own[:, j * NT + tc * 128:j * NT + (tc) * 128 + 128],
                        g_ind_sb[:, j * 32:(j + 1) * 32],
                        start=(j == 0), stop=(j == 31)),
                        r=([d_key(g_ind_sb)] + (xn_own_keys if j == 0 else [])),
                        w=[("ps_cs",)], tok=(j == 31),
                        reg_r=([d_key(g_ind_sb)] + xn_own_keys) if j == 31 else None)
                b.step("V", lambda e, tc=tc: e.tensor_copy(
                    colsq[:, tc * 32:(tc + 1) * 32], ps_cs[:]),
                    r=[("ps_cs",)], w=[("colsq", tc)])
                b.step("T", lambda e, tc=tc: e.transpose(
                    ps_csT[:], colsq[:, tc * 32:(tc + 1) * 32], ident_bf[:]),
                    r=[("colsq", tc), d_key(ident_bf)], w=[("ps_csT",)])
                b.step("V", lambda e, tc=tc: e.tensor_copy(
                    colsumsT[0:32, tc * 128:(tc + 1) * 128], ps_csT[:]),
                    r=[("ps_csT",)], w=[("colsumsT", tc)])
            b.step("Y", lambda e: e.dma_start(
                out=xn_scr[:].rearrange("(j p) n -> p j n", p=128),
                in_=xnT_own[:].rearrange("p (j n) -> p j n", j=32)),
                r=xn_own_keys, w=["xn_scr"], dma="xs")
            if dbg:
                b.step("Y", lambda e: e.dma_start(out=dbg_t["dbg_xnT"][:], in_=xnT_own[:]),
                       r=xn_own_keys, dma="dbg")
                b.step("Y", lambda e: e.dma_start(out=dbg_t["dbg_cq"][:], in_=c_qT[:]),
                       r=[("c_qT", m, t) for m in range(12) for t in range(2)], dma="dbg")
                b.step("Y", lambda e: e.dma_start(out=dbg_t["dbg_ckv"][:], in_=c_kvT[:]),
                       r=[("c_kvT", t) for t in range(16)], dma="dbg")
                b.step("Y", lambda e: e.dma_start(out=dbg_t["dbg_cs"][:], in_=colsumsT[:]),
                       r=[("colsumsT", tc) for tc in range(8)], dma="dbg")
            b.barrier()
        es_xn.close()

        es_oT = top.enter_context(ExitStack())
        oT = es_oT.enter_context(
            nc.sbuf_tensor("oT", [128, 32 * NT], BF16, side="right"))
        with ExitStack() as ph:
            pe = ph.enter_context
            wqun_bf = [pe(nc.sbuf_tensor(f"wqun{i}", [128, 12 * 256], BF16)) for i in range(2)]
            wqur_bf = [pe(nc.sbuf_tensor(f"wqur{i}", [128, 12 * 128], BF16)) for i in range(2)]
            wukT_bf = [pe(nc.sbuf_tensor(f"wukT{i}", [128, 4 * 256], BF16)) for i in range(2)]
            wuvT_bf = [pe(nc.sbuf_tensor(f"wuvT{i}", [128, 4 * 256], BF16)) for i in range(2)]
            q_nopeT = [pe(nc.sbuf_tensor(f"qnope{i}", [128, NT], BF16)) for i in range(2)]
            qrx = [pe(nc.sbuf_tensor(f"qrx{i}", [67, NT], BF16)) for i in range(2)]
            kn = [pe(nc.sbuf_tensor(f"kn{i}", [128, SEQ], BF16)) for i in range(2)]
            v_grp = pe(nc.sbuf_tensor("v_grp", [128, 16 * 256], BF16))
            rope_tmp = pe(nc.sbuf_tensor("rope_tmp", [128, NT], F32))
            rot = pe(nc.sbuf_tensor("rot", [128, NT], F32))
            prA = pe(nc.sbuf_tensor("prA", [128, NT], F32))
            prB = pe(nc.sbuf_tensor("prB", [128, NT], F32))
            PT = [pe(nc.sbuf_tensor(f"PT{i}", [128, 512], BF16)) for i in range(2)]
            mcol = pe(nc.sbuf_tensor("mcol", [128, 1], F32))
            mtmp = pe(nc.sbuf_tensor("mtmp", [128, 1], F32))
            mneg = pe(nc.sbuf_tensor("mneg", [128, 1], F32))
            recip_row = pe(nc.sbuf_tensor("recip_row", [1, 512], F32))
            rb_sb = pe(nc.sbuf_tensor("rb_sb", [128, 512], F32))
            pool = [pe(nc.psum_tensor(f"poolB{i}", [128, 512], F32)) for i in range(4)]
            ps_st = [pe(nc.psum_tensor(f"ps_stB{i}", [128, 512], F32)) for i in range(2)]
            ps_out = pe(nc.psum_tensor("ps_outB", [128, 512], F32))
            ps_sum = pe(nc.psum_tensor("ps_sumB", [1, 512], F32))

            for p in range(2):
                b.step("G", lambda e, p=p: e.dma_start(out=qrx[p][64:67, :], in_=q_rowsinit[:]),
                       w=[("qrx_rows", p)], dma=f"qi{p}")

            ckv_keys = [("c_kvT", t) for t in range(16)]
            cq_keys = [("c_qT", m, t) for m in range(12) for t in range(2)]
            krx_keys = ["krx_rope", "krx_ones", "krx_mask"]

            def wdma(grp):
                gb = grp % 2
                for (buf, src, ncols) in [(wqun_bf, wqun_t, 256), (wqur_bf, wqur_t, 128),
                                          (wukT_bf, wukT_t, 256), (wuvT_bf, wuvT_t, 256)]:
                    b.step("G", lambda e, buf=buf, src=src, grp=grp: e.dma_start(
                        out=buf[gb][:].rearrange("p (k n) -> p k n", k=src.shape[1]),
                        in_=src[grp].rearrange("k p n -> p k n")),
                        w=[(id(buf), gb)], dma=f"wg{gb}")
                b.flush(f"wg{gb}", [(id(buf), gb) for buf in
                                    (wqun_bf, wqur_bf, wukT_bf, wuvT_bf)])

            wdma(0)
            pool_i = [0]

            def nxt_pool():
                i = pool_i[0] % 4
                pool_i[0] += 1
                return i

            for grp in range(16):
                gb = grp % 2
                if grp + 1 < 16:
                    wdma(grp + 1)
                for p in range(2):
                    for t in range(2):
                        pb = nxt_pool()
                        for K in range(12):
                            b.step("T", lambda e, p=p, t=t, K=K, pb=pb: e.matmul(
                                pool[pb][:],
                                wqun_bf[gb][:, K * 256 + p * 128:K * 256 + p * 128 + 128],
                                c_qT[:, K * NT + t * 512:K * NT + t * 512 + 512],
                                start=(K == 0), stop=(K == 11)),
                                r=[(id(wqun_bf), gb), ("c_qT", K, t)],
                                w=[("pool", pb)], tok=(K == 11),
                                reg_r=([(id(wqun_bf), gb)] + cq_keys) if K == 11 else None)
                        b.step("V", lambda e, p=p, t=t, pb=pb: e.tensor_copy(
                            q_nopeT[p][:, t * 512:t * 512 + 512], pool[pb][:]),
                            r=[("pool", pb)], w=[("qn", p, t)])
                for t in range(2):
                    pb = nxt_pool()
                    for K in range(12):
                        b.step("T", lambda e, t=t, K=K, pb=pb: e.matmul(
                            pool[pb][:],
                            wqur_bf[gb][:, K * 128:(K + 1) * 128],
                            c_qT[:, K * NT + t * 512:K * NT + t * 512 + 512],
                            start=(K == 0), stop=(K == 11)),
                            r=[(id(wqur_bf), gb), ("c_qT", K, t)],
                            w=[("pool", pb)], tok=(K == 11),
                            reg_r=([(id(wqur_bf), gb)] + cq_keys) if K == 11 else None)
                    b.step("V", lambda e, t=t, pb=pb: e.tensor_copy(
                        rope_tmp[:, t * 512:t * 512 + 512], pool[pb][:]),
                        r=[("pool", pb)], w=[("rope_tmp", t)])
                rt_keys = [("rope_tmp", 0), ("rope_tmp", 1)]
                for (dsta, srca) in [(0, 32), (32, 0), (64, 96), (96, 64)]:
                    neg = (dsta in (0, 64))
                    b.step("V", lambda e, dsta=dsta, srca=srca, neg=neg:
                           (e.tensor_scalar_mul(rot[dsta:dsta + 32, :],
                                                rope_tmp[srca:srca + 32, :], -1.0)
                            if neg else
                            e.tensor_copy(rot[dsta:dsta + 32, :],
                                          rope_tmp[srca:srca + 32, :])),
                           r=rt_keys, w=[("rot", dsta)])
                rot_keys = [("rot", a) for a in (0, 32, 64, 96)]
                b.step("V", lambda e: e.tensor_mul(prA[:], rope_tmp[:], cosq_sb[:]),
                       r=rt_keys + [d_key(cosq_sb)], w=["prA"])
                b.step("V", lambda e: e.tensor_mul(prB[:], rot[:], sinq_sb[:]),
                       r=rot_keys + [d_key(sinq_sb)], w=["prB"])
                for p in range(2):
                    b.step("V", lambda e, p=p: e.tensor_add(
                        qrx[p][0:64, :], prA[p * 64:p * 64 + 64, :],
                        prB[p * 64:p * 64 + 64, :]),
                        r=["prA", "prB"], w=[("qrx", p)])
                for p in range(2):
                    for s4 in range(4):
                        pb = nxt_pool()
                        for K in range(4):
                            b.step("T", lambda e, p=p, s4=s4, K=K, pb=pb: e.matmul(
                                pool[pb][:],
                                wukT_bf[gb][:, K * 256 + p * 128:K * 256 + p * 128 + 128],
                                c_kvT[:, K * SEQ + s4 * 512:K * SEQ + s4 * 512 + 512],
                                start=(K == 0), stop=(K == 3)),
                                r=[(id(wukT_bf), gb)] + (ckv_keys if K == 0 else []),
                                w=[("pool", pb)], tok=(K == 3),
                                reg_r=([(id(wukT_bf), gb)] + ckv_keys) if K == 3 else None)
                        b.step("V", lambda e, p=p, s4=s4, pb=pb: e.tensor_copy(
                            kn[p][:, s4 * 512:s4 * 512 + 512], pool[pb][:]),
                            r=[("pool", pb)], w=[("kn", p, s4)])
                for st in range(16):
                    pb = nxt_pool()
                    for K in range(4):
                        b.step("T", lambda e, st=st, K=K, pb=pb: e.matmul(
                            pool[pb][:, 0:256],
                            c_kvT[:, K * SEQ + st * 128:K * SEQ + st * 128 + 128],
                            wuvT_bf[gb][:, K * 256:(K + 1) * 256],
                            start=(K == 0), stop=(K == 3)),
                            r=[(id(wuvT_bf), gb)] + (ckv_keys if K == 0 else []),
                            w=[("pool", pb)], tok=(K == 3),
                            reg_r=([(id(wuvT_bf), gb)] + ckv_keys) if K == 3 else None)
                    b.step("V", lambda e, st=st, pb=pb: e.tensor_copy(
                        v_grp[:, st * 256:st * 256 + 256], pool[pb][:, 0:256]),
                        r=[("pool", pb)], w=[("v", st)])

                kn_keys = lambda p: [("kn", p, s4) for s4 in range(4)]
                for p in range(2):
                    h_abs = grp * 2 + p
                    for g in range(2):
                        for s in range(4):
                            qcol = g * 512 + s * 128
                            for w in range(NW[g]):
                                pb = nxt_pool()
                                b.step("T", lambda e, p=p, qcol=qcol, w=w, pb=pb: e.matmul(
                                    pool[pb][:], q_nopeT[p][:, qcol:qcol + 128],
                                    kn[p][:, w * 512:w * 512 + 512],
                                    start=True, stop=False),
                                    r=[("qn", p, g), ("kn", p, w)],
                                    w=[("pool", pb)], tok=False)
                                b.step("T", lambda e, p=p, qcol=qcol, w=w, pb=pb: e.matmul(
                                    pool[pb][:], qrx[p][0:64, qcol:qcol + 128],
                                    k_ropeX[0:64, w * 512:w * 512 + 512],
                                    start=False, stop=True),
                                    r=[("qrx", p), "krx_rope"],
                                    w=[("pool", pb)], tok=True,
                                    reg_r=[("qn", p, g), ("kn", p, w), ("qrx", p), "krx_rope"])
                                if w == 0:
                                    b.step("V", lambda e, pb=pb: e.reduce_max(
                                        mcol[:], pool[pb][:], axis=AX.X),
                                        r=[("pool", pb)], w=["mcol"])
                                else:
                                    b.step("V", lambda e, pb=pb: e.reduce_max(
                                        mtmp[:], pool[pb][:], axis=AX.X),
                                        r=[("pool", pb)], w=["mtmp"])
                                    b.step("V", lambda e: e.tensor_max(
                                        mcol[:], mcol[:], mtmp[:]),
                                        r=["mcol", "mtmp"], w=["mcol"])
                            b.step("V", lambda e: e.tensor_scalar_mul(mneg[:], mcol[:], -1.0),
                                   r=["mcol"], w=["mneg"])
                            b.step("T", lambda e, s=s: e.transpose(
                                ps_sum[0:1, s * 128:s * 128 + 128], mneg[:], ident_f[:]),
                                r=["mneg", d_key(ident_f)], w=[("ps_sum",)])
                            b.step("V", lambda e, p=p, qcol=qcol, s=s: e.tensor_copy(
                                qrx[p][64:65, qcol:qcol + 128],
                                ps_sum[0:1, s * 128:s * 128 + 128]),
                                r=[("ps_sum",), ("qrx_rows", p)], w=[("qrx_nm", p, g)])
                        nch = NCH[g]
                        d0 = DIAG0[g]
                        qsl = slice(g * 512, g * 512 + 512)

                        def emit_ST(c):
                            bank = c % 2
                            b.step("T", lambda e, p=p, c=c, bank=bank: e.matmul(
                                ps_st[bank][:], kn[p][:, c * 128:c * 128 + 128],
                                q_nopeT[p][:, qsl], start=True, stop=False),
                                r=[("kn", p, c // 4), ("qn", p, g)],
                                w=[("st", bank)], tok=False)
                            dd = c - d0
                            last = not (0 <= dd < 4)
                            b.step("T", lambda e, p=p, c=c, bank=bank, last=last: e.matmul(
                                ps_st[bank][:], k_ropeX[:, c * 128:c * 128 + 128],
                                qrx[p][0:67, qsl], start=False, stop=last),
                                r=krx_keys + [("qrx", p), ("qrx_nm", p, g), ("qrx_rows", p)],
                                w=[("st", bank)], tok=last,
                                reg_r=([("kn", p, c // 4), ("qn", p, g), ("qrx", p),
                                        ("qrx_nm", p, g), ("qrx_rows", p)] + krx_keys)
                                if last else None)
                            if not last:
                                b.step("T", lambda e, bank=bank, dd=dd: e.matmul(
                                    ps_st[bank][:], ident_bf[:],
                                    tri_sb[:, dd * 512:dd * 512 + 512],
                                    start=False, stop=True),
                                    r=[d_key(ident_bf), d_key(tri_sb)],
                                    w=[("st", bank)], tok=True,
                                    reg_r=[("kn", p, c // 4), ("qn", p, g), ("qrx", p),
                                           ("qrx_nm", p, g), ("qrx_rows", p),
                                           d_key(ident_bf), d_key(tri_sb)] + krx_keys)

                        emit_ST(0)
                        for c in range(nch):
                            bank = c % 2
                            if c + 1 < nch:
                                emit_ST(c + 1)
                            b.step("S", lambda e, c=c, bank=bank: e.activation(
                                PT[bank][:], ps_st[bank][:], AF.Exp, scale=SCALE),
                                r=[("st", bank)], w=[("PT", bank)])
                            b.step("T", lambda e, p=p, c=c, bank=bank: e.matmul(
                                ps_out[:], v_grp[:, c * 256 + p * 128:c * 256 + p * 128 + 128],
                                PT[bank][:], start=(c == 0), stop=(c == nch - 1)),
                                r=[("v", c), ("PT", bank)],
                                w=[("ps_out",)], tok=False)
                            b.step("T", lambda e, c=c, bank=bank: e.matmul(
                                ps_sum[0:1, :], ones_col[:], PT[bank][:],
                                start=(c == 0), stop=(c == nch - 1)),
                                r=["ones_col", ("PT", bank)],
                                w=([("ps_sum",), ("ps_out",)] if c == nch - 1
                                   else [("ps_sum",)]),
                                tok=True,
                                reg_r=["ones_col", ("PT", bank), ("v", c)])
                        b.step("V", lambda e: e.reciprocal(recip_row[:], ps_sum[0:1, :]),
                               r=[("ps_sum",)], w=["recip_row"])
                        b.step("T", lambda e: e.matmul(
                            ps_st[0][:], ones_row_f[:], recip_row[:],
                            start=True, stop=True),
                            r=["ones_row_f", "recip_row"], w=[("st", 0)])
                        b.step("V", lambda e: e.tensor_copy(rb_sb[:], ps_st[0][:]),
                               r=[("st", 0)], w=["rb_sb"])
                        b.step("V", lambda e, h_abs=h_abs, g=g: e.tensor_tensor(
                            oT[:, h_abs * NT + g * 512:h_abs * NT + g * 512 + 512],
                            ps_out[:], rb_sb[:], ALU.mult),
                            r=[("ps_out",), "rb_sb"], w=[("oT", h_abs, g)])
            if dbg:
                b.step("Y", lambda e: e.dma_start(out=dbg_t["dbg_oT"][:], in_=oT[:]),
                       r=[("oT", hh, gg) for hh in range(32) for gg in range(2)], dma="dbg")
            b.barrier()
        es_ckv.close()

        oT_keys = [("oT", hh, gg) for hh in range(32) for gg in range(2)]
        with ExitStack() as ph:
            pe = ph.enter_context
            wout_cb = [pe(nc.sbuf_tensor(f"wout{i}", [128, 32 * 128], BF16)) for i in range(4)]
            sa_sb = [pe(nc.sbuf_tensor(f"sa_sb{i}", [128, 512], BF16)) for i in range(4)]
            psC = [pe(nc.psum_tensor(f"psC{i}", [128, 512], F32)) for i in range(8)]

            def wout_dma(ob):
                b.step("G", lambda e, ob=ob: e.dma_start(
                    out=wout_cb[ob % 4][:].rearrange("p (k n) -> p k n", k=32),
                    in_=wout_t[ob].rearrange("k p n -> p k n")),
                    w=[("wout_cb", ob % 4)], dma=f"wo{ob % 4}")

            wout_dma(0)
            wout_dma(1)
            for og in range(16):
                if 2 * og + 2 < 32:
                    wout_dma(2 * og + 2)
                if 2 * og + 3 < 32:
                    wout_dma(2 * og + 3)
                for K in range(32):
                    for bb in range(2):
                        ob = og * 2 + bb
                        for th in range(2):
                            pb = (og % 2) * 4 + bb * 2 + th
                            b.step("T", lambda e, K=K, ob=ob, th=th, pb=pb: e.matmul(
                                psC[pb][:], wout_cb[ob % 4][:, K * 128:(K + 1) * 128],
                                oT[:, K * NT + th * 512:K * NT + th * 512 + 512],
                                start=(K == 0), stop=(K == 31)),
                                r=[("wout_cb", ob % 4)] + (oT_keys if K == 0 else []),
                                w=[("psC", pb)], tok=(K == 31),
                                reg_r=([("wout_cb", ob % 4)] + oT_keys) if K == 31 else None)
                for bb in range(2):
                    ob = og * 2 + bb
                    for th in range(2):
                        pb = (og % 2) * 4 + bb * 2 + th
                        sb = bb * 2 + th
                        b.step("V", lambda e, pb=pb, sb=sb: e.tensor_copy(
                            sa_sb[sb][:], psC[pb][:]),
                            r=[("psC", pb)], w=[("sa_sb", sb)])
                        b.step("Y", lambda e, ob=ob, th=th, sb=sb: e.dma_start(
                            out=sa_scr[ob * 128:(ob + 1) * 128, th * 512:th * 512 + 512],
                            in_=sa_sb[sb][:]),
                            r=[("sa_sb", sb)], w=[("sa_scr", ob, th)], dma=f"sa{sb}")
            b.barrier()
        es_oT.close()

        sa_keys = [("sa_scr", ob, th) for ob in range(32) for th in range(2)]
        with ExitStack() as ph:
            pe = ph.enter_context
            qcb_i = [pe(nc.sbuf_tensor(f"qcb_i{i}", [128, 32 * 128], I32)) for i in range(2)]
            qcb_bf = [pe(nc.sbuf_tensor(f"qcb_bf{i}", [128, 4096], BF16)) for i in range(2)]
            s_ob = [pe(nc.sbuf_tensor(f"s_ob{i}", [128, 4096], F32)) for i in range(2)]
            w_grp = pe(nc.sbuf_tensor("w_grp", [128, 4 * 4096], BF16))
            xnj = [pe(nc.sbuf_tensor(f"xnj{i}", [128, 512], BF16)) for i in range(2)]
            s_sl = [pe(nc.sbuf_tensor(f"s_sl{i}", [128, 2048], BF16)) for i in range(2)]
            x_oh = [pe(nc.sbuf_tensor(f"x_oh{i}", [128, 2048], F32)) for i in range(2)]
            g_tile = [pe(nc.sbuf_tensor(f"g_tile{i}", [128, 512], F32)) for i in range(2)]
            m_tile = [pe(nc.sbuf_tensor(f"m_tile{i}", [128, 512], BF16)) for i in range(2)]
            out_sb = [pe(nc.sbuf_tensor(f"out_sb{i}", [128, 2048], F32)) for i in range(2)]
            psD = [pe(nc.psum_tensor(f"psD{i}", [128, 512], F32)) for i in range(4)]
            ps_trD = pe(nc.psum_tensor("ps_trD", [128, 512], BF16))

            cs_keys = [("colsumsT", tc) for tc in range(8)]

            def dequant(og, bb):
                gbk = og * 4 + bb
                sl = bb % 2
                b.step("Y", lambda e, gbk=gbk, sl=sl: e.dma_start(
                    out=qcb_i[sl][:].rearrange("p (j n) -> p j n", j=32),
                    in_=q_tiled[gbk].rearrange("j p n -> p j n")),
                    w=[("qcb_i", sl)], dma=f"qcb{sl}")
                b.step("G", lambda e, gbk=gbk, sl=sl: e.dma_start(
                    out=s_ob[sl][:].rearrange("p (j n) -> p j n", j=32),
                    in_=scalesT_d[:, :, gbk * 128:(gbk + 1) * 128].broadcast_to([128, 32, 128])),
                    w=[("s_ob", sl)], dma=f"sob{sl}")
                b.step("S", lambda e, sl=sl: e.copy(qcb_bf[sl][:], qcb_i[sl][:]),
                       r=[("qcb_i", sl)], w=[("qcb_bf", sl)])
                b.step("G", lambda e, bb=bb, sl=sl: e.tensor_mul(
                    w_grp[:, bb * 4096:(bb + 1) * 4096], qcb_bf[sl][:], s_ob[sl][:]),
                    r=[("qcb_bf", sl), ("s_ob", sl)], w=[("w_grp", bb)])

            for og in range(8):
                for bb in range(4):
                    dequant(og, bb)
                for th in range(2):
                    tb = th % 2
                    b.step("Y", lambda e, og=og, th=th, tb=tb: e.dma_start(
                        out=s_sl[tb][:].rearrange("p (b n) -> p b n", b=4),
                        in_=sa_scr[:].rearrange("(a p) n -> p a n", p=128)
                        [:, og * 4:og * 4 + 4, th * 512:th * 512 + 512]),
                        r=sa_keys, w=[("s_sl", tb)], dma=f"sl{tb}")
                    b.step("Y", lambda e, og=og, th=th, tb=tb: e.dma_start(
                        out=x_oh[tb][:].rearrange("p (t n) -> p t n", t=4),
                        in_=x_own[:].rearrange("(t p) n -> p t n", p=128)
                        [:, th * 4:th * 4 + 4, og * 512:og * 512 + 512]),
                        w=[("x_oh", tb)], dma=f"xh{tb}")
                    for bb in range(4):
                        b.step("T", lambda e, og=og, bb=bb, th=th: e.matmul(
                            psD[bb][:], zT_sb[:, (og * 4 + bb) * 128:(og * 4 + bb) * 128 + 128],
                            colsumsT[0:32, th * 512:th * 512 + 512],
                            start=True, stop=False),
                            r=[d_key(zT_sb)] + cs_keys,
                            w=[("psD", bb)], tok=False)
                    for j in range(32):
                        jb = j % 2
                        b.step("Y", lambda e, j=j, th=th, jb=jb: e.dma_start(
                            out=xnj[jb][:],
                            in_=xn_scr[j * 128:(j + 1) * 128, th * 512:th * 512 + 512]),
                            r=["xn_scr"], w=[("xnj", jb)], dma=f"xj{jb}")
                        for bb in range(4):
                            qlast = (bb == 3)
                            b.step("T", lambda e, j=j, bb=bb, jb=jb: e.matmul(
                                psD[bb][:], w_grp[:, bb * 4096 + j * 128:bb * 4096 + j * 128 + 128],
                                xnj[jb][:], start=False, stop=(j == 31)),
                                r=[("w_grp", bb), ("xnj", jb)],
                                w=([("psD", q) for q in range(4)] if (j == 31 and qlast)
                                   else [("psD", bb)]),
                                tok=qlast,
                                reg_r=([("xnj", jb), d_key(zT_sb)] + cs_keys
                                       + [("w_grp", q) for q in range(4)]) if qlast else None)
                    for bb in range(4):
                        gbk = og * 4 + bb
                        b.step("S", lambda e, bb=bb, gbk=gbk: e.activation(
                            g_tile[bb % 2][:], psD[bb][:], AF.Silu,
                            bias=bias_sb[:, gbk:gbk + 1]),
                            r=[("psD", bb), d_key(bias_sb)], w=[("g_tile", bb % 2)])
                        b.step("V", lambda e, bb=bb, tb=tb: e.tensor_tensor(
                            m_tile[bb % 2][:], g_tile[bb % 2][:],
                            s_sl[tb][:, bb * 512:bb * 512 + 512], ALU.mult),
                            r=[("g_tile", bb % 2), ("s_sl", tb)], w=[("m_tile", bb % 2)])
                        for ts in range(4):
                            b.step("T", lambda e, bb=bb, ts=ts: e.transpose(
                                ps_trD[:, ts * 128:ts * 128 + 128],
                                m_tile[bb % 2][:, ts * 128:ts * 128 + 128], ident_bf[:]),
                                r=[("m_tile", bb % 2), d_key(ident_bf)],
                                w=[("ps_trD",)], tok=(ts == 3),
                                reg_r=[("m_tile", bb % 2), d_key(ident_bf)] if ts == 3 else None)
                        for ts in range(4):
                            b.step("V", lambda e, bb=bb, ts=ts, tb=tb: e.tensor_tensor(
                                out_sb[tb][:, ts * 512 + bb * 128:ts * 512 + bb * 128 + 128],
                                ps_trD[:, ts * 128:ts * 128 + 128],
                                x_oh[tb][:, ts * 512 + bb * 128:ts * 512 + bb * 128 + 128],
                                ALU.add),
                                r=[("ps_trD",), ("x_oh", tb)], w=[("out_sb", tb)])
                    for ts in range(4):
                        b.step("Y", lambda e, og=og, th=th, ts=ts, tb=tb: e.dma_start(
                            out=out_own[th * 512 + ts * 128:th * 512 + ts * 128 + 128,
                                        og * 512:og * 512 + 512],
                            in_=out_sb[tb][:, ts * 512:ts * 512 + 512]),
                            r=[("out_sb", tb)],
                            w=[("out", og, th, ts)], dma=f"ot{tb}")
            def fin(e):
                for sm in b.dsems.values():
                    if sm.n > 0:
                        e.wait_ge(sm.h, sm.n)
                return e.nop()
            b.step("Y", fin, tok=False)

    return nc, dbg_t


def d_key(t):
    return ("sb", t.name if hasattr(t, "name") else id(t))



def kernel(**inputs):
    dbg = int(os.environ.get("KDBG", "0"))
    x = np.asarray(inputs["L0_attn_hidden_states"], np.float32)
    shared = _prep_shared(inputs)
    nc, _ = build_nc(dbg=dbg)

    in_maps = []
    for core in range(8):
        bq, h = divmod(core, 2)
        m = dict(shared)
        m.update(_prep_core(x, bq, h))
        im = {
            "x_full": m["x_full"], "x_own": m["x_own"],
            "cos_k": m["cos_k"], "sin_k": m["sin_k"],
            "cos_q": m["cos_q"], "sin_q": m["sin_q"],
            "maskrows": m["maskrows"], "q_rowsinit": m["q_rowsinit"],
            "consts_f": m["consts_f"], "ident_bf": m["ident_bf"],
            "ident_f": m["ident_f"], "tri": m["tri"], "g_ind": m["g_ind"],
            "wkv_t": m["wkv_t"], "wkr_t": m["wkr_t"], "wqd_t": m["wqd_t"],
            "wqun_t": m["wqun_t"], "wqur_t": m["wqur_t"],
            "wukT_t": m["wukT_t"], "wuvT_t": m["wuvT_t"],
            "wout_t": m["wout_t"], "q_tiled": m["q_tiled"],
            "scalesT": m["scalesT"], "zT": m["zT"], "bias_r": m["bias_r"],
        }
        in_maps.append(im)

    res = run_bass_kernel_spmd(nc, in_maps, core_ids=list(range(8)))
    out = np.empty((4, SEQ, D), np.float32)
    for core in range(8):
        bq, h = divmod(core, 2)
        out[bq][_own_rows(h)] = res.results[core]["out_own"]
    return out


# version 02
# speedup vs baseline: 1.2536x; 1.2536x over previous
import os
import math
import numpy as np
import ml_dtypes
from contextlib import ExitStack

import concourse.bass as bass
import concourse.mybir as mybir
from concourse.bass_utils import run_bass_kernel_spmd

F32 = mybir.dt.float32
BF16 = mybir.dt.bfloat16
I32 = mybir.dt.int32
AF = mybir.ActivationFunctionType
AX = mybir.AxisListType
ALU = mybir.AluOpType
BF = ml_dtypes.bfloat16

D = 4096
SEQ = 2048
NT = 1024
H = 32
QL = 1536
KVL = 512
DN = 128
DR = 64
DV = 128
NEG = -1.0e9
SCALE = float(np.float32(1.0) / np.float32(np.sqrt(np.float32(DN + DR))))
EPS = 1.0e-5

NCH = (8, 16)
NW = (2, 4)
DIAG0 = (4, 12)



def _yarn_tables():
    i = np.arange(0, DR, 2, dtype=np.float32)
    freqs = (np.float32(1.0) / (np.float32(10000.0) ** (i / np.float32(DR)))).astype(np.float32)
    wavelengths = (np.float32(2.0 * np.pi) / freqs).astype(np.float32)
    ramp = np.clip((wavelengths / np.float32(4096.0) - np.float32(1.0)) / np.float32(31.0),
                   0.0, 1.0).astype(np.float32)
    scale = (np.float32(1.0) - ramp + ramp * np.float32(40.0)).astype(np.float32)
    return (freqs / scale).astype(np.float32)


def _cos_sin(pos, reps):
    inv = _yarn_tables()
    f = pos.astype(np.float32)[None, :] * inv[:, None]
    c = np.cos(f).astype(np.float32)
    s = np.sin(f).astype(np.float32)
    return np.tile(c, (reps, 1)), np.tile(s, (reps, 1))


def _perm_blocks(h):
    if h == 0:
        return [4, 5, 6, 7, 0, 1, 2, 3, 8, 9, 10, 11, 12, 13, 14, 15]
    return [0, 1, 2, 3, 4, 5, 6, 7, 12, 13, 14, 15, 8, 9, 10, 11]


def _own_rows(h):
    if h == 0:
        return np.r_[0:512, 1536:2048]
    return np.r_[512:1536]


def _maskrows(h):
    m = np.zeros((2, SEQ), np.float32)
    blk = lambda c: slice(c * 128, (c + 1) * 128)
    if h == 0:
        for c in range(0, 4):
            m[0, blk(c)] = NEG
        for c in range(8, 16):
            m[0, blk(c)] = NEG
    else:
        for c in range(8, 16):
            m[0, blk(c)] = NEG
        for c in range(8, 12):
            m[1, blk(c)] = NEG
    return m


def _tri_consts():
    out = np.zeros((128, 4 * 512), np.float32)
    st = np.arange(128)[:, None]
    for d in range(4):
        sq = np.arange(512)[None, :]
        m = np.where(sq < 128 * d, NEG, 0.0)
        inblk = (sq >= 128 * d) & (sq < 128 * (d + 1))
        m = np.where(inblk & (st > (sq - 128 * d)), NEG, m)
        out[:, d * 512:(d + 1) * 512] = m
    return out.astype(BF)


def _tile2(w, kb, mb):
    K, M = w.shape
    t = w.reshape(K // kb, kb, M // mb, mb).transpose(2, 0, 1, 3)
    return np.ascontiguousarray(t)


def _prep_shared(inp):
    d = {}
    d["wkv_t"] = np.ascontiguousarray(
        inp["L0_attn_w_kv_down"].reshape(32, 128, KVL))
    d["wkr_t"] = np.ascontiguousarray(
        inp["L0_attn_w_k_rope"].reshape(32, 128, DR))
    d["wqd_t"] = _tile2(inp["L0_attn_w_q_down"], 128, 128)
    wqun = _tile2(inp["L0_attn_w_q_up_nope"], 128, 256)
    d["wqun_t"] = wqun
    d["wqur_t"] = _tile2(inp["L0_attn_w_q_up_rope"], 128, 128)
    wukT = np.ascontiguousarray(inp["L0_attn_w_uk"].T)
    wuvT = np.ascontiguousarray(inp["L0_attn_w_uv"].T)
    d["wukT_t"] = _tile2(wukT, 128, 256)
    d["wuvT_t"] = _tile2(wuvT, 128, 256)
    d["wout_t"] = _tile2(inp["L0_attn_w_out"], 128, 128)
    q = inp["L0_ffn_qweight"].reshape(4096, 4096)
    qT = q.T
    d["q_tiled"] = _tile2(qT, 128, 128).astype(np.int32)
    d["scalesT"] = np.ascontiguousarray(
        inp["L0_ffn_scales"].reshape(4096, 32).T)[None]
    d["zT"] = np.ascontiguousarray(
        inp["L0_ffn_scaled_zeros"].reshape(4096, 32).T)
    d["bias_r"] = np.ascontiguousarray(
        inp["L0_ffn_bias"].reshape(32, 128).T)
    d["ident_bf"] = np.eye(128, dtype=BF)
    d["ident_f"] = np.eye(128, dtype=np.float32)
    d["tri"] = _tri_consts()
    g = np.zeros((32, 128, 32), BF)
    for j in range(32):
        g[j, :, j] = BF(1.0)
    d["g_ind"] = g
    cf = np.zeros((128, 8), np.float32)
    cf[:, 0] = EPS
    cf[:, 1] = SCALE
    cf[:, 2] = 0.0
    cf[:, 3] = 1.0
    cf[:, 4] = -1.0
    d["consts_f"] = cf
    qi = np.zeros((3, NT), np.float32)
    qi[1, 0:512] = 1.0
    qi[2, 512:NT] = 1.0
    d["q_rowsinit"] = qi
    return d


def _prep_core(x, b, h):
    d = {}
    perm = _perm_blocks(h)
    rows = np.concatenate([np.arange(c * 128, (c + 1) * 128) for c in perm])
    d["x_full"] = np.ascontiguousarray(x[b][rows])
    own = _own_rows(h)
    d["x_own"] = np.ascontiguousarray(x[b][own])
    pos_k = rows.astype(np.float32)
    ck, sk = _cos_sin(pos_k, 2)
    d["cos_k"], d["sin_k"] = ck, sk
    pos_q = own.astype(np.float32)
    cq, sq = _cos_sin(pos_q, 4)
    d["cos_q"], d["sin_q"] = cq, sq
    d["maskrows"] = _maskrows(h)
    return d



class _Sem:
    def __init__(self, h, name):
        self.h, self.name, self.n = h, name, 0


class _B:

    SEC = {"T": "tensor", "V": "vector", "S": "scalar", "G": "gpsimd", "Y": "sync"}

    def __init__(self, nc, blk, sems, sem_pool):
        self.nc, self.blk = nc, blk
        self.sems = sems
        self.sem_pool = list(sem_pool)
        self.dsems = {}
        self.last_w = {}
        self.readers = {}
        self.waited = {k: {} for k in self.SEC}
        self.pending = {}

    def dsem(self, stream):
        if stream not in self.dsems:
            self.dsems[stream] = _Sem(self.sem_pool.pop(), f"d_{stream}")
        return self.dsems[stream]

    def _need(self, r, w):
        need = {}
        for k in r:
            for s, v in self.last_w.get(k, {}).items():
                if need.get(s, 0) < v:
                    need[s] = v
        for k in w:
            for dd in (self.last_w.get(k, {}), self.readers.get(k, {})):
                for s, v in dd.items():
                    if need.get(s, 0) < v:
                        need[s] = v
        return need

    def step(self, ename, fn, r=(), w=(), tok=True, dma=None, reg_r=None):
        sem = self.dsem(dma) if dma else self.sems[ename]
        need = self._need(r, w)
        wt = self.waited[ename]
        waits = []
        for sm, v in sorted(need.items(), key=lambda kv: kv[0].name):
            if wt.get(sm, 0) < v:
                if sm is self.sems.get(ename) and v > sm.n:
                    raise AssertionError(
                        f"own-sem deadlock {ename} wait {v} > issued {sm.n}")
                waits.append((sm, v))
                wt[sm] = v
        if not tok:
            self.pending.setdefault(ename, []).append((waits, fn, None, 0))
            return None
        step_n = 16 if dma else 1
        sem.n += step_n
        self.pending.setdefault(ename, []).append((waits, fn, sem, step_n))
        t = (sem, sem.n)
        for k in w:
            self.last_w[k] = {sem: sem.n}
            self.readers[k] = {}
        for k in (reg_r if reg_r is not None else r):
            d = self.readers.setdefault(k, {})
            if d.get(sem, 0) < sem.n:
                d[sem] = sem.n
        return t

    def flush_pending(self):
        for ename in list(self.SEC):
            q = self.pending.get(ename)
            if not q:
                continue
            self.pending[ename] = []

            def body(e, q=q):
                for waits, fn, sem, step_n in q:
                    for sm, v in waits:
                        e.wait_ge(sm.h, v)
                    inst = fn(e)
                    if sem is not None:
                        inst.then_inc(sem.h, step_n)

            getattr(self.blk, self.SEC[ename])(body)

    def flush(self, stream, keys):
        sem = self.dsem(stream)
        for k in keys:
            self.last_w[k] = {sem: sem.n}
            self.readers[k] = {}

    def barrier(self):
        self.flush_pending()
        ph = self.sems["ph"]
        target = ph.n + 5
        ph.n = target
        allsems = [self.sems[k] for k in ("T", "V", "S", "G")] + list(self.dsems.values())

        for en in self.SEC:
            def body(e, en=en):
                for s in allsems:
                    if s.n > 0 and self.waited[en].get(s, 0) < s.n:
                        e.wait_ge(s.h, s.n)
                        self.waited[en][s] = s.n
                e.nop().then_inc(ph.h, 1)
                e.wait_ge(ph.h, target)
                self.waited[en][ph] = target
            getattr(self.blk, self.SEC[en])(body)
        self.last_w = {}
        self.readers = {}


def build_nc(dbg=0):
    nc = bass.Bass("TRN2", target_bir_lowering=False)
    dt_in = {}

    def din(name, shape, dtype):
        t = nc.dram_tensor(name, list(shape), dtype, kind="ExternalInput").ap()
        dt_in[name] = t
        return t

    x_full = din("x_full", (SEQ, D), F32)
    x_own = din("x_own", (NT, D), F32)
    cos_k = din("cos_k", (64, SEQ), F32)
    sin_k = din("sin_k", (64, SEQ), F32)
    cos_q = din("cos_q", (128, NT), F32)
    sin_q = din("sin_q", (128, NT), F32)
    maskrows = din("maskrows", (2, SEQ), F32)
    q_rowsinit = din("q_rowsinit", (3, NT), F32)
    consts_f = din("consts_f", (128, 8), F32)
    ident_bf_d = din("ident_bf", (128, 128), BF16)
    ident_f_d = din("ident_f", (128, 128), F32)
    tri_d = din("tri", (128, 2048), BF16)
    g_ind_d = din("g_ind", (32, 128, 32), BF16)
    wkv_t = din("wkv_t", (32, 128, KVL), F32)
    wkr_t = din("wkr_t", (32, 128, DR), F32)
    wqd_t = din("wqd_t", (12, 32, 128, 128), F32)
    wqun_t = din("wqun_t", (16, 12, 128, 256), F32)
    wqur_t = din("wqur_t", (16, 12, 128, 128), F32)
    wukT_t = din("wukT_t", (16, 4, 128, 256), F32)
    wuvT_t = din("wuvT_t", (16, 4, 128, 256), F32)
    wout_t = din("wout_t", (32, 32, 128, 128), F32)
    q_tiled = din("q_tiled", (32, 32, 128, 128), I32)
    scalesT_d = din("scalesT", (1, 32, 4096), F32)
    zT_d = din("zT", (32, 4096), F32)
    bias_d = din("bias_r", (128, 32), F32)

    out_own = nc.dram_tensor("out_own", [NT, D], F32, kind="ExternalOutput").ap()
    dbg_t = {}
    if dbg:
        for nme, shp, dt in [
            ("dbg_xnT", (128, 32 * NT), BF16), ("dbg_cq", (128, 12 * NT), BF16),
            ("dbg_ckv", (128, 4 * SEQ), BF16), ("dbg_krx", (67, SEQ), BF16),
            ("dbg_oT", (128, 32 * NT), BF16), ("dbg_cs", (32, NT), BF16),
        ]:
            dbg_t[nme] = nc.dram_tensor(nme, list(shp), dt, kind="ExternalOutput").ap()

    xn_scr = nc.dram_tensor("xn_scr", [D, NT], BF16, kind="Internal").ap()
    sa_scr = nc.dram_tensor("sa_scr", [D, NT], BF16, kind="Internal").ap()

    with ExitStack() as top:
        e = top.enter_context
        ident_bf = e(nc.sbuf_tensor("ident_bf_s", [128, 128], BF16))
        ident_f = e(nc.sbuf_tensor("ident_f_s", [128, 128], F32))
        tri_sb = e(nc.sbuf_tensor("tri_s", [128, 2048], BF16))
        cf_sb = e(nc.sbuf_tensor("consts_f_s", [128, 8], F32))
        ones_col = e(nc.sbuf_tensor("ones_col", [128, 1], BF16))
        ones_row_f = e(nc.sbuf_tensor("ones_row_f", [1, 128], F32))
        cosq_sb = e(nc.sbuf_tensor("cosq", [128, NT], F32))
        sinq_sb = e(nc.sbuf_tensor("sinq", [128, NT], F32))
        k_ropeX = e(nc.sbuf_tensor("k_ropeX", [67, SEQ], BF16))
        colsumsT = e(nc.sbuf_tensor("colsumsT", [32, NT], BF16))
        zT_sb = e(nc.sbuf_tensor("zT_s", [32, 4096], BF16))
        bias_sb = e(nc.sbuf_tensor("bias_s", [128, 32], F32))
        g_ind_sb = e(nc.sbuf_tensor("g_ind_s", [128, 32 * 32], BF16))

        nc.const_aps.register(F32, EPS, cf_sb[:, 0:1])
        nc.const_aps.register(F32, SCALE, cf_sb[:, 1:2])
        nc.const_aps.register(F32, 0.0, cf_sb[:, 2:3])
        nc.const_aps.register(F32, 1.0, cf_sb[:, 3:4])
        nc.const_aps.register(F32, -1.0, cf_sb[:, 4:5])

        sems = {}
        for nm in ("T", "V", "S", "G", "Y", "ph"):
            sems[nm] = _Sem(e(nc.semaphore(f"sem_{nm}")), nm)
        sem_pool = [e(nc.semaphore(f"sem_d{i}")) for i in range(48)]
        blk = e(nc.Block())
        b = _B(nc, blk, sems, sem_pool)

        es_ckv = top.enter_context(ExitStack())
        c_kvT = es_ckv.enter_context(nc.sbuf_tensor("c_kvT", [128, 4 * SEQ], BF16))
        c_qT = es_ckv.enter_context(nc.sbuf_tensor("c_qT", [128, 12 * NT], BF16))
        es_kr = top.enter_context(ExitStack())
        k_rope_raw = es_kr.enter_context(
            nc.sbuf_tensor("k_rope_raw", [64, SEQ], F32))
        cosk_sb = es_kr.enter_context(nc.sbuf_tensor("cosk", [64, SEQ], F32))
        sink_sb = es_kr.enter_context(nc.sbuf_tensor("sink", [64, SEQ], F32))

        for dst, src in [(ident_bf, ident_bf_d), (ident_f, ident_f_d),
                         (tri_sb, tri_d), (cf_sb, consts_f),
                         (cosk_sb, cos_k), (sink_sb, sin_k),
                         (cosq_sb, cos_q), (sinq_sb, sin_q),
                         (bias_sb, bias_d)]:
            b.step("Y", lambda e, d=dst, s=src: e.dma_start(out=d[:], in_=s[:]),
                   w=[d_key(dst)], dma="c0")
        b.step("G", lambda e: e.dma_start(out=zT_sb[:], in_=zT_d[:]),
               w=[d_key(zT_sb)], dma="c1")
        b.step("Y", lambda e: e.dma_start(
            out=g_ind_sb[:].rearrange("p (j c) -> p j c", j=32),
            in_=g_ind_d[:].rearrange("j p c -> p j c")),
            w=[d_key(g_ind_sb)], dma="c0")
        b.flush("c0", [d_key(t) for t in (ident_bf, ident_f, tri_sb, cf_sb,
                                          cosk_sb, sink_sb, cosq_sb, sinq_sb,
                                          bias_sb, g_ind_sb)])
        b.step("V", lambda e: e.memset(ones_col[:], 1.0), w=["ones_col"])
        b.step("V", lambda e: e.memset(ones_row_f[:], 1.0), w=["ones_row_f"])
        b.barrier()

        with ExitStack() as ph:
            pe = ph.enter_context
            wkv_bf = pe(nc.sbuf_tensor("wkv_bf", [128, 32 * 512], BF16))
            wkr_bf = pe(nc.sbuf_tensor("wkr_bf", [128, 32 * 64], BF16))
            xa = [pe(nc.sbuf_tensor(f"xa{i}", [128, D], F32)) for i in range(2)]
            xnb = [pe(nc.sbuf_tensor(f"xnb{i}", [128, D], BF16)) for i in range(2)]
            xnT_tile = [pe(nc.sbuf_tensor(f"xnTt{i}", [128, D], BF16)) for i in range(2)]
            stats = pe(nc.sbuf_tensor("stats", [128, 48], F32))
            aggr = pe(nc.sbuf_tensor("aggr", [128, 2], F32))
            sd = pe(nc.sbuf_tensor("sd", [128, 1], F32))
            rstd = pe(nc.sbuf_tensor("rstd", [128, 1], F32))
            ps_tA = [pe(nc.psum_tensor(f"ps_tA{i}", [128, 1024], BF16)) for i in range(2)]
            ps_kv = pe(nc.psum_tensor("ps_kv", [128, 512], F32))
            ps_kr = pe(nc.psum_tensor("ps_kr", [64, 128], F32))

            b.step("G", lambda e: e.dma_start(
                out=wkv_bf[:].rearrange("p (k m) -> p k m", k=32),
                in_=wkv_t[:].rearrange("k p m -> p k m")),
                w=["wkv_bf"], dma="wkv")
            b.step("G", lambda e: e.dma_start(
                out=wkr_bf[:].rearrange("p (k m) -> p k m", k=32),
                in_=wkr_t[:].rearrange("k p m -> p k m")),
                w=["wkr_bf"], dma="wkr")

            def ln_tile(src_ap, t, bi, dst_tile):
                b.step("Y", lambda e: e.dma_start(
                    out=xa[bi][:], in_=src_ap[t * 128:(t + 1) * 128, :]),
                    r=[], w=[("xa", bi)], dma=f"xa{bi}")
                for u in range(8):
                    b.step("V", lambda e, u=u: e.bn_stats(
                        stats[:, u * 6:(u + 1) * 6], xa[bi][:, u * 512:(u + 1) * 512]),
                        r=[("xa", bi)], w=[("stats", u)])
                b.step("V", lambda e: e.bn_aggr(aggr[:], stats[:, 0:48]),
                       r=[("stats", u) for u in range(8)], w=["aggr"])
                b.step("S", lambda e: e.activation(sd[:], aggr[:, 1:2], AF.Sqrt, bias=EPS),
                       r=["aggr"], w=["sd"])
                b.step("V", lambda e: e.reciprocal(rstd[:], sd[:]), r=["sd"], w=["rstd"])
                b.step("V", lambda e: e.tensor_scalar(
                    out=xnb[bi][:], in0=xa[bi][:], scalar1=aggr[:, 0:1],
                    scalar2=rstd[:], op0=ALU.subtract, op1=ALU.mult),
                    r=[("xa", bi), "aggr", "rstd"], w=[("xnb", bi)])
                for q in range(4):
                    pbank = q % 2
                    for u in range(8):
                        c = q * 8 + u
                        b.step("T", lambda e, c=c, u=u, pbank=pbank: e.transpose(
                            ps_tA[pbank][:, u * 128:(u + 1) * 128],
                            xnb[bi][:, c * 128:(c + 1) * 128], ident_bf[:]),
                            r=[("xnb", bi), d_key(ident_bf)],
                            w=[("ps_tA", pbank)],
                            tok=(u == 7),
                            reg_r=[("xnb", bi), d_key(ident_bf)] if u == 7 else None)
                    b.step("V", lambda e, q=q, pbank=pbank: e.tensor_copy(
                        dst_tile[:].rearrange("p (c n) -> p c n", c=32)[:, q * 8:(q + 1) * 8, :],
                        ps_tA[pbank][:].rearrange("p (c n) -> p c n", c=8)),
                        r=[("ps_tA", pbank)], w=[("dstq", id(dst_tile), q)])

            for t in range(16):
                bi = t % 2
                ln_tile(x_full, t, bi, xnT_tile[bi])
                for m in range(4):
                    for K in range(32):
                        b.step("T", lambda e, m=m, K=K, bi=bi: e.matmul(
                            ps_kv[:, m * 128:(m + 1) * 128],
                            wkv_bf[:, K * 512 + m * 128:K * 512 + (m + 1) * 128],
                            xnT_tile[bi][:, K * 128:(K + 1) * 128],
                            start=(K == 0), stop=(K == 31)),
                            r=["wkv_bf", ("dstq", id(xnT_tile[bi]), K // 8)],
                            w=[("ps_kv", m)], tok=(K == 31),
                            reg_r=(["wkv_bf"] + [("dstq", id(xnT_tile[bi]), qq) for qq in range(4)]) if K == 31 else None)
                b.step("V", lambda e, t=t: e.tensor_copy(
                    c_kvT[:].rearrange("p (m n) -> p m n", m=4)[:, :, t * 128:(t + 1) * 128],
                    ps_kv[:].rearrange("p (m n) -> p m n", m=4)),
                    r=[("ps_kv", m) for m in range(4)], w=[("c_kvT", t)])
                for K in range(32):
                    b.step("T", lambda e, K=K, bi=bi: e.matmul(
                        ps_kr[:, :],
                        wkr_bf[:, K * 64:(K + 1) * 64],
                        xnT_tile[bi][:, K * 128:(K + 1) * 128],
                        start=(K == 0), stop=(K == 31)),
                        r=["wkr_bf", ("dstq", id(xnT_tile[bi]), K // 8)],
                        w=[("ps_kr",)], tok=(K == 31),
                        reg_r=(["wkr_bf"] + [("dstq", id(xnT_tile[bi]), qq) for qq in range(4)]) if K == 31 else None)
                b.step("V", lambda e, t=t: e.tensor_copy(
                    k_rope_raw[:, t * 128:(t + 1) * 128], ps_kr[:, :]),
                    r=[("ps_kr",)], w=[("k_rope_raw", t)])
            b.barrier()

        with ExitStack() as ph:
            pe = ph.enter_context
            rotk = pe(nc.sbuf_tensor("rotk", [64, SEQ], F32))
            pA = pe(nc.sbuf_tensor("ropeA", [64, SEQ], F32))
            pB = pe(nc.sbuf_tensor("ropeB", [64, SEQ], F32))
            b.step("V", lambda e: e.tensor_scalar_mul(
                rotk[0:32, :], k_rope_raw[32:64, :], -1.0),
                r=["k_rope_raw"], w=["rotk_a"])
            b.step("V", lambda e: e.tensor_copy(rotk[32:64, :], k_rope_raw[0:32, :]),
                   r=["k_rope_raw"], w=["rotk_b"])
            b.step("V", lambda e: e.tensor_mul(pA[:], k_rope_raw[0:64, :], cosk_sb[:]),
                   r=["k_rope_raw", d_key(cosk_sb)], w=["pA"])
            b.step("V", lambda e: e.tensor_mul(pB[:], rotk[:], sink_sb[:]),
                   r=["rotk_a", "rotk_b", d_key(sink_sb)], w=["pB"])
            b.step("V", lambda e: e.tensor_add(k_ropeX[0:64, :], pA[:], pB[:]),
                   r=["pA", "pB"], w=["krx_rope"])
            b.step("V", lambda e: e.memset(k_ropeX[64:65, :], 1.0), w=["krx_ones"])
            b.step("G", lambda e: e.dma_start(out=k_ropeX[65:67, :], in_=maskrows[:]),
                   w=["krx_mask"], dma="mk")
            if dbg:
                b.step("Y", lambda e: e.dma_start(out=dbg_t["dbg_krx"][:], in_=k_ropeX[:]),
                       r=["krx_rope", "krx_ones", "krx_mask"], dma="dbg")
            b.barrier()
        es_kr.close()

        es_xn = top.enter_context(ExitStack())
        xnT_own = es_xn.enter_context(
            nc.sbuf_tensor("xnT_own", [128, 32 * NT], BF16, side="right"))
        with ExitStack() as ph:
            pe = ph.enter_context
            xa = [pe(nc.sbuf_tensor(f"xaq{i}", [128, D], F32)) for i in range(2)]
            xnb = [pe(nc.sbuf_tensor(f"xnbq{i}", [128, D], BF16)) for i in range(2)]
            stats = pe(nc.sbuf_tensor("statsq", [128, 48], F32))
            aggr = pe(nc.sbuf_tensor("aggrq", [128, 2], F32))
            sd = pe(nc.sbuf_tensor("sdq", [128, 1], F32))
            rstd = pe(nc.sbuf_tensor("rstdq", [128, 1], F32))
            ps_tA = [pe(nc.psum_tensor(f"ps_tAq{i}", [128, 1024], BF16)) for i in range(2)]

            for t in range(8):
                bi = t % 2
                b.step("Y", lambda e, t=t, bi=bi: e.dma_start(
                    out=xa[bi][:], in_=x_own[t * 128:(t + 1) * 128, :]),
                    w=[("xa", bi)], dma=f"xa{bi}")
                for u in range(8):
                    b.step("V", lambda e, u=u, bi=bi: e.bn_stats(
                        stats[:, u * 6:(u + 1) * 6], xa[bi][:, u * 512:(u + 1) * 512]),
                        r=[("xa", bi)], w=[("stats", u)])
                b.step("V", lambda e: e.bn_aggr(aggr[:], stats[:, 0:48]),
                       r=[("stats", u) for u in range(8)], w=["aggr"])
                b.step("S", lambda e: e.activation(sd[:], aggr[:, 1:2], AF.Sqrt, bias=EPS),
                       r=["aggr"], w=["sd"])
                b.step("V", lambda e: e.reciprocal(rstd[:], sd[:]), r=["sd"], w=["rstd"])
                b.step("V", lambda e, bi=bi: e.tensor_scalar(
                    out=xnb[bi][:], in0=xa[bi][:], scalar1=aggr[:, 0:1],
                    scalar2=rstd[:], op0=ALU.subtract, op1=ALU.mult),
                    r=[("xa", bi), "aggr", "rstd"], w=[("xnb", bi)])
                for q in range(4):
                    pbank = q % 2
                    for u in range(8):
                        c = q * 8 + u
                        b.step("T", lambda e, c=c, u=u, pbank=pbank, bi=bi: e.transpose(
                            ps_tA[pbank][:, u * 128:(u + 1) * 128],
                            xnb[bi][:, c * 128:(c + 1) * 128], ident_bf[:]),
                            r=[("xnb", bi), d_key(ident_bf)],
                            w=[("ps_tA", pbank)], tok=(u == 7),
                            reg_r=[("xnb", bi), d_key(ident_bf)] if u == 7 else None)
                    b.step("V", lambda e, q=q, t=t, pbank=pbank: e.tensor_copy(
                        xnT_own[:].rearrange("p (c n) -> p c n", c=32)
                        [:, q * 8:(q + 1) * 8, t * 128:(t + 1) * 128],
                        ps_tA[pbank][:].rearrange("p (c n) -> p c n", c=8)),
                        r=[("ps_tA", pbank)], w=[("xnT_own", t, q)])
            b.barrier()

        with ExitStack() as ph:
            pe = ph.enter_context
            wqd_cb = [pe(nc.sbuf_tensor(f"wqd_cb{i}", [128, 32 * 128], BF16)) for i in range(2)]
            colsq = pe(nc.sbuf_tensor("colsq", [128, 8 * 32], BF16))
            ps_cq = [pe(nc.psum_tensor(f"ps_cq{i}", [128, 512], F32)) for i in range(4)]
            ps_cs = pe(nc.psum_tensor("ps_cs", [128, 32], F32))
            ps_csT = pe(nc.psum_tensor("ps_csT", [32, 128], BF16))

            xn_own_keys = [("xnT_own", t, q) for t in range(8) for q in range(4)]
            for m in range(12):
                b.step("G", lambda e, m=m: e.dma_start(
                    out=wqd_cb[m % 2][:].rearrange("p (k n) -> p k n", k=32),
                    in_=wqd_t[m].rearrange("k p n -> p k n")),
                    w=[("wqd_cb", m % 2)], dma=f"wqd{m % 2}")
                for t in range(2):
                    pb = (m % 2) * 2 + t
                    for K in range(32):
                        b.step("T", lambda e, m=m, t=t, K=K, pb=pb: e.matmul(
                            ps_cq[pb][:],
                            wqd_cb[m % 2][:, K * 128:(K + 1) * 128],
                            xnT_own[:, K * NT + t * 512:K * NT + t * 512 + 512],
                            start=(K == 0), stop=(K == 31)),
                            r=([("wqd_cb", m % 2)] + (xn_own_keys if K == 0 else [])),
                            w=[("ps_cq", pb)], tok=(K == 31),
                            reg_r=([("wqd_cb", m % 2)] + xn_own_keys) if K == 31 else None)
                    b.step("V", lambda e, m=m, t=t, pb=pb: e.tensor_copy(
                        c_qT[:, m * NT + t * 512:m * NT + t * 512 + 512], ps_cq[pb][:]),
                        r=[("ps_cq", pb)], w=[("c_qT", m, t)])
            for tc in range(8):
                for j in range(32):
                    b.step("T", lambda e, tc=tc, j=j: e.matmul(
                        ps_cs[:], xnT_own[:, j * NT + tc * 128:j * NT + (tc) * 128 + 128],
                        g_ind_sb[:, j * 32:(j + 1) * 32],
                        start=(j == 0), stop=(j == 31)),
                        r=([d_key(g_ind_sb)] + (xn_own_keys if j == 0 else [])),
                        w=[("ps_cs",)], tok=(j == 31),
                        reg_r=([d_key(g_ind_sb)] + xn_own_keys) if j == 31 else None)
                b.step("V", lambda e, tc=tc: e.tensor_copy(
                    colsq[:, tc * 32:(tc + 1) * 32], ps_cs[:]),
                    r=[("ps_cs",)], w=[("colsq", tc)])
                b.step("T", lambda e, tc=tc: e.transpose(
                    ps_csT[:], colsq[:, tc * 32:(tc + 1) * 32], ident_bf[:]),
                    r=[("colsq", tc), d_key(ident_bf)], w=[("ps_csT",)])
                b.step("V", lambda e, tc=tc: e.tensor_copy(
                    colsumsT[0:32, tc * 128:(tc + 1) * 128], ps_csT[:]),
                    r=[("ps_csT",)], w=[("colsumsT", tc)])
            b.step("Y", lambda e: e.dma_start(
                out=xn_scr[:].rearrange("(j p) n -> p j n", p=128),
                in_=xnT_own[:].rearrange("p (j n) -> p j n", j=32)),
                r=xn_own_keys, w=["xn_scr"], dma="xs")
            if dbg:
                b.step("Y", lambda e: e.dma_start(out=dbg_t["dbg_xnT"][:], in_=xnT_own[:]),
                       r=xn_own_keys, dma="dbg")
                b.step("Y", lambda e: e.dma_start(out=dbg_t["dbg_cq"][:], in_=c_qT[:]),
                       r=[("c_qT", m, t) for m in range(12) for t in range(2)], dma="dbg")
                b.step("Y", lambda e: e.dma_start(out=dbg_t["dbg_ckv"][:], in_=c_kvT[:]),
                       r=[("c_kvT", t) for t in range(16)], dma="dbg")
                b.step("Y", lambda e: e.dma_start(out=dbg_t["dbg_cs"][:], in_=colsumsT[:]),
                       r=[("colsumsT", tc) for tc in range(8)], dma="dbg")
            b.barrier()
        es_xn.close()

        es_oT = top.enter_context(ExitStack())
        oT = es_oT.enter_context(
            nc.sbuf_tensor("oT", [128, 32 * NT], BF16, side="right"))
        with ExitStack() as ph:
            pe = ph.enter_context
            wqun_bf = [pe(nc.sbuf_tensor(f"wqun{i}", [128, 12 * 256], BF16)) for i in range(2)]
            wqur_bf = [pe(nc.sbuf_tensor(f"wqur{i}", [128, 12 * 128], BF16)) for i in range(2)]
            wukT_bf = [pe(nc.sbuf_tensor(f"wukT{i}", [128, 4 * 256], BF16)) for i in range(2)]
            wuvT_bf = [pe(nc.sbuf_tensor(f"wuvT{i}", [128, 4 * 256], BF16)) for i in range(2)]
            q_nopeT = [pe(nc.sbuf_tensor(f"qnope{i}", [128, NT], BF16)) for i in range(2)]
            qrx = [pe(nc.sbuf_tensor(f"qrx{i}", [67, NT], BF16)) for i in range(2)]
            kn = [pe(nc.sbuf_tensor(f"kn{i}", [128, SEQ], BF16)) for i in range(2)]
            v_grp = pe(nc.sbuf_tensor("v_grp", [128, 16 * 256], BF16))
            rope_tmp = pe(nc.sbuf_tensor("rope_tmp", [128, NT], F32))
            rot = pe(nc.sbuf_tensor("rot", [128, NT], F32))
            prA = pe(nc.sbuf_tensor("prA", [128, NT], F32))
            prB = pe(nc.sbuf_tensor("prB", [128, NT], F32))
            PT = [pe(nc.sbuf_tensor(f"PT{i}", [128, 512], BF16)) for i in range(2)]
            mcol = pe(nc.sbuf_tensor("mcol", [128, 1], F32))
            mtmp = pe(nc.sbuf_tensor("mtmp", [128, 1], F32))
            mneg = pe(nc.sbuf_tensor("mneg", [128, 1], F32))
            recip_row = pe(nc.sbuf_tensor("recip_row", [1, 512], F32))
            rb_sb = pe(nc.sbuf_tensor("rb_sb", [128, 512], F32))
            pool = [pe(nc.psum_tensor(f"poolB{i}", [128, 512], F32)) for i in range(4)]
            ps_st = [pe(nc.psum_tensor(f"ps_stB{i}", [128, 512], F32)) for i in range(2)]
            ps_out = pe(nc.psum_tensor("ps_outB", [128, 512], F32))
            ps_sum = pe(nc.psum_tensor("ps_sumB", [1, 512], F32))

            for p in range(2):
                b.step("G", lambda e, p=p: e.dma_start(out=qrx[p][64:67, :], in_=q_rowsinit[:]),
                       w=[("qrx_rows", p)], dma=f"qi{p}")

            ckv_keys = [("c_kvT", t) for t in range(16)]
            cq_keys = [("c_qT", m, t) for m in range(12) for t in range(2)]
            krx_keys = ["krx_rope", "krx_ones", "krx_mask"]

            def wdma(grp):
                gb = grp % 2
                for (buf, src, ncols) in [(wqun_bf, wqun_t, 256), (wqur_bf, wqur_t, 128),
                                          (wukT_bf, wukT_t, 256), (wuvT_bf, wuvT_t, 256)]:
                    b.step("G", lambda e, buf=buf, src=src, grp=grp: e.dma_start(
                        out=buf[gb][:].rearrange("p (k n) -> p k n", k=src.shape[1]),
                        in_=src[grp].rearrange("k p n -> p k n")),
                        w=[(id(buf), gb)], dma=f"wg{gb}")
                b.flush(f"wg{gb}", [(id(buf), gb) for buf in
                                    (wqun_bf, wqur_bf, wukT_bf, wuvT_bf)])

            wdma(0)
            pool_i = [0]

            def nxt_pool():
                i = pool_i[0] % 4
                pool_i[0] += 1
                return i

            for grp in range(16):
                gb = grp % 2
                if grp + 1 < 16:
                    wdma(grp + 1)
                for p in range(2):
                    for t in range(2):
                        pb = nxt_pool()
                        for K in range(12):
                            b.step("T", lambda e, p=p, t=t, K=K, pb=pb, gb=gb: e.matmul(
                                pool[pb][:],
                                wqun_bf[gb][:, K * 256 + p * 128:K * 256 + p * 128 + 128],
                                c_qT[:, K * NT + t * 512:K * NT + t * 512 + 512],
                                start=(K == 0), stop=(K == 11)),
                                r=[(id(wqun_bf), gb), ("c_qT", K, t)],
                                w=[("pool", pb)], tok=(K == 11),
                                reg_r=([(id(wqun_bf), gb)] + cq_keys) if K == 11 else None)
                        b.step("V", lambda e, p=p, t=t, pb=pb: e.tensor_copy(
                            q_nopeT[p][:, t * 512:t * 512 + 512], pool[pb][:]),
                            r=[("pool", pb)], w=[("qn", p, t)])
                for t in range(2):
                    pb = nxt_pool()
                    for K in range(12):
                        b.step("T", lambda e, t=t, K=K, pb=pb, gb=gb: e.matmul(
                            pool[pb][:],
                            wqur_bf[gb][:, K * 128:(K + 1) * 128],
                            c_qT[:, K * NT + t * 512:K * NT + t * 512 + 512],
                            start=(K == 0), stop=(K == 11)),
                            r=[(id(wqur_bf), gb), ("c_qT", K, t)],
                            w=[("pool", pb)], tok=(K == 11),
                            reg_r=([(id(wqur_bf), gb)] + cq_keys) if K == 11 else None)
                    b.step("V", lambda e, t=t, pb=pb: e.tensor_copy(
                        rope_tmp[:, t * 512:t * 512 + 512], pool[pb][:]),
                        r=[("pool", pb)], w=[("rope_tmp", t)])
                rt_keys = [("rope_tmp", 0), ("rope_tmp", 1)]
                for (dsta, srca) in [(0, 32), (32, 0), (64, 96), (96, 64)]:
                    neg = (dsta in (0, 64))
                    b.step("V", lambda e, dsta=dsta, srca=srca, neg=neg:
                           (e.tensor_scalar_mul(rot[dsta:dsta + 32, :],
                                                rope_tmp[srca:srca + 32, :], -1.0)
                            if neg else
                            e.tensor_copy(rot[dsta:dsta + 32, :],
                                          rope_tmp[srca:srca + 32, :])),
                           r=rt_keys, w=[("rot", dsta)])
                rot_keys = [("rot", a) for a in (0, 32, 64, 96)]
                b.step("V", lambda e: e.tensor_mul(prA[:], rope_tmp[:], cosq_sb[:]),
                       r=rt_keys + [d_key(cosq_sb)], w=["prA"])
                b.step("V", lambda e: e.tensor_mul(prB[:], rot[:], sinq_sb[:]),
                       r=rot_keys + [d_key(sinq_sb)], w=["prB"])
                for p in range(2):
                    b.step("V", lambda e, p=p: e.tensor_add(
                        qrx[p][0:64, :], prA[p * 64:p * 64 + 64, :],
                        prB[p * 64:p * 64 + 64, :]),
                        r=["prA", "prB"], w=[("qrx", p)])
                for p in range(2):
                    for s4 in range(4):
                        pb = nxt_pool()
                        for K in range(4):
                            b.step("T", lambda e, p=p, s4=s4, K=K, pb=pb, gb=gb: e.matmul(
                                pool[pb][:],
                                wukT_bf[gb][:, K * 256 + p * 128:K * 256 + p * 128 + 128],
                                c_kvT[:, K * SEQ + s4 * 512:K * SEQ + s4 * 512 + 512],
                                start=(K == 0), stop=(K == 3)),
                                r=[(id(wukT_bf), gb)] + (ckv_keys if K == 0 else []),
                                w=[("pool", pb)], tok=(K == 3),
                                reg_r=([(id(wukT_bf), gb)] + ckv_keys) if K == 3 else None)
                        b.step("V", lambda e, p=p, s4=s4, pb=pb: e.tensor_copy(
                            kn[p][:, s4 * 512:s4 * 512 + 512], pool[pb][:]),
                            r=[("pool", pb)], w=[("kn", p, s4)])
                for st in range(16):
                    pb = nxt_pool()
                    for K in range(4):
                        b.step("T", lambda e, st=st, K=K, pb=pb, gb=gb: e.matmul(
                            pool[pb][:, 0:256],
                            c_kvT[:, K * SEQ + st * 128:K * SEQ + st * 128 + 128],
                            wuvT_bf[gb][:, K * 256:(K + 1) * 256],
                            start=(K == 0), stop=(K == 3)),
                            r=[(id(wuvT_bf), gb)] + (ckv_keys if K == 0 else []),
                            w=[("pool", pb)], tok=(K == 3),
                            reg_r=([(id(wuvT_bf), gb)] + ckv_keys) if K == 3 else None)
                    b.step("V", lambda e, st=st, pb=pb: e.tensor_copy(
                        v_grp[:, st * 256:st * 256 + 256], pool[pb][:, 0:256]),
                        r=[("pool", pb)], w=[("v", st)])

                kn_keys = lambda p: [("kn", p, s4) for s4 in range(4)]
                for p in range(2):
                    h_abs = grp * 2 + p
                    for g in range(2):
                        for s in range(4):
                            qcol = g * 512 + s * 128
                            for w in range(NW[g]):
                                pb = nxt_pool()
                                b.step("T", lambda e, p=p, qcol=qcol, w=w, pb=pb: e.matmul(
                                    pool[pb][:], q_nopeT[p][:, qcol:qcol + 128],
                                    kn[p][:, w * 512:w * 512 + 512],
                                    start=True, stop=False),
                                    r=[("qn", p, g), ("kn", p, w)],
                                    w=[("pool", pb)], tok=False)
                                b.step("T", lambda e, p=p, qcol=qcol, w=w, pb=pb: e.matmul(
                                    pool[pb][:], qrx[p][0:64, qcol:qcol + 128],
                                    k_ropeX[0:64, w * 512:w * 512 + 512],
                                    start=False, stop=True),
                                    r=[("qrx", p), "krx_rope"],
                                    w=[("pool", pb)], tok=True,
                                    reg_r=[("qn", p, g), ("kn", p, w), ("qrx", p), "krx_rope"])
                                if w == 0:
                                    b.step("V", lambda e, pb=pb: e.reduce_max(
                                        mcol[:], pool[pb][:], axis=AX.X),
                                        r=[("pool", pb)], w=["mcol"])
                                else:
                                    b.step("V", lambda e, pb=pb: e.reduce_max(
                                        mtmp[:], pool[pb][:], axis=AX.X),
                                        r=[("pool", pb)], w=["mtmp"])
                                    b.step("V", lambda e: e.tensor_max(
                                        mcol[:], mcol[:], mtmp[:]),
                                        r=["mcol", "mtmp"], w=["mcol"])
                            b.step("V", lambda e: e.tensor_scalar_mul(mneg[:], mcol[:], -1.0),
                                   r=["mcol"], w=["mneg"])
                            b.step("T", lambda e, s=s: e.transpose(
                                ps_sum[0:1, s * 128:s * 128 + 128], mneg[:], ident_f[:]),
                                r=["mneg", d_key(ident_f)], w=[("ps_sum",)])
                            b.step("V", lambda e, p=p, qcol=qcol, s=s: e.tensor_copy(
                                qrx[p][64:65, qcol:qcol + 128],
                                ps_sum[0:1, s * 128:s * 128 + 128]),
                                r=[("ps_sum",), ("qrx_rows", p)], w=[("qrx_nm", p, g)])
                        nch = NCH[g]
                        d0 = DIAG0[g]
                        qsl = slice(g * 512, g * 512 + 512)

                        def emit_ST(c, qsl=qsl, d0=d0, p=p, g=g):
                            bank = c % 2
                            b.step("T", lambda e, p=p, c=c, bank=bank, qsl=qsl: e.matmul(
                                ps_st[bank][:], kn[p][:, c * 128:c * 128 + 128],
                                q_nopeT[p][:, qsl], start=True, stop=False),
                                r=[("kn", p, c // 4), ("qn", p, g)],
                                w=[("st", bank)], tok=False)
                            dd = c - d0
                            last = not (0 <= dd < 4)
                            b.step("T", lambda e, p=p, c=c, bank=bank, last=last, qsl=qsl: e.matmul(
                                ps_st[bank][:], k_ropeX[:, c * 128:c * 128 + 128],
                                qrx[p][0:67, qsl], start=False, stop=last),
                                r=krx_keys + [("qrx", p), ("qrx_nm", p, g), ("qrx_rows", p)],
                                w=[("st", bank)], tok=last,
                                reg_r=([("kn", p, c // 4), ("qn", p, g), ("qrx", p),
                                        ("qrx_nm", p, g), ("qrx_rows", p)] + krx_keys)
                                if last else None)
                            if not last:
                                b.step("T", lambda e, bank=bank, dd=dd: e.matmul(
                                    ps_st[bank][:], ident_bf[:],
                                    tri_sb[:, dd * 512:dd * 512 + 512],
                                    start=False, stop=True),
                                    r=[d_key(ident_bf), d_key(tri_sb)],
                                    w=[("st", bank)], tok=True,
                                    reg_r=[("kn", p, c // 4), ("qn", p, g), ("qrx", p),
                                           ("qrx_nm", p, g), ("qrx_rows", p),
                                           d_key(ident_bf), d_key(tri_sb)] + krx_keys)

                        emit_ST(0)
                        for c in range(nch):
                            bank = c % 2
                            if c + 1 < nch:
                                emit_ST(c + 1)
                            b.step("S", lambda e, c=c, bank=bank: e.activation(
                                PT[bank][:], ps_st[bank][:], AF.Exp, scale=SCALE),
                                r=[("st", bank)], w=[("PT", bank)])
                            b.step("T", lambda e, p=p, c=c, bank=bank, nch=nch: e.matmul(
                                ps_out[:], v_grp[:, c * 256 + p * 128:c * 256 + p * 128 + 128],
                                PT[bank][:], start=(c == 0), stop=(c == nch - 1)),
                                r=[("v", c), ("PT", bank)],
                                w=[("ps_out",)], tok=False)
                            b.step("T", lambda e, c=c, bank=bank, nch=nch: e.matmul(
                                ps_sum[0:1, :], ones_col[:], PT[bank][:],
                                start=(c == 0), stop=(c == nch - 1)),
                                r=["ones_col", ("PT", bank)],
                                w=([("ps_sum",), ("ps_out",)] if c == nch - 1
                                   else [("ps_sum",)]),
                                tok=True,
                                reg_r=["ones_col", ("PT", bank), ("v", c)])
                        b.step("V", lambda e: e.reciprocal(recip_row[:], ps_sum[0:1, :]),
                               r=[("ps_sum",)], w=["recip_row"])
                        b.step("T", lambda e: e.matmul(
                            ps_st[0][:], ones_row_f[:], recip_row[:],
                            start=True, stop=True),
                            r=["ones_row_f", "recip_row"], w=[("st", 0)])
                        b.step("V", lambda e: e.tensor_copy(rb_sb[:], ps_st[0][:]),
                               r=[("st", 0)], w=["rb_sb"])
                        b.step("V", lambda e, h_abs=h_abs, g=g: e.tensor_tensor(
                            oT[:, h_abs * NT + g * 512:h_abs * NT + g * 512 + 512],
                            ps_out[:], rb_sb[:], ALU.mult),
                            r=[("ps_out",), "rb_sb"], w=[("oT", h_abs, g)])
            if dbg:
                b.step("Y", lambda e: e.dma_start(out=dbg_t["dbg_oT"][:], in_=oT[:]),
                       r=[("oT", hh, gg) for hh in range(32) for gg in range(2)], dma="dbg")
            b.barrier()
        es_ckv.close()

        oT_keys = [("oT", hh, gg) for hh in range(32) for gg in range(2)]
        with ExitStack() as ph:
            pe = ph.enter_context
            wout_cb = [pe(nc.sbuf_tensor(f"wout{i}", [128, 32 * 128], BF16)) for i in range(4)]
            sa_sb = [pe(nc.sbuf_tensor(f"sa_sb{i}", [128, 512], BF16)) for i in range(4)]
            psC = [pe(nc.psum_tensor(f"psC{i}", [128, 512], F32)) for i in range(8)]

            def wout_dma(ob):
                b.step("G", lambda e, ob=ob: e.dma_start(
                    out=wout_cb[ob % 4][:].rearrange("p (k n) -> p k n", k=32),
                    in_=wout_t[ob].rearrange("k p n -> p k n")),
                    w=[("wout_cb", ob % 4)], dma=f"wo{ob % 4}")

            wout_dma(0)
            wout_dma(1)
            for og in range(16):
                if 2 * og + 2 < 32:
                    wout_dma(2 * og + 2)
                if 2 * og + 3 < 32:
                    wout_dma(2 * og + 3)
                for K in range(32):
                    for bb in range(2):
                        ob = og * 2 + bb
                        for th in range(2):
                            pb = (og % 2) * 4 + bb * 2 + th
                            b.step("T", lambda e, K=K, ob=ob, th=th, pb=pb: e.matmul(
                                psC[pb][:], wout_cb[ob % 4][:, K * 128:(K + 1) * 128],
                                oT[:, K * NT + th * 512:K * NT + th * 512 + 512],
                                start=(K == 0), stop=(K == 31)),
                                r=[("wout_cb", ob % 4)] + (oT_keys if K == 0 else []),
                                w=[("psC", pb)], tok=(K == 31),
                                reg_r=([("wout_cb", ob % 4)] + oT_keys) if K == 31 else None)
                for bb in range(2):
                    ob = og * 2 + bb
                    for th in range(2):
                        pb = (og % 2) * 4 + bb * 2 + th
                        sb = bb * 2 + th
                        b.step("V", lambda e, pb=pb, sb=sb: e.tensor_copy(
                            sa_sb[sb][:], psC[pb][:]),
                            r=[("psC", pb)], w=[("sa_sb", sb)])
                        b.step("Y", lambda e, ob=ob, th=th, sb=sb: e.dma_start(
                            out=sa_scr[ob * 128:(ob + 1) * 128, th * 512:th * 512 + 512],
                            in_=sa_sb[sb][:]),
                            r=[("sa_sb", sb)], w=[("sa_scr", ob, th)], dma=f"sa{sb}")
            b.barrier()
        es_oT.close()

        sa_keys = [("sa_scr", ob, th) for ob in range(32) for th in range(2)]
        with ExitStack() as ph:
            pe = ph.enter_context
            qcb_i = [pe(nc.sbuf_tensor(f"qcb_i{i}", [128, 32 * 128], I32)) for i in range(2)]
            qcb_bf = [pe(nc.sbuf_tensor(f"qcb_bf{i}", [128, 4096], BF16)) for i in range(2)]
            s_ob = [pe(nc.sbuf_tensor(f"s_ob{i}", [128, 4096], F32)) for i in range(2)]
            w_grp = pe(nc.sbuf_tensor("w_grp", [128, 4 * 4096], BF16))
            xnj = [pe(nc.sbuf_tensor(f"xnj{i}", [128, 512], BF16)) for i in range(2)]
            s_sl = [pe(nc.sbuf_tensor(f"s_sl{i}", [128, 2048], BF16)) for i in range(2)]
            x_oh = [pe(nc.sbuf_tensor(f"x_oh{i}", [128, 2048], F32)) for i in range(2)]
            g_tile = [pe(nc.sbuf_tensor(f"g_tile{i}", [128, 512], F32)) for i in range(2)]
            m_tile = [pe(nc.sbuf_tensor(f"m_tile{i}", [128, 512], BF16)) for i in range(2)]
            out_sb = [pe(nc.sbuf_tensor(f"out_sb{i}", [128, 2048], F32)) for i in range(2)]
            psD = [pe(nc.psum_tensor(f"psD{i}", [128, 512], F32)) for i in range(4)]
            ps_trD = pe(nc.psum_tensor("ps_trD", [128, 512], BF16))

            cs_keys = [("colsumsT", tc) for tc in range(8)]

            def dequant(og, bb):
                gbk = og * 4 + bb
                sl = bb % 2
                b.step("Y", lambda e, gbk=gbk, sl=sl: e.dma_start(
                    out=qcb_i[sl][:].rearrange("p (j n) -> p j n", j=32),
                    in_=q_tiled[gbk].rearrange("j p n -> p j n")),
                    w=[("qcb_i", sl)], dma=f"qcb{sl}")
                b.step("G", lambda e, gbk=gbk, sl=sl: e.dma_start(
                    out=s_ob[sl][:].rearrange("p (j n) -> p j n", j=32),
                    in_=scalesT_d[:, :, gbk * 128:(gbk + 1) * 128].broadcast_to([128, 32, 128])),
                    w=[("s_ob", sl)], dma=f"sob{sl}")
                b.step("S", lambda e, sl=sl: e.copy(qcb_bf[sl][:], qcb_i[sl][:]),
                       r=[("qcb_i", sl)], w=[("qcb_bf", sl)])
                b.step("G", lambda e, bb=bb, sl=sl: e.tensor_mul(
                    w_grp[:, bb * 4096:(bb + 1) * 4096], qcb_bf[sl][:], s_ob[sl][:]),
                    r=[("qcb_bf", sl), ("s_ob", sl)], w=[("w_grp", bb)])

            for og in range(8):
                for bb in range(4):
                    dequant(og, bb)
                for th in range(2):
                    tb = th % 2
                    b.step("Y", lambda e, og=og, th=th, tb=tb: e.dma_start(
                        out=s_sl[tb][:].rearrange("p (b n) -> p b n", b=4),
                        in_=sa_scr[:].rearrange("(a p) n -> p a n", p=128)
                        [:, og * 4:og * 4 + 4, th * 512:th * 512 + 512]),
                        r=sa_keys, w=[("s_sl", tb)], dma=f"sl{tb}")
                    b.step("Y", lambda e, og=og, th=th, tb=tb: e.dma_start(
                        out=x_oh[tb][:].rearrange("p (t n) -> p t n", t=4),
                        in_=x_own[:].rearrange("(t p) n -> p t n", p=128)
                        [:, th * 4:th * 4 + 4, og * 512:og * 512 + 512]),
                        w=[("x_oh", tb)], dma=f"xh{tb}")
                    for bb in range(4):
                        b.step("T", lambda e, og=og, bb=bb, th=th: e.matmul(
                            psD[bb][:], zT_sb[:, (og * 4 + bb) * 128:(og * 4 + bb) * 128 + 128],
                            colsumsT[0:32, th * 512:th * 512 + 512],
                            start=True, stop=False),
                            r=[d_key(zT_sb)] + cs_keys,
                            w=[("psD", bb)], tok=False)
                    for j in range(32):
                        jb = j % 2
                        b.step("Y", lambda e, j=j, th=th, jb=jb: e.dma_start(
                            out=xnj[jb][:],
                            in_=xn_scr[j * 128:(j + 1) * 128, th * 512:th * 512 + 512]),
                            r=["xn_scr"], w=[("xnj", jb)], dma=f"xj{jb}")
                        for bb in range(4):
                            qlast = (bb == 3)
                            b.step("T", lambda e, j=j, bb=bb, jb=jb: e.matmul(
                                psD[bb][:], w_grp[:, bb * 4096 + j * 128:bb * 4096 + j * 128 + 128],
                                xnj[jb][:], start=False, stop=(j == 31)),
                                r=[("w_grp", bb), ("xnj", jb)],
                                w=([("psD", q) for q in range(4)] if (j == 31 and qlast)
                                   else [("psD", bb)]),
                                tok=qlast,
                                reg_r=([("xnj", jb), d_key(zT_sb)] + cs_keys
                                       + [("w_grp", q) for q in range(4)]) if qlast else None)
                    for bb in range(4):
                        gbk = og * 4 + bb
                        b.step("S", lambda e, bb=bb, gbk=gbk: e.activation(
                            g_tile[bb % 2][:], psD[bb][:], AF.Silu,
                            bias=bias_sb[:, gbk:gbk + 1]),
                            r=[("psD", bb), d_key(bias_sb)], w=[("g_tile", bb % 2)])
                        b.step("V", lambda e, bb=bb, tb=tb: e.tensor_tensor(
                            m_tile[bb % 2][:], g_tile[bb % 2][:],
                            s_sl[tb][:, bb * 512:bb * 512 + 512], ALU.mult),
                            r=[("g_tile", bb % 2), ("s_sl", tb)], w=[("m_tile", bb % 2)])
                        for ts in range(4):
                            b.step("T", lambda e, bb=bb, ts=ts: e.transpose(
                                ps_trD[:, ts * 128:ts * 128 + 128],
                                m_tile[bb % 2][:, ts * 128:ts * 128 + 128], ident_bf[:]),
                                r=[("m_tile", bb % 2), d_key(ident_bf)],
                                w=[("ps_trD",)], tok=(ts == 3),
                                reg_r=[("m_tile", bb % 2), d_key(ident_bf)] if ts == 3 else None)
                        for ts in range(4):
                            b.step("V", lambda e, bb=bb, ts=ts, tb=tb: e.tensor_tensor(
                                out_sb[tb][:, ts * 512 + bb * 128:ts * 512 + bb * 128 + 128],
                                ps_trD[:, ts * 128:ts * 128 + 128],
                                x_oh[tb][:, ts * 512 + bb * 128:ts * 512 + bb * 128 + 128],
                                ALU.add),
                                r=[("ps_trD",), ("x_oh", tb)], w=[("out_sb", tb)])
                    for ts in range(4):
                        b.step("Y", lambda e, og=og, th=th, ts=ts, tb=tb: e.dma_start(
                            out=out_own[th * 512 + ts * 128:th * 512 + ts * 128 + 128,
                                        og * 512:og * 512 + 512],
                            in_=out_sb[tb][:, ts * 512:ts * 512 + 512]),
                            r=[("out_sb", tb)],
                            w=[("out", og, th, ts)], dma=f"ot{tb}")
            def fin(e):
                for sm in b.dsems.values():
                    if sm.n > 0:
                        e.wait_ge(sm.h, sm.n)
                return e.nop()
            b.step("Y", fin, tok=False)
            b.flush_pending()

    return nc, dbg_t


def d_key(t):
    return ("sb", t.name if hasattr(t, "name") else id(t))



def kernel(**inputs):
    dbg = int(os.environ.get("KDBG", "0"))
    x = np.asarray(inputs["L0_attn_hidden_states"], np.float32)
    shared = _prep_shared(inputs)
    nc, _ = build_nc(dbg=dbg)

    in_maps = []
    for core in range(8):
        bq, h = divmod(core, 2)
        m = dict(shared)
        m.update(_prep_core(x, bq, h))
        im = {
            "x_full": m["x_full"], "x_own": m["x_own"],
            "cos_k": m["cos_k"], "sin_k": m["sin_k"],
            "cos_q": m["cos_q"], "sin_q": m["sin_q"],
            "maskrows": m["maskrows"], "q_rowsinit": m["q_rowsinit"],
            "consts_f": m["consts_f"], "ident_bf": m["ident_bf"],
            "ident_f": m["ident_f"], "tri": m["tri"], "g_ind": m["g_ind"],
            "wkv_t": m["wkv_t"], "wkr_t": m["wkr_t"], "wqd_t": m["wqd_t"],
            "wqun_t": m["wqun_t"], "wqur_t": m["wqur_t"],
            "wukT_t": m["wukT_t"], "wuvT_t": m["wuvT_t"],
            "wout_t": m["wout_t"], "q_tiled": m["q_tiled"],
            "scalesT": m["scalesT"], "zT": m["zT"], "bias_r": m["bias_r"],
        }
        in_maps.append(im)

    res = run_bass_kernel_spmd(nc, in_maps, core_ids=list(range(8)))
    out = np.empty((4, SEQ, D), np.float32)
    for core in range(8):
        bq, h = divmod(core, 2)
        out[bq][_own_rows(h)] = res.results[core]["out_own"]
    return out


# version 03
# speedup vs baseline: 1.4233x; 1.1354x over previous
import os
import math
import numpy as np
import ml_dtypes
from contextlib import ExitStack

import concourse.bass as bass
import concourse.mybir as mybir
from concourse.bass_utils import run_bass_kernel_spmd

F32 = mybir.dt.float32
BF16 = mybir.dt.bfloat16
I32 = mybir.dt.int32
AF = mybir.ActivationFunctionType
AX = mybir.AxisListType
ALU = mybir.AluOpType
BF = ml_dtypes.bfloat16

D = 4096
SEQ = 2048
NT = 1024
H = 32
QL = 1536
KVL = 512
DN = 128
DR = 64
DV = 128
NEG = -1.0e9
SCALE = float(np.float32(1.0) / np.float32(np.sqrt(np.float32(DN + DR))))
EPS = 1.0e-5

NCH = (8, 16)
NW = (2, 4)
DIAG0 = (4, 12)



def _yarn_tables():
    i = np.arange(0, DR, 2, dtype=np.float32)
    freqs = (np.float32(1.0) / (np.float32(10000.0) ** (i / np.float32(DR)))).astype(np.float32)
    wavelengths = (np.float32(2.0 * np.pi) / freqs).astype(np.float32)
    ramp = np.clip((wavelengths / np.float32(4096.0) - np.float32(1.0)) / np.float32(31.0),
                   0.0, 1.0).astype(np.float32)
    scale = (np.float32(1.0) - ramp + ramp * np.float32(40.0)).astype(np.float32)
    return (freqs / scale).astype(np.float32)


def _cos_sin(pos, reps):
    inv = _yarn_tables()
    f = pos.astype(np.float32)[None, :] * inv[:, None]
    c = np.cos(f).astype(np.float32)
    s = np.sin(f).astype(np.float32)
    return np.tile(c, (reps, 1)), np.tile(s, (reps, 1))


def _perm_blocks(h):
    if h == 0:
        return [4, 5, 6, 7, 0, 1, 2, 3, 8, 9, 10, 11, 12, 13, 14, 15]
    return [0, 1, 2, 3, 4, 5, 6, 7, 12, 13, 14, 15, 8, 9, 10, 11]


def _own_rows(h):
    if h == 0:
        return np.r_[0:512, 1536:2048]
    return np.r_[512:1536]


def _maskrows(h):
    m = np.zeros((2, SEQ), np.float32)
    blk = lambda c: slice(c * 128, (c + 1) * 128)
    if h == 0:
        for c in range(0, 4):
            m[0, blk(c)] = NEG
        for c in range(8, 16):
            m[0, blk(c)] = NEG
    else:
        for c in range(8, 16):
            m[0, blk(c)] = NEG
        for c in range(8, 12):
            m[1, blk(c)] = NEG
    return m


def _tri_consts():
    out = np.zeros((128, 4 * 512), np.float32)
    st = np.arange(128)[:, None]
    for d in range(4):
        sq = np.arange(512)[None, :]
        m = np.where(sq < 128 * d, NEG, 0.0)
        inblk = (sq >= 128 * d) & (sq < 128 * (d + 1))
        m = np.where(inblk & (st > (sq - 128 * d)), NEG, m)
        out[:, d * 512:(d + 1) * 512] = m
    return out.astype(BF)


def _tile2(w, kb, mb):
    K, M = w.shape
    t = w.reshape(K // kb, kb, M // mb, mb).transpose(2, 0, 1, 3)
    return np.ascontiguousarray(t)


def _prep_shared(inp):
    d = {}
    d["wkv_t"] = np.ascontiguousarray(
        inp["L0_attn_w_kv_down"].reshape(32, 128, KVL))
    d["wkr_t"] = np.ascontiguousarray(
        inp["L0_attn_w_k_rope"].reshape(32, 128, DR))
    d["wqd_t"] = _tile2(inp["L0_attn_w_q_down"], 128, 128)
    wqun = _tile2(inp["L0_attn_w_q_up_nope"], 128, 256)
    d["wqun_t"] = wqun
    d["wqur_t"] = _tile2(inp["L0_attn_w_q_up_rope"], 128, 128)
    wukT = np.ascontiguousarray(inp["L0_attn_w_uk"].T)
    wuvT = np.ascontiguousarray(inp["L0_attn_w_uv"].T)
    d["wukT_t"] = _tile2(wukT, 128, 256)
    d["wuvT_t"] = _tile2(wuvT, 128, 256)
    d["wout_t"] = _tile2(inp["L0_attn_w_out"], 128, 128)
    q = inp["L0_ffn_qweight"].reshape(4096, 4096)
    qT = q.T
    d["q_tiled"] = _tile2(qT, 128, 128).astype(np.int32)
    d["scalesT"] = np.ascontiguousarray(
        inp["L0_ffn_scales"].reshape(4096, 32).T)[None]
    d["zT"] = np.ascontiguousarray(
        inp["L0_ffn_scaled_zeros"].reshape(4096, 32).T)
    d["bias_r"] = np.ascontiguousarray(
        inp["L0_ffn_bias"].reshape(32, 128).T)
    d["ident_bf"] = np.eye(128, dtype=BF)
    d["ident_f"] = np.eye(128, dtype=np.float32)
    d["tri"] = _tri_consts()
    g = np.zeros((32, 128, 32), BF)
    for j in range(32):
        g[j, :, j] = BF(1.0)
    d["g_ind"] = g
    cf = np.zeros((128, 8), np.float32)
    cf[:, 0] = EPS
    cf[:, 1] = SCALE
    cf[:, 2] = 0.0
    cf[:, 3] = 1.0
    cf[:, 4] = -1.0
    d["consts_f"] = cf
    qi = np.zeros((3, NT), np.float32)
    qi[1, 0:512] = 1.0
    qi[2, 512:NT] = 1.0
    d["q_rowsinit"] = qi
    return d


def _prep_core(x, b, h):
    d = {}
    perm = _perm_blocks(h)
    rows = np.concatenate([np.arange(c * 128, (c + 1) * 128) for c in perm])
    d["x_full"] = np.ascontiguousarray(x[b][rows])
    own = _own_rows(h)
    d["x_own"] = np.ascontiguousarray(x[b][own])
    pos_k = rows.astype(np.float32)
    ck, sk = _cos_sin(pos_k, 2)
    d["cos_k"], d["sin_k"] = ck, sk
    pos_q = own.astype(np.float32)
    cq, sq = _cos_sin(pos_q, 4)
    d["cos_q"], d["sin_q"] = cq, sq
    d["maskrows"] = _maskrows(h)
    return d



class _Sem:
    def __init__(self, h, name):
        self.h, self.name, self.n = h, name, 0


class _B:

    SEC = {"T": "tensor", "V": "vector", "S": "scalar", "G": "gpsimd", "Y": "sync"}

    def __init__(self, nc, blk, sems, sem_pool):
        self.nc, self.blk = nc, blk
        self.sems = sems
        self.sem_pool = list(sem_pool)
        self.dsems = {}
        self.last_w = {}
        self.readers = {}
        self.waited = {k: {} for k in self.SEC}
        self.pending = {}

    def dsem(self, stream):
        if stream not in self.dsems:
            self.dsems[stream] = _Sem(self.sem_pool.pop(), f"d_{stream}")
        return self.dsems[stream]

    def _need(self, r, w):
        need = {}
        for k in r:
            for s, v in self.last_w.get(k, {}).items():
                if need.get(s, 0) < v:
                    need[s] = v
        for k in w:
            for dd in (self.last_w.get(k, {}), self.readers.get(k, {})):
                for s, v in dd.items():
                    if need.get(s, 0) < v:
                        need[s] = v
        return need

    def step(self, ename, fn, r=(), w=(), tok=True, dma=None, reg_r=None):
        sem = self.dsem(dma) if dma else self.sems[ename]
        need = self._need(r, w)
        wt = self.waited[ename]
        waits = []
        for sm, v in sorted(need.items(), key=lambda kv: kv[0].name):
            if wt.get(sm, 0) < v:
                if sm is self.sems.get(ename) and v > sm.n:
                    raise AssertionError(
                        f"own-sem deadlock {ename} wait {v} > issued {sm.n}")
                waits.append((sm, v))
                wt[sm] = v
        if not tok:
            self.pending.setdefault(ename, []).append((waits, fn, None, 0))
            return None
        step_n = 16 if dma else 1
        sem.n += step_n
        self.pending.setdefault(ename, []).append((waits, fn, sem, step_n))
        t = (sem, sem.n)
        for k in w:
            self.last_w[k] = {sem: sem.n}
            self.readers[k] = {}
        for k in (reg_r if reg_r is not None else r):
            d = self.readers.setdefault(k, {})
            if d.get(sem, 0) < sem.n:
                d[sem] = sem.n
        return t

    def flush_pending(self):
        for ename in list(self.SEC):
            q = self.pending.get(ename)
            if not q:
                continue
            self.pending[ename] = []

            def body(e, q=q):
                for waits, fn, sem, step_n in q:
                    for sm, v in waits:
                        e.wait_ge(sm.h, v)
                    inst = fn(e)
                    if sem is not None:
                        inst.then_inc(sem.h, step_n)

            getattr(self.blk, self.SEC[ename])(body)

    def flush(self, stream, keys):
        sem = self.dsem(stream)
        for k in keys:
            self.last_w[k] = {sem: sem.n}
            self.readers[k] = {}

    def barrier(self):
        self.flush_pending()
        ph = self.sems["ph"]
        target = ph.n + 5
        ph.n = target
        allsems = [self.sems[k] for k in ("T", "V", "S", "G")] + list(self.dsems.values())

        for en in self.SEC:
            def body(e, en=en):
                for s in allsems:
                    if s.n > 0 and self.waited[en].get(s, 0) < s.n:
                        e.wait_ge(s.h, s.n)
                        self.waited[en][s] = s.n
                e.nop().then_inc(ph.h, 1)
                e.wait_ge(ph.h, target)
                self.waited[en][ph] = target
            getattr(self.blk, self.SEC[en])(body)
        self.last_w = {}
        self.readers = {}


def build_nc(dbg=0):
    nc = bass.Bass("TRN2", target_bir_lowering=False)
    dt_in = {}

    def din(name, shape, dtype):
        t = nc.dram_tensor(name, list(shape), dtype, kind="ExternalInput").ap()
        dt_in[name] = t
        return t

    x_full = din("x_full", (SEQ, D), F32)
    x_own = din("x_own", (NT, D), F32)
    cos_k = din("cos_k", (64, SEQ), F32)
    sin_k = din("sin_k", (64, SEQ), F32)
    cos_q = din("cos_q", (128, NT), F32)
    sin_q = din("sin_q", (128, NT), F32)
    maskrows = din("maskrows", (2, SEQ), F32)
    q_rowsinit = din("q_rowsinit", (3, NT), F32)
    consts_f = din("consts_f", (128, 8), F32)
    ident_bf_d = din("ident_bf", (128, 128), BF16)
    ident_f_d = din("ident_f", (128, 128), F32)
    tri_d = din("tri", (128, 2048), BF16)
    g_ind_d = din("g_ind", (32, 128, 32), BF16)
    wkv_t = din("wkv_t", (32, 128, KVL), F32)
    wkr_t = din("wkr_t", (32, 128, DR), F32)
    wqd_t = din("wqd_t", (12, 32, 128, 128), F32)
    wqun_t = din("wqun_t", (16, 12, 128, 256), F32)
    wqur_t = din("wqur_t", (16, 12, 128, 128), F32)
    wukT_t = din("wukT_t", (16, 4, 128, 256), F32)
    wuvT_t = din("wuvT_t", (16, 4, 128, 256), F32)
    wout_t = din("wout_t", (32, 32, 128, 128), F32)
    q_tiled = din("q_tiled", (32, 32, 128, 128), I32)
    scalesT_d = din("scalesT", (1, 32, 4096), F32)
    zT_d = din("zT", (32, 4096), F32)
    bias_d = din("bias_r", (128, 32), F32)

    out_own = nc.dram_tensor("out_own", [NT, D], F32, kind="ExternalOutput").ap()
    dbg_t = {}
    if dbg:
        for nme, shp, dt in [
            ("dbg_xnT", (128, 32 * NT), BF16), ("dbg_cq", (128, 12 * NT), BF16),
            ("dbg_ckv", (128, 4 * SEQ), BF16), ("dbg_krx", (67, SEQ), BF16),
            ("dbg_oT", (128, 32 * NT), BF16), ("dbg_cs", (32, NT), BF16),
        ]:
            dbg_t[nme] = nc.dram_tensor(nme, list(shp), dt, kind="ExternalOutput").ap()

    xn_scr = nc.dram_tensor("xn_scr", [D, NT], BF16, kind="Internal").ap()
    sa_scr = nc.dram_tensor("sa_scr", [D, NT], BF16, kind="Internal").ap()

    with ExitStack() as top:
        e = top.enter_context
        ident_bf = e(nc.sbuf_tensor("ident_bf_s", [128, 128], BF16))
        ident_f = e(nc.sbuf_tensor("ident_f_s", [128, 128], F32))
        tri_sb = e(nc.sbuf_tensor("tri_s", [128, 2048], BF16))
        cf_sb = e(nc.sbuf_tensor("consts_f_s", [128, 8], F32))
        ones_col = e(nc.sbuf_tensor("ones_col", [128, 1], BF16))
        ones_row_f = e(nc.sbuf_tensor("ones_row_f", [1, 128], F32))
        cosq_sb = e(nc.sbuf_tensor("cosq", [128, NT], F32))
        sinq_sb = e(nc.sbuf_tensor("sinq", [128, NT], F32))
        k_ropeX = e(nc.sbuf_tensor("k_ropeX", [67, SEQ], BF16))
        colsumsT = e(nc.sbuf_tensor("colsumsT", [32, NT], BF16))
        zT_sb = e(nc.sbuf_tensor("zT_s", [32, 4096], BF16))
        bias_sb = e(nc.sbuf_tensor("bias_s", [128, 32], F32))
        g_ind_sb = e(nc.sbuf_tensor("g_ind_s", [128, 32 * 32], BF16))

        nc.const_aps.register(F32, EPS, cf_sb[:, 0:1])
        nc.const_aps.register(F32, SCALE, cf_sb[:, 1:2])
        nc.const_aps.register(F32, 0.0, cf_sb[:, 2:3])
        nc.const_aps.register(F32, 1.0, cf_sb[:, 3:4])
        nc.const_aps.register(F32, -1.0, cf_sb[:, 4:5])

        sems = {}
        for nm in ("T", "V", "S", "G", "Y", "ph"):
            sems[nm] = _Sem(e(nc.semaphore(f"sem_{nm}")), nm)
        sem_pool = [e(nc.semaphore(f"sem_d{i}")) for i in range(48)]
        blk = e(nc.Block())
        b = _B(nc, blk, sems, sem_pool)

        es_ckv = top.enter_context(ExitStack())
        c_kvT = es_ckv.enter_context(nc.sbuf_tensor("c_kvT", [128, 4 * SEQ], BF16))
        c_qT = es_ckv.enter_context(nc.sbuf_tensor("c_qT", [128, 12 * NT], BF16))
        es_kr = top.enter_context(ExitStack())
        k_rope_raw = es_kr.enter_context(
            nc.sbuf_tensor("k_rope_raw", [64, SEQ], F32))
        cosk_sb = es_kr.enter_context(nc.sbuf_tensor("cosk", [64, SEQ], F32))
        sink_sb = es_kr.enter_context(nc.sbuf_tensor("sink", [64, SEQ], F32))

        for dst, src in [(ident_bf, ident_bf_d), (ident_f, ident_f_d),
                         (tri_sb, tri_d), (cf_sb, consts_f),
                         (cosk_sb, cos_k), (sink_sb, sin_k),
                         (cosq_sb, cos_q), (sinq_sb, sin_q),
                         (bias_sb, bias_d)]:
            b.step("Y", lambda e, d=dst, s=src: e.dma_start(out=d[:], in_=s[:]),
                   w=[d_key(dst)], dma="c0")
        b.step("G", lambda e: e.dma_start(out=zT_sb[:], in_=zT_d[:]),
               w=[d_key(zT_sb)], dma="c1")
        b.step("Y", lambda e: e.dma_start(
            out=g_ind_sb[:].rearrange("p (j c) -> p j c", j=32),
            in_=g_ind_d[:].rearrange("j p c -> p j c")),
            w=[d_key(g_ind_sb)], dma="c0")
        b.flush("c0", [d_key(t) for t in (ident_bf, ident_f, tri_sb, cf_sb,
                                          cosk_sb, sink_sb, cosq_sb, sinq_sb,
                                          bias_sb, g_ind_sb)])
        b.step("V", lambda e: e.memset(ones_col[:], 1.0), w=["ones_col"])
        b.step("V", lambda e: e.memset(ones_row_f[:], 1.0), w=["ones_row_f"])
        b.barrier()

        with ExitStack() as ph:
            pe = ph.enter_context
            wkv_bf = pe(nc.sbuf_tensor("wkv_bf", [128, 32 * 512], BF16))
            wkr_bf = pe(nc.sbuf_tensor("wkr_bf", [128, 32 * 64], BF16))
            xa = [pe(nc.sbuf_tensor(f"xa{i}", [128, D], F32)) for i in range(2)]
            xnb = [pe(nc.sbuf_tensor(f"xnb{i}", [128, D], BF16)) for i in range(2)]
            xnT_tile = [pe(nc.sbuf_tensor(f"xnTt{i}", [128, D], BF16)) for i in range(2)]
            stats = pe(nc.sbuf_tensor("stats", [128, 48], F32))
            aggr = pe(nc.sbuf_tensor("aggr", [128, 2], F32))
            sd = pe(nc.sbuf_tensor("sd", [128, 1], F32))
            rstd = pe(nc.sbuf_tensor("rstd", [128, 1], F32))
            ps_tA = [pe(nc.psum_tensor(f"ps_tA{i}", [128, 1024], BF16)) for i in range(2)]
            ps_kv = pe(nc.psum_tensor("ps_kv", [128, 512], F32))
            ps_kr = pe(nc.psum_tensor("ps_kr", [64, 128], F32))

            b.step("G", lambda e: e.dma_start(
                out=wkv_bf[:].rearrange("p (k m) -> p k m", k=32),
                in_=wkv_t[:].rearrange("k p m -> p k m")),
                w=["wkv_bf"], dma="wkv")
            b.step("G", lambda e: e.dma_start(
                out=wkr_bf[:].rearrange("p (k m) -> p k m", k=32),
                in_=wkr_t[:].rearrange("k p m -> p k m")),
                w=["wkr_bf"], dma="wkr")

            def ln_tile(src_ap, t, bi, dst_tile):
                b.step("Y", lambda e: e.dma_start(
                    out=xa[bi][:], in_=src_ap[t * 128:(t + 1) * 128, :]),
                    r=[], w=[("xa", bi)], dma=f"xa{bi}")
                for u in range(8):
                    b.step("V", lambda e, u=u: e.bn_stats(
                        stats[:, u * 6:(u + 1) * 6], xa[bi][:, u * 512:(u + 1) * 512]),
                        r=[("xa", bi)], w=[("stats", u)])
                b.step("V", lambda e: e.bn_aggr(aggr[:], stats[:, 0:48]),
                       r=[("stats", u) for u in range(8)], w=["aggr"])
                b.step("S", lambda e: e.activation(sd[:], aggr[:, 1:2], AF.Sqrt, bias=EPS),
                       r=["aggr"], w=["sd"])
                b.step("V", lambda e: e.reciprocal(rstd[:], sd[:]), r=["sd"], w=["rstd"])
                b.step("V", lambda e: e.tensor_scalar(
                    out=xnb[bi][:], in0=xa[bi][:], scalar1=aggr[:, 0:1],
                    scalar2=rstd[:], op0=ALU.subtract, op1=ALU.mult),
                    r=[("xa", bi), "aggr", "rstd"], w=[("xnb", bi)])
                for q in range(4):
                    pbank = q % 2
                    for u in range(8):
                        c = q * 8 + u
                        b.step("T", lambda e, c=c, u=u, pbank=pbank: e.transpose(
                            ps_tA[pbank][:, u * 128:(u + 1) * 128],
                            xnb[bi][:, c * 128:(c + 1) * 128], ident_bf[:]),
                            r=[("xnb", bi), d_key(ident_bf)],
                            w=[("ps_tA", pbank)],
                            tok=(u == 7),
                            reg_r=[("xnb", bi), d_key(ident_bf)] if u == 7 else None)
                    b.step("V", lambda e, q=q, pbank=pbank: e.tensor_copy(
                        dst_tile[:].rearrange("p (c n) -> p c n", c=32)[:, q * 8:(q + 1) * 8, :],
                        ps_tA[pbank][:].rearrange("p (c n) -> p c n", c=8)),
                        r=[("ps_tA", pbank)], w=[("dstq", id(dst_tile), q)])

            for t in range(16):
                bi = t % 2
                ln_tile(x_full, t, bi, xnT_tile[bi])
                for m in range(4):
                    for K in range(32):
                        b.step("T", lambda e, m=m, K=K, bi=bi: e.matmul(
                            ps_kv[:, m * 128:(m + 1) * 128],
                            wkv_bf[:, K * 512 + m * 128:K * 512 + (m + 1) * 128],
                            xnT_tile[bi][:, K * 128:(K + 1) * 128],
                            start=(K == 0), stop=(K == 31)),
                            r=["wkv_bf", ("dstq", id(xnT_tile[bi]), K // 8)],
                            w=[("ps_kv", m)], tok=(K == 31),
                            reg_r=(["wkv_bf"] + [("dstq", id(xnT_tile[bi]), qq) for qq in range(4)]) if K == 31 else None)
                b.step("V", lambda e, t=t: e.tensor_copy(
                    c_kvT[:].rearrange("p (m n) -> p m n", m=4)[:, :, t * 128:(t + 1) * 128],
                    ps_kv[:].rearrange("p (m n) -> p m n", m=4)),
                    r=[("ps_kv", m) for m in range(4)], w=[("c_kvT", t)])
                for K in range(32):
                    b.step("T", lambda e, K=K, bi=bi: e.matmul(
                        ps_kr[:, :],
                        wkr_bf[:, K * 64:(K + 1) * 64],
                        xnT_tile[bi][:, K * 128:(K + 1) * 128],
                        start=(K == 0), stop=(K == 31)),
                        r=["wkr_bf", ("dstq", id(xnT_tile[bi]), K // 8)],
                        w=[("ps_kr",)], tok=(K == 31),
                        reg_r=(["wkr_bf"] + [("dstq", id(xnT_tile[bi]), qq) for qq in range(4)]) if K == 31 else None)
                b.step("V", lambda e, t=t: e.tensor_copy(
                    k_rope_raw[:, t * 128:(t + 1) * 128], ps_kr[:, :]),
                    r=[("ps_kr",)], w=[("k_rope_raw", t)])
            b.barrier()

        with ExitStack() as ph:
            pe = ph.enter_context
            rotk = pe(nc.sbuf_tensor("rotk", [64, SEQ], F32))
            pA = pe(nc.sbuf_tensor("ropeA", [64, SEQ], F32))
            pB = pe(nc.sbuf_tensor("ropeB", [64, SEQ], F32))
            b.step("V", lambda e: e.tensor_scalar_mul(
                rotk[0:32, :], k_rope_raw[32:64, :], -1.0),
                r=["k_rope_raw"], w=["rotk_a"])
            b.step("V", lambda e: e.tensor_copy(rotk[32:64, :], k_rope_raw[0:32, :]),
                   r=["k_rope_raw"], w=["rotk_b"])
            b.step("V", lambda e: e.tensor_mul(pA[:], k_rope_raw[0:64, :], cosk_sb[:]),
                   r=["k_rope_raw", d_key(cosk_sb)], w=["pA"])
            b.step("V", lambda e: e.tensor_mul(pB[:], rotk[:], sink_sb[:]),
                   r=["rotk_a", "rotk_b", d_key(sink_sb)], w=["pB"])
            b.step("V", lambda e: e.tensor_add(k_ropeX[0:64, :], pA[:], pB[:]),
                   r=["pA", "pB"], w=["krx_rope"])
            b.step("V", lambda e: e.memset(k_ropeX[64:65, :], 1.0), w=["krx_ones"])
            b.step("G", lambda e: e.dma_start(out=k_ropeX[65:67, :], in_=maskrows[:]),
                   w=["krx_mask"], dma="mk")
            if dbg:
                b.step("Y", lambda e: e.dma_start(out=dbg_t["dbg_krx"][:], in_=k_ropeX[:]),
                       r=["krx_rope", "krx_ones", "krx_mask"], dma="dbg")
            b.barrier()
        es_kr.close()

        es_xn = top.enter_context(ExitStack())
        xnT_own = es_xn.enter_context(
            nc.sbuf_tensor("xnT_own", [128, 32 * NT], BF16, side="right"))
        with ExitStack() as ph:
            pe = ph.enter_context
            xa = [pe(nc.sbuf_tensor(f"xaq{i}", [128, D], F32)) for i in range(2)]
            xnb = [pe(nc.sbuf_tensor(f"xnbq{i}", [128, D], BF16)) for i in range(2)]
            stats = pe(nc.sbuf_tensor("statsq", [128, 48], F32))
            aggr = pe(nc.sbuf_tensor("aggrq", [128, 2], F32))
            sd = pe(nc.sbuf_tensor("sdq", [128, 1], F32))
            rstd = pe(nc.sbuf_tensor("rstdq", [128, 1], F32))
            ps_tA = [pe(nc.psum_tensor(f"ps_tAq{i}", [128, 1024], BF16)) for i in range(2)]

            for t in range(8):
                bi = t % 2
                b.step("Y", lambda e, t=t, bi=bi: e.dma_start(
                    out=xa[bi][:], in_=x_own[t * 128:(t + 1) * 128, :]),
                    w=[("xa", bi)], dma=f"xa{bi}")
                for u in range(8):
                    b.step("V", lambda e, u=u, bi=bi: e.bn_stats(
                        stats[:, u * 6:(u + 1) * 6], xa[bi][:, u * 512:(u + 1) * 512]),
                        r=[("xa", bi)], w=[("stats", u)])
                b.step("V", lambda e: e.bn_aggr(aggr[:], stats[:, 0:48]),
                       r=[("stats", u) for u in range(8)], w=["aggr"])
                b.step("S", lambda e: e.activation(sd[:], aggr[:, 1:2], AF.Sqrt, bias=EPS),
                       r=["aggr"], w=["sd"])
                b.step("V", lambda e: e.reciprocal(rstd[:], sd[:]), r=["sd"], w=["rstd"])
                b.step("V", lambda e, bi=bi: e.tensor_scalar(
                    out=xnb[bi][:], in0=xa[bi][:], scalar1=aggr[:, 0:1],
                    scalar2=rstd[:], op0=ALU.subtract, op1=ALU.mult),
                    r=[("xa", bi), "aggr", "rstd"], w=[("xnb", bi)])
                for q in range(4):
                    pbank = q % 2
                    for u in range(8):
                        c = q * 8 + u
                        b.step("T", lambda e, c=c, u=u, pbank=pbank, bi=bi: e.transpose(
                            ps_tA[pbank][:, u * 128:(u + 1) * 128],
                            xnb[bi][:, c * 128:(c + 1) * 128], ident_bf[:]),
                            r=[("xnb", bi), d_key(ident_bf)],
                            w=[("ps_tA", pbank)], tok=(u == 7),
                            reg_r=[("xnb", bi), d_key(ident_bf)] if u == 7 else None)
                    b.step("V", lambda e, q=q, t=t, pbank=pbank: e.tensor_copy(
                        xnT_own[:].rearrange("p (c n) -> p c n", c=32)
                        [:, q * 8:(q + 1) * 8, t * 128:(t + 1) * 128],
                        ps_tA[pbank][:].rearrange("p (c n) -> p c n", c=8)),
                        r=[("ps_tA", pbank)], w=[("xnT_own", t, q)])
            b.barrier()

        with ExitStack() as ph:
            pe = ph.enter_context
            wqd_cb = [pe(nc.sbuf_tensor(f"wqd_cb{i}", [128, 32 * 128], BF16)) for i in range(2)]
            colsq = pe(nc.sbuf_tensor("colsq", [128, 8 * 32], BF16))
            ps_cq = [pe(nc.psum_tensor(f"ps_cq{i}", [128, 512], F32)) for i in range(4)]
            ps_cs = pe(nc.psum_tensor("ps_cs", [128, 32], F32))
            ps_csT = pe(nc.psum_tensor("ps_csT", [32, 128], BF16))

            xn_own_keys = [("xnT_own", t, q) for t in range(8) for q in range(4)]
            for m in range(12):
                b.step("G", lambda e, m=m: e.dma_start(
                    out=wqd_cb[m % 2][:].rearrange("p (k n) -> p k n", k=32),
                    in_=wqd_t[m].rearrange("k p n -> p k n")),
                    w=[("wqd_cb", m % 2)], dma=f"wqd{m % 2}")
                for t in range(2):
                    pb = (m % 2) * 2 + t
                    for K in range(32):
                        b.step("T", lambda e, m=m, t=t, K=K, pb=pb: e.matmul(
                            ps_cq[pb][:],
                            wqd_cb[m % 2][:, K * 128:(K + 1) * 128],
                            xnT_own[:, K * NT + t * 512:K * NT + t * 512 + 512],
                            start=(K == 0), stop=(K == 31)),
                            r=([("wqd_cb", m % 2)] + (xn_own_keys if K == 0 else [])),
                            w=[("ps_cq", pb)], tok=(K == 31),
                            reg_r=([("wqd_cb", m % 2)] + xn_own_keys) if K == 31 else None)
                    b.step("V", lambda e, m=m, t=t, pb=pb: e.tensor_copy(
                        c_qT[:, m * NT + t * 512:m * NT + t * 512 + 512], ps_cq[pb][:]),
                        r=[("ps_cq", pb)], w=[("c_qT", m, t)])
            for tc in range(8):
                for j in range(32):
                    b.step("T", lambda e, tc=tc, j=j: e.matmul(
                        ps_cs[:], xnT_own[:, j * NT + tc * 128:j * NT + (tc) * 128 + 128],
                        g_ind_sb[:, j * 32:(j + 1) * 32],
                        start=(j == 0), stop=(j == 31)),
                        r=([d_key(g_ind_sb)] + (xn_own_keys if j == 0 else [])),
                        w=[("ps_cs",)], tok=(j == 31),
                        reg_r=([d_key(g_ind_sb)] + xn_own_keys) if j == 31 else None)
                b.step("V", lambda e, tc=tc: e.tensor_copy(
                    colsq[:, tc * 32:(tc + 1) * 32], ps_cs[:]),
                    r=[("ps_cs",)], w=[("colsq", tc)])
                b.step("T", lambda e, tc=tc: e.transpose(
                    ps_csT[:], colsq[:, tc * 32:(tc + 1) * 32], ident_bf[:]),
                    r=[("colsq", tc), d_key(ident_bf)], w=[("ps_csT",)])
                b.step("V", lambda e, tc=tc: e.tensor_copy(
                    colsumsT[0:32, tc * 128:(tc + 1) * 128], ps_csT[:]),
                    r=[("ps_csT",)], w=[("colsumsT", tc)])
            b.step("Y", lambda e: e.dma_start(
                out=xn_scr[:].rearrange("(j p) n -> p j n", p=128),
                in_=xnT_own[:].rearrange("p (j n) -> p j n", j=32)),
                r=xn_own_keys, w=["xn_scr"], dma="xs")
            if dbg:
                b.step("Y", lambda e: e.dma_start(out=dbg_t["dbg_xnT"][:], in_=xnT_own[:]),
                       r=xn_own_keys, dma="dbg")
                b.step("Y", lambda e: e.dma_start(out=dbg_t["dbg_cq"][:], in_=c_qT[:]),
                       r=[("c_qT", m, t) for m in range(12) for t in range(2)], dma="dbg")
                b.step("Y", lambda e: e.dma_start(out=dbg_t["dbg_ckv"][:], in_=c_kvT[:]),
                       r=[("c_kvT", t) for t in range(16)], dma="dbg")
                b.step("Y", lambda e: e.dma_start(out=dbg_t["dbg_cs"][:], in_=colsumsT[:]),
                       r=[("colsumsT", tc) for tc in range(8)], dma="dbg")
            b.barrier()
        es_xn.close()

        es_oT = top.enter_context(ExitStack())
        oT = es_oT.enter_context(
            nc.sbuf_tensor("oT", [128, 32 * NT], BF16, side="right"))
        with ExitStack() as ph:
            pe = ph.enter_context
            wqun_bf = [pe(nc.sbuf_tensor(f"wqun{i}", [128, 12 * 256], BF16)) for i in range(2)]
            wqur_bf = [pe(nc.sbuf_tensor(f"wqur{i}", [128, 12 * 128], BF16)) for i in range(2)]
            wukT_bf = [pe(nc.sbuf_tensor(f"wukT{i}", [128, 4 * 256], BF16)) for i in range(2)]
            wuvT_bf = [pe(nc.sbuf_tensor(f"wuvT{i}", [128, 4 * 256], BF16)) for i in range(2)]
            q_nopeT = [pe(nc.sbuf_tensor(f"qnope{i}", [128, NT], BF16)) for i in range(2)]
            qrx = [pe(nc.sbuf_tensor(f"qrx{i}", [67, NT], BF16)) for i in range(2)]
            kn = [pe(nc.sbuf_tensor(f"kn{i}", [128, SEQ], BF16)) for i in range(2)]
            v_grp = pe(nc.sbuf_tensor("v_grp", [128, 16 * 256], BF16))
            rope_tmp = pe(nc.sbuf_tensor("rope_tmp", [128, NT], F32))
            rot = pe(nc.sbuf_tensor("rot", [128, NT], F32))
            prA = pe(nc.sbuf_tensor("prA", [128, NT], F32))
            prB = pe(nc.sbuf_tensor("prB", [128, NT], F32))
            PT = [pe(nc.sbuf_tensor(f"PT{i}", [128, 512], BF16)) for i in range(2)]
            mcol = pe(nc.sbuf_tensor("mcol", [128, 1], F32))
            mtmp = pe(nc.sbuf_tensor("mtmp", [128, 1], F32))
            mneg = pe(nc.sbuf_tensor("mneg", [128, 1], F32))
            recip_row = pe(nc.sbuf_tensor("recip_row", [1, 512], F32))
            lnrow = pe(nc.sbuf_tensor("lnrow", [1, 512], F32))
            rb_sb = pe(nc.sbuf_tensor("rb_sb", [128, 512], F32))
            pool = [pe(nc.psum_tensor(f"poolB{i}", [128, 512], F32)) for i in range(4)]
            ps_st = [pe(nc.psum_tensor(f"ps_stB{i}", [128, 512], F32)) for i in range(2)]
            ps_out = pe(nc.psum_tensor("ps_outB", [128, 512], F32))
            ps_sum = pe(nc.psum_tensor("ps_sumB", [1, 512], F32))

            for p in range(2):
                b.step("G", lambda e, p=p: e.dma_start(out=qrx[p][64:67, :], in_=q_rowsinit[:]),
                       w=[("qrx_rows", p)], dma=f"qi{p}")

            ckv_keys = [("c_kvT", t) for t in range(16)]
            cq_keys = [("c_qT", m, t) for m in range(12) for t in range(2)]
            krx_keys = ["krx_rope", "krx_ones", "krx_mask"]

            def wdma(grp):
                gb = grp % 2
                for (buf, src, ncols) in [(wqun_bf, wqun_t, 256), (wqur_bf, wqur_t, 128),
                                          (wukT_bf, wukT_t, 256), (wuvT_bf, wuvT_t, 256)]:
                    b.step("G", lambda e, buf=buf, src=src, grp=grp: e.dma_start(
                        out=buf[gb][:].rearrange("p (k n) -> p k n", k=src.shape[1]),
                        in_=src[grp].rearrange("k p n -> p k n")),
                        w=[(id(buf), gb)], dma=f"wg{gb}")
                b.flush(f"wg{gb}", [(id(buf), gb) for buf in
                                    (wqun_bf, wqur_bf, wukT_bf, wuvT_bf)])

            wdma(0)
            pool_i = [0]

            def nxt_pool():
                i = pool_i[0] % 4
                pool_i[0] += 1
                return i

            for grp in range(16):
                gb = grp % 2
                if grp + 1 < 16:
                    wdma(grp + 1)
                for p in range(2):
                    for t in range(2):
                        pb = nxt_pool()
                        for K in range(12):
                            b.step("T", lambda e, p=p, t=t, K=K, pb=pb, gb=gb: e.matmul(
                                pool[pb][:],
                                wqun_bf[gb][:, K * 256 + p * 128:K * 256 + p * 128 + 128],
                                c_qT[:, K * NT + t * 512:K * NT + t * 512 + 512],
                                start=(K == 0), stop=(K == 11)),
                                r=[(id(wqun_bf), gb), ("c_qT", K, t)],
                                w=[("pool", pb)], tok=(K == 11),
                                reg_r=([(id(wqun_bf), gb)] + cq_keys) if K == 11 else None)
                        b.step("V", lambda e, p=p, t=t, pb=pb: e.tensor_copy(
                            q_nopeT[p][:, t * 512:t * 512 + 512], pool[pb][:]),
                            r=[("pool", pb)], w=[("qn", p, t)])
                for t in range(2):
                    pb = nxt_pool()
                    for K in range(12):
                        b.step("T", lambda e, t=t, K=K, pb=pb, gb=gb: e.matmul(
                            pool[pb][:],
                            wqur_bf[gb][:, K * 128:(K + 1) * 128],
                            c_qT[:, K * NT + t * 512:K * NT + t * 512 + 512],
                            start=(K == 0), stop=(K == 11)),
                            r=[(id(wqur_bf), gb), ("c_qT", K, t)],
                            w=[("pool", pb)], tok=(K == 11),
                            reg_r=([(id(wqur_bf), gb)] + cq_keys) if K == 11 else None)
                    b.step("V", lambda e, t=t, pb=pb: e.tensor_copy(
                        rope_tmp[:, t * 512:t * 512 + 512], pool[pb][:]),
                        r=[("pool", pb)], w=[("rope_tmp", t)])
                rt_keys = [("rope_tmp", 0), ("rope_tmp", 1)]
                for (dsta, srca) in [(0, 32), (32, 0), (64, 96), (96, 64)]:
                    neg = (dsta in (0, 64))
                    b.step("V", lambda e, dsta=dsta, srca=srca, neg=neg:
                           (e.tensor_scalar_mul(rot[dsta:dsta + 32, :],
                                                rope_tmp[srca:srca + 32, :], -1.0)
                            if neg else
                            e.tensor_copy(rot[dsta:dsta + 32, :],
                                          rope_tmp[srca:srca + 32, :])),
                           r=rt_keys, w=[("rot", dsta)])
                rot_keys = [("rot", a) for a in (0, 32, 64, 96)]
                b.step("V", lambda e: e.tensor_mul(prA[:], rope_tmp[:], cosq_sb[:]),
                       r=rt_keys + [d_key(cosq_sb)], w=["prA"])
                b.step("V", lambda e: e.tensor_mul(prB[:], rot[:], sinq_sb[:]),
                       r=rot_keys + [d_key(sinq_sb)], w=["prB"])
                for p in range(2):
                    b.step("V", lambda e, p=p: e.tensor_add(
                        qrx[p][0:64, :], prA[p * 64:p * 64 + 64, :],
                        prB[p * 64:p * 64 + 64, :]),
                        r=["prA", "prB"], w=[("qrx", p)])
                for p in range(2):
                    for s4 in range(4):
                        pb = nxt_pool()
                        for K in range(4):
                            b.step("T", lambda e, p=p, s4=s4, K=K, pb=pb, gb=gb: e.matmul(
                                pool[pb][:],
                                wukT_bf[gb][:, K * 256 + p * 128:K * 256 + p * 128 + 128],
                                c_kvT[:, K * SEQ + s4 * 512:K * SEQ + s4 * 512 + 512],
                                start=(K == 0), stop=(K == 3)),
                                r=[(id(wukT_bf), gb)] + (ckv_keys if K == 0 else []),
                                w=[("pool", pb)], tok=(K == 3),
                                reg_r=([(id(wukT_bf), gb)] + ckv_keys) if K == 3 else None)
                        b.step("S", lambda e, p=p, s4=s4, pb=pb: e.copy(
                            kn[p][:, s4 * 512:s4 * 512 + 512], pool[pb][:]),
                            r=[("pool", pb)], w=[("kn", p, s4)])
                for st in range(16):
                    pb = nxt_pool()
                    for K in range(4):
                        b.step("T", lambda e, st=st, K=K, pb=pb, gb=gb: e.matmul(
                            pool[pb][:, 0:256],
                            c_kvT[:, K * SEQ + st * 128:K * SEQ + st * 128 + 128],
                            wuvT_bf[gb][:, K * 256:(K + 1) * 256],
                            start=(K == 0), stop=(K == 3)),
                            r=[(id(wuvT_bf), gb)] + (ckv_keys if K == 0 else []),
                            w=[("pool", pb)], tok=(K == 3),
                            reg_r=([(id(wuvT_bf), gb)] + ckv_keys) if K == 3 else None)
                    b.step("S", lambda e, st=st, pb=pb: e.copy(
                        v_grp[:, st * 256:st * 256 + 256], pool[pb][:, 0:256]),
                        r=[("pool", pb)], w=[("v", st)])

                kn_keys = lambda p: [("kn", p, s4) for s4 in range(4)]
                for p in range(2):
                    for g in range(2):
                        for s in range(4):
                            qcol = g * 512 + s * 128
                            for w in range(NW[g]):
                                pb = nxt_pool()
                                b.step("T", lambda e, p=p, qcol=qcol, w=w, pb=pb: e.matmul(
                                    pool[pb][:], q_nopeT[p][:, qcol:qcol + 128],
                                    kn[p][:, w * 512:w * 512 + 512],
                                    start=True, stop=False),
                                    r=[("qn", p, g), ("kn", p, w)],
                                    w=[("pool", pb)], tok=False)
                                b.step("T", lambda e, p=p, qcol=qcol, w=w, pb=pb: e.matmul(
                                    pool[pb][:], qrx[p][0:64, qcol:qcol + 128],
                                    k_ropeX[0:64, w * 512:w * 512 + 512],
                                    start=False, stop=True),
                                    r=[("qrx", p), "krx_rope"],
                                    w=[("pool", pb)], tok=True,
                                    reg_r=[("qn", p, g), ("kn", p, w), ("qrx", p), "krx_rope"])
                                if w == 0:
                                    b.step("V", lambda e, pb=pb: e.reduce_max(
                                        mcol[:], pool[pb][:], axis=AX.X),
                                        r=[("pool", pb)], w=["mcol"])
                                else:
                                    b.step("V", lambda e, pb=pb: e.reduce_max(
                                        mtmp[:], pool[pb][:], axis=AX.X),
                                        r=[("pool", pb)], w=["mtmp"])
                                    b.step("V", lambda e: e.tensor_max(
                                        mcol[:], mcol[:], mtmp[:]),
                                        r=["mcol", "mtmp"], w=["mcol"])
                            b.step("V", lambda e: e.tensor_scalar_mul(mneg[:], mcol[:], -1.0),
                                   r=["mcol"], w=["mneg"])
                            b.step("T", lambda e, s=s: e.transpose(
                                ps_sum[0:1, s * 128:s * 128 + 128], mneg[:], ident_f[:]),
                                r=["mneg", d_key(ident_f)], w=[("ps_sum",)])
                            b.step("V", lambda e, p=p, qcol=qcol, s=s: e.tensor_copy(
                                qrx[p][64:65, qcol:qcol + 128],
                                ps_sum[0:1, s * 128:s * 128 + 128]),
                                r=[("ps_sum",), ("qrx_rows", p)], w=[("qrx_nm", p, g)])
                for p in range(2):
                    h_abs = grp * 2 + p
                    for g in range(2):
                        nch = NCH[g]
                        d0 = DIAG0[g]
                        qsl = slice(g * 512, g * 512 + 512)

                        def emit_ST(c, qsl=qsl, d0=d0, p=p, g=g):
                            bank = c % 2
                            b.step("T", lambda e, p=p, c=c, bank=bank, qsl=qsl: e.matmul(
                                ps_st[bank][:], kn[p][:, c * 128:c * 128 + 128],
                                q_nopeT[p][:, qsl], start=True, stop=False),
                                r=[("kn", p, c // 4), ("qn", p, g)],
                                w=[("st", bank)], tok=False)
                            dd = c - d0
                            last = not (0 <= dd < 4)
                            b.step("T", lambda e, p=p, c=c, bank=bank, last=last, qsl=qsl: e.matmul(
                                ps_st[bank][:], k_ropeX[:, c * 128:c * 128 + 128],
                                qrx[p][0:67, qsl], start=False, stop=last),
                                r=krx_keys + [("qrx", p), ("qrx_nm", p, g), ("qrx_rows", p)],
                                w=[("st", bank)], tok=last,
                                reg_r=([("kn", p, c // 4), ("qn", p, g), ("qrx", p),
                                        ("qrx_nm", p, g), ("qrx_rows", p)] + krx_keys)
                                if last else None)
                            if not last:
                                b.step("T", lambda e, bank=bank, dd=dd: e.matmul(
                                    ps_st[bank][:], ident_bf[:],
                                    tri_sb[:, dd * 512:dd * 512 + 512],
                                    start=False, stop=True),
                                    r=[d_key(ident_bf), d_key(tri_sb)],
                                    w=[("st", bank)], tok=True,
                                    reg_r=[("kn", p, c // 4), ("qn", p, g), ("qrx", p),
                                           ("qrx_nm", p, g), ("qrx_rows", p),
                                           d_key(ident_bf), d_key(tri_sb)] + krx_keys)

                        emit_ST(0)
                        for c in range(nch):
                            bank = c % 2
                            if c + 1 < nch:
                                emit_ST(c + 1)
                            b.step("S", lambda e, c=c, bank=bank: e.activation(
                                PT[bank][:], ps_st[bank][:], AF.Exp, scale=SCALE),
                                r=[("st", bank)], w=[("PT", bank)])
                            b.step("T", lambda e, p=p, c=c, bank=bank, nch=nch: e.matmul(
                                ps_out[:], v_grp[:, c * 256 + p * 128:c * 256 + p * 128 + 128],
                                PT[bank][:], start=(c == 0), stop=(c == nch - 1)),
                                r=[("v", c), ("PT", bank)],
                                w=[("ps_out",)], tok=False)
                            b.step("T", lambda e, c=c, bank=bank, nch=nch: e.matmul(
                                ps_sum[0:1, :], ones_col[:], PT[bank][:],
                                start=(c == 0), stop=(c == nch - 1)),
                                r=["ones_col", ("PT", bank)],
                                w=([("ps_sum",), ("ps_out",)] if c == nch - 1
                                   else [("ps_sum",)]),
                                tok=True,
                                reg_r=["ones_col", ("PT", bank), ("v", c)])
                        b.step("S", lambda e: e.activation(
                            lnrow[:], ps_sum[0:1, :], AF.Ln),
                            r=[("ps_sum",)], w=["lnrow"])
                        b.step("S", lambda e: e.activation(
                            recip_row[:], lnrow[:], AF.Exp, scale=-1.0),
                            r=["lnrow"], w=["recip_row"])
                        b.step("T", lambda e: e.matmul(
                            ps_st[0][:], ones_row_f[:], recip_row[:],
                            start=True, stop=True),
                            r=["ones_row_f", "recip_row"], w=[("st", 0)])
                        b.step("S", lambda e: e.copy(rb_sb[:], ps_st[0][:]),
                               r=[("st", 0)], w=["rb_sb"])
                        b.step("V", lambda e, h_abs=h_abs, g=g: e.tensor_tensor(
                            oT[:, h_abs * NT + g * 512:h_abs * NT + g * 512 + 512],
                            ps_out[:], rb_sb[:], ALU.mult),
                            r=[("ps_out",), "rb_sb"], w=[("oT", h_abs, g)])
            if dbg:
                b.step("Y", lambda e: e.dma_start(out=dbg_t["dbg_oT"][:], in_=oT[:]),
                       r=[("oT", hh, gg) for hh in range(32) for gg in range(2)], dma="dbg")
            b.barrier()
        es_ckv.close()

        oT_keys = [("oT", hh, gg) for hh in range(32) for gg in range(2)]
        with ExitStack() as ph:
            pe = ph.enter_context
            wout_cb = [pe(nc.sbuf_tensor(f"wout{i}", [128, 32 * 128], BF16)) for i in range(4)]
            sa_sb = [pe(nc.sbuf_tensor(f"sa_sb{i}", [128, 512], BF16)) for i in range(4)]
            psC = [pe(nc.psum_tensor(f"psC{i}", [128, 512], F32)) for i in range(8)]

            def wout_dma(ob):
                b.step("G", lambda e, ob=ob: e.dma_start(
                    out=wout_cb[ob % 4][:].rearrange("p (k n) -> p k n", k=32),
                    in_=wout_t[ob].rearrange("k p n -> p k n")),
                    w=[("wout_cb", ob % 4)], dma=f"wo{ob % 4}")

            wout_dma(0)
            wout_dma(1)
            for og in range(16):
                if 2 * og + 2 < 32:
                    wout_dma(2 * og + 2)
                if 2 * og + 3 < 32:
                    wout_dma(2 * og + 3)
                for K in range(32):
                    for bb in range(2):
                        ob = og * 2 + bb
                        for th in range(2):
                            pb = (og % 2) * 4 + bb * 2 + th
                            b.step("T", lambda e, K=K, ob=ob, th=th, pb=pb: e.matmul(
                                psC[pb][:], wout_cb[ob % 4][:, K * 128:(K + 1) * 128],
                                oT[:, K * NT + th * 512:K * NT + th * 512 + 512],
                                start=(K == 0), stop=(K == 31)),
                                r=[("wout_cb", ob % 4)] + (oT_keys if K == 0 else []),
                                w=[("psC", pb)], tok=(K == 31),
                                reg_r=([("wout_cb", ob % 4)] + oT_keys) if K == 31 else None)
                for bb in range(2):
                    ob = og * 2 + bb
                    for th in range(2):
                        pb = (og % 2) * 4 + bb * 2 + th
                        sb = bb * 2 + th
                        b.step("V", lambda e, pb=pb, sb=sb: e.tensor_copy(
                            sa_sb[sb][:], psC[pb][:]),
                            r=[("psC", pb)], w=[("sa_sb", sb)])
                        b.step("Y", lambda e, ob=ob, th=th, sb=sb: e.dma_start(
                            out=sa_scr[ob * 128:(ob + 1) * 128, th * 512:th * 512 + 512],
                            in_=sa_sb[sb][:]),
                            r=[("sa_sb", sb)], w=[("sa_scr", ob, th)], dma=f"sa{sb}")
            b.barrier()
        es_oT.close()

        sa_keys = [("sa_scr", ob, th) for ob in range(32) for th in range(2)]
        with ExitStack() as ph:
            pe = ph.enter_context
            qcb_i = [pe(nc.sbuf_tensor(f"qcb_i{i}", [128, 32 * 128], I32)) for i in range(2)]
            qcb_bf = [pe(nc.sbuf_tensor(f"qcb_bf{i}", [128, 4096], BF16)) for i in range(1)]
            s_ob = [pe(nc.sbuf_tensor(f"s_ob{i}", [128, 4096], BF16)) for i in range(2)]
            w_grp = [pe(nc.sbuf_tensor(f"w_grp{i}", [128, 4 * 4096], BF16)) for i in range(2)]
            xnj = [pe(nc.sbuf_tensor(f"xnj{i}", [128, 1024], BF16)) for i in range(4)]
            s_sl = [pe(nc.sbuf_tensor(f"s_sl{i}", [128, 2048], BF16)) for i in range(1)]
            x_oh = [pe(nc.sbuf_tensor(f"x_oh{i}", [128, 2048], F32)) for i in range(2)]
            g_tile = [pe(nc.sbuf_tensor(f"g_tile{i}", [128, 512], F32)) for i in range(1)]
            m_tile = [pe(nc.sbuf_tensor(f"m_tile{i}", [128, 512], BF16)) for i in range(1)]
            out_sb = [pe(nc.sbuf_tensor(f"out_sb{i}", [128, 2048], F32)) for i in range(2)]
            psD = [pe(nc.psum_tensor(f"psD{i}", [128, 512], F32)) for i in range(4)]
            ps_trD = pe(nc.psum_tensor("ps_trD", [128, 512], BF16))

            cs_keys = [("colsumsT", tc) for tc in range(8)]

            def dequant(og, bb):
                gbk = og * 4 + bb
                sl = bb % 2
                b.step("Y", lambda e, gbk=gbk, sl=sl: e.dma_start(
                    out=qcb_i[sl][:].rearrange("p (j n) -> p j n", j=32),
                    in_=q_tiled[gbk].rearrange("j p n -> p j n")),
                    w=[("qcb_i", sl)], dma=f"qcb{sl}")
                b.step("G", lambda e, gbk=gbk, sl=sl: e.dma_start(
                    out=s_ob[sl][:].rearrange("p (j n) -> p j n", j=32),
                    in_=scalesT_d[:, :, gbk * 128:(gbk + 1) * 128].broadcast_to([128, 32, 128])),
                    w=[("s_ob", sl)], dma=f"sob{sl}")
                b.step("S", lambda e, sl=sl: e.copy(qcb_bf[0][:], qcb_i[sl][:]),
                       r=[("qcb_i", sl)], w=[("qcb_bf",)])
                b.step("G", lambda e, og=og, bb=bb, sl=sl: e.tensor_mul(
                    w_grp[og % 2][:, bb * 4096:(bb + 1) * 4096], qcb_bf[0][:], s_ob[sl][:]),
                    r=[("qcb_bf",), ("s_ob", sl)], w=[("w_grp", og % 2, bb)])

            for og in range(8):
                for bb in range(4):
                    dequant(og, bb)
                for th in range(2):
                    tb = th % 2
                    b.step("Y", lambda e, og=og, th=th, tb=tb: e.dma_start(
                        out=s_sl[0][:].rearrange("p (b n) -> p b n", b=4),
                        in_=sa_scr[:].rearrange("(a p) n -> p a n", p=128)
                        [:, og * 4:og * 4 + 4, th * 512:th * 512 + 512]),
                        r=sa_keys, w=[("s_sl",)], dma="sl0")
                    b.step("Y", lambda e, og=og, th=th, tb=tb: e.dma_start(
                        out=x_oh[tb][:].rearrange("p (t n) -> p t n", t=4),
                        in_=x_own[:].rearrange("(t p) n -> p t n", p=128)
                        [:, th * 4:th * 4 + 4, og * 512:og * 512 + 512]),
                        w=[("x_oh", tb)], dma=f"xh{tb}")
                    for bb in range(4):
                        b.step("T", lambda e, og=og, bb=bb, th=th: e.matmul(
                            psD[bb][:], zT_sb[:, (og * 4 + bb) * 128:(og * 4 + bb) * 128 + 128],
                            colsumsT[0:32, th * 512:th * 512 + 512],
                            start=True, stop=False),
                            r=[d_key(zT_sb)] + cs_keys,
                            w=[("psD", bb)], tok=False)
                    for j in range(32):
                        jp, jh = j // 2, j % 2
                        jb = jp % 4
                        if jh == 0:
                            b.step("S", lambda e, jp=jp, th=th, jb=jb: e.dma_start(
                                out=xnj[jb][:].rearrange("p (a n) -> p a n", a=2),
                                in_=xn_scr[:].rearrange("(a p) n -> p a n", p=128)
                                [:, 2 * jp:2 * jp + 2, th * 512:th * 512 + 512]),
                                r=["xn_scr"], w=[("xnj", jb)], dma=f"xj{jb}")
                        for bb in range(4):
                            qlast = (bb == 3)
                            b.step("T", lambda e, og=og, j=j, bb=bb, jb=jb, jh=jh: e.matmul(
                                psD[bb][:],
                                w_grp[og % 2][:, bb * 4096 + j * 128:bb * 4096 + j * 128 + 128],
                                xnj[jb][:, jh * 512:jh * 512 + 512],
                                start=False, stop=(j == 31)),
                                r=[("w_grp", og % 2, bb), ("xnj", jb)],
                                w=([("psD", q) for q in range(4)] if (j == 31 and qlast)
                                   else [("psD", bb)]),
                                tok=qlast,
                                reg_r=([("xnj", jb), d_key(zT_sb)] + cs_keys
                                       + [("w_grp", og % 2, q) for q in range(4)]) if qlast else None)
                    for bb in range(4):
                        gbk = og * 4 + bb
                        b.step("S", lambda e, bb=bb, gbk=gbk: e.activation(
                            g_tile[0][:], psD[bb][:], AF.Silu,
                            bias=bias_sb[:, gbk:gbk + 1]),
                            r=[("psD", bb), d_key(bias_sb)], w=[("g_tile",)])
                        b.step("V", lambda e, bb=bb, tb=tb: e.tensor_tensor(
                            m_tile[0][:], g_tile[0][:],
                            s_sl[0][:, bb * 512:bb * 512 + 512], ALU.mult),
                            r=[("g_tile",), ("s_sl",)], w=[("m_tile",)])
                        for ts in range(4):
                            b.step("T", lambda e, bb=bb, ts=ts: e.transpose(
                                ps_trD[:, ts * 128:ts * 128 + 128],
                                m_tile[0][:, ts * 128:ts * 128 + 128], ident_bf[:]),
                                r=[("m_tile",), d_key(ident_bf)],
                                w=[("ps_trD",)], tok=(ts == 3),
                                reg_r=[("m_tile",), d_key(ident_bf)] if ts == 3 else None)
                        for ts in range(4):
                            b.step("V", lambda e, bb=bb, ts=ts, tb=tb: e.tensor_tensor(
                                out_sb[tb][:, ts * 512 + bb * 128:ts * 512 + bb * 128 + 128],
                                ps_trD[:, ts * 128:ts * 128 + 128],
                                x_oh[tb][:, ts * 512 + bb * 128:ts * 512 + bb * 128 + 128],
                                ALU.add),
                                r=[("ps_trD",), ("x_oh", tb)], w=[("out_sb", tb)])
                    for ts in range(4):
                        b.step("Y", lambda e, og=og, th=th, ts=ts, tb=tb: e.dma_start(
                            out=out_own[th * 512 + ts * 128:th * 512 + ts * 128 + 128,
                                        og * 512:og * 512 + 512],
                            in_=out_sb[tb][:, ts * 512:ts * 512 + 512]),
                            r=[("out_sb", tb)],
                            w=[("out", og, th, ts)], dma=f"ot{tb}")
            def fin(e):
                for sm in b.dsems.values():
                    if sm.n > 0:
                        e.wait_ge(sm.h, sm.n)
                return e.nop()
            b.step("Y", fin, tok=False)
            b.flush_pending()

    return nc, dbg_t


def d_key(t):
    return ("sb", t.name if hasattr(t, "name") else id(t))



def kernel(**inputs):
    dbg = int(os.environ.get("KDBG", "0"))
    x = np.asarray(inputs["L0_attn_hidden_states"], np.float32)
    shared = _prep_shared(inputs)
    nc, _ = build_nc(dbg=dbg)

    in_maps = []
    for core in range(8):
        bq, h = divmod(core, 2)
        m = dict(shared)
        m.update(_prep_core(x, bq, h))
        im = {
            "x_full": m["x_full"], "x_own": m["x_own"],
            "cos_k": m["cos_k"], "sin_k": m["sin_k"],
            "cos_q": m["cos_q"], "sin_q": m["sin_q"],
            "maskrows": m["maskrows"], "q_rowsinit": m["q_rowsinit"],
            "consts_f": m["consts_f"], "ident_bf": m["ident_bf"],
            "ident_f": m["ident_f"], "tri": m["tri"], "g_ind": m["g_ind"],
            "wkv_t": m["wkv_t"], "wkr_t": m["wkr_t"], "wqd_t": m["wqd_t"],
            "wqun_t": m["wqun_t"], "wqur_t": m["wqur_t"],
            "wukT_t": m["wukT_t"], "wuvT_t": m["wuvT_t"],
            "wout_t": m["wout_t"], "q_tiled": m["q_tiled"],
            "scalesT": m["scalesT"], "zT": m["zT"], "bias_r": m["bias_r"],
        }
        in_maps.append(im)

    res = run_bass_kernel_spmd(nc, in_maps, core_ids=list(range(8)))
    out = np.empty((4, SEQ, D), np.float32)
    for core in range(8):
        bq, h = divmod(core, 2)
        out[bq][_own_rows(h)] = res.results[core]["out_own"]
    return out


# version 04
# speedup vs baseline: 1.6011x; 1.1249x over previous
import os
import math
import numpy as np
import ml_dtypes
from contextlib import ExitStack

import concourse.bass as bass
import concourse.mybir as mybir
from concourse.bass_utils import run_bass_kernel_spmd

F32 = mybir.dt.float32
BF16 = mybir.dt.bfloat16
I32 = mybir.dt.int32
AF = mybir.ActivationFunctionType
AX = mybir.AxisListType
ALU = mybir.AluOpType
BF = ml_dtypes.bfloat16

D = 4096
SEQ = 2048
NT = 1024
H = 32
QL = 1536
KVL = 512
DN = 128
DR = 64
DV = 128
NEG = -1.0e9
SCALE = float(np.float32(1.0) / np.float32(np.sqrt(np.float32(DN + DR))))
EPS = 1.0e-5

NCH = (8, 16)
NW = (2, 4)
DIAG0 = (4, 12)



def _yarn_tables():
    i = np.arange(0, DR, 2, dtype=np.float32)
    freqs = (np.float32(1.0) / (np.float32(10000.0) ** (i / np.float32(DR)))).astype(np.float32)
    wavelengths = (np.float32(2.0 * np.pi) / freqs).astype(np.float32)
    ramp = np.clip((wavelengths / np.float32(4096.0) - np.float32(1.0)) / np.float32(31.0),
                   0.0, 1.0).astype(np.float32)
    scale = (np.float32(1.0) - ramp + ramp * np.float32(40.0)).astype(np.float32)
    return (freqs / scale).astype(np.float32)


def _cos_sin(pos, reps):
    inv = _yarn_tables()
    f = pos.astype(np.float32)[None, :] * inv[:, None]
    c = np.cos(f).astype(np.float32)
    s = np.sin(f).astype(np.float32)
    return np.tile(c, (reps, 1)), np.tile(s, (reps, 1))


def _perm_blocks(h):
    if h == 0:
        return [4, 5, 6, 7, 0, 1, 2, 3, 8, 9, 10, 11, 12, 13, 14, 15]
    return [0, 1, 2, 3, 4, 5, 6, 7, 12, 13, 14, 15, 8, 9, 10, 11]


def _own_rows(h):
    if h == 0:
        return np.r_[0:512, 1536:2048]
    return np.r_[512:1536]


def _maskrows(h):
    m = np.zeros((2, SEQ), np.float32)
    blk = lambda c: slice(c * 128, (c + 1) * 128)
    if h == 0:
        for c in range(0, 4):
            m[0, blk(c)] = NEG
        for c in range(8, 16):
            m[0, blk(c)] = NEG
    else:
        for c in range(8, 16):
            m[0, blk(c)] = NEG
        for c in range(8, 12):
            m[1, blk(c)] = NEG
    return m


def _tri_consts():
    out = np.zeros((128, 4 * 512), np.float32)
    st = np.arange(128)[:, None]
    for d in range(4):
        sq = np.arange(512)[None, :]
        m = np.where(sq < 128 * d, NEG, 0.0)
        inblk = (sq >= 128 * d) & (sq < 128 * (d + 1))
        m = np.where(inblk & (st > (sq - 128 * d)), NEG, m)
        out[:, d * 512:(d + 1) * 512] = m
    return out.astype(BF)


def _tile2(w, kb, mb):
    K, M = w.shape
    t = w.reshape(K // kb, kb, M // mb, mb).transpose(2, 0, 1, 3)
    return np.ascontiguousarray(t)


def _prep_shared(inp):
    d = {}
    d["wkv_t"] = np.ascontiguousarray(
        inp["L0_attn_w_kv_down"].reshape(32, 128, KVL))
    d["wkr_t"] = np.ascontiguousarray(
        inp["L0_attn_w_k_rope"].reshape(32, 128, DR))
    d["wqd_t"] = _tile2(inp["L0_attn_w_q_down"], 128, 128)
    wqun = _tile2(inp["L0_attn_w_q_up_nope"], 128, 256)
    d["wqun_t"] = wqun
    d["wqur_t"] = _tile2(inp["L0_attn_w_q_up_rope"], 128, 128)
    wukT = np.ascontiguousarray(inp["L0_attn_w_uk"].T)
    wuvT = np.ascontiguousarray(inp["L0_attn_w_uv"].T)
    d["wukT_t"] = _tile2(wukT, 128, 256)
    d["wuvT_t"] = _tile2(wuvT, 128, 256)
    d["wout_t"] = _tile2(inp["L0_attn_w_out"], 128, 128)
    q = inp["L0_ffn_qweight"].reshape(4096, 4096)
    qT = q.T
    d["q_tiled"] = _tile2(qT, 128, 128).astype(np.int32)
    d["scalesT"] = np.ascontiguousarray(
        inp["L0_ffn_scales"].reshape(4096, 32).T)[None]
    d["zT"] = np.ascontiguousarray(
        inp["L0_ffn_scaled_zeros"].reshape(4096, 32).T)
    d["bias_r"] = np.ascontiguousarray(
        inp["L0_ffn_bias"].reshape(32, 128).T)
    d["ident_bf"] = np.eye(128, dtype=BF)
    d["ident_f"] = np.eye(128, dtype=np.float32)
    d["tri"] = _tri_consts()
    g = np.zeros((32, 128, 32), BF)
    for j in range(32):
        g[j, :, j] = BF(1.0)
    d["g_ind"] = g
    cf = np.zeros((128, 8), np.float32)
    cf[:, 0] = EPS
    cf[:, 1] = SCALE
    cf[:, 2] = 0.0
    cf[:, 3] = 1.0
    cf[:, 4] = -1.0
    d["consts_f"] = cf
    qi = np.zeros((3, NT), np.float32)
    qi[1, 0:512] = 1.0
    qi[2, 512:NT] = 1.0
    d["q_rowsinit"] = qi
    return d


def _prep_core(x, b, h):
    d = {}
    perm = _perm_blocks(h)
    rows = np.concatenate([np.arange(c * 128, (c + 1) * 128) for c in perm])
    d["x_full"] = np.ascontiguousarray(x[b][rows])
    own = _own_rows(h)
    d["x_own"] = np.ascontiguousarray(x[b][own])
    pos_k = rows.astype(np.float32)
    ck, sk = _cos_sin(pos_k, 2)
    d["cos_k"], d["sin_k"] = ck, sk
    pos_q = own.astype(np.float32)
    cq, sq = _cos_sin(pos_q, 4)
    d["cos_q"], d["sin_q"] = cq, sq
    d["maskrows"] = _maskrows(h)
    return d



class _Sem:
    def __init__(self, h, name):
        self.h, self.name, self.n = h, name, 0


class _B:

    SEC = {"T": "tensor", "V": "vector", "S": "scalar", "G": "gpsimd", "Y": "sync"}

    def __init__(self, nc, blk, sems, sem_pool):
        self.nc, self.blk = nc, blk
        self.sems = sems
        self.sem_pool = list(sem_pool)
        self.dsems = {}
        self.last_w = {}
        self.readers = {}
        self.waited = {k: {} for k in self.SEC}
        self.pending = {}

    def dsem(self, stream):
        if stream not in self.dsems:
            self.dsems[stream] = _Sem(self.sem_pool.pop(), f"d_{stream}")
        return self.dsems[stream]

    def _need(self, r, w):
        need = {}
        for k in r:
            for s, v in self.last_w.get(k, {}).items():
                if need.get(s, 0) < v:
                    need[s] = v
        for k in w:
            for dd in (self.last_w.get(k, {}), self.readers.get(k, {})):
                for s, v in dd.items():
                    if need.get(s, 0) < v:
                        need[s] = v
        return need

    def step(self, ename, fn, r=(), w=(), tok=True, dma=None, reg_r=None):
        sem = self.dsem(dma) if dma else self.sems[ename]
        need = self._need(r, w)
        wt = self.waited[ename]
        waits = []
        for sm, v in sorted(need.items(), key=lambda kv: kv[0].name):
            if wt.get(sm, 0) < v:
                if sm is self.sems.get(ename) and v > sm.n:
                    raise AssertionError(
                        f"own-sem deadlock {ename} wait {v} > issued {sm.n}")
                waits.append((sm, v))
                wt[sm] = v
        if not tok:
            self.pending.setdefault(ename, []).append((waits, fn, None, 0))
            return None
        step_n = 16 if dma else 1
        sem.n += step_n
        self.pending.setdefault(ename, []).append((waits, fn, sem, step_n))
        t = (sem, sem.n)
        for k in w:
            self.last_w[k] = {sem: sem.n}
            self.readers[k] = {}
        for k in (reg_r if reg_r is not None else r):
            d = self.readers.setdefault(k, {})
            if d.get(sem, 0) < sem.n:
                d[sem] = sem.n
        return t

    def flush_pending(self):
        for ename in list(self.SEC):
            q = self.pending.get(ename)
            if not q:
                continue
            self.pending[ename] = []

            def body(e, q=q):
                for waits, fn, sem, step_n in q:
                    for sm, v in waits:
                        e.wait_ge(sm.h, v)
                    inst = fn(e)
                    if sem is not None:
                        inst.then_inc(sem.h, step_n)

            getattr(self.blk, self.SEC[ename])(body)

    def flush(self, stream, keys):
        sem = self.dsem(stream)
        for k in keys:
            self.last_w[k] = {sem: sem.n}
            self.readers[k] = {}

    def barrier(self):
        self.flush_pending()
        ph = self.sems["ph"]
        target = ph.n + 5
        ph.n = target
        allsems = [self.sems[k] for k in ("T", "V", "S", "G")] + list(self.dsems.values())

        for en in self.SEC:
            def body(e, en=en):
                for s in allsems:
                    if s.n > 0 and self.waited[en].get(s, 0) < s.n:
                        e.wait_ge(s.h, s.n)
                        self.waited[en][s] = s.n
                e.nop().then_inc(ph.h, 1)
                e.wait_ge(ph.h, target)
                self.waited[en][ph] = target
            getattr(self.blk, self.SEC[en])(body)
        self.last_w = {}
        self.readers = {}


def build_nc(dbg=0):
    nc = bass.Bass("TRN2", target_bir_lowering=False)
    dt_in = {}

    def din(name, shape, dtype):
        t = nc.dram_tensor(name, list(shape), dtype, kind="ExternalInput").ap()
        dt_in[name] = t
        return t

    x_full = din("x_full", (SEQ, D), F32)
    x_own = din("x_own", (NT, D), F32)
    cos_k = din("cos_k", (64, SEQ), F32)
    sin_k = din("sin_k", (64, SEQ), F32)
    cos_q = din("cos_q", (128, NT), F32)
    sin_q = din("sin_q", (128, NT), F32)
    maskrows = din("maskrows", (2, SEQ), F32)
    q_rowsinit = din("q_rowsinit", (3, NT), F32)
    consts_f = din("consts_f", (128, 8), F32)
    ident_bf_d = din("ident_bf", (128, 128), BF16)
    ident_f_d = din("ident_f", (128, 128), F32)
    tri_d = din("tri", (128, 2048), BF16)
    g_ind_d = din("g_ind", (32, 128, 32), BF16)
    wkv_t = din("wkv_t", (32, 128, KVL), F32)
    wkr_t = din("wkr_t", (32, 128, DR), F32)
    wqd_t = din("wqd_t", (12, 32, 128, 128), F32)
    wqun_t = din("wqun_t", (16, 12, 128, 256), F32)
    wqur_t = din("wqur_t", (16, 12, 128, 128), F32)
    wukT_t = din("wukT_t", (16, 4, 128, 256), F32)
    wuvT_t = din("wuvT_t", (16, 4, 128, 256), F32)
    wout_t = din("wout_t", (32, 32, 128, 128), F32)
    q_tiled = din("q_tiled", (32, 32, 128, 128), I32)
    scalesT_d = din("scalesT", (1, 32, 4096), F32)
    zT_d = din("zT", (32, 4096), F32)
    bias_d = din("bias_r", (128, 32), F32)

    out_own = nc.dram_tensor("out_own", [NT, D], F32, kind="ExternalOutput").ap()
    dbg_t = {}
    if dbg:
        for nme, shp, dt in [
            ("dbg_xnT", (128, 32 * NT), BF16), ("dbg_cq", (128, 12 * NT), BF16),
            ("dbg_ckv", (128, 4 * SEQ), BF16), ("dbg_krx", (67, SEQ), BF16),
            ("dbg_oT", (128, 32 * NT), BF16), ("dbg_cs", (32, NT), BF16),
        ]:
            dbg_t[nme] = nc.dram_tensor(nme, list(shp), dt, kind="ExternalOutput").ap()

    xn_scr = nc.dram_tensor("xn_scr", [D, NT], BF16, kind="Internal").ap()
    sa_scr = nc.dram_tensor("sa_scr", [D, NT], BF16, kind="Internal").ap()

    with ExitStack() as top:
        e = top.enter_context
        ident_bf = e(nc.sbuf_tensor("ident_bf_s", [128, 128], BF16))
        ident_f = e(nc.sbuf_tensor("ident_f_s", [128, 128], F32))
        tri_sb = e(nc.sbuf_tensor("tri_s", [128, 2048], BF16))
        cf_sb = e(nc.sbuf_tensor("consts_f_s", [128, 8], F32))
        ones_col = e(nc.sbuf_tensor("ones_col", [128, 1], BF16))
        ones_row_f = e(nc.sbuf_tensor("ones_row_f", [1, 128], F32))
        cosq_sb = e(nc.sbuf_tensor("cosq", [128, NT], F32))
        sinq_sb = e(nc.sbuf_tensor("sinq", [128, NT], F32))
        k_ropeX = e(nc.sbuf_tensor("k_ropeX", [67, SEQ], BF16))
        colsumsT = e(nc.sbuf_tensor("colsumsT", [32, NT], BF16))
        zT_sb = e(nc.sbuf_tensor("zT_s", [32, 4096], BF16))
        bias_sb = e(nc.sbuf_tensor("bias_s", [128, 32], F32))
        g_ind_sb = e(nc.sbuf_tensor("g_ind_s", [128, 32 * 32], BF16))

        nc.const_aps.register(F32, EPS, cf_sb[:, 0:1])
        nc.const_aps.register(F32, SCALE, cf_sb[:, 1:2])
        nc.const_aps.register(F32, 0.0, cf_sb[:, 2:3])
        nc.const_aps.register(F32, 1.0, cf_sb[:, 3:4])
        nc.const_aps.register(F32, -1.0, cf_sb[:, 4:5])

        sems = {}
        for nm in ("T", "V", "S", "G", "Y", "ph"):
            sems[nm] = _Sem(e(nc.semaphore(f"sem_{nm}")), nm)
        sem_pool = [e(nc.semaphore(f"sem_d{i}")) for i in range(48)]
        blk = e(nc.Block())
        b = _B(nc, blk, sems, sem_pool)

        es_ckv = top.enter_context(ExitStack())
        c_kvT = es_ckv.enter_context(nc.sbuf_tensor("c_kvT", [128, 4 * SEQ], BF16))
        c_qT = es_ckv.enter_context(nc.sbuf_tensor("c_qT", [128, 12 * NT], BF16))
        es_kr = top.enter_context(ExitStack())
        k_rope_raw = es_kr.enter_context(
            nc.sbuf_tensor("k_rope_raw", [64, SEQ], F32))
        cosk_sb = es_kr.enter_context(nc.sbuf_tensor("cosk", [64, SEQ], F32))
        sink_sb = es_kr.enter_context(nc.sbuf_tensor("sink", [64, SEQ], F32))

        for dst, src in [(ident_bf, ident_bf_d), (ident_f, ident_f_d),
                         (tri_sb, tri_d), (cf_sb, consts_f),
                         (cosk_sb, cos_k), (sink_sb, sin_k),
                         (cosq_sb, cos_q), (sinq_sb, sin_q),
                         (bias_sb, bias_d)]:
            b.step("Y", lambda e, d=dst, s=src: e.dma_start(out=d[:], in_=s[:]),
                   w=[d_key(dst)], dma="c0")
        b.step("G", lambda e: e.dma_start(out=zT_sb[:], in_=zT_d[:]),
               w=[d_key(zT_sb)], dma="c1")
        b.step("Y", lambda e: e.dma_start(
            out=g_ind_sb[:].rearrange("p (j c) -> p j c", j=32),
            in_=g_ind_d[:].rearrange("j p c -> p j c")),
            w=[d_key(g_ind_sb)], dma="c0")
        b.flush("c0", [d_key(t) for t in (ident_bf, ident_f, tri_sb, cf_sb,
                                          cosk_sb, sink_sb, cosq_sb, sinq_sb,
                                          bias_sb, g_ind_sb)])
        b.step("V", lambda e: e.memset(ones_col[:], 1.0), w=["ones_col"])
        b.step("V", lambda e: e.memset(ones_row_f[:], 1.0), w=["ones_row_f"])
        b.barrier()

        with ExitStack() as ph:
            pe = ph.enter_context
            wkv_bf = pe(nc.sbuf_tensor("wkv_bf", [128, 32 * 512], BF16))
            wkr_bf = pe(nc.sbuf_tensor("wkr_bf", [128, 32 * 64], BF16))
            xa = [pe(nc.sbuf_tensor(f"xa{i}", [128, D], F32)) for i in range(2)]
            xnb = [pe(nc.sbuf_tensor(f"xnb{i}", [128, D], BF16)) for i in range(2)]
            xnT_tile = [pe(nc.sbuf_tensor(f"xnTt{i}", [128, D], BF16)) for i in range(2)]
            stats = pe(nc.sbuf_tensor("stats", [128, 48], F32))
            aggr = pe(nc.sbuf_tensor("aggr", [128, 2], F32))
            sd = pe(nc.sbuf_tensor("sd", [128, 1], F32))
            rstd = pe(nc.sbuf_tensor("rstd", [128, 1], F32))
            ps_tA = [pe(nc.psum_tensor(f"ps_tA{i}", [128, 1024], BF16)) for i in range(2)]
            ps_kv = pe(nc.psum_tensor("ps_kv", [128, 512], F32))
            ps_kr = pe(nc.psum_tensor("ps_kr", [64, 128], F32))

            b.step("G", lambda e: e.dma_start(
                out=wkv_bf[:].rearrange("p (k m) -> p k m", k=32),
                in_=wkv_t[:].rearrange("k p m -> p k m")),
                w=["wkv_bf"], dma="wkv")
            b.step("G", lambda e: e.dma_start(
                out=wkr_bf[:].rearrange("p (k m) -> p k m", k=32),
                in_=wkr_t[:].rearrange("k p m -> p k m")),
                w=["wkr_bf"], dma="wkr")

            def ln_tile(src_ap, t, bi, dst_tile):
                b.step("Y", lambda e: e.dma_start(
                    out=xa[bi][:], in_=src_ap[t * 128:(t + 1) * 128, :]),
                    r=[], w=[("xa", bi)], dma=f"xa{bi}")
                for u in range(8):
                    b.step("V", lambda e, u=u: e.bn_stats(
                        stats[:, u * 6:(u + 1) * 6], xa[bi][:, u * 512:(u + 1) * 512]),
                        r=[("xa", bi)], w=[("stats", u)])
                b.step("V", lambda e: e.bn_aggr(aggr[:], stats[:, 0:48]),
                       r=[("stats", u) for u in range(8)], w=["aggr"])
                b.step("S", lambda e: e.activation(sd[:], aggr[:, 1:2], AF.Sqrt, bias=EPS),
                       r=["aggr"], w=["sd"])
                b.step("V", lambda e: e.reciprocal(rstd[:], sd[:]), r=["sd"], w=["rstd"])
                b.step("V", lambda e: e.tensor_scalar(
                    out=xnb[bi][:], in0=xa[bi][:], scalar1=aggr[:, 0:1],
                    scalar2=rstd[:], op0=ALU.subtract, op1=ALU.mult),
                    r=[("xa", bi), "aggr", "rstd"], w=[("xnb", bi)])
                for q in range(4):
                    pbank = q % 2
                    for u in range(8):
                        c = q * 8 + u
                        b.step("T", lambda e, c=c, u=u, pbank=pbank: e.transpose(
                            ps_tA[pbank][:, u * 128:(u + 1) * 128],
                            xnb[bi][:, c * 128:(c + 1) * 128], ident_bf[:]),
                            r=[("xnb", bi), d_key(ident_bf)],
                            w=[("ps_tA", pbank)],
                            tok=(u == 7),
                            reg_r=[("xnb", bi), d_key(ident_bf)] if u == 7 else None)
                    b.step("V", lambda e, q=q, pbank=pbank: e.tensor_copy(
                        dst_tile[:].rearrange("p (c n) -> p c n", c=32)[:, q * 8:(q + 1) * 8, :],
                        ps_tA[pbank][:].rearrange("p (c n) -> p c n", c=8)),
                        r=[("ps_tA", pbank)], w=[("dstq", id(dst_tile), q)])

            for t in range(16):
                bi = t % 2
                ln_tile(x_full, t, bi, xnT_tile[bi])
                for m in range(4):
                    for K in range(32):
                        b.step("T", lambda e, m=m, K=K, bi=bi: e.matmul(
                            ps_kv[:, m * 128:(m + 1) * 128],
                            wkv_bf[:, K * 512 + m * 128:K * 512 + (m + 1) * 128],
                            xnT_tile[bi][:, K * 128:(K + 1) * 128],
                            start=(K == 0), stop=(K == 31)),
                            r=["wkv_bf", ("dstq", id(xnT_tile[bi]), K // 8)],
                            w=[("ps_kv", m)], tok=(K == 31),
                            reg_r=(["wkv_bf"] + [("dstq", id(xnT_tile[bi]), qq) for qq in range(4)]) if K == 31 else None)
                b.step("V", lambda e, t=t: e.tensor_copy(
                    c_kvT[:].rearrange("p (m n) -> p m n", m=4)[:, :, t * 128:(t + 1) * 128],
                    ps_kv[:].rearrange("p (m n) -> p m n", m=4)),
                    r=[("ps_kv", m) for m in range(4)], w=[("c_kvT", t)])
                for K in range(32):
                    b.step("T", lambda e, K=K, bi=bi: e.matmul(
                        ps_kr[:, :],
                        wkr_bf[:, K * 64:(K + 1) * 64],
                        xnT_tile[bi][:, K * 128:(K + 1) * 128],
                        start=(K == 0), stop=(K == 31)),
                        r=["wkr_bf", ("dstq", id(xnT_tile[bi]), K // 8)],
                        w=[("ps_kr",)], tok=(K == 31),
                        reg_r=(["wkr_bf"] + [("dstq", id(xnT_tile[bi]), qq) for qq in range(4)]) if K == 31 else None)
                b.step("V", lambda e, t=t: e.tensor_copy(
                    k_rope_raw[:, t * 128:(t + 1) * 128], ps_kr[:, :]),
                    r=[("ps_kr",)], w=[("k_rope_raw", t)])
            b.barrier()

        with ExitStack() as ph:
            pe = ph.enter_context
            rotk = pe(nc.sbuf_tensor("rotk", [64, SEQ], F32))
            pA = pe(nc.sbuf_tensor("ropeA", [64, SEQ], F32))
            pB = pe(nc.sbuf_tensor("ropeB", [64, SEQ], F32))
            b.step("V", lambda e: e.tensor_scalar_mul(
                rotk[0:32, :], k_rope_raw[32:64, :], -1.0),
                r=["k_rope_raw"], w=["rotk_a"])
            b.step("V", lambda e: e.tensor_copy(rotk[32:64, :], k_rope_raw[0:32, :]),
                   r=["k_rope_raw"], w=["rotk_b"])
            b.step("V", lambda e: e.tensor_mul(pA[:], k_rope_raw[0:64, :], cosk_sb[:]),
                   r=["k_rope_raw", d_key(cosk_sb)], w=["pA"])
            b.step("V", lambda e: e.tensor_mul(pB[:], rotk[:], sink_sb[:]),
                   r=["rotk_a", "rotk_b", d_key(sink_sb)], w=["pB"])
            b.step("V", lambda e: e.tensor_add(k_ropeX[0:64, :], pA[:], pB[:]),
                   r=["pA", "pB"], w=["krx_rope"])
            b.step("V", lambda e: e.memset(k_ropeX[64:65, :], 1.0), w=["krx_ones"])
            b.step("G", lambda e: e.dma_start(out=k_ropeX[65:67, :], in_=maskrows[:]),
                   w=["krx_mask"], dma="mk")
            if dbg:
                b.step("Y", lambda e: e.dma_start(out=dbg_t["dbg_krx"][:], in_=k_ropeX[:]),
                       r=["krx_rope", "krx_ones", "krx_mask"], dma="dbg")
            b.barrier()
        es_kr.close()

        es_xn = top.enter_context(ExitStack())
        xnT_own = es_xn.enter_context(
            nc.sbuf_tensor("xnT_own", [128, 32 * NT], BF16, side="right"))
        with ExitStack() as ph:
            pe = ph.enter_context
            xa = [pe(nc.sbuf_tensor(f"xaq{i}", [128, D], F32)) for i in range(2)]
            xnb = [pe(nc.sbuf_tensor(f"xnbq{i}", [128, D], BF16)) for i in range(2)]
            stats = pe(nc.sbuf_tensor("statsq", [128, 48], F32))
            aggr = pe(nc.sbuf_tensor("aggrq", [128, 2], F32))
            sd = pe(nc.sbuf_tensor("sdq", [128, 1], F32))
            rstd = pe(nc.sbuf_tensor("rstdq", [128, 1], F32))
            ps_tA = [pe(nc.psum_tensor(f"ps_tAq{i}", [128, 1024], BF16)) for i in range(2)]

            for t in range(8):
                bi = t % 2
                b.step("Y", lambda e, t=t, bi=bi: e.dma_start(
                    out=xa[bi][:], in_=x_own[t * 128:(t + 1) * 128, :]),
                    w=[("xa", bi)], dma=f"xa{bi}")
                for u in range(8):
                    b.step("V", lambda e, u=u, bi=bi: e.bn_stats(
                        stats[:, u * 6:(u + 1) * 6], xa[bi][:, u * 512:(u + 1) * 512]),
                        r=[("xa", bi)], w=[("stats", u)])
                b.step("V", lambda e: e.bn_aggr(aggr[:], stats[:, 0:48]),
                       r=[("stats", u) for u in range(8)], w=["aggr"])
                b.step("S", lambda e: e.activation(sd[:], aggr[:, 1:2], AF.Sqrt, bias=EPS),
                       r=["aggr"], w=["sd"])
                b.step("V", lambda e: e.reciprocal(rstd[:], sd[:]), r=["sd"], w=["rstd"])
                b.step("V", lambda e, bi=bi: e.tensor_scalar(
                    out=xnb[bi][:], in0=xa[bi][:], scalar1=aggr[:, 0:1],
                    scalar2=rstd[:], op0=ALU.subtract, op1=ALU.mult),
                    r=[("xa", bi), "aggr", "rstd"], w=[("xnb", bi)])
                for q in range(4):
                    pbank = q % 2
                    for u in range(8):
                        c = q * 8 + u
                        b.step("T", lambda e, c=c, u=u, pbank=pbank, bi=bi: e.transpose(
                            ps_tA[pbank][:, u * 128:(u + 1) * 128],
                            xnb[bi][:, c * 128:(c + 1) * 128], ident_bf[:]),
                            r=[("xnb", bi), d_key(ident_bf)],
                            w=[("ps_tA", pbank)], tok=(u == 7),
                            reg_r=[("xnb", bi), d_key(ident_bf)] if u == 7 else None)
                    b.step("V", lambda e, q=q, t=t, pbank=pbank: e.tensor_copy(
                        xnT_own[:].rearrange("p (c n) -> p c n", c=32)
                        [:, q * 8:(q + 1) * 8, t * 128:(t + 1) * 128],
                        ps_tA[pbank][:].rearrange("p (c n) -> p c n", c=8)),
                        r=[("ps_tA", pbank)], w=[("xnT_own", t, q)])
            b.barrier()

        with ExitStack() as ph:
            pe = ph.enter_context
            wqd_cb = [pe(nc.sbuf_tensor(f"wqd_cb{i}", [128, 32 * 128], BF16)) for i in range(2)]
            colsq = pe(nc.sbuf_tensor("colsq", [128, 8 * 32], BF16))
            ps_cq = [pe(nc.psum_tensor(f"ps_cq{i}", [128, 512], F32)) for i in range(4)]
            ps_cs = pe(nc.psum_tensor("ps_cs", [128, 32], F32))
            ps_csT = pe(nc.psum_tensor("ps_csT", [32, 128], BF16))

            xn_own_keys = [("xnT_own", t, q) for t in range(8) for q in range(4)]
            for m in range(12):
                b.step("G", lambda e, m=m: e.dma_start(
                    out=wqd_cb[m % 2][:].rearrange("p (k n) -> p k n", k=32),
                    in_=wqd_t[m].rearrange("k p n -> p k n")),
                    w=[("wqd_cb", m % 2)], dma=f"wqd{m % 2}")
                for t in range(2):
                    pb = (m % 2) * 2 + t
                    for K in range(32):
                        b.step("T", lambda e, m=m, t=t, K=K, pb=pb: e.matmul(
                            ps_cq[pb][:],
                            wqd_cb[m % 2][:, K * 128:(K + 1) * 128],
                            xnT_own[:, K * NT + t * 512:K * NT + t * 512 + 512],
                            start=(K == 0), stop=(K == 31)),
                            r=([("wqd_cb", m % 2)] + (xn_own_keys if K == 0 else [])),
                            w=[("ps_cq", pb)], tok=(K == 31),
                            reg_r=([("wqd_cb", m % 2)] + xn_own_keys) if K == 31 else None)
                    b.step("V", lambda e, m=m, t=t, pb=pb: e.tensor_copy(
                        c_qT[:, m * NT + t * 512:m * NT + t * 512 + 512], ps_cq[pb][:]),
                        r=[("ps_cq", pb)], w=[("c_qT", m, t)])
            for tc in range(8):
                for j in range(32):
                    b.step("T", lambda e, tc=tc, j=j: e.matmul(
                        ps_cs[:], xnT_own[:, j * NT + tc * 128:j * NT + (tc) * 128 + 128],
                        g_ind_sb[:, j * 32:(j + 1) * 32],
                        start=(j == 0), stop=(j == 31)),
                        r=([d_key(g_ind_sb)] + (xn_own_keys if j == 0 else [])),
                        w=[("ps_cs",)], tok=(j == 31),
                        reg_r=([d_key(g_ind_sb)] + xn_own_keys) if j == 31 else None)
                b.step("V", lambda e, tc=tc: e.tensor_copy(
                    colsq[:, tc * 32:(tc + 1) * 32], ps_cs[:]),
                    r=[("ps_cs",)], w=[("colsq", tc)])
                b.step("T", lambda e, tc=tc: e.transpose(
                    ps_csT[:], colsq[:, tc * 32:(tc + 1) * 32], ident_bf[:]),
                    r=[("colsq", tc), d_key(ident_bf)], w=[("ps_csT",)])
                b.step("V", lambda e, tc=tc: e.tensor_copy(
                    colsumsT[0:32, tc * 128:(tc + 1) * 128], ps_csT[:]),
                    r=[("ps_csT",)], w=[("colsumsT", tc)])
            b.step("Y", lambda e: e.dma_start(
                out=xn_scr[:].rearrange("(j p) n -> p j n", p=128),
                in_=xnT_own[:].rearrange("p (j n) -> p j n", j=32)),
                r=xn_own_keys, w=["xn_scr"], dma="xs")
            if dbg:
                b.step("Y", lambda e: e.dma_start(out=dbg_t["dbg_xnT"][:], in_=xnT_own[:]),
                       r=xn_own_keys, dma="dbg")
                b.step("Y", lambda e: e.dma_start(out=dbg_t["dbg_cq"][:], in_=c_qT[:]),
                       r=[("c_qT", m, t) for m in range(12) for t in range(2)], dma="dbg")
                b.step("Y", lambda e: e.dma_start(out=dbg_t["dbg_ckv"][:], in_=c_kvT[:]),
                       r=[("c_kvT", t) for t in range(16)], dma="dbg")
                b.step("Y", lambda e: e.dma_start(out=dbg_t["dbg_cs"][:], in_=colsumsT[:]),
                       r=[("colsumsT", tc) for tc in range(8)], dma="dbg")
            b.barrier()
        es_xn.close()

        es_oT = top.enter_context(ExitStack())
        oT = es_oT.enter_context(
            nc.sbuf_tensor("oT", [128, 32 * NT], BF16, side="right"))
        with ExitStack() as ph:
            pe = ph.enter_context
            wqun_bf = [pe(nc.sbuf_tensor(f"wqun{i}", [128, 12 * 256], BF16)) for i in range(2)]
            wqur_bf = [pe(nc.sbuf_tensor(f"wqur{i}", [128, 12 * 128], BF16)) for i in range(2)]
            wukT_bf = [pe(nc.sbuf_tensor(f"wukT{i}", [128, 4 * 256], BF16)) for i in range(2)]
            wuvT_bf = [pe(nc.sbuf_tensor(f"wuvT{i}", [128, 4 * 256], BF16)) for i in range(2)]
            q_nopeT = [pe(nc.sbuf_tensor(f"qnope{i}", [128, NT], BF16)) for i in range(2)]
            qrx = [pe(nc.sbuf_tensor(f"qrx{i}", [67, NT], BF16)) for i in range(2)]
            kn = [pe(nc.sbuf_tensor(f"kn{i}", [128, SEQ], BF16)) for i in range(2)]
            v_grp = pe(nc.sbuf_tensor("v_grp", [128, 16 * 256], BF16))
            rope_tmp = pe(nc.sbuf_tensor("rope_tmp", [128, NT], F32))
            rot = pe(nc.sbuf_tensor("rot", [128, NT], F32))
            prA = pe(nc.sbuf_tensor("prA", [128, NT], F32))
            prB = pe(nc.sbuf_tensor("prB", [128, NT], F32))
            PT = [pe(nc.sbuf_tensor(f"PT{i}", [128, 512], BF16)) for i in range(2)]
            mcol4 = pe(nc.sbuf_tensor("mcol4", [128, 16], F32))
            mtmp = pe(nc.sbuf_tensor("mtmp", [128, 1], F32))
            mneg4 = pe(nc.sbuf_tensor("mneg4", [128, 16], F32))
            recip_row = pe(nc.sbuf_tensor("recip_row", [1, 512], F32))
            lnrow = pe(nc.sbuf_tensor("lnrow", [1, 512], F32))
            rb_sb = pe(nc.sbuf_tensor("rb_sb", [128, 512], F32))
            pool_t = pe(nc.psum_tensor("poolB", [128, 2048], F32))
            pool = [pool_t[:, i * 512:(i + 1) * 512] for i in range(4)]
            ps_st = [pe(nc.psum_tensor(f"ps_stB{i}", [128, 512], F32)) for i in range(2)]
            ps_out = pe(nc.psum_tensor("ps_outB", [128, 512], F32))
            ps_sum = pe(nc.psum_tensor("ps_sumB", [1, 512], F32))

            for p in range(2):
                b.step("G", lambda e, p=p: e.dma_start(out=qrx[p][64:67, :], in_=q_rowsinit[:]),
                       w=[("qrx_rows", p)], dma=f"qi{p}")

            ckv_keys = [("c_kvT", t) for t in range(16)]
            cq_keys = [("c_qT", m, t) for m in range(12) for t in range(2)]
            krx_keys = ["krx_rope", "krx_ones", "krx_mask"]

            def wdma(grp):
                gb = grp % 2
                for (buf, src, ncols) in [(wqun_bf, wqun_t, 256), (wqur_bf, wqur_t, 128),
                                          (wukT_bf, wukT_t, 256), (wuvT_bf, wuvT_t, 256)]:
                    b.step("G", lambda e, buf=buf, src=src, grp=grp: e.dma_start(
                        out=buf[gb][:].rearrange("p (k n) -> p k n", k=src.shape[1]),
                        in_=src[grp].rearrange("k p n -> p k n")),
                        w=[(id(buf), gb)], dma=f"wg{gb}")
                b.flush(f"wg{gb}", [(id(buf), gb) for buf in
                                    (wqun_bf, wqur_bf, wukT_bf, wuvT_bf)])

            wdma(0)
            pool_i = [0]

            def nxt_pool():
                i = pool_i[0] % 4
                pool_i[0] += 1
                return i

            for grp in range(16):
                gb = grp % 2
                if grp + 1 < 16:
                    wdma(grp + 1)
                for p in range(2):
                    for t in range(2):
                        pb = nxt_pool()
                        for K in range(12):
                            b.step("T", lambda e, p=p, t=t, K=K, pb=pb, gb=gb: e.matmul(
                                pool[pb],
                                wqun_bf[gb][:, K * 256 + p * 128:K * 256 + p * 128 + 128],
                                c_qT[:, K * NT + t * 512:K * NT + t * 512 + 512],
                                start=(K == 0), stop=(K == 11)),
                                r=[(id(wqun_bf), gb), ("c_qT", K, t)],
                                w=[("pool", pb)], tok=(K == 11),
                                reg_r=([(id(wqun_bf), gb)] + cq_keys) if K == 11 else None)
                        b.step("V", lambda e, p=p, t=t, pb=pb: e.tensor_copy(
                            q_nopeT[p][:, t * 512:t * 512 + 512], pool[pb]),
                            r=[("pool", pb)], w=[("qn", p, t)])
                for t in range(2):
                    pb = nxt_pool()
                    for K in range(12):
                        b.step("T", lambda e, t=t, K=K, pb=pb, gb=gb: e.matmul(
                            pool[pb],
                            wqur_bf[gb][:, K * 128:(K + 1) * 128],
                            c_qT[:, K * NT + t * 512:K * NT + t * 512 + 512],
                            start=(K == 0), stop=(K == 11)),
                            r=[(id(wqur_bf), gb), ("c_qT", K, t)],
                            w=[("pool", pb)], tok=(K == 11),
                            reg_r=([(id(wqur_bf), gb)] + cq_keys) if K == 11 else None)
                    b.step("V", lambda e, t=t, pb=pb: e.tensor_copy(
                        rope_tmp[:, t * 512:t * 512 + 512], pool[pb]),
                        r=[("pool", pb)], w=[("rope_tmp", t)])
                rt_keys = [("rope_tmp", 0), ("rope_tmp", 1)]
                for (dsta, srca) in [(0, 32), (32, 0), (64, 96), (96, 64)]:
                    neg = (dsta in (0, 64))
                    b.step("V", lambda e, dsta=dsta, srca=srca, neg=neg:
                           (e.tensor_scalar_mul(rot[dsta:dsta + 32, :],
                                                rope_tmp[srca:srca + 32, :], -1.0)
                            if neg else
                            e.tensor_copy(rot[dsta:dsta + 32, :],
                                          rope_tmp[srca:srca + 32, :])),
                           r=rt_keys, w=[("rot", dsta)])
                rot_keys = [("rot", a) for a in (0, 32, 64, 96)]
                b.step("V", lambda e: e.tensor_mul(prA[:], rope_tmp[:], cosq_sb[:]),
                       r=rt_keys + [d_key(cosq_sb)], w=["prA"])
                b.step("V", lambda e: e.tensor_mul(prB[:], rot[:], sinq_sb[:]),
                       r=rot_keys + [d_key(sinq_sb)], w=["prB"])
                for p in range(2):
                    b.step("V", lambda e, p=p: e.tensor_add(
                        qrx[p][0:64, :], prA[p * 64:p * 64 + 64, :],
                        prB[p * 64:p * 64 + 64, :]),
                        r=["prA", "prB"], w=[("qrx", p)])
                for p in range(2):
                    for s4 in range(4):
                        pb = nxt_pool()
                        for K in range(4):
                            b.step("T", lambda e, p=p, s4=s4, K=K, pb=pb, gb=gb: e.matmul(
                                pool[pb],
                                wukT_bf[gb][:, K * 256 + p * 128:K * 256 + p * 128 + 128],
                                c_kvT[:, K * SEQ + s4 * 512:K * SEQ + s4 * 512 + 512],
                                start=(K == 0), stop=(K == 3)),
                                r=[(id(wukT_bf), gb)] + (ckv_keys if K == 0 else []),
                                w=[("pool", pb)], tok=(K == 3),
                                reg_r=([(id(wukT_bf), gb)] + ckv_keys) if K == 3 else None)
                        b.step("S", lambda e, p=p, s4=s4, pb=pb: e.copy(
                            kn[p][:, s4 * 512:s4 * 512 + 512], pool[pb]),
                            r=[("pool", pb)], w=[("kn", p, s4)])
                for st in range(16):
                    pb = nxt_pool()
                    for K in range(4):
                        b.step("T", lambda e, st=st, K=K, pb=pb, gb=gb: e.matmul(
                            pool[pb][:, 0:256],
                            c_kvT[:, K * SEQ + st * 128:K * SEQ + st * 128 + 128],
                            wuvT_bf[gb][:, K * 256:(K + 1) * 256],
                            start=(K == 0), stop=(K == 3)),
                            r=[(id(wuvT_bf), gb)] + (ckv_keys if K == 0 else []),
                            w=[("pool", pb)], tok=(K == 3),
                            reg_r=([(id(wuvT_bf), gb)] + ckv_keys) if K == 3 else None)
                    b.step("S", lambda e, st=st, pb=pb: e.copy(
                        v_grp[:, st * 256:st * 256 + 256], pool[pb][:, 0:256]),
                        r=[("pool", pb)], w=[("v", st)])

                kn_keys = lambda p: [("kn", p, s4) for s4 in range(4)]
                tails = []
                for p in range(2):
                    for g in range(2):
                        pgi = p * 2 + g
                        for s in range(4):
                            qcol = g * 512 + s * 128
                            nw = NW[g]
                            base = 0 if nw == 4 else (2 * s) % 4
                            for w in range(nw):
                                pb = (base + w) % 4
                                b.step("T", lambda e, p=p, qcol=qcol, w=w, pb=pb: e.matmul(
                                    pool[pb], q_nopeT[p][:, qcol:qcol + 128],
                                    kn[p][:, w * 512:w * 512 + 512],
                                    start=True, stop=False),
                                    r=[("qn", p, g), ("kn", p, w)],
                                    w=[("pool", pb)], tok=False)
                                b.step("T", lambda e, p=p, qcol=qcol, w=w, pb=pb: e.matmul(
                                    pool[pb], qrx[p][0:64, qcol:qcol + 128],
                                    k_ropeX[0:64, w * 512:w * 512 + 512],
                                    start=False, stop=True),
                                    r=[("qrx", p), "krx_rope"],
                                    w=[("pool", pb)], tok=True,
                                    reg_r=[("qn", p, g), ("kn", p, w), ("qrx", p), "krx_rope"])
                                if w % 2 == 1:
                                    pb0 = (base + w - 1) % 4
                                    mc = pgi * 4 + s
                                    if w == 1:
                                        b.step("V", lambda e, pb0=pb0, mc=mc: e.reduce_max(
                                            mcol4[:, mc:mc + 1],
                                            pool_t[:, pb0 * 512:pb0 * 512 + 1024],
                                            axis=AX.X),
                                            r=[("pool", pb0), ("pool", pb0 + 1)],
                                            w=[("mcol4", mc)])
                                    else:
                                        b.step("V", lambda e, pb0=pb0: e.reduce_max(
                                            mtmp[:],
                                            pool_t[:, pb0 * 512:pb0 * 512 + 1024],
                                            axis=AX.X),
                                            r=[("pool", pb0), ("pool", pb0 + 1)],
                                            w=["mtmp"])
                                        b.step("V", lambda e, mc=mc: e.tensor_max(
                                            mcol4[:, mc:mc + 1], mcol4[:, mc:mc + 1],
                                            mtmp[:]),
                                            r=[("mcol4", mc), "mtmp"], w=[("mcol4", mc)])

                        def tail(p=p, g=g, pgi=pgi):
                            b.step("V", lambda e: e.tensor_scalar_mul(
                                mneg4[:, pgi * 4:pgi * 4 + 4],
                                mcol4[:, pgi * 4:pgi * 4 + 4], -1.0),
                                r=[("mcol4", pgi * 4 + ss) for ss in range(4)],
                                w=[("mneg4", pgi)])
                            for ss in range(4):
                                b.step("T", lambda e, ss=ss: e.transpose(
                                    ps_sum[0:1, ss * 128:ss * 128 + 128],
                                    mneg4[:, pgi * 4 + ss:pgi * 4 + ss + 1], ident_f[:]),
                                    r=[("mneg4", pgi), d_key(ident_f)], w=[("ps_sum",)])
                            b.step("V", lambda e: e.tensor_copy(
                                qrx[p][64:65, g * 512:g * 512 + 512],
                                ps_sum[0:1, 0:512]),
                                r=[("ps_sum",), ("qrx_rows", p)], w=[("qrx_nm", p, g)])
                        tails.append(tail)
                for tail in tails:
                    tail()
                for p in range(2):
                    h_abs = grp * 2 + p
                    for g in range(2):
                        nch = NCH[g]
                        d0 = DIAG0[g]
                        qsl = slice(g * 512, g * 512 + 512)

                        def emit_ST(c, qsl=qsl, d0=d0, p=p, g=g):
                            bank = c % 2
                            b.step("T", lambda e, p=p, c=c, bank=bank, qsl=qsl: e.matmul(
                                ps_st[bank][:], kn[p][:, c * 128:c * 128 + 128],
                                q_nopeT[p][:, qsl], start=True, stop=False),
                                r=[("kn", p, c // 4), ("qn", p, g)],
                                w=[("st", bank)], tok=False)
                            dd = c - d0
                            last = not (0 <= dd < 4)
                            b.step("T", lambda e, p=p, c=c, bank=bank, last=last, qsl=qsl: e.matmul(
                                ps_st[bank][:], k_ropeX[:, c * 128:c * 128 + 128],
                                qrx[p][0:67, qsl], start=False, stop=last),
                                r=krx_keys + [("qrx", p), ("qrx_nm", p, g), ("qrx_rows", p)],
                                w=[("st", bank)], tok=last,
                                reg_r=([("kn", p, c // 4), ("qn", p, g), ("qrx", p),
                                        ("qrx_nm", p, g), ("qrx_rows", p)] + krx_keys)
                                if last else None)
                            if not last:
                                b.step("T", lambda e, bank=bank, dd=dd: e.matmul(
                                    ps_st[bank][:], ident_bf[:],
                                    tri_sb[:, dd * 512:dd * 512 + 512],
                                    start=False, stop=True),
                                    r=[d_key(ident_bf), d_key(tri_sb)],
                                    w=[("st", bank)], tok=True,
                                    reg_r=[("kn", p, c // 4), ("qn", p, g), ("qrx", p),
                                           ("qrx_nm", p, g), ("qrx_rows", p),
                                           d_key(ident_bf), d_key(tri_sb)] + krx_keys)

                        emit_ST(0)
                        for c in range(nch):
                            bank = c % 2
                            if c + 1 < nch:
                                emit_ST(c + 1)
                            b.step("S", lambda e, c=c, bank=bank: e.activation(
                                PT[bank][:], ps_st[bank][:], AF.Exp, scale=SCALE),
                                r=[("st", bank)], w=[("PT", bank)])
                            b.step("T", lambda e, p=p, c=c, bank=bank, nch=nch: e.matmul(
                                ps_out[:], v_grp[:, c * 256 + p * 128:c * 256 + p * 128 + 128],
                                PT[bank][:], start=(c == 0), stop=(c == nch - 1)),
                                r=[("v", c), ("PT", bank)],
                                w=[("ps_out",)], tok=False)
                            b.step("T", lambda e, c=c, bank=bank, nch=nch: e.matmul(
                                ps_sum[0:1, :], ones_col[:], PT[bank][:],
                                start=(c == 0), stop=(c == nch - 1)),
                                r=["ones_col", ("PT", bank)],
                                w=([("ps_sum",), ("ps_out",)] if c == nch - 1
                                   else [("ps_sum",)]),
                                tok=True,
                                reg_r=["ones_col", ("PT", bank), ("v", c)])
                        b.step("S", lambda e: e.activation(
                            lnrow[:], ps_sum[0:1, :], AF.Ln),
                            r=[("ps_sum",)], w=["lnrow"])
                        b.step("S", lambda e: e.activation(
                            recip_row[:], lnrow[:], AF.Exp, scale=-1.0),
                            r=["lnrow"], w=["recip_row"])
                        b.step("T", lambda e: e.matmul(
                            ps_st[0][:], ones_row_f[:], recip_row[:],
                            start=True, stop=True),
                            r=["ones_row_f", "recip_row"], w=[("st", 0)])
                        b.step("S", lambda e: e.copy(rb_sb[:], ps_st[0][:]),
                               r=[("st", 0)], w=["rb_sb"])
                        b.step("V", lambda e, h_abs=h_abs, g=g: e.tensor_tensor(
                            oT[:, h_abs * NT + g * 512:h_abs * NT + g * 512 + 512],
                            ps_out[:], rb_sb[:], ALU.mult),
                            r=[("ps_out",), "rb_sb"], w=[("oT", h_abs, g)])
            if dbg:
                b.step("Y", lambda e: e.dma_start(out=dbg_t["dbg_oT"][:], in_=oT[:]),
                       r=[("oT", hh, gg) for hh in range(32) for gg in range(2)], dma="dbg")
            b.barrier()
        es_ckv.close()

        oT_keys = [("oT", hh, gg) for hh in range(32) for gg in range(2)]
        with ExitStack() as ph:
            pe = ph.enter_context
            wout_cb = [pe(nc.sbuf_tensor(f"wout{i}", [128, 32 * 128], BF16)) for i in range(4)]
            sa_sb = [pe(nc.sbuf_tensor(f"sa_sb{i}", [128, 512], BF16)) for i in range(4)]
            psC = [pe(nc.psum_tensor(f"psC{i}", [128, 512], F32)) for i in range(8)]

            def wout_dma(ob):
                b.step("G", lambda e, ob=ob: e.dma_start(
                    out=wout_cb[ob % 4][:].rearrange("p (k n) -> p k n", k=32),
                    in_=wout_t[ob].rearrange("k p n -> p k n")),
                    w=[("wout_cb", ob % 4)], dma=f"wo{ob % 4}")

            wout_dma(0)
            wout_dma(1)
            for og in range(16):
                if 2 * og + 2 < 32:
                    wout_dma(2 * og + 2)
                if 2 * og + 3 < 32:
                    wout_dma(2 * og + 3)
                for K in range(32):
                    for bb in range(2):
                        ob = og * 2 + bb
                        for th in range(2):
                            pb = (og % 2) * 4 + bb * 2 + th
                            b.step("T", lambda e, K=K, ob=ob, th=th, pb=pb: e.matmul(
                                psC[pb][:], wout_cb[ob % 4][:, K * 128:(K + 1) * 128],
                                oT[:, K * NT + th * 512:K * NT + th * 512 + 512],
                                start=(K == 0), stop=(K == 31)),
                                r=[("wout_cb", ob % 4)] + (oT_keys if K == 0 else []),
                                w=[("psC", pb)], tok=(K == 31),
                                reg_r=([("wout_cb", ob % 4)] + oT_keys) if K == 31 else None)
                for bb in range(2):
                    ob = og * 2 + bb
                    for th in range(2):
                        pb = (og % 2) * 4 + bb * 2 + th
                        sb = bb * 2 + th
                        b.step("V", lambda e, pb=pb, sb=sb: e.tensor_copy(
                            sa_sb[sb][:], psC[pb][:]),
                            r=[("psC", pb)], w=[("sa_sb", sb)])
                        b.step("Y", lambda e, ob=ob, th=th, sb=sb: e.dma_start(
                            out=sa_scr[ob * 128:(ob + 1) * 128, th * 512:th * 512 + 512],
                            in_=sa_sb[sb][:]),
                            r=[("sa_sb", sb)], w=[("sa_scr", ob, th)], dma=f"sa{sb}")
            b.barrier()
        es_oT.close()

        sa_keys = [("sa_scr", ob, th) for ob in range(32) for th in range(2)]
        with ExitStack() as ph:
            pe = ph.enter_context
            qcb_i = [pe(nc.sbuf_tensor(f"qcb_i{i}", [128, 32 * 128], I32)) for i in range(1)]
            qcb_bf = [pe(nc.sbuf_tensor(f"qcb_bf{i}", [128, 4096], BF16)) for i in range(1)]
            s_ob = [pe(nc.sbuf_tensor(f"s_ob{i}", [128, 4096], BF16)) for i in range(2)]
            w_grp = [pe(nc.sbuf_tensor(f"w_grp{i}", [128, 4 * 4096], BF16)) for i in range(2)]
            xnj = [pe(nc.sbuf_tensor(f"xnj{i}", [128, 1024], BF16)) for i in range(4)]
            s_sl = [pe(nc.sbuf_tensor(f"s_sl{i}", [128, 2048], BF16)) for i in range(2)]
            x_oh = [pe(nc.sbuf_tensor(f"x_oh{i}", [128, 2048], F32)) for i in range(2)]
            g8 = pe(nc.sbuf_tensor("g8", [128, 8 * 512], BF16))
            m_tile = [pe(nc.sbuf_tensor(f"m_tile{i}", [128, 512], BF16)) for i in range(2)]
            out_sb = [pe(nc.sbuf_tensor(f"out_sb{i}", [128, 2048], F32)) for i in range(2)]
            psD = [pe(nc.psum_tensor(f"psD{i}", [128, 512], F32)) for i in range(4)]
            ps_trD = [pe(nc.psum_tensor(f"ps_trD{i}", [128, 512], BF16)) for i in range(2)]

            cs_keys = [("colsumsT", tc) for tc in range(8)]

            def dequant(og, bb):
                gbk = og * 4 + bb
                sl = bb % 2
                b.step("Y", lambda e, gbk=gbk: e.dma_start(
                    out=qcb_i[0][:].rearrange("p (j n) -> p j n", j=32),
                    in_=q_tiled[gbk].rearrange("j p n -> p j n")),
                    w=[("qcb_i",)], dma="qcb0")
                b.step("G", lambda e, gbk=gbk, sl=sl: e.dma_start(
                    out=s_ob[sl][:].rearrange("p (j n) -> p j n", j=32),
                    in_=scalesT_d[:, :, gbk * 128:(gbk + 1) * 128].broadcast_to([128, 32, 128])),
                    w=[("s_ob", sl)], dma=f"sob{sl}")
                b.step("S", lambda e: e.copy(qcb_bf[0][:], qcb_i[0][:]),
                       r=[("qcb_i",)], w=[("qcb_bf",)])
                b.step("G", lambda e, og=og, bb=bb, sl=sl: e.tensor_mul(
                    w_grp[og % 2][:, bb * 4096:(bb + 1) * 4096], qcb_bf[0][:], s_ob[sl][:]),
                    r=[("qcb_bf",), ("s_ob", sl)], w=[("w_grp", og % 2, bb)])

            for og in range(8):
                for bb in range(4):
                    dequant(og, bb)
                for th in range(2):
                    b.step("Y", lambda e, og=og, th=th: e.dma_start(
                        out=s_sl[th][:].rearrange("p (b n) -> p b n", b=4),
                        in_=sa_scr[:].rearrange("(a p) n -> p a n", p=128)
                        [:, og * 4:og * 4 + 4, th * 512:th * 512 + 512]),
                        r=sa_keys, w=[("s_sl", th)], dma=f"sl{th}")
                    b.step("Y", lambda e, og=og, th=th: e.dma_start(
                        out=x_oh[th][:].rearrange("p (t n) -> p t n", t=4),
                        in_=x_own[:].rearrange("(t p) n -> p t n", p=128)
                        [:, th * 4:th * 4 + 4, og * 512:og * 512 + 512]),
                        w=[("x_oh", th)], dma=f"xh{th}")
                    for bb in range(4):
                        b.step("T", lambda e, og=og, bb=bb, th=th: e.matmul(
                            psD[bb][:], zT_sb[:, (og * 4 + bb) * 128:(og * 4 + bb) * 128 + 128],
                            colsumsT[0:32, th * 512:th * 512 + 512],
                            start=True, stop=False),
                            r=[d_key(zT_sb)] + cs_keys,
                            w=[("psD", bb)], tok=False)
                    for j in range(32):
                        jp, jh = j // 2, j % 2
                        jb = jp % 4
                        if jh == 0:
                            b.step("S", lambda e, jp=jp, th=th, jb=jb: e.dma_start(
                                out=xnj[jb][:].rearrange("p (a n) -> p a n", a=2),
                                in_=xn_scr[:].rearrange("(a p) n -> p a n", p=128)
                                [:, 2 * jp:2 * jp + 2, th * 512:th * 512 + 512]),
                                r=["xn_scr"], w=[("xnj", jb)], dma=f"xj{jb}")
                        for bb in range(4):
                            qlast = (bb == 3)
                            b.step("T", lambda e, og=og, j=j, bb=bb, jb=jb, jh=jh: e.matmul(
                                psD[bb][:],
                                w_grp[og % 2][:, bb * 4096 + j * 128:bb * 4096 + j * 128 + 128],
                                xnj[jb][:, jh * 512:jh * 512 + 512],
                                start=False, stop=(j == 31)),
                                r=[("w_grp", og % 2, bb), ("xnj", jb)],
                                w=([("psD", q) for q in range(4)] if (j == 31 and qlast)
                                   else [("psD", bb)]),
                                tok=qlast,
                                reg_r=([("xnj", jb), d_key(zT_sb)] + cs_keys
                                       + [("w_grp", og % 2, q) for q in range(4)]) if qlast else None)
                    for bb in range(4):
                        gbk = og * 4 + bb
                        b.step("S", lambda e, bb=bb, th=th, gbk=gbk: e.activation(
                            g8[:, (th * 4 + bb) * 512:(th * 4 + bb) * 512 + 512],
                            psD[bb][:], AF.Silu, bias=bias_sb[:, gbk:gbk + 1]),
                            r=[("psD", bb), d_key(bias_sb)], w=[("g8", th, bb)])
                for th in range(2):
                    for bb in range(4):
                        mb = bb % 2
                        b.step("V", lambda e, th=th, bb=bb, mb=mb: e.tensor_tensor(
                            m_tile[mb][:], g8[:, (th * 4 + bb) * 512:(th * 4 + bb) * 512 + 512],
                            s_sl[th][:, bb * 512:bb * 512 + 512], ALU.mult),
                            r=[("g8", th, bb), ("s_sl", th)], w=[("m_tile", mb)])
                        for ts in range(4):
                            b.step("T", lambda e, mb=mb, ts=ts: e.transpose(
                                ps_trD[mb][:, ts * 128:ts * 128 + 128],
                                m_tile[mb][:, ts * 128:ts * 128 + 128], ident_bf[:]),
                                r=[("m_tile", mb), d_key(ident_bf)],
                                w=[("ps_trD", mb)], tok=(ts == 3),
                                reg_r=[("m_tile", mb), d_key(ident_bf)] if ts == 3 else None)
                        for ts in range(4):
                            b.step("V", lambda e, th=th, bb=bb, mb=mb, ts=ts: e.tensor_tensor(
                                out_sb[th][:, ts * 512 + bb * 128:ts * 512 + bb * 128 + 128],
                                ps_trD[mb][:, ts * 128:ts * 128 + 128],
                                x_oh[th][:, ts * 512 + bb * 128:ts * 512 + bb * 128 + 128],
                                ALU.add),
                                r=[("ps_trD", mb), ("x_oh", th)], w=[("out_sb", th)])
                    for ts in range(4):
                        b.step("Y", lambda e, og=og, th=th, ts=ts: e.dma_start(
                            out=out_own[th * 512 + ts * 128:th * 512 + ts * 128 + 128,
                                        og * 512:og * 512 + 512],
                            in_=out_sb[th][:, ts * 512:ts * 512 + 512]),
                            r=[("out_sb", th)],
                            w=[("out", og, th, ts)], dma=f"ot{th}")
            def fin(e):
                for sm in b.dsems.values():
                    if sm.n > 0:
                        e.wait_ge(sm.h, sm.n)
                return e.nop()
            b.step("Y", fin, tok=False)
            b.flush_pending()

    return nc, dbg_t


def d_key(t):
    return ("sb", t.name if hasattr(t, "name") else id(t))



def kernel(**inputs):
    dbg = int(os.environ.get("KDBG", "0"))
    x = np.asarray(inputs["L0_attn_hidden_states"], np.float32)
    shared = _prep_shared(inputs)
    nc, _ = build_nc(dbg=dbg)

    in_maps = []
    for core in range(8):
        bq, h = divmod(core, 2)
        m = dict(shared)
        m.update(_prep_core(x, bq, h))
        im = {
            "x_full": m["x_full"], "x_own": m["x_own"],
            "cos_k": m["cos_k"], "sin_k": m["sin_k"],
            "cos_q": m["cos_q"], "sin_q": m["sin_q"],
            "maskrows": m["maskrows"], "q_rowsinit": m["q_rowsinit"],
            "consts_f": m["consts_f"], "ident_bf": m["ident_bf"],
            "ident_f": m["ident_f"], "tri": m["tri"], "g_ind": m["g_ind"],
            "wkv_t": m["wkv_t"], "wkr_t": m["wkr_t"], "wqd_t": m["wqd_t"],
            "wqun_t": m["wqun_t"], "wqur_t": m["wqur_t"],
            "wukT_t": m["wukT_t"], "wuvT_t": m["wuvT_t"],
            "wout_t": m["wout_t"], "q_tiled": m["q_tiled"],
            "scalesT": m["scalesT"], "zT": m["zT"], "bias_r": m["bias_r"],
        }
        in_maps.append(im)

    res = run_bass_kernel_spmd(nc, in_maps, core_ids=list(range(8)))
    out = np.empty((4, SEQ, D), np.float32)
    for core in range(8):
        bq, h = divmod(core, 2)
        out[bq][_own_rows(h)] = res.results[core]["out_own"]
    return out


# version 05
# speedup vs baseline: 1.8406x; 1.1496x over previous
import os
import math
import numpy as np
import ml_dtypes
from contextlib import ExitStack

import concourse.bass as bass
import concourse.mybir as mybir
from concourse.bass_utils import run_bass_kernel_spmd

F32 = mybir.dt.float32
BF16 = mybir.dt.bfloat16
I32 = mybir.dt.int32
AF = mybir.ActivationFunctionType
AX = mybir.AxisListType
ALU = mybir.AluOpType
BF = ml_dtypes.bfloat16

D = 4096
SEQ = 2048
NT = 1024
H = 32
QL = 1536
KVL = 512
DN = 128
DR = 64
DV = 128
NEG = -1.0e9
SCALE = float(np.float32(1.0) / np.float32(np.sqrt(np.float32(DN + DR))))
EPS = 1.0e-5

NCH = (8, 16)
NW = (2, 4)
DIAG0 = (4, 12)



def _yarn_tables():
    i = np.arange(0, DR, 2, dtype=np.float32)
    freqs = (np.float32(1.0) / (np.float32(10000.0) ** (i / np.float32(DR)))).astype(np.float32)
    wavelengths = (np.float32(2.0 * np.pi) / freqs).astype(np.float32)
    ramp = np.clip((wavelengths / np.float32(4096.0) - np.float32(1.0)) / np.float32(31.0),
                   0.0, 1.0).astype(np.float32)
    scale = (np.float32(1.0) - ramp + ramp * np.float32(40.0)).astype(np.float32)
    return (freqs / scale).astype(np.float32)


def _cos_sin(pos, reps):
    inv = _yarn_tables()
    f = pos.astype(np.float32)[None, :] * inv[:, None]
    c = np.cos(f).astype(np.float32)
    s = np.sin(f).astype(np.float32)
    return np.tile(c, (reps, 1)), np.tile(s, (reps, 1))


def _perm_blocks(h):
    if h == 0:
        return [4, 5, 6, 7, 0, 1, 2, 3, 8, 9, 10, 11, 12, 13, 14, 15]
    return [0, 1, 2, 3, 4, 5, 6, 7, 12, 13, 14, 15, 8, 9, 10, 11]


def _own_rows(h):
    if h == 0:
        return np.r_[0:512, 1536:2048]
    return np.r_[512:1536]


def _maskrows(h):
    m = np.zeros((2, SEQ), np.float32)
    blk = lambda c: slice(c * 128, (c + 1) * 128)
    if h == 0:
        for c in range(0, 4):
            m[0, blk(c)] = NEG
        for c in range(8, 16):
            m[0, blk(c)] = NEG
    else:
        for c in range(8, 16):
            m[0, blk(c)] = NEG
        for c in range(8, 12):
            m[1, blk(c)] = NEG
    return m


def _tri_consts():
    out = np.zeros((128, 4 * 512), np.float32)
    st = np.arange(128)[:, None]
    for d in range(4):
        sq = np.arange(512)[None, :]
        m = np.where(sq < 128 * d, NEG, 0.0)
        inblk = (sq >= 128 * d) & (sq < 128 * (d + 1))
        m = np.where(inblk & (st > (sq - 128 * d)), NEG, m)
        out[:, d * 512:(d + 1) * 512] = m
    return out.astype(BF)


def _tile2(w, kb, mb):
    K, M = w.shape
    t = w.reshape(K // kb, kb, M // mb, mb).transpose(2, 1, 0, 3)
    return np.ascontiguousarray(t)


def _prep_shared(inp):
    d = {}
    d["wkv_t"] = np.ascontiguousarray(
        inp["L0_attn_w_kv_down"].reshape(32, 128, KVL).transpose(1, 0, 2))
    d["wkr_t"] = np.ascontiguousarray(
        inp["L0_attn_w_k_rope"].reshape(32, 128, DR).transpose(1, 0, 2))
    d["wqd_t"] = _tile2(inp["L0_attn_w_q_down"], 128, 128)
    wqun = _tile2(inp["L0_attn_w_q_up_nope"], 128, 256)
    d["wqun_t"] = wqun
    d["wqur_t"] = _tile2(inp["L0_attn_w_q_up_rope"], 128, 128)
    wukT = np.ascontiguousarray(inp["L0_attn_w_uk"].T)
    wuvT = np.ascontiguousarray(inp["L0_attn_w_uv"].T)
    d["wukT_t"] = _tile2(wukT, 128, 256)
    d["wuvT_t"] = _tile2(wuvT, 128, 256)
    d["wout_t"] = _tile2(inp["L0_attn_w_out"], 128, 128)
    q = inp["L0_ffn_qweight"].reshape(4096, 4096)
    qT = q.T
    d["q_tiled"] = _tile2(qT, 128, 128).astype(np.int32)
    sc = inp["L0_ffn_scales"].reshape(32, 128, 32)
    sc = sc.transpose(0, 2, 1)[:, None]
    d["scales_bc"] = np.ascontiguousarray(
        np.broadcast_to(sc, (32, 128, 32, 128)))
    d["zT"] = np.ascontiguousarray(
        inp["L0_ffn_scaled_zeros"].reshape(4096, 32).T)
    d["bias_r"] = np.ascontiguousarray(
        inp["L0_ffn_bias"].reshape(32, 128).T)
    d["ident_bf"] = np.eye(128, dtype=BF)
    d["ident_f"] = np.eye(128, dtype=np.float32)
    d["tri"] = _tri_consts()
    g = np.zeros((128, 32, 32), BF)
    for j in range(32):
        g[:, j, j] = BF(1.0)
    d["g_ind"] = g
    cf = np.zeros((128, 8), np.float32)
    cf[:, 0] = EPS
    cf[:, 1] = SCALE
    cf[:, 2] = 0.0
    cf[:, 3] = 1.0
    cf[:, 4] = -1.0
    d["consts_f"] = cf
    qi = np.zeros((3, NT), np.float32)
    qi[1, 0:512] = 1.0
    qi[2, 512:NT] = 1.0
    d["q_rowsinit"] = qi
    return d


def _prep_core(x, b, h):
    d = {}
    perm = _perm_blocks(h)
    rows = np.concatenate([np.arange(c * 128, (c + 1) * 128) for c in perm])
    d["x_full"] = np.ascontiguousarray(x[b][rows])
    own = _own_rows(h)
    xo = np.ascontiguousarray(x[b][own])
    d["x_own"] = xo
    d["x_oh_t"] = np.ascontiguousarray(
        xo.reshape(2, 4, 128, 8, 512).transpose(3, 0, 2, 1, 4))
    pos_k = rows.astype(np.float32)
    ck, sk = _cos_sin(pos_k, 2)
    d["cos_k"], d["sin_k"] = ck, sk
    pos_q = own.astype(np.float32)
    cq, sq = _cos_sin(pos_q, 4)
    d["cos_q"], d["sin_q"] = cq, sq
    d["maskrows"] = _maskrows(h)
    return d



class _Sem:
    def __init__(self, h, name):
        self.h, self.name, self.n = h, name, 0


class _B:

    SEC = {"T": "tensor", "V": "vector", "S": "scalar", "G": "gpsimd", "Y": "sync"}

    def __init__(self, nc, blk, sems, sem_pool):
        self.nc, self.blk = nc, blk
        self.sems = sems
        self.sem_pool = list(sem_pool)
        self.dsems = {}
        self.last_w = {}
        self.readers = {}
        self.waited = {k: {} for k in self.SEC}
        self.pending = {}

    def dsem(self, stream):
        if stream not in self.dsems:
            self.dsems[stream] = _Sem(self.sem_pool.pop(), f"d_{stream}")
        return self.dsems[stream]

    def _need(self, r, w):
        need = {}
        for k in r:
            for s, v in self.last_w.get(k, {}).items():
                if need.get(s, 0) < v:
                    need[s] = v
        for k in w:
            for dd in (self.last_w.get(k, {}), self.readers.get(k, {})):
                for s, v in dd.items():
                    if need.get(s, 0) < v:
                        need[s] = v
        return need

    def step(self, ename, fn, r=(), w=(), tok=True, dma=None, reg_r=None):
        sem = self.dsem(dma) if dma else self.sems[ename]
        need = self._need(r, w)
        wt = self.waited[ename]
        waits = []
        for sm, v in sorted(need.items(), key=lambda kv: kv[0].name):
            if wt.get(sm, 0) < v:
                if sm is self.sems.get(ename) and v > sm.n:
                    raise AssertionError(
                        f"own-sem deadlock {ename} wait {v} > issued {sm.n}")
                waits.append((sm, v))
                wt[sm] = v
        if not tok:
            self.pending.setdefault(ename, []).append((waits, fn, None, 0))
            return None
        step_n = 16 if dma else 1
        sem.n += step_n
        self.pending.setdefault(ename, []).append((waits, fn, sem, step_n))
        t = (sem, sem.n)
        for k in w:
            self.last_w[k] = {sem: sem.n}
            self.readers[k] = {}
        for k in (reg_r if reg_r is not None else r):
            d = self.readers.setdefault(k, {})
            if d.get(sem, 0) < sem.n:
                d[sem] = sem.n
        return t

    def flush_pending(self):
        for ename in list(self.SEC):
            q = self.pending.get(ename)
            if not q:
                continue
            self.pending[ename] = []

            def body(e, q=q):
                for waits, fn, sem, step_n in q:
                    for sm, v in waits:
                        e.wait_ge(sm.h, v)
                    inst = fn(e)
                    if sem is not None:
                        inst.then_inc(sem.h, step_n)

            getattr(self.blk, self.SEC[ename])(body)

    def flush(self, stream, keys):
        sem = self.dsem(stream)
        for k in keys:
            self.last_w[k] = {sem: sem.n}
            self.readers[k] = {}

    def barrier(self):
        self.flush_pending()
        ph = self.sems["ph"]
        target = ph.n + 5
        ph.n = target
        allsems = [self.sems[k] for k in ("T", "V", "S", "G")] + list(self.dsems.values())

        for en in self.SEC:
            def body(e, en=en):
                for s in allsems:
                    if s.n > 0 and self.waited[en].get(s, 0) < s.n:
                        e.wait_ge(s.h, s.n)
                        self.waited[en][s] = s.n
                e.nop().then_inc(ph.h, 1)
                e.wait_ge(ph.h, target)
                self.waited[en][ph] = target
            getattr(self.blk, self.SEC[en])(body)
        self.last_w = {}
        self.readers = {}


def build_nc(dbg=0):
    nc = bass.Bass("TRN2", target_bir_lowering=False)
    dt_in = {}

    def din(name, shape, dtype):
        t = nc.dram_tensor(name, list(shape), dtype, kind="ExternalInput").ap()
        dt_in[name] = t
        return t

    x_full = din("x_full", (SEQ, D), F32)
    x_own = din("x_own", (NT, D), F32)
    x_oh_t = din("x_oh_t", (8, 2, 128, 4, 512), F32)
    cos_k = din("cos_k", (64, SEQ), F32)
    sin_k = din("sin_k", (64, SEQ), F32)
    cos_q = din("cos_q", (128, NT), F32)
    sin_q = din("sin_q", (128, NT), F32)
    maskrows = din("maskrows", (2, SEQ), F32)
    q_rowsinit = din("q_rowsinit", (3, NT), F32)
    consts_f = din("consts_f", (128, 8), F32)
    ident_bf_d = din("ident_bf", (128, 128), BF16)
    ident_f_d = din("ident_f", (128, 128), F32)
    tri_d = din("tri", (128, 2048), BF16)
    g_ind_d = din("g_ind", (128, 32, 32), BF16)
    wkv_t = din("wkv_t", (128, 32, KVL), F32)
    wkr_t = din("wkr_t", (128, 32, DR), F32)
    wqd_t = din("wqd_t", (12, 128, 32, 128), F32)
    wqun_t = din("wqun_t", (16, 128, 12, 256), F32)
    wqur_t = din("wqur_t", (16, 128, 12, 128), F32)
    wukT_t = din("wukT_t", (16, 128, 4, 256), F32)
    wuvT_t = din("wuvT_t", (16, 128, 4, 256), F32)
    wout_t = din("wout_t", (32, 128, 32, 128), F32)
    q_tiled = din("q_tiled", (32, 128, 32, 128), I32)
    scales_bc_d = din("scales_bc", (32, 128, 32, 128), F32)
    zT_d = din("zT", (32, 4096), F32)
    bias_d = din("bias_r", (128, 32), F32)

    out_own = nc.dram_tensor("out_own", [NT, D], F32, kind="ExternalOutput").ap()
    dbg_t = {}
    if dbg:
        for nme, shp, dt in [
            ("dbg_xnT", (128, 32 * NT), BF16), ("dbg_cq", (128, 12 * NT), BF16),
            ("dbg_ckv", (128, 4 * SEQ), BF16), ("dbg_krx", (67, SEQ), BF16),
            ("dbg_oT", (128, 32 * NT), BF16), ("dbg_cs", (32, NT), BF16),
        ]:
            dbg_t[nme] = nc.dram_tensor(nme, list(shp), dt, kind="ExternalOutput").ap()

    xn_scr = nc.dram_tensor("xn_scr", [D, NT], BF16, kind="Internal").ap()
    sa_scr = nc.dram_tensor("sa_scr", [D, NT], BF16, kind="Internal").ap()

    with ExitStack() as top:
        e = top.enter_context
        ident_bf = e(nc.sbuf_tensor("ident_bf_s", [128, 128], BF16))
        ident_f = e(nc.sbuf_tensor("ident_f_s", [128, 128], F32))
        tri_sb = e(nc.sbuf_tensor("tri_s", [128, 2048], BF16))
        cf_sb = e(nc.sbuf_tensor("consts_f_s", [128, 8], F32))
        ones_col = e(nc.sbuf_tensor("ones_col", [128, 1], BF16))
        ones_row_f = e(nc.sbuf_tensor("ones_row_f", [1, 128], F32))
        cosq_sb = e(nc.sbuf_tensor("cosq", [128, NT], F32))
        sinq_sb = e(nc.sbuf_tensor("sinq", [128, NT], F32))
        k_ropeX = e(nc.sbuf_tensor("k_ropeX", [67, SEQ], BF16))
        colsumsT = e(nc.sbuf_tensor("colsumsT", [32, NT], BF16))
        zT_sb = e(nc.sbuf_tensor("zT_s", [32, 4096], BF16))
        bias_sb = e(nc.sbuf_tensor("bias_s", [128, 32], F32))
        g_ind_sb = e(nc.sbuf_tensor("g_ind_s", [128, 32 * 32], BF16))

        nc.const_aps.register(F32, EPS, cf_sb[:, 0:1])
        nc.const_aps.register(F32, SCALE, cf_sb[:, 1:2])
        nc.const_aps.register(F32, 0.0, cf_sb[:, 2:3])
        nc.const_aps.register(F32, 1.0, cf_sb[:, 3:4])
        nc.const_aps.register(F32, -1.0, cf_sb[:, 4:5])

        sems = {}
        for nm in ("T", "V", "S", "G", "Y", "ph"):
            sems[nm] = _Sem(e(nc.semaphore(f"sem_{nm}")), nm)
        sem_pool = [e(nc.semaphore(f"sem_d{i}")) for i in range(48)]
        blk = e(nc.Block())
        b = _B(nc, blk, sems, sem_pool)

        es_ckv = top.enter_context(ExitStack())
        c_kvT = es_ckv.enter_context(nc.sbuf_tensor("c_kvT", [128, 4 * SEQ], BF16))
        c_qT = es_ckv.enter_context(nc.sbuf_tensor("c_qT", [128, 12 * NT], BF16))
        es_kr = top.enter_context(ExitStack())
        k_rope_raw = es_kr.enter_context(
            nc.sbuf_tensor("k_rope_raw", [64, SEQ], F32))
        cosk_sb = es_kr.enter_context(nc.sbuf_tensor("cosk", [64, SEQ], F32))
        sink_sb = es_kr.enter_context(nc.sbuf_tensor("sink", [64, SEQ], F32))

        for dst, src in [(ident_bf, ident_bf_d), (ident_f, ident_f_d),
                         (tri_sb, tri_d), (cf_sb, consts_f),
                         (cosk_sb, cos_k), (sink_sb, sin_k),
                         (cosq_sb, cos_q), (sinq_sb, sin_q),
                         (bias_sb, bias_d)]:
            b.step("Y", lambda e, d=dst, s=src: e.dma_start(out=d[:], in_=s[:]),
                   w=[d_key(dst)], dma="c0")
        b.step("G", lambda e: e.dma_start(out=zT_sb[:], in_=zT_d[:]),
               w=[d_key(zT_sb)], dma="c1")
        b.step("Y", lambda e: e.dma_start(
            out=g_ind_sb[:].rearrange("p (j c) -> p j c", j=32),
            in_=g_ind_d[:]),
            w=[d_key(g_ind_sb)], dma="c0")
        b.flush("c0", [d_key(t) for t in (ident_bf, ident_f, tri_sb, cf_sb,
                                          cosk_sb, sink_sb, cosq_sb, sinq_sb,
                                          bias_sb, g_ind_sb)])
        b.step("V", lambda e: e.memset(ones_col[:], 1.0), w=["ones_col"])
        b.step("V", lambda e: e.memset(ones_row_f[:], 1.0), w=["ones_row_f"])
        b.barrier()

        with ExitStack() as ph:
            pe = ph.enter_context
            wkv_bf = pe(nc.sbuf_tensor("wkv_bf", [128, 32 * 512], BF16))
            wkr_bf = pe(nc.sbuf_tensor("wkr_bf", [128, 32 * 64], BF16))
            xa = [pe(nc.sbuf_tensor(f"xa{i}", [128, D], F32)) for i in range(2)]
            xnb = [pe(nc.sbuf_tensor(f"xnb{i}", [128, D], BF16)) for i in range(2)]
            xnT_tile = [pe(nc.sbuf_tensor(f"xnTt{i}", [128, D], BF16)) for i in range(2)]
            stats = pe(nc.sbuf_tensor("stats", [128, 48], F32))
            aggr = pe(nc.sbuf_tensor("aggr", [128, 2], F32))
            sd = pe(nc.sbuf_tensor("sd", [128, 1], F32))
            rstd = pe(nc.sbuf_tensor("rstd", [128, 1], F32))
            ps_tA = [pe(nc.psum_tensor(f"ps_tA{i}", [128, 1024], BF16)) for i in range(2)]
            ps_kv = pe(nc.psum_tensor("ps_kv", [128, 512], F32))
            ps_kr = pe(nc.psum_tensor("ps_kr", [64, 128], F32))

            b.step("G", lambda e: e.dma_start(
                out=wkv_bf[:].rearrange("p (k m) -> p k m", k=32),
                in_=wkv_t[:]),
                w=["wkv_bf"], dma="wkv")
            b.step("G", lambda e: e.dma_start(
                out=wkr_bf[:].rearrange("p (k m) -> p k m", k=32),
                in_=wkr_t[:]),
                w=["wkr_bf"], dma="wkr")

            def ln_tile(src_ap, t, bi, dst_tile):
                b.step("Y", lambda e: e.dma_start(
                    out=xa[bi][:], in_=src_ap[t * 128:(t + 1) * 128, :]),
                    r=[], w=[("xa", bi)], dma=f"xa{bi}")
                for u in range(8):
                    b.step("V", lambda e, u=u: e.bn_stats(
                        stats[:, u * 6:(u + 1) * 6], xa[bi][:, u * 512:(u + 1) * 512]),
                        r=[("xa", bi)], w=[("stats", u)])
                b.step("V", lambda e: e.bn_aggr(aggr[:], stats[:, 0:48]),
                       r=[("stats", u) for u in range(8)], w=["aggr"])
                b.step("S", lambda e: e.activation(sd[:], aggr[:, 1:2], AF.Sqrt, bias=EPS),
                       r=["aggr"], w=["sd"])
                b.step("V", lambda e: e.reciprocal(rstd[:], sd[:]), r=["sd"], w=["rstd"])
                b.step("V", lambda e: e.tensor_scalar(
                    out=xnb[bi][:], in0=xa[bi][:], scalar1=aggr[:, 0:1],
                    scalar2=rstd[:], op0=ALU.subtract, op1=ALU.mult),
                    r=[("xa", bi), "aggr", "rstd"], w=[("xnb", bi)])
                for q in range(4):
                    pbank = q % 2
                    for u in range(8):
                        c = q * 8 + u
                        b.step("T", lambda e, c=c, u=u, pbank=pbank: e.transpose(
                            ps_tA[pbank][:, u * 128:(u + 1) * 128],
                            xnb[bi][:, c * 128:(c + 1) * 128], ident_bf[:]),
                            r=[("xnb", bi), d_key(ident_bf)],
                            w=[("ps_tA", pbank)],
                            tok=(u == 7),
                            reg_r=[("xnb", bi), d_key(ident_bf)] if u == 7 else None)
                    b.step("V", lambda e, q=q, pbank=pbank: e.tensor_copy(
                        dst_tile[:].rearrange("p (c n) -> p c n", c=32)[:, q * 8:(q + 1) * 8, :],
                        ps_tA[pbank][:].rearrange("p (c n) -> p c n", c=8)),
                        r=[("ps_tA", pbank)], w=[("dstq", id(dst_tile), q)])

            for t in range(16):
                bi = t % 2
                ln_tile(x_full, t, bi, xnT_tile[bi])
                for m in range(4):
                    for K in range(32):
                        b.step("T", lambda e, m=m, K=K, bi=bi: e.matmul(
                            ps_kv[:, m * 128:(m + 1) * 128],
                            wkv_bf[:, K * 512 + m * 128:K * 512 + (m + 1) * 128],
                            xnT_tile[bi][:, K * 128:(K + 1) * 128],
                            start=(K == 0), stop=(K == 31)),
                            r=["wkv_bf", ("dstq", id(xnT_tile[bi]), K // 8)],
                            w=[("ps_kv", m)], tok=(K == 31),
                            reg_r=(["wkv_bf"] + [("dstq", id(xnT_tile[bi]), qq) for qq in range(4)]) if K == 31 else None)
                b.step("V", lambda e, t=t: e.tensor_copy(
                    c_kvT[:].rearrange("p (m n) -> p m n", m=4)[:, :, t * 128:(t + 1) * 128],
                    ps_kv[:].rearrange("p (m n) -> p m n", m=4)),
                    r=[("ps_kv", m) for m in range(4)], w=[("c_kvT", t)])
                for K in range(32):
                    b.step("T", lambda e, K=K, bi=bi: e.matmul(
                        ps_kr[:, :],
                        wkr_bf[:, K * 64:(K + 1) * 64],
                        xnT_tile[bi][:, K * 128:(K + 1) * 128],
                        start=(K == 0), stop=(K == 31)),
                        r=["wkr_bf", ("dstq", id(xnT_tile[bi]), K // 8)],
                        w=[("ps_kr",)], tok=(K == 31),
                        reg_r=(["wkr_bf"] + [("dstq", id(xnT_tile[bi]), qq) for qq in range(4)]) if K == 31 else None)
                b.step("V", lambda e, t=t: e.tensor_copy(
                    k_rope_raw[:, t * 128:(t + 1) * 128], ps_kr[:, :]),
                    r=[("ps_kr",)], w=[("k_rope_raw", t)])
            b.barrier()

        with ExitStack() as ph:
            pe = ph.enter_context
            rotk = pe(nc.sbuf_tensor("rotk", [64, SEQ], F32))
            pA = pe(nc.sbuf_tensor("ropeA", [64, SEQ], F32))
            pB = pe(nc.sbuf_tensor("ropeB", [64, SEQ], F32))
            b.step("V", lambda e: e.tensor_scalar_mul(
                rotk[0:32, :], k_rope_raw[32:64, :], -1.0),
                r=["k_rope_raw"], w=["rotk_a"])
            b.step("V", lambda e: e.tensor_copy(rotk[32:64, :], k_rope_raw[0:32, :]),
                   r=["k_rope_raw"], w=["rotk_b"])
            b.step("V", lambda e: e.tensor_mul(pA[:], k_rope_raw[0:64, :], cosk_sb[:]),
                   r=["k_rope_raw", d_key(cosk_sb)], w=["pA"])
            b.step("V", lambda e: e.tensor_mul(pB[:], rotk[:], sink_sb[:]),
                   r=["rotk_a", "rotk_b", d_key(sink_sb)], w=["pB"])
            b.step("V", lambda e: e.tensor_add(k_ropeX[0:64, :], pA[:], pB[:]),
                   r=["pA", "pB"], w=["krx_rope"])
            b.step("V", lambda e: e.memset(k_ropeX[64:65, :], 1.0), w=["krx_ones"])
            b.step("G", lambda e: e.dma_start(out=k_ropeX[65:67, :], in_=maskrows[:]),
                   w=["krx_mask"], dma="mk")
            if dbg:
                b.step("Y", lambda e: e.dma_start(out=dbg_t["dbg_krx"][:], in_=k_ropeX[:]),
                       r=["krx_rope", "krx_ones", "krx_mask"], dma="dbg")
            b.barrier()
        es_kr.close()

        es_xn = top.enter_context(ExitStack())
        xnT_own = es_xn.enter_context(
            nc.sbuf_tensor("xnT_own", [128, 32 * NT], BF16, side="right"))
        with ExitStack() as ph:
            pe = ph.enter_context
            xa = [pe(nc.sbuf_tensor(f"xaq{i}", [128, D], F32)) for i in range(2)]
            xnb = [pe(nc.sbuf_tensor(f"xnbq{i}", [128, D], BF16)) for i in range(2)]
            stats = pe(nc.sbuf_tensor("statsq", [128, 48], F32))
            aggr = pe(nc.sbuf_tensor("aggrq", [128, 2], F32))
            sd = pe(nc.sbuf_tensor("sdq", [128, 1], F32))
            rstd = pe(nc.sbuf_tensor("rstdq", [128, 1], F32))
            ps_tA = [pe(nc.psum_tensor(f"ps_tAq{i}", [128, 1024], BF16)) for i in range(2)]

            for t in range(8):
                bi = t % 2
                b.step("Y", lambda e, t=t, bi=bi: e.dma_start(
                    out=xa[bi][:], in_=x_own[t * 128:(t + 1) * 128, :]),
                    w=[("xa", bi)], dma=f"xa{bi}")
                for u in range(8):
                    b.step("V", lambda e, u=u, bi=bi: e.bn_stats(
                        stats[:, u * 6:(u + 1) * 6], xa[bi][:, u * 512:(u + 1) * 512]),
                        r=[("xa", bi)], w=[("stats", u)])
                b.step("V", lambda e: e.bn_aggr(aggr[:], stats[:, 0:48]),
                       r=[("stats", u) for u in range(8)], w=["aggr"])
                b.step("S", lambda e: e.activation(sd[:], aggr[:, 1:2], AF.Sqrt, bias=EPS),
                       r=["aggr"], w=["sd"])
                b.step("V", lambda e: e.reciprocal(rstd[:], sd[:]), r=["sd"], w=["rstd"])
                b.step("V", lambda e, bi=bi: e.tensor_scalar(
                    out=xnb[bi][:], in0=xa[bi][:], scalar1=aggr[:, 0:1],
                    scalar2=rstd[:], op0=ALU.subtract, op1=ALU.mult),
                    r=[("xa", bi), "aggr", "rstd"], w=[("xnb", bi)])
                for q in range(4):
                    pbank = q % 2
                    for u in range(8):
                        c = q * 8 + u
                        b.step("T", lambda e, c=c, u=u, pbank=pbank, bi=bi: e.transpose(
                            ps_tA[pbank][:, u * 128:(u + 1) * 128],
                            xnb[bi][:, c * 128:(c + 1) * 128], ident_bf[:]),
                            r=[("xnb", bi), d_key(ident_bf)],
                            w=[("ps_tA", pbank)], tok=(u == 7),
                            reg_r=[("xnb", bi), d_key(ident_bf)] if u == 7 else None)
                    b.step("V", lambda e, q=q, t=t, pbank=pbank: e.tensor_copy(
                        xnT_own[:].rearrange("p (c n) -> p c n", c=32)
                        [:, q * 8:(q + 1) * 8, t * 128:(t + 1) * 128],
                        ps_tA[pbank][:].rearrange("p (c n) -> p c n", c=8)),
                        r=[("ps_tA", pbank)], w=[("xnT_own", t, q)])
            b.barrier()

        with ExitStack() as ph:
            pe = ph.enter_context
            wqd_cb = [pe(nc.sbuf_tensor(f"wqd_cb{i}", [128, 32 * 128], BF16)) for i in range(2)]
            colsq = pe(nc.sbuf_tensor("colsq", [128, 8 * 32], BF16))
            ps_cq = [pe(nc.psum_tensor(f"ps_cq{i}", [128, 512], F32)) for i in range(4)]
            ps_cs = pe(nc.psum_tensor("ps_cs", [128, 32], F32))
            ps_csT = pe(nc.psum_tensor("ps_csT", [32, 128], BF16))

            xn_own_keys = [("xnT_own", t, q) for t in range(8) for q in range(4)]
            for m in range(12):
                b.step("G", lambda e, m=m: e.dma_start(
                    out=wqd_cb[m % 2][:].rearrange("p (k n) -> p k n", k=32),
                    in_=wqd_t[m]),
                    w=[("wqd_cb", m % 2)], dma=f"wqd{m % 2}")
                for t in range(2):
                    pb = (m % 2) * 2 + t
                    for K in range(32):
                        b.step("T", lambda e, m=m, t=t, K=K, pb=pb: e.matmul(
                            ps_cq[pb][:],
                            wqd_cb[m % 2][:, K * 128:(K + 1) * 128],
                            xnT_own[:, K * NT + t * 512:K * NT + t * 512 + 512],
                            start=(K == 0), stop=(K == 31)),
                            r=([("wqd_cb", m % 2)] + (xn_own_keys if K == 0 else [])),
                            w=[("ps_cq", pb)], tok=(K == 31),
                            reg_r=([("wqd_cb", m % 2)] + xn_own_keys) if K == 31 else None)
                    b.step("V", lambda e, m=m, t=t, pb=pb: e.tensor_copy(
                        c_qT[:, m * NT + t * 512:m * NT + t * 512 + 512], ps_cq[pb][:]),
                        r=[("ps_cq", pb)], w=[("c_qT", m, t)])
            for tc in range(8):
                for j in range(32):
                    b.step("T", lambda e, tc=tc, j=j: e.matmul(
                        ps_cs[:], xnT_own[:, j * NT + tc * 128:j * NT + (tc) * 128 + 128],
                        g_ind_sb[:, j * 32:(j + 1) * 32],
                        start=(j == 0), stop=(j == 31)),
                        r=([d_key(g_ind_sb)] + (xn_own_keys if j == 0 else [])),
                        w=[("ps_cs",)], tok=(j == 31),
                        reg_r=([d_key(g_ind_sb)] + xn_own_keys) if j == 31 else None)
                b.step("V", lambda e, tc=tc: e.tensor_copy(
                    colsq[:, tc * 32:(tc + 1) * 32], ps_cs[:]),
                    r=[("ps_cs",)], w=[("colsq", tc)])
                b.step("T", lambda e, tc=tc: e.transpose(
                    ps_csT[:], colsq[:, tc * 32:(tc + 1) * 32], ident_bf[:]),
                    r=[("colsq", tc), d_key(ident_bf)], w=[("ps_csT",)])
                b.step("V", lambda e, tc=tc: e.tensor_copy(
                    colsumsT[0:32, tc * 128:(tc + 1) * 128], ps_csT[:]),
                    r=[("ps_csT",)], w=[("colsumsT", tc)])
            b.step("Y", lambda e: e.dma_start(
                out=xn_scr[:].rearrange("(j p) n -> p j n", p=128),
                in_=xnT_own[:].rearrange("p (j n) -> p j n", j=32)),
                r=xn_own_keys, w=["xn_scr"], dma="xs")
            if dbg:
                b.step("Y", lambda e: e.dma_start(out=dbg_t["dbg_xnT"][:], in_=xnT_own[:]),
                       r=xn_own_keys, dma="dbg")
                b.step("Y", lambda e: e.dma_start(out=dbg_t["dbg_cq"][:], in_=c_qT[:]),
                       r=[("c_qT", m, t) for m in range(12) for t in range(2)], dma="dbg")
                b.step("Y", lambda e: e.dma_start(out=dbg_t["dbg_ckv"][:], in_=c_kvT[:]),
                       r=[("c_kvT", t) for t in range(16)], dma="dbg")
                b.step("Y", lambda e: e.dma_start(out=dbg_t["dbg_cs"][:], in_=colsumsT[:]),
                       r=[("colsumsT", tc) for tc in range(8)], dma="dbg")
            b.barrier()
        es_xn.close()

        es_oT = top.enter_context(ExitStack())
        oT = es_oT.enter_context(
            nc.sbuf_tensor("oT", [128, 32 * NT], BF16, side="right"))
        with ExitStack() as ph:
            pe = ph.enter_context
            wqun_bf = [pe(nc.sbuf_tensor(f"wqun{i}", [128, 12 * 256], BF16)) for i in range(2)]
            wqur_bf = [pe(nc.sbuf_tensor(f"wqur{i}", [128, 12 * 128], BF16)) for i in range(2)]
            wukT_bf = [pe(nc.sbuf_tensor(f"wukT{i}", [128, 4 * 256], BF16)) for i in range(2)]
            wuvT_bf = [pe(nc.sbuf_tensor(f"wuvT{i}", [128, 4 * 256], BF16)) for i in range(2)]
            q_nopeT = [pe(nc.sbuf_tensor(f"qnope{i}", [128, NT], BF16)) for i in range(2)]
            qrx = [pe(nc.sbuf_tensor(f"qrx{i}", [67, NT], BF16)) for i in range(2)]
            kn = [pe(nc.sbuf_tensor(f"kn{i}", [128, SEQ], BF16)) for i in range(2)]
            v_grp = pe(nc.sbuf_tensor("v_grp", [128, 16 * 256], BF16))
            rope_tmp = pe(nc.sbuf_tensor("rope_tmp", [128, NT], F32))
            rot = pe(nc.sbuf_tensor("rot", [128, NT], F32))
            prA = pe(nc.sbuf_tensor("prA", [128, NT], F32))
            prB = pe(nc.sbuf_tensor("prB", [128, NT], F32))
            PT = [pe(nc.sbuf_tensor(f"PT{i}", [128, 512], BF16)) for i in range(2)]
            mcol4 = pe(nc.sbuf_tensor("mcol4", [128, 16], F32))
            mtmp = pe(nc.sbuf_tensor("mtmp", [128, 1], F32))
            mneg4 = pe(nc.sbuf_tensor("mneg4", [128, 16], F32))
            recip_row = pe(nc.sbuf_tensor("recip_row", [1, 512], F32))
            lnrow = pe(nc.sbuf_tensor("lnrow", [1, 512], F32))
            rb_sb = pe(nc.sbuf_tensor("rb_sb", [128, 512], F32))
            pool_t = pe(nc.psum_tensor("poolB", [128, 2048], F32))
            pool = [pool_t[:, i * 512:(i + 1) * 512] for i in range(4)]
            ps_st = [pe(nc.psum_tensor(f"ps_stB{i}", [128, 512], F32)) for i in range(2)]
            ps_out = pe(nc.psum_tensor("ps_outB", [128, 512], F32))
            ps_sum = pe(nc.psum_tensor("ps_sumB", [1, 512], F32))

            for p in range(2):
                b.step("G", lambda e, p=p: e.dma_start(out=qrx[p][64:67, :], in_=q_rowsinit[:]),
                       w=[("qrx_rows", p)], dma=f"qi{p}")

            ckv_keys = [("c_kvT", t) for t in range(16)]
            cq_keys = [("c_qT", m, t) for m in range(12) for t in range(2)]
            krx_keys = ["krx_rope", "krx_ones", "krx_mask"]

            def wdma(grp):
                gb = grp % 2
                for (buf, src, ncols) in [(wqun_bf, wqun_t, 256), (wqur_bf, wqur_t, 128),
                                          (wukT_bf, wukT_t, 256), (wuvT_bf, wuvT_t, 256)]:
                    b.step("G", lambda e, buf=buf, src=src, grp=grp: e.dma_start(
                        out=buf[gb][:].rearrange("p (k n) -> p k n", k=src.shape[2]),
                        in_=src[grp]),
                        w=[(id(buf), gb)], dma=f"wg{gb}")
                b.flush(f"wg{gb}", [(id(buf), gb) for buf in
                                    (wqun_bf, wqur_bf, wukT_bf, wuvT_bf)])

            wdma(0)
            pool_i = [0]

            def nxt_pool():
                i = pool_i[0] % 4
                pool_i[0] += 1
                return i

            for grp in range(16):
                gb = grp % 2
                if grp + 1 < 16:
                    wdma(grp + 1)
                for p in range(2):
                    for t in range(2):
                        pb = nxt_pool()
                        for K in range(12):
                            b.step("T", lambda e, p=p, t=t, K=K, pb=pb, gb=gb: e.matmul(
                                pool[pb],
                                wqun_bf[gb][:, K * 256 + p * 128:K * 256 + p * 128 + 128],
                                c_qT[:, K * NT + t * 512:K * NT + t * 512 + 512],
                                start=(K == 0), stop=(K == 11)),
                                r=[(id(wqun_bf), gb), ("c_qT", K, t)],
                                w=[("pool", pb)], tok=(K == 11),
                                reg_r=([(id(wqun_bf), gb)] + cq_keys) if K == 11 else None)
                        b.step("V", lambda e, p=p, t=t, pb=pb: e.tensor_copy(
                            q_nopeT[p][:, t * 512:t * 512 + 512], pool[pb]),
                            r=[("pool", pb)], w=[("qn", p, t)])
                for t in range(2):
                    pb = nxt_pool()
                    for K in range(12):
                        b.step("T", lambda e, t=t, K=K, pb=pb, gb=gb: e.matmul(
                            pool[pb],
                            wqur_bf[gb][:, K * 128:(K + 1) * 128],
                            c_qT[:, K * NT + t * 512:K * NT + t * 512 + 512],
                            start=(K == 0), stop=(K == 11)),
                            r=[(id(wqur_bf), gb), ("c_qT", K, t)],
                            w=[("pool", pb)], tok=(K == 11),
                            reg_r=([(id(wqur_bf), gb)] + cq_keys) if K == 11 else None)
                    b.step("V", lambda e, t=t, pb=pb: e.tensor_copy(
                        rope_tmp[:, t * 512:t * 512 + 512], pool[pb]),
                        r=[("pool", pb)], w=[("rope_tmp", t)])
                rt_keys = [("rope_tmp", 0), ("rope_tmp", 1)]
                for (dsta, srca) in [(0, 32), (32, 0), (64, 96), (96, 64)]:
                    neg = (dsta in (0, 64))
                    b.step("V", lambda e, dsta=dsta, srca=srca, neg=neg:
                           (e.tensor_scalar_mul(rot[dsta:dsta + 32, :],
                                                rope_tmp[srca:srca + 32, :], -1.0)
                            if neg else
                            e.tensor_copy(rot[dsta:dsta + 32, :],
                                          rope_tmp[srca:srca + 32, :])),
                           r=rt_keys, w=[("rot", dsta)])
                rot_keys = [("rot", a) for a in (0, 32, 64, 96)]
                b.step("V", lambda e: e.tensor_mul(prA[:], rope_tmp[:], cosq_sb[:]),
                       r=rt_keys + [d_key(cosq_sb)], w=["prA"])
                b.step("V", lambda e: e.tensor_mul(prB[:], rot[:], sinq_sb[:]),
                       r=rot_keys + [d_key(sinq_sb)], w=["prB"])
                for p in range(2):
                    b.step("V", lambda e, p=p: e.tensor_add(
                        qrx[p][0:64, :], prA[p * 64:p * 64 + 64, :],
                        prB[p * 64:p * 64 + 64, :]),
                        r=["prA", "prB"], w=[("qrx", p)])
                for p in range(2):
                    for s4 in range(4):
                        pb = nxt_pool()
                        for K in range(4):
                            b.step("T", lambda e, p=p, s4=s4, K=K, pb=pb, gb=gb: e.matmul(
                                pool[pb],
                                wukT_bf[gb][:, K * 256 + p * 128:K * 256 + p * 128 + 128],
                                c_kvT[:, K * SEQ + s4 * 512:K * SEQ + s4 * 512 + 512],
                                start=(K == 0), stop=(K == 3)),
                                r=[(id(wukT_bf), gb)] + (ckv_keys if K == 0 else []),
                                w=[("pool", pb)], tok=(K == 3),
                                reg_r=([(id(wukT_bf), gb)] + ckv_keys) if K == 3 else None)
                        b.step("S", lambda e, p=p, s4=s4, pb=pb: e.copy(
                            kn[p][:, s4 * 512:s4 * 512 + 512], pool[pb]),
                            r=[("pool", pb)], w=[("kn", p, s4)])
                for st in range(16):
                    pb = nxt_pool()
                    for K in range(4):
                        b.step("T", lambda e, st=st, K=K, pb=pb, gb=gb: e.matmul(
                            pool[pb][:, 0:256],
                            c_kvT[:, K * SEQ + st * 128:K * SEQ + st * 128 + 128],
                            wuvT_bf[gb][:, K * 256:(K + 1) * 256],
                            start=(K == 0), stop=(K == 3)),
                            r=[(id(wuvT_bf), gb)] + (ckv_keys if K == 0 else []),
                            w=[("pool", pb)], tok=(K == 3),
                            reg_r=([(id(wuvT_bf), gb)] + ckv_keys) if K == 3 else None)
                    b.step("S", lambda e, st=st, pb=pb: e.copy(
                        v_grp[:, st * 256:st * 256 + 256], pool[pb][:, 0:256]),
                        r=[("pool", pb)], w=[("v", st)])

                kn_keys = lambda p: [("kn", p, s4) for s4 in range(4)]
                tails = []
                for p in range(2):
                    for g in range(2):
                        pgi = p * 2 + g
                        for s in range(4):
                            qcol = g * 512 + s * 128
                            nw = NW[g]
                            base = 0 if nw == 4 else (2 * s) % 4
                            for w in range(nw):
                                pb = (base + w) % 4
                                b.step("T", lambda e, p=p, qcol=qcol, w=w, pb=pb: e.matmul(
                                    pool[pb], q_nopeT[p][:, qcol:qcol + 128],
                                    kn[p][:, w * 512:w * 512 + 512],
                                    start=True, stop=False),
                                    r=[("qn", p, g), ("kn", p, w)],
                                    w=[("pool", pb)], tok=False)
                                b.step("T", lambda e, p=p, qcol=qcol, w=w, pb=pb: e.matmul(
                                    pool[pb], qrx[p][0:64, qcol:qcol + 128],
                                    k_ropeX[0:64, w * 512:w * 512 + 512],
                                    start=False, stop=True),
                                    r=[("qrx", p), "krx_rope"],
                                    w=[("pool", pb)], tok=True,
                                    reg_r=[("qn", p, g), ("kn", p, w), ("qrx", p), "krx_rope"])
                                if w % 2 == 1:
                                    pb0 = (base + w - 1) % 4
                                    mc = pgi * 4 + s
                                    if w == 1:
                                        b.step("V", lambda e, pb0=pb0, mc=mc: e.reduce_max(
                                            mcol4[:, mc:mc + 1],
                                            pool_t[:, pb0 * 512:pb0 * 512 + 1024],
                                            axis=AX.X),
                                            r=[("pool", pb0), ("pool", pb0 + 1)],
                                            w=[("mcol4", mc)])
                                    else:
                                        b.step("V", lambda e, pb0=pb0: e.reduce_max(
                                            mtmp[:],
                                            pool_t[:, pb0 * 512:pb0 * 512 + 1024],
                                            axis=AX.X),
                                            r=[("pool", pb0), ("pool", pb0 + 1)],
                                            w=["mtmp"])
                                        b.step("V", lambda e, mc=mc: e.tensor_max(
                                            mcol4[:, mc:mc + 1], mcol4[:, mc:mc + 1],
                                            mtmp[:]),
                                            r=[("mcol4", mc), "mtmp"], w=[("mcol4", mc)])

                        def tail(p=p, g=g, pgi=pgi):
                            b.step("V", lambda e: e.tensor_scalar_mul(
                                mneg4[:, pgi * 4:pgi * 4 + 4],
                                mcol4[:, pgi * 4:pgi * 4 + 4], -1.0),
                                r=[("mcol4", pgi * 4 + ss) for ss in range(4)],
                                w=[("mneg4", pgi)])
                            for ss in range(4):
                                b.step("T", lambda e, ss=ss: e.transpose(
                                    ps_sum[0:1, ss * 128:ss * 128 + 128],
                                    mneg4[:, pgi * 4 + ss:pgi * 4 + ss + 1], ident_f[:]),
                                    r=[("mneg4", pgi), d_key(ident_f)], w=[("ps_sum",)])
                            b.step("V", lambda e: e.tensor_copy(
                                qrx[p][64:65, g * 512:g * 512 + 512],
                                ps_sum[0:1, 0:512]),
                                r=[("ps_sum",), ("qrx_rows", p)], w=[("qrx_nm", p, g)])
                        tails.append(tail)
                for tail in tails:
                    tail()
                for p in range(2):
                    h_abs = grp * 2 + p
                    for g in range(2):
                        nch = NCH[g]
                        d0 = DIAG0[g]
                        qsl = slice(g * 512, g * 512 + 512)

                        def emit_ST(c, qsl=qsl, d0=d0, p=p, g=g):
                            bank = c % 2
                            b.step("T", lambda e, p=p, c=c, bank=bank, qsl=qsl: e.matmul(
                                ps_st[bank][:], kn[p][:, c * 128:c * 128 + 128],
                                q_nopeT[p][:, qsl], start=True, stop=False),
                                r=[("kn", p, c // 4), ("qn", p, g)],
                                w=[("st", bank)], tok=False)
                            dd = c - d0
                            last = not (0 <= dd < 4)
                            b.step("T", lambda e, p=p, c=c, bank=bank, last=last, qsl=qsl: e.matmul(
                                ps_st[bank][:], k_ropeX[:, c * 128:c * 128 + 128],
                                qrx[p][0:67, qsl], start=False, stop=last),
                                r=krx_keys + [("qrx", p), ("qrx_nm", p, g), ("qrx_rows", p)],
                                w=[("st", bank)], tok=last,
                                reg_r=([("kn", p, c // 4), ("qn", p, g), ("qrx", p),
                                        ("qrx_nm", p, g), ("qrx_rows", p)] + krx_keys)
                                if last else None)
                            if not last:
                                b.step("T", lambda e, bank=bank, dd=dd: e.matmul(
                                    ps_st[bank][:], ident_bf[:],
                                    tri_sb[:, dd * 512:dd * 512 + 512],
                                    start=False, stop=True),
                                    r=[d_key(ident_bf), d_key(tri_sb)],
                                    w=[("st", bank)], tok=True,
                                    reg_r=[("kn", p, c // 4), ("qn", p, g), ("qrx", p),
                                           ("qrx_nm", p, g), ("qrx_rows", p),
                                           d_key(ident_bf), d_key(tri_sb)] + krx_keys)

                        emit_ST(0)
                        for c in range(nch):
                            bank = c % 2
                            if c + 1 < nch:
                                emit_ST(c + 1)
                            b.step("S", lambda e, c=c, bank=bank: e.activation(
                                PT[bank][:], ps_st[bank][:], AF.Exp, scale=SCALE),
                                r=[("st", bank)], w=[("PT", bank)])
                            b.step("T", lambda e, p=p, c=c, bank=bank, nch=nch: e.matmul(
                                ps_out[:], v_grp[:, c * 256 + p * 128:c * 256 + p * 128 + 128],
                                PT[bank][:], start=(c == 0), stop=(c == nch - 1)),
                                r=[("v", c), ("PT", bank)],
                                w=[("ps_out",)], tok=False)
                            b.step("T", lambda e, c=c, bank=bank, nch=nch: e.matmul(
                                ps_sum[0:1, :], ones_col[:], PT[bank][:],
                                start=(c == 0), stop=(c == nch - 1)),
                                r=["ones_col", ("PT", bank)],
                                w=([("ps_sum",), ("ps_out",)] if c == nch - 1
                                   else [("ps_sum",)]),
                                tok=True,
                                reg_r=["ones_col", ("PT", bank), ("v", c)])
                        b.step("S", lambda e: e.activation(
                            lnrow[:], ps_sum[0:1, :], AF.Ln),
                            r=[("ps_sum",)], w=["lnrow"])
                        b.step("S", lambda e: e.activation(
                            recip_row[:], lnrow[:], AF.Exp, scale=-1.0),
                            r=["lnrow"], w=["recip_row"])
                        b.step("T", lambda e: e.matmul(
                            ps_st[0][:], ones_row_f[:], recip_row[:],
                            start=True, stop=True),
                            r=["ones_row_f", "recip_row"], w=[("st", 0)])
                        b.step("S", lambda e: e.copy(rb_sb[:], ps_st[0][:]),
                               r=[("st", 0)], w=["rb_sb"])
                        b.step("V", lambda e, h_abs=h_abs, g=g: e.tensor_tensor(
                            oT[:, h_abs * NT + g * 512:h_abs * NT + g * 512 + 512],
                            ps_out[:], rb_sb[:], ALU.mult),
                            r=[("ps_out",), "rb_sb"], w=[("oT", h_abs, g)])
            if dbg:
                b.step("Y", lambda e: e.dma_start(out=dbg_t["dbg_oT"][:], in_=oT[:]),
                       r=[("oT", hh, gg) for hh in range(32) for gg in range(2)], dma="dbg")
            b.barrier()
        es_ckv.close()

        oT_keys = [("oT", hh, gg) for hh in range(32) for gg in range(2)]
        with ExitStack() as ph:
            pe = ph.enter_context
            wout_cb = [pe(nc.sbuf_tensor(f"wout{i}", [128, 32 * 128], BF16)) for i in range(4)]
            sa_sb = [pe(nc.sbuf_tensor(f"sa_sb{i}", [128, 512], BF16)) for i in range(4)]
            psC = [pe(nc.psum_tensor(f"psC{i}", [128, 512], F32)) for i in range(8)]

            def wout_dma(ob):
                b.step("G", lambda e, ob=ob: e.dma_start(
                    out=wout_cb[ob % 4][:].rearrange("p (k n) -> p k n", k=32),
                    in_=wout_t[ob]),
                    w=[("wout_cb", ob % 4)], dma=f"wo{ob % 4}")

            wout_dma(0)
            wout_dma(1)
            for og in range(16):
                if 2 * og + 2 < 32:
                    wout_dma(2 * og + 2)
                if 2 * og + 3 < 32:
                    wout_dma(2 * og + 3)
                for K in range(32):
                    for bb in range(2):
                        ob = og * 2 + bb
                        for th in range(2):
                            pb = (og % 2) * 4 + bb * 2 + th
                            b.step("T", lambda e, K=K, ob=ob, th=th, pb=pb: e.matmul(
                                psC[pb][:], wout_cb[ob % 4][:, K * 128:(K + 1) * 128],
                                oT[:, K * NT + th * 512:K * NT + th * 512 + 512],
                                start=(K == 0), stop=(K == 31)),
                                r=[("wout_cb", ob % 4)] + (oT_keys if K == 0 else []),
                                w=[("psC", pb)], tok=(K == 31),
                                reg_r=([("wout_cb", ob % 4)] + oT_keys) if K == 31 else None)
                for bb in range(2):
                    ob = og * 2 + bb
                    for th in range(2):
                        pb = (og % 2) * 4 + bb * 2 + th
                        sb = bb * 2 + th
                        b.step("V", lambda e, pb=pb, sb=sb: e.tensor_copy(
                            sa_sb[sb][:], psC[pb][:]),
                            r=[("psC", pb)], w=[("sa_sb", sb)])
                        b.step("Y", lambda e, ob=ob, th=th, sb=sb: e.dma_start(
                            out=sa_scr[ob * 128:(ob + 1) * 128, th * 512:th * 512 + 512],
                            in_=sa_sb[sb][:]),
                            r=[("sa_sb", sb)], w=[("sa_scr", ob, th)], dma=f"sa{sb}")
            b.barrier()
        es_oT.close()

        sa_keys = [("sa_scr", ob, th) for ob in range(32) for th in range(2)]
        with ExitStack() as ph:
            pe = ph.enter_context
            qcb_i = [pe(nc.sbuf_tensor(f"qcb_i{i}", [128, 32 * 128], I32)) for i in range(1)]
            qcb_bf = [pe(nc.sbuf_tensor(f"qcb_bf{i}", [128, 4096], BF16)) for i in range(1)]
            s_ob = [pe(nc.sbuf_tensor(f"s_ob{i}", [128, 4096], BF16)) for i in range(2)]
            w_grp = [pe(nc.sbuf_tensor(f"w_grp{i}", [128, 4 * 4096], BF16)) for i in range(2)]
            xnj = [pe(nc.sbuf_tensor(f"xnj{i}", [128, 1024], BF16)) for i in range(4)]
            s_sl = [pe(nc.sbuf_tensor(f"s_sl{i}", [128, 2048], BF16)) for i in range(2)]
            x_oh = [pe(nc.sbuf_tensor(f"x_oh{i}", [128, 2048], F32)) for i in range(2)]
            g8 = pe(nc.sbuf_tensor("g8", [128, 8 * 512], BF16))
            m_tile = [pe(nc.sbuf_tensor(f"m_tile{i}", [128, 512], BF16)) for i in range(2)]
            out_sb = [pe(nc.sbuf_tensor(f"out_sb{i}", [128, 2048], F32)) for i in range(2)]
            psD = [pe(nc.psum_tensor(f"psD{i}", [128, 512], F32)) for i in range(4)]
            ps_trD = [pe(nc.psum_tensor(f"ps_trD{i}", [128, 512], BF16)) for i in range(2)]

            cs_keys = [("colsumsT", tc) for tc in range(8)]

            def dequant(og, bb):
                gbk = og * 4 + bb
                sl = bb % 2
                b.step("Y", lambda e, gbk=gbk: e.dma_start(
                    out=qcb_i[0][:].rearrange("p (j n) -> p j n", j=32),
                    in_=q_tiled[gbk]),
                    w=[("qcb_i",)], dma="qcb0")
                b.step("G", lambda e, gbk=gbk, sl=sl: e.dma_start(
                    out=s_ob[sl][:].rearrange("p (j n) -> p j n", j=32),
                    in_=scales_bc_d[gbk]),
                    w=[("s_ob", sl)], dma=f"sob{sl}")
                b.step("S", lambda e: e.copy(qcb_bf[0][:], qcb_i[0][:]),
                       r=[("qcb_i",)], w=[("qcb_bf",)])
                b.step("G", lambda e, og=og, bb=bb, sl=sl: e.tensor_mul(
                    w_grp[og % 2][:, bb * 4096:(bb + 1) * 4096], qcb_bf[0][:], s_ob[sl][:]),
                    r=[("qcb_bf",), ("s_ob", sl)], w=[("w_grp", og % 2, bb)])

            for og in range(8):
                for bb in range(4):
                    dequant(og, bb)
                for th in range(2):
                    b.step("Y", lambda e, og=og, th=th: e.dma_start(
                        out=s_sl[th][:].rearrange("p (b n) -> p b n", b=4),
                        in_=sa_scr[:].rearrange("(a p) n -> p a n", p=128)
                        [:, og * 4:og * 4 + 4, th * 512:th * 512 + 512]),
                        r=sa_keys, w=[("s_sl", th)], dma=f"sl{th}")
                    b.step("Y", lambda e, og=og, th=th: e.dma_start(
                        out=x_oh[th][:].rearrange("p (t n) -> p t n", t=4),
                        in_=x_oh_t[og, th]),
                        w=[("x_oh", th)], dma=f"xh{th}")
                    for bb in range(4):
                        b.step("T", lambda e, og=og, bb=bb, th=th: e.matmul(
                            psD[bb][:], zT_sb[:, (og * 4 + bb) * 128:(og * 4 + bb) * 128 + 128],
                            colsumsT[0:32, th * 512:th * 512 + 512],
                            start=True, stop=False),
                            r=[d_key(zT_sb)] + cs_keys,
                            w=[("psD", bb)], tok=False)
                    for j in range(32):
                        jp, jh = j // 2, j % 2
                        jb = jp % 4
                        if jh == 0:
                            b.step("S", lambda e, jp=jp, th=th, jb=jb: e.dma_start(
                                out=xnj[jb][:].rearrange("p (a n) -> p a n", a=2),
                                in_=xn_scr[:].rearrange("(a p) n -> p a n", p=128)
                                [:, 2 * jp:2 * jp + 2, th * 512:th * 512 + 512]),
                                r=["xn_scr"], w=[("xnj", jb)], dma=f"xj{jb}")
                        for bb in range(4):
                            qlast = (bb == 3)
                            b.step("T", lambda e, og=og, j=j, bb=bb, jb=jb, jh=jh: e.matmul(
                                psD[bb][:],
                                w_grp[og % 2][:, bb * 4096 + j * 128:bb * 4096 + j * 128 + 128],
                                xnj[jb][:, jh * 512:jh * 512 + 512],
                                start=False, stop=(j == 31)),
                                r=[("w_grp", og % 2, bb), ("xnj", jb)],
                                w=([("psD", q) for q in range(4)] if (j == 31 and qlast)
                                   else [("psD", bb)]),
                                tok=qlast,
                                reg_r=([("xnj", jb), d_key(zT_sb)] + cs_keys
                                       + [("w_grp", og % 2, q) for q in range(4)]) if qlast else None)
                    for bb in range(4):
                        gbk = og * 4 + bb
                        b.step("S", lambda e, bb=bb, th=th, gbk=gbk: e.activation(
                            g8[:, (th * 4 + bb) * 512:(th * 4 + bb) * 512 + 512],
                            psD[bb][:], AF.Silu, bias=bias_sb[:, gbk:gbk + 1]),
                            r=[("psD", bb), d_key(bias_sb)], w=[("g8", th, bb)])
                for th in range(2):
                    for bb in range(4):
                        mb = bb % 2
                        b.step("V", lambda e, th=th, bb=bb, mb=mb: e.tensor_tensor(
                            m_tile[mb][:], g8[:, (th * 4 + bb) * 512:(th * 4 + bb) * 512 + 512],
                            s_sl[th][:, bb * 512:bb * 512 + 512], ALU.mult),
                            r=[("g8", th, bb), ("s_sl", th)], w=[("m_tile", mb)])
                        for ts in range(4):
                            b.step("T", lambda e, mb=mb, ts=ts: e.transpose(
                                ps_trD[mb][:, ts * 128:ts * 128 + 128],
                                m_tile[mb][:, ts * 128:ts * 128 + 128], ident_bf[:]),
                                r=[("m_tile", mb), d_key(ident_bf)],
                                w=[("ps_trD", mb)], tok=(ts == 3),
                                reg_r=[("m_tile", mb), d_key(ident_bf)] if ts == 3 else None)
                        for ts in range(4):
                            b.step("V", lambda e, th=th, bb=bb, mb=mb, ts=ts: e.tensor_tensor(
                                out_sb[th][:, ts * 512 + bb * 128:ts * 512 + bb * 128 + 128],
                                ps_trD[mb][:, ts * 128:ts * 128 + 128],
                                x_oh[th][:, ts * 512 + bb * 128:ts * 512 + bb * 128 + 128],
                                ALU.add),
                                r=[("ps_trD", mb), ("x_oh", th)], w=[("out_sb", th)])
                    for ts in range(4):
                        b.step("Y", lambda e, og=og, th=th, ts=ts: e.dma_start(
                            out=out_own[th * 512 + ts * 128:th * 512 + ts * 128 + 128,
                                        og * 512:og * 512 + 512],
                            in_=out_sb[th][:, ts * 512:ts * 512 + 512]),
                            r=[("out_sb", th)],
                            w=[("out", og, th, ts)], dma=f"ot{th}")
            def fin(e):
                for sm in b.dsems.values():
                    if sm.n > 0:
                        e.wait_ge(sm.h, sm.n)
                return e.nop()
            b.step("Y", fin, tok=False)
            b.flush_pending()

    return nc, dbg_t


def d_key(t):
    return ("sb", t.name if hasattr(t, "name") else id(t))



def kernel(**inputs):
    dbg = int(os.environ.get("KDBG", "0"))
    x = np.asarray(inputs["L0_attn_hidden_states"], np.float32)
    shared = _prep_shared(inputs)
    nc, _ = build_nc(dbg=dbg)

    in_maps = []
    for core in range(8):
        bq, h = divmod(core, 2)
        m = dict(shared)
        m.update(_prep_core(x, bq, h))
        im = {
            "x_full": m["x_full"], "x_own": m["x_own"],
            "cos_k": m["cos_k"], "sin_k": m["sin_k"],
            "cos_q": m["cos_q"], "sin_q": m["sin_q"],
            "maskrows": m["maskrows"], "q_rowsinit": m["q_rowsinit"],
            "consts_f": m["consts_f"], "ident_bf": m["ident_bf"],
            "ident_f": m["ident_f"], "tri": m["tri"], "g_ind": m["g_ind"],
            "wkv_t": m["wkv_t"], "wkr_t": m["wkr_t"], "wqd_t": m["wqd_t"],
            "wqun_t": m["wqun_t"], "wqur_t": m["wqur_t"],
            "wukT_t": m["wukT_t"], "wuvT_t": m["wuvT_t"],
            "wout_t": m["wout_t"], "q_tiled": m["q_tiled"],
            "scales_bc": m["scales_bc"], "zT": m["zT"], "bias_r": m["bias_r"],
            "x_oh_t": m["x_oh_t"],
        }
        in_maps.append(im)

    res = run_bass_kernel_spmd(nc, in_maps, core_ids=list(range(8)))
    out = np.empty((4, SEQ, D), np.float32)
    for core in range(8):
        bq, h = divmod(core, 2)
        out[bq][_own_rows(h)] = res.results[core]["out_own"]
    return out


# version 07
# speedup vs baseline: 1.9018x; 1.0332x over previous
import os
import math
import numpy as np
import ml_dtypes
from contextlib import ExitStack

import concourse.bass as bass
import concourse.mybir as mybir
from concourse.bass_utils import run_bass_kernel_spmd

F32 = mybir.dt.float32
BF16 = mybir.dt.bfloat16
I32 = mybir.dt.int32
AF = mybir.ActivationFunctionType
AX = mybir.AxisListType
ALU = mybir.AluOpType
BF = ml_dtypes.bfloat16

D = 4096
SEQ = 2048
NT = 1024
H = 32
QL = 1536
KVL = 512
DN = 128
DR = 64
DV = 128
NEG = -1.0e9
SCALE = float(np.float32(1.0) / np.float32(np.sqrt(np.float32(DN + DR))))
EPS = 1.0e-5

NCH = (8, 16)
NW = (2, 4)
DIAG0 = (4, 12)



def _yarn_tables():
    i = np.arange(0, DR, 2, dtype=np.float32)
    freqs = (np.float32(1.0) / (np.float32(10000.0) ** (i / np.float32(DR)))).astype(np.float32)
    wavelengths = (np.float32(2.0 * np.pi) / freqs).astype(np.float32)
    ramp = np.clip((wavelengths / np.float32(4096.0) - np.float32(1.0)) / np.float32(31.0),
                   0.0, 1.0).astype(np.float32)
    scale = (np.float32(1.0) - ramp + ramp * np.float32(40.0)).astype(np.float32)
    return (freqs / scale).astype(np.float32)


def _cos_sin(pos, reps):
    inv = _yarn_tables()
    f = pos.astype(np.float32)[None, :] * inv[:, None]
    c = np.cos(f).astype(np.float32)
    s = np.sin(f).astype(np.float32)
    return np.tile(c, (reps, 1)), np.tile(s, (reps, 1))


def _perm_blocks(h):
    if h == 0:
        return [4, 5, 6, 7, 0, 1, 2, 3, 8, 9, 10, 11, 12, 13, 14, 15]
    return [0, 1, 2, 3, 4, 5, 6, 7, 12, 13, 14, 15, 8, 9, 10, 11]


def _own_rows(h):
    if h == 0:
        return np.r_[0:512, 1536:2048]
    return np.r_[512:1536]


def _maskrows(h):
    m = np.zeros((2, SEQ), np.float32)
    blk = lambda c: slice(c * 128, (c + 1) * 128)
    if h == 0:
        for c in range(0, 4):
            m[0, blk(c)] = NEG
        for c in range(8, 16):
            m[0, blk(c)] = NEG
    else:
        for c in range(8, 16):
            m[0, blk(c)] = NEG
        for c in range(8, 12):
            m[1, blk(c)] = NEG
    return m


def _tri_consts():
    out = np.zeros((128, 4 * 512), np.float32)
    st = np.arange(128)[:, None]
    for d in range(4):
        sq = np.arange(512)[None, :]
        m = np.where(sq < 128 * d, NEG, 0.0)
        inblk = (sq >= 128 * d) & (sq < 128 * (d + 1))
        m = np.where(inblk & (st > (sq - 128 * d)), NEG, m)
        out[:, d * 512:(d + 1) * 512] = m
    return out.astype(BF)


def _tile2(w, kb, mb):
    K, M = w.shape
    t = w.reshape(K // kb, kb, M // mb, mb).transpose(2, 1, 0, 3)
    return np.ascontiguousarray(t)


def _prep_shared(inp):
    d = {}
    d["wkv_t"] = np.ascontiguousarray(
        inp["L0_attn_w_kv_down"].reshape(32, 128, KVL).transpose(1, 0, 2))
    d["wkr_t"] = np.ascontiguousarray(
        inp["L0_attn_w_k_rope"].reshape(32, 128, DR).transpose(1, 0, 2))
    d["wqd_t"] = _tile2(inp["L0_attn_w_q_down"], 128, 128)
    wqun = _tile2(inp["L0_attn_w_q_up_nope"], 128, 256)
    d["wqun_t"] = wqun
    d["wqur_t"] = _tile2(inp["L0_attn_w_q_up_rope"], 128, 128)
    wukT = np.ascontiguousarray(inp["L0_attn_w_uk"].T)
    wuvT = np.ascontiguousarray(inp["L0_attn_w_uv"].T)
    d["wukT_t"] = _tile2(wukT, 128, 256)
    d["wuvT_t"] = _tile2(wuvT, 128, 256)
    d["wout_t"] = _tile2(inp["L0_attn_w_out"], 128, 128)
    q = inp["L0_ffn_qweight"].reshape(4096, 4096)
    qT = q.T
    d["q_tiled"] = _tile2(qT, 128, 128).astype(np.int8)
    sc = inp["L0_ffn_scales"].reshape(32, 128, 32)
    d["sc_rows"] = np.ascontiguousarray(
        sc.transpose(0, 2, 1).reshape(32, 4096))
    d["zT"] = np.ascontiguousarray(
        inp["L0_ffn_scaled_zeros"].reshape(4096, 32).T)
    d["bias_r"] = np.ascontiguousarray(
        inp["L0_ffn_bias"].reshape(32, 128).T)
    d["ident_bf"] = np.eye(128, dtype=BF)
    d["ident_f"] = np.eye(128, dtype=np.float32)
    d["tri"] = _tri_consts()
    g = np.zeros((128, 32, 32), BF)
    for j in range(32):
        g[:, j, j] = BF(1.0)
    d["g_ind"] = g
    cf = np.zeros((128, 8), np.float32)
    cf[:, 0] = EPS
    cf[:, 1] = SCALE
    cf[:, 2] = 0.0
    cf[:, 3] = 1.0
    cf[:, 4] = -1.0
    d["consts_f"] = cf
    qi = np.zeros((3, NT), np.float32)
    qi[1, 0:512] = 1.0
    qi[2, 512:NT] = 1.0
    d["q_rowsinit"] = qi
    return d


def _prep_core(x, b, h):
    d = {}
    perm = _perm_blocks(h)
    rows = np.concatenate([np.arange(c * 128, (c + 1) * 128) for c in perm])
    d["x_full"] = np.ascontiguousarray(x[b][rows])
    own = _own_rows(h)
    xo = np.ascontiguousarray(x[b][own])
    d["x_own"] = xo
    d["x_blk_t"] = np.ascontiguousarray(
        xo.reshape(8, 128, 32, 128).transpose(2, 1, 0, 3))
    pos_k = rows.astype(np.float32)
    ck, sk = _cos_sin(pos_k, 2)
    d["cos_k"], d["sin_k"] = ck, sk
    pos_q = own.astype(np.float32)
    cq, sq = _cos_sin(pos_q, 4)
    d["cos_q"], d["sin_q"] = cq, sq
    d["maskrows"] = _maskrows(h)
    return d



class _Sem:
    def __init__(self, h, name):
        self.h, self.name, self.n = h, name, 0


class _B:

    SEC = {"T": "tensor", "V": "vector", "S": "scalar", "G": "gpsimd", "Y": "sync"}

    def __init__(self, nc, blk, sems, sem_pool):
        self.nc, self.blk = nc, blk
        self.sems = sems
        self.sem_pool = list(sem_pool)
        self.dsems = {}
        self.last_w = {}
        self.readers = {}
        self.waited = {k: {} for k in self.SEC}
        self.pending = {}

    def dsem(self, stream):
        if stream not in self.dsems:
            self.dsems[stream] = _Sem(self.sem_pool.pop(), f"d_{stream}")
        return self.dsems[stream]

    def _need(self, r, w):
        need = {}
        for k in r:
            for s, v in self.last_w.get(k, {}).items():
                if need.get(s, 0) < v:
                    need[s] = v
        for k in w:
            for dd in (self.last_w.get(k, {}), self.readers.get(k, {})):
                for s, v in dd.items():
                    if need.get(s, 0) < v:
                        need[s] = v
        return need

    def step(self, ename, fn, r=(), w=(), tok=True, dma=None, reg_r=None):
        sem = self.dsem(dma) if dma else self.sems[ename]
        need = self._need(r, w)
        wt = self.waited[ename]
        waits = []
        for sm, v in sorted(need.items(), key=lambda kv: kv[0].name):
            if wt.get(sm, 0) < v:
                if sm is self.sems.get(ename) and v > sm.n:
                    raise AssertionError(
                        f"own-sem deadlock {ename} wait {v} > issued {sm.n}")
                waits.append((sm, v))
                wt[sm] = v
        if not tok:
            self.pending.setdefault(ename, []).append((waits, fn, None, 0))
            return None
        step_n = 16 if dma else 1
        sem.n += step_n
        self.pending.setdefault(ename, []).append((waits, fn, sem, step_n))
        t = (sem, sem.n)
        for k in w:
            self.last_w[k] = {sem: sem.n}
            self.readers[k] = {}
        for k in (reg_r if reg_r is not None else r):
            d = self.readers.setdefault(k, {})
            if d.get(sem, 0) < sem.n:
                d[sem] = sem.n
        return t

    def flush_pending(self):
        for ename in list(self.SEC):
            q = self.pending.get(ename)
            if not q:
                continue
            self.pending[ename] = []

            def body(e, q=q):
                for waits, fn, sem, step_n in q:
                    for sm, v in waits:
                        e.wait_ge(sm.h, v)
                    inst = fn(e)
                    if sem is not None:
                        inst.then_inc(sem.h, step_n)

            getattr(self.blk, self.SEC[ename])(body)

    def flush(self, stream, keys):
        sem = self.dsem(stream)
        for k in keys:
            self.last_w[k] = {sem: sem.n}
            self.readers[k] = {}

    def barrier(self):
        self.flush_pending()
        ph = self.sems["ph"]
        target = ph.n + 5
        ph.n = target
        allsems = [self.sems[k] for k in ("T", "V", "S", "G")] + list(self.dsems.values())

        for en in self.SEC:
            def body(e, en=en):
                for s in allsems:
                    if s.n > 0 and self.waited[en].get(s, 0) < s.n:
                        e.wait_ge(s.h, s.n)
                        self.waited[en][s] = s.n
                e.nop().then_inc(ph.h, 1)
                e.wait_ge(ph.h, target)
                self.waited[en][ph] = target
            getattr(self.blk, self.SEC[en])(body)
        self.last_w = {}
        self.readers = {}


def build_nc(dbg=0):
    nc = bass.Bass("TRN2", target_bir_lowering=False)
    dt_in = {}

    def din(name, shape, dtype):
        t = nc.dram_tensor(name, list(shape), dtype, kind="ExternalInput").ap()
        dt_in[name] = t
        return t

    x_full = din("x_full", (SEQ, D), F32)
    x_own = din("x_own", (NT, D), F32)
    x_blk = din("x_blk_t", (32, 128, 8, 128), F32)
    cos_k = din("cos_k", (64, SEQ), F32)
    sin_k = din("sin_k", (64, SEQ), F32)
    cos_q = din("cos_q", (128, NT), F32)
    sin_q = din("sin_q", (128, NT), F32)
    maskrows = din("maskrows", (2, SEQ), F32)
    q_rowsinit = din("q_rowsinit", (3, NT), F32)
    consts_f = din("consts_f", (128, 8), F32)
    ident_bf_d = din("ident_bf", (128, 128), BF16)
    ident_f_d = din("ident_f", (128, 128), F32)
    tri_d = din("tri", (128, 2048), BF16)
    g_ind_d = din("g_ind", (128, 32, 32), BF16)
    wkv_t = din("wkv_t", (128, 32, KVL), F32)
    wkr_t = din("wkr_t", (128, 32, DR), F32)
    wqd_t = din("wqd_t", (12, 128, 32, 128), F32)
    wqun_t = din("wqun_t", (16, 128, 12, 256), F32)
    wqur_t = din("wqur_t", (16, 128, 12, 128), F32)
    wukT_t = din("wukT_t", (16, 128, 4, 256), F32)
    wuvT_t = din("wuvT_t", (16, 128, 4, 256), F32)
    wout_t = din("wout_t", (32, 128, 32, 128), F32)
    q_tiled = din("q_tiled", (32, 128, 32, 128), mybir.dt.int8)
    sc_rows_d = din("sc_rows", (32, 4096), F32)
    zT_d = din("zT", (32, 4096), F32)
    bias_d = din("bias_r", (128, 32), F32)

    out_own = nc.dram_tensor("out_own", [NT, D], F32, kind="ExternalOutput").ap()
    dbg_t = {}
    if dbg:
        for nme, shp, dt in [
            ("dbg_xnT", (128, 32 * NT), BF16), ("dbg_cq", (128, 12 * NT), BF16),
            ("dbg_ckv", (128, 4 * SEQ), BF16), ("dbg_krx", (67, SEQ), BF16),
            ("dbg_oT", (128, 32 * NT), BF16), ("dbg_cs", (32, NT), BF16),
        ]:
            dbg_t[nme] = nc.dram_tensor(nme, list(shp), dt, kind="ExternalOutput").ap()

    xn_scr = nc.dram_tensor("xn_scr", [D, NT], BF16, kind="Internal").ap()
    sa_scr = nc.dram_tensor("sa_scr", [D, NT], BF16, kind="Internal").ap()

    with ExitStack() as top:
        e = top.enter_context
        ident_bf = e(nc.sbuf_tensor("ident_bf_s", [128, 128], BF16))
        ident_f = e(nc.sbuf_tensor("ident_f_s", [128, 128], F32))
        cf_sb = e(nc.sbuf_tensor("consts_f_s", [128, 8], F32))
        ones_col = e(nc.sbuf_tensor("ones_col", [128, 1], BF16))
        ones_row_f = e(nc.sbuf_tensor("ones_row_f", [1, 128], F32))
        ones_row_bf = e(nc.sbuf_tensor("ones_row_bf", [1, 128], BF16))
        colsumsT = e(nc.sbuf_tensor("colsumsT", [32, NT], BF16))
        zT_sb = e(nc.sbuf_tensor("zT_s", [32, 4096], BF16))
        bias_sb = e(nc.sbuf_tensor("bias_s", [128, 32], F32))
        es_roB = top.enter_context(ExitStack())
        tri_sb = es_roB.enter_context(nc.sbuf_tensor("tri_s", [128, 2048], BF16))
        cosq_sb = es_roB.enter_context(nc.sbuf_tensor("cosq", [128, NT], F32))
        sinq_sb = es_roB.enter_context(nc.sbuf_tensor("sinq", [128, NT], F32))
        k_ropeX = es_roB.enter_context(nc.sbuf_tensor("k_ropeX", [67, SEQ], BF16))
        g_ind_sb = es_roB.enter_context(nc.sbuf_tensor("g_ind_s", [128, 32 * 32], BF16))

        nc.const_aps.register(F32, EPS, cf_sb[:, 0:1])
        nc.const_aps.register(F32, SCALE, cf_sb[:, 1:2])
        nc.const_aps.register(F32, 0.0, cf_sb[:, 2:3])
        nc.const_aps.register(F32, 1.0, cf_sb[:, 3:4])
        nc.const_aps.register(F32, -1.0, cf_sb[:, 4:5])

        sems = {}
        for nm in ("T", "V", "S", "G", "Y", "ph"):
            sems[nm] = _Sem(e(nc.semaphore(f"sem_{nm}")), nm)
        sem_pool = [e(nc.semaphore(f"sem_d{i}")) for i in range(48)]
        blk = e(nc.Block())
        b = _B(nc, blk, sems, sem_pool)

        es_ckv = top.enter_context(ExitStack())
        c_kvT = es_ckv.enter_context(nc.sbuf_tensor("c_kvT", [128, 4 * SEQ], BF16))
        c_qT = es_ckv.enter_context(nc.sbuf_tensor("c_qT", [128, 12 * NT], BF16))
        es_kr = top.enter_context(ExitStack())
        k_rope_raw = es_kr.enter_context(
            nc.sbuf_tensor("k_rope_raw", [64, SEQ], F32))
        cosk_sb = es_kr.enter_context(nc.sbuf_tensor("cosk", [64, SEQ], F32))
        sink_sb = es_kr.enter_context(nc.sbuf_tensor("sink", [64, SEQ], F32))

        for dst, src in [(ident_bf, ident_bf_d), (ident_f, ident_f_d),
                         (tri_sb, tri_d), (cf_sb, consts_f),
                         (cosk_sb, cos_k), (sink_sb, sin_k),
                         (cosq_sb, cos_q), (sinq_sb, sin_q),
                         (bias_sb, bias_d)]:
            b.step("Y", lambda e, d=dst, s=src: e.dma_start(out=d[:], in_=s[:]),
                   w=[d_key(dst)], dma="c0")
        b.step("G", lambda e: e.dma_start(out=zT_sb[:], in_=zT_d[:]),
               w=[d_key(zT_sb)], dma="c1")
        b.step("Y", lambda e: e.dma_start(
            out=g_ind_sb[:].rearrange("p (j c) -> p j c", j=32),
            in_=g_ind_d[:]),
            w=[d_key(g_ind_sb)], dma="c0")
        b.flush("c0", [d_key(t) for t in (ident_bf, ident_f, tri_sb, cf_sb,
                                          cosk_sb, sink_sb, cosq_sb, sinq_sb,
                                          bias_sb, g_ind_sb)])
        b.step("V", lambda e: e.memset(ones_col[:], 1.0), w=["ones_col"])
        b.step("V", lambda e: e.memset(ones_row_f[:], 1.0), w=["ones_row_f"])
        b.step("V", lambda e: e.memset(ones_row_bf[:], 1.0), w=["ones_row_bf"])
        b.barrier()

        with ExitStack() as ph:
            pe = ph.enter_context
            wkv_bf = pe(nc.sbuf_tensor("wkv_bf", [128, 32 * 512], BF16))
            wkr_bf = pe(nc.sbuf_tensor("wkr_bf", [128, 32 * 64], BF16))
            xa = [pe(nc.sbuf_tensor(f"xa{i}", [128, D], F32)) for i in range(2)]
            xnb = [pe(nc.sbuf_tensor(f"xnb{i}", [128, D], BF16)) for i in range(2)]
            xnT_tile = [pe(nc.sbuf_tensor(f"xnTt{i}", [128, D], BF16)) for i in range(2)]
            stats = pe(nc.sbuf_tensor("stats", [128, 48], F32))
            aggr = pe(nc.sbuf_tensor("aggr", [128, 2], F32))
            sd = pe(nc.sbuf_tensor("sd", [128, 1], F32))
            rstd = pe(nc.sbuf_tensor("rstd", [128, 1], F32))
            ps_tA = [pe(nc.psum_tensor(f"ps_tA{i}", [128, 1024], BF16)) for i in range(2)]
            ps_kv = pe(nc.psum_tensor("ps_kv", [128, 512], F32))
            ps_kr = pe(nc.psum_tensor("ps_kr", [64, 128], F32))

            b.step("G", lambda e: e.dma_start(
                out=wkv_bf[:].rearrange("p (k m) -> p k m", k=32),
                in_=wkv_t[:]),
                w=["wkv_bf"], dma="wkv")
            b.step("G", lambda e: e.dma_start(
                out=wkr_bf[:].rearrange("p (k m) -> p k m", k=32),
                in_=wkr_t[:]),
                w=["wkr_bf"], dma="wkr")

            def ln_tile(src_ap, t, bi, dst_tile):
                b.step("Y", lambda e: e.dma_start(
                    out=xa[bi][:], in_=src_ap[t * 128:(t + 1) * 128, :]),
                    r=[], w=[("xa", bi)], dma=f"xa{bi}")
                for u in range(8):
                    b.step("V", lambda e, u=u: e.bn_stats(
                        stats[:, u * 6:(u + 1) * 6], xa[bi][:, u * 512:(u + 1) * 512]),
                        r=[("xa", bi)], w=[("stats", u)])
                b.step("V", lambda e: e.bn_aggr(aggr[:], stats[:, 0:48]),
                       r=[("stats", u) for u in range(8)], w=["aggr"])
                b.step("S", lambda e: e.activation(sd[:], aggr[:, 1:2], AF.Sqrt, bias=EPS),
                       r=["aggr"], w=["sd"])
                b.step("V", lambda e: e.reciprocal(rstd[:], sd[:]), r=["sd"], w=["rstd"])
                b.step("V", lambda e: e.tensor_scalar(
                    out=xnb[bi][:], in0=xa[bi][:], scalar1=aggr[:, 0:1],
                    scalar2=rstd[:], op0=ALU.subtract, op1=ALU.mult),
                    r=[("xa", bi), "aggr", "rstd"], w=[("xnb", bi)])
                for q in range(4):
                    pbank = q % 2
                    for u in range(8):
                        c = q * 8 + u
                        b.step("T", lambda e, c=c, u=u, pbank=pbank: e.transpose(
                            ps_tA[pbank][:, u * 128:(u + 1) * 128],
                            xnb[bi][:, c * 128:(c + 1) * 128], ident_bf[:]),
                            r=[("xnb", bi), d_key(ident_bf)],
                            w=[("ps_tA", pbank)],
                            tok=(u == 7),
                            reg_r=[("xnb", bi), d_key(ident_bf)] if u == 7 else None)
                    b.step("V", lambda e, q=q, pbank=pbank: e.tensor_copy(
                        dst_tile[:].rearrange("p (c n) -> p c n", c=32)[:, q * 8:(q + 1) * 8, :],
                        ps_tA[pbank][:].rearrange("p (c n) -> p c n", c=8)),
                        r=[("ps_tA", pbank)], w=[("dstq", id(dst_tile), q)])

            for t in range(16):
                bi = t % 2
                ln_tile(x_full, t, bi, xnT_tile[bi])
                for m in range(4):
                    for K in range(32):
                        b.step("T", lambda e, m=m, K=K, bi=bi: e.matmul(
                            ps_kv[:, m * 128:(m + 1) * 128],
                            wkv_bf[:, K * 512 + m * 128:K * 512 + (m + 1) * 128],
                            xnT_tile[bi][:, K * 128:(K + 1) * 128],
                            start=(K == 0), stop=(K == 31)),
                            r=["wkv_bf", ("dstq", id(xnT_tile[bi]), K // 8)],
                            w=[("ps_kv", m)], tok=(K == 31),
                            reg_r=(["wkv_bf"] + [("dstq", id(xnT_tile[bi]), qq) for qq in range(4)]) if K == 31 else None)
                b.step("V", lambda e, t=t: e.tensor_copy(
                    c_kvT[:].rearrange("p (m n) -> p m n", m=4)[:, :, t * 128:(t + 1) * 128],
                    ps_kv[:].rearrange("p (m n) -> p m n", m=4)),
                    r=[("ps_kv", m) for m in range(4)], w=[("c_kvT", t)])
                for K in range(32):
                    b.step("T", lambda e, K=K, bi=bi: e.matmul(
                        ps_kr[:, :],
                        wkr_bf[:, K * 64:(K + 1) * 64],
                        xnT_tile[bi][:, K * 128:(K + 1) * 128],
                        start=(K == 0), stop=(K == 31)),
                        r=["wkr_bf", ("dstq", id(xnT_tile[bi]), K // 8)],
                        w=[("ps_kr",)], tok=(K == 31),
                        reg_r=(["wkr_bf"] + [("dstq", id(xnT_tile[bi]), qq) for qq in range(4)]) if K == 31 else None)
                b.step("V", lambda e, t=t: e.tensor_copy(
                    k_rope_raw[:, t * 128:(t + 1) * 128], ps_kr[:, :]),
                    r=[("ps_kr",)], w=[("k_rope_raw", t)])
            b.barrier()

        with ExitStack() as ph:
            pe = ph.enter_context
            rotk = pe(nc.sbuf_tensor("rotk", [64, SEQ], F32))
            pA = pe(nc.sbuf_tensor("ropeA", [64, SEQ], F32))
            pB = pe(nc.sbuf_tensor("ropeB", [64, SEQ], F32))
            b.step("V", lambda e: e.tensor_scalar_mul(
                rotk[0:32, :], k_rope_raw[32:64, :], -1.0),
                r=["k_rope_raw"], w=["rotk_a"])
            b.step("V", lambda e: e.tensor_copy(rotk[32:64, :], k_rope_raw[0:32, :]),
                   r=["k_rope_raw"], w=["rotk_b"])
            b.step("V", lambda e: e.tensor_mul(pA[:], k_rope_raw[0:64, :], cosk_sb[:]),
                   r=["k_rope_raw", d_key(cosk_sb)], w=["pA"])
            b.step("V", lambda e: e.tensor_mul(pB[:], rotk[:], sink_sb[:]),
                   r=["rotk_a", "rotk_b", d_key(sink_sb)], w=["pB"])
            b.step("V", lambda e: e.tensor_add(k_ropeX[0:64, :], pA[:], pB[:]),
                   r=["pA", "pB"], w=["krx_rope"])
            b.step("V", lambda e: e.memset(k_ropeX[64:65, :], 1.0), w=["krx_ones"])
            b.step("G", lambda e: e.dma_start(out=k_ropeX[65:67, :], in_=maskrows[:]),
                   w=["krx_mask"], dma="mk")
            if dbg:
                b.step("Y", lambda e: e.dma_start(out=dbg_t["dbg_krx"][:], in_=k_ropeX[:]),
                       r=["krx_rope", "krx_ones", "krx_mask"], dma="dbg")
            b.barrier()
        es_kr.close()

        es_xn = top.enter_context(ExitStack())
        xnT_own = es_xn.enter_context(
            nc.sbuf_tensor("xnT_own", [128, 32 * NT], BF16, side="right"))
        with ExitStack() as ph:
            pe = ph.enter_context
            xa = [pe(nc.sbuf_tensor(f"xaq{i}", [128, D], F32)) for i in range(2)]
            xnb = [pe(nc.sbuf_tensor(f"xnbq{i}", [128, D], BF16)) for i in range(2)]
            stats = pe(nc.sbuf_tensor("statsq", [128, 48], F32))
            aggr = pe(nc.sbuf_tensor("aggrq", [128, 2], F32))
            sd = pe(nc.sbuf_tensor("sdq", [128, 1], F32))
            rstd = pe(nc.sbuf_tensor("rstdq", [128, 1], F32))
            ps_tA = [pe(nc.psum_tensor(f"ps_tAq{i}", [128, 1024], BF16)) for i in range(2)]

            for t in range(8):
                bi = t % 2
                b.step("Y", lambda e, t=t, bi=bi: e.dma_start(
                    out=xa[bi][:], in_=x_own[t * 128:(t + 1) * 128, :]),
                    w=[("xa", bi)], dma=f"xa{bi}")
                for u in range(8):
                    b.step("V", lambda e, u=u, bi=bi: e.bn_stats(
                        stats[:, u * 6:(u + 1) * 6], xa[bi][:, u * 512:(u + 1) * 512]),
                        r=[("xa", bi)], w=[("stats", u)])
                b.step("V", lambda e: e.bn_aggr(aggr[:], stats[:, 0:48]),
                       r=[("stats", u) for u in range(8)], w=["aggr"])
                b.step("S", lambda e: e.activation(sd[:], aggr[:, 1:2], AF.Sqrt, bias=EPS),
                       r=["aggr"], w=["sd"])
                b.step("V", lambda e: e.reciprocal(rstd[:], sd[:]), r=["sd"], w=["rstd"])
                b.step("V", lambda e, bi=bi: e.tensor_scalar(
                    out=xnb[bi][:], in0=xa[bi][:], scalar1=aggr[:, 0:1],
                    scalar2=rstd[:], op0=ALU.subtract, op1=ALU.mult),
                    r=[("xa", bi), "aggr", "rstd"], w=[("xnb", bi)])
                for q in range(4):
                    pbank = q % 2
                    for u in range(8):
                        c = q * 8 + u
                        b.step("T", lambda e, c=c, u=u, pbank=pbank, bi=bi: e.transpose(
                            ps_tA[pbank][:, u * 128:(u + 1) * 128],
                            xnb[bi][:, c * 128:(c + 1) * 128], ident_bf[:]),
                            r=[("xnb", bi), d_key(ident_bf)],
                            w=[("ps_tA", pbank)], tok=(u == 7),
                            reg_r=[("xnb", bi), d_key(ident_bf)] if u == 7 else None)
                    b.step("V", lambda e, q=q, t=t, pbank=pbank: e.tensor_copy(
                        xnT_own[:].rearrange("p (c n) -> p c n", c=32)
                        [:, q * 8:(q + 1) * 8, t * 128:(t + 1) * 128],
                        ps_tA[pbank][:].rearrange("p (c n) -> p c n", c=8)),
                        r=[("ps_tA", pbank)], w=[("xnT_own", t, q)])
            b.barrier()

        with ExitStack() as ph:
            pe = ph.enter_context
            wqd_cb = [pe(nc.sbuf_tensor(f"wqd_cb{i}", [128, 32 * 128], BF16)) for i in range(2)]
            colsq = pe(nc.sbuf_tensor("colsq", [128, 8 * 32], BF16))
            ps_cq = [pe(nc.psum_tensor(f"ps_cq{i}", [128, 512], F32)) for i in range(4)]
            ps_cs = pe(nc.psum_tensor("ps_cs", [128, 32], F32))
            ps_csT = pe(nc.psum_tensor("ps_csT", [32, 128], BF16))

            xn_own_keys = [("xnT_own", t, q) for t in range(8) for q in range(4)]
            for m in range(12):
                b.step("G", lambda e, m=m: e.dma_start(
                    out=wqd_cb[m % 2][:].rearrange("p (k n) -> p k n", k=32),
                    in_=wqd_t[m]),
                    w=[("wqd_cb", m % 2)], dma=f"wqd{m % 2}")
                for t in range(2):
                    pb = (m % 2) * 2 + t
                    for K in range(32):
                        b.step("T", lambda e, m=m, t=t, K=K, pb=pb: e.matmul(
                            ps_cq[pb][:],
                            wqd_cb[m % 2][:, K * 128:(K + 1) * 128],
                            xnT_own[:, K * NT + t * 512:K * NT + t * 512 + 512],
                            start=(K == 0), stop=(K == 31)),
                            r=([("wqd_cb", m % 2)] + (xn_own_keys if K == 0 else [])),
                            w=[("ps_cq", pb)], tok=(K == 31),
                            reg_r=([("wqd_cb", m % 2)] + xn_own_keys) if K == 31 else None)
                    b.step("V", lambda e, m=m, t=t, pb=pb: e.tensor_copy(
                        c_qT[:, m * NT + t * 512:m * NT + t * 512 + 512], ps_cq[pb][:]),
                        r=[("ps_cq", pb)], w=[("c_qT", m, t)])
            for tc in range(8):
                for j in range(32):
                    b.step("T", lambda e, tc=tc, j=j: e.matmul(
                        ps_cs[:], xnT_own[:, j * NT + tc * 128:j * NT + (tc) * 128 + 128],
                        g_ind_sb[:, j * 32:(j + 1) * 32],
                        start=(j == 0), stop=(j == 31)),
                        r=([d_key(g_ind_sb)] + (xn_own_keys if j == 0 else [])),
                        w=[("ps_cs",)], tok=(j == 31),
                        reg_r=([d_key(g_ind_sb)] + xn_own_keys) if j == 31 else None)
                b.step("V", lambda e, tc=tc: e.tensor_copy(
                    colsq[:, tc * 32:(tc + 1) * 32], ps_cs[:]),
                    r=[("ps_cs",)], w=[("colsq", tc)])
                b.step("T", lambda e, tc=tc: e.transpose(
                    ps_csT[:], colsq[:, tc * 32:(tc + 1) * 32], ident_bf[:]),
                    r=[("colsq", tc), d_key(ident_bf)], w=[("ps_csT",)])
                b.step("V", lambda e, tc=tc: e.tensor_copy(
                    colsumsT[0:32, tc * 128:(tc + 1) * 128], ps_csT[:]),
                    r=[("ps_csT",)], w=[("colsumsT", tc)])
            b.step("Y", lambda e: e.dma_start(
                out=xn_scr[:].rearrange("(j p) n -> p j n", p=128),
                in_=xnT_own[:].rearrange("p (j n) -> p j n", j=32)),
                r=xn_own_keys, w=["xn_scr"], dma="xs")
            if dbg:
                b.step("Y", lambda e: e.dma_start(out=dbg_t["dbg_xnT"][:], in_=xnT_own[:]),
                       r=xn_own_keys, dma="dbg")
                b.step("Y", lambda e: e.dma_start(out=dbg_t["dbg_cq"][:], in_=c_qT[:]),
                       r=[("c_qT", m, t) for m in range(12) for t in range(2)], dma="dbg")
                b.step("Y", lambda e: e.dma_start(out=dbg_t["dbg_ckv"][:], in_=c_kvT[:]),
                       r=[("c_kvT", t) for t in range(16)], dma="dbg")
                b.step("Y", lambda e: e.dma_start(out=dbg_t["dbg_cs"][:], in_=colsumsT[:]),
                       r=[("colsumsT", tc) for tc in range(8)], dma="dbg")
            b.barrier()
        es_xn.close()

        es_oT = top.enter_context(ExitStack())
        oT = es_oT.enter_context(
            nc.sbuf_tensor("oT", [128, 32 * NT], BF16, side="right"))
        with ExitStack() as ph:
            pe = ph.enter_context
            wqun_bf = [pe(nc.sbuf_tensor(f"wqun{i}", [128, 12 * 256], BF16)) for i in range(2)]
            wqur_bf = [pe(nc.sbuf_tensor(f"wqur{i}", [128, 12 * 128], BF16)) for i in range(2)]
            wukT_bf = [pe(nc.sbuf_tensor(f"wukT{i}", [128, 4 * 256], BF16)) for i in range(2)]
            wuvT_bf = [pe(nc.sbuf_tensor(f"wuvT{i}", [128, 4 * 256], BF16)) for i in range(2)]
            q_nopeT = [pe(nc.sbuf_tensor(f"qnope{i}", [128, NT], BF16)) for i in range(2)]
            qrx = [pe(nc.sbuf_tensor(f"qrx{i}", [67, NT], BF16)) for i in range(2)]
            kn = [pe(nc.sbuf_tensor(f"kn{i}", [128, SEQ], BF16)) for i in range(2)]
            v_grp = pe(nc.sbuf_tensor("v_grp", [128, 16 * 256], BF16))
            rope_tmp = pe(nc.sbuf_tensor("rope_tmp", [128, NT], F32))
            rot = pe(nc.sbuf_tensor("rot", [128, NT], F32))
            prA = pe(nc.sbuf_tensor("prA", [128, NT], F32))
            prB = pe(nc.sbuf_tensor("prB", [128, NT], F32))
            PT = [pe(nc.sbuf_tensor(f"PT{i}", [128, 512], BF16)) for i in range(2)]
            mcol4 = pe(nc.sbuf_tensor("mcol4", [128, 16], F32))
            mtmp = pe(nc.sbuf_tensor("mtmp", [128, 1], F32))
            mneg4 = pe(nc.sbuf_tensor("mneg4", [128, 16], F32))
            recip_row = pe(nc.sbuf_tensor("recip_row", [1, 512], F32))
            lnrow = pe(nc.sbuf_tensor("lnrow", [1, 512], F32))
            rb_sb = pe(nc.sbuf_tensor("rb_sb", [128, 512], F32))
            pool_t = pe(nc.psum_tensor("poolB", [128, 2048], F32))
            pool = [pool_t[:, i * 512:(i + 1) * 512] for i in range(4)]
            ps_st = [pe(nc.psum_tensor(f"ps_stB{i}", [128, 512], F32)) for i in range(2)]
            ps_out = pe(nc.psum_tensor("ps_outB", [128, 512], F32))
            ps_sum = pe(nc.psum_tensor("ps_sumB", [1, 512], F32))

            for p in range(2):
                b.step("G", lambda e, p=p: e.dma_start(out=qrx[p][64:67, :], in_=q_rowsinit[:]),
                       w=[("qrx_rows", p)], dma=f"qi{p}")

            ckv_keys = [("c_kvT", t) for t in range(16)]
            cq_keys = [("c_qT", m, t) for m in range(12) for t in range(2)]
            krx_keys = ["krx_rope", "krx_ones", "krx_mask"]

            def wdma(grp):
                gb = grp % 2
                for (buf, src, ncols) in [(wqun_bf, wqun_t, 256), (wqur_bf, wqur_t, 128),
                                          (wukT_bf, wukT_t, 256), (wuvT_bf, wuvT_t, 256)]:
                    b.step("G", lambda e, buf=buf, src=src, grp=grp: e.dma_start(
                        out=buf[gb][:].rearrange("p (k n) -> p k n", k=src.shape[2]),
                        in_=src[grp]),
                        w=[(id(buf), gb)], dma=f"wg{gb}")
                b.flush(f"wg{gb}", [(id(buf), gb) for buf in
                                    (wqun_bf, wqur_bf, wukT_bf, wuvT_bf)])

            wdma(0)
            pool_i = [0]

            def nxt_pool():
                i = pool_i[0] % 4
                pool_i[0] += 1
                return i

            for grp in range(16):
                gb = grp % 2
                if grp + 1 < 16:
                    wdma(grp + 1)
                for p in range(2):
                    for t in range(2):
                        pb = nxt_pool()
                        for K in range(12):
                            b.step("T", lambda e, p=p, t=t, K=K, pb=pb, gb=gb: e.matmul(
                                pool[pb],
                                wqun_bf[gb][:, K * 256 + p * 128:K * 256 + p * 128 + 128],
                                c_qT[:, K * NT + t * 512:K * NT + t * 512 + 512],
                                start=(K == 0), stop=(K == 11)),
                                r=[(id(wqun_bf), gb), ("c_qT", K, t)],
                                w=[("pool", pb)], tok=(K == 11),
                                reg_r=([(id(wqun_bf), gb)] + cq_keys) if K == 11 else None)
                        b.step("V", lambda e, p=p, t=t, pb=pb: e.tensor_copy(
                            q_nopeT[p][:, t * 512:t * 512 + 512], pool[pb]),
                            r=[("pool", pb)], w=[("qn", p, t)])
                for t in range(2):
                    pb = nxt_pool()
                    for K in range(12):
                        b.step("T", lambda e, t=t, K=K, pb=pb, gb=gb: e.matmul(
                            pool[pb],
                            wqur_bf[gb][:, K * 128:(K + 1) * 128],
                            c_qT[:, K * NT + t * 512:K * NT + t * 512 + 512],
                            start=(K == 0), stop=(K == 11)),
                            r=[(id(wqur_bf), gb), ("c_qT", K, t)],
                            w=[("pool", pb)], tok=(K == 11),
                            reg_r=([(id(wqur_bf), gb)] + cq_keys) if K == 11 else None)
                    b.step("V", lambda e, t=t, pb=pb: e.tensor_copy(
                        rope_tmp[:, t * 512:t * 512 + 512], pool[pb]),
                        r=[("pool", pb)], w=[("rope_tmp", t)])
                rt_keys = [("rope_tmp", 0), ("rope_tmp", 1)]
                for (dsta, srca) in [(0, 32), (32, 0), (64, 96), (96, 64)]:
                    neg = (dsta in (0, 64))
                    b.step("V", lambda e, dsta=dsta, srca=srca, neg=neg:
                           (e.tensor_scalar_mul(rot[dsta:dsta + 32, :],
                                                rope_tmp[srca:srca + 32, :], -1.0)
                            if neg else
                            e.tensor_copy(rot[dsta:dsta + 32, :],
                                          rope_tmp[srca:srca + 32, :])),
                           r=rt_keys, w=[("rot", dsta)])
                rot_keys = [("rot", a) for a in (0, 32, 64, 96)]
                b.step("V", lambda e: e.tensor_mul(prA[:], rope_tmp[:], cosq_sb[:]),
                       r=rt_keys + [d_key(cosq_sb)], w=["prA"])
                b.step("V", lambda e: e.tensor_mul(prB[:], rot[:], sinq_sb[:]),
                       r=rot_keys + [d_key(sinq_sb)], w=["prB"])
                for p in range(2):
                    b.step("V", lambda e, p=p: e.tensor_add(
                        qrx[p][0:64, :], prA[p * 64:p * 64 + 64, :],
                        prB[p * 64:p * 64 + 64, :]),
                        r=["prA", "prB"], w=[("qrx", p)])
                for p in range(2):
                    for s4 in range(4):
                        pb = nxt_pool()
                        for K in range(4):
                            b.step("T", lambda e, p=p, s4=s4, K=K, pb=pb, gb=gb: e.matmul(
                                pool[pb],
                                wukT_bf[gb][:, K * 256 + p * 128:K * 256 + p * 128 + 128],
                                c_kvT[:, K * SEQ + s4 * 512:K * SEQ + s4 * 512 + 512],
                                start=(K == 0), stop=(K == 3)),
                                r=[(id(wukT_bf), gb)] + (ckv_keys if K == 0 else []),
                                w=[("pool", pb)], tok=(K == 3),
                                reg_r=([(id(wukT_bf), gb)] + ckv_keys) if K == 3 else None)
                        b.step("S", lambda e, p=p, s4=s4, pb=pb: e.copy(
                            kn[p][:, s4 * 512:s4 * 512 + 512], pool[pb]),
                            r=[("pool", pb)], w=[("kn", p, s4)])
                for st in range(16):
                    pb = nxt_pool()
                    for K in range(4):
                        b.step("T", lambda e, st=st, K=K, pb=pb, gb=gb: e.matmul(
                            pool[pb][:, 0:256],
                            c_kvT[:, K * SEQ + st * 128:K * SEQ + st * 128 + 128],
                            wuvT_bf[gb][:, K * 256:(K + 1) * 256],
                            start=(K == 0), stop=(K == 3)),
                            r=[(id(wuvT_bf), gb)] + (ckv_keys if K == 0 else []),
                            w=[("pool", pb)], tok=(K == 3),
                            reg_r=([(id(wuvT_bf), gb)] + ckv_keys) if K == 3 else None)
                    b.step("S", lambda e, st=st, pb=pb: e.copy(
                        v_grp[:, st * 256:st * 256 + 256], pool[pb][:, 0:256]),
                        r=[("pool", pb)], w=[("v", st)])

                kn_keys = lambda p: [("kn", p, s4) for s4 in range(4)]
                tails = []
                for p in range(2):
                    for g in range(2):
                        pgi = p * 2 + g
                        for s in range(4):
                            qcol = g * 512 + s * 128
                            nw = NW[g]
                            base = 0 if nw == 4 else (2 * s) % 4
                            for w in range(nw):
                                pb = (base + w) % 4
                                b.step("T", lambda e, p=p, qcol=qcol, w=w, pb=pb: e.matmul(
                                    pool[pb], q_nopeT[p][:, qcol:qcol + 128],
                                    kn[p][:, w * 512:w * 512 + 512],
                                    start=True, stop=False),
                                    r=[("qn", p, g), ("kn", p, w)],
                                    w=[("pool", pb)], tok=False)
                                b.step("T", lambda e, p=p, qcol=qcol, w=w, pb=pb: e.matmul(
                                    pool[pb], qrx[p][0:64, qcol:qcol + 128],
                                    k_ropeX[0:64, w * 512:w * 512 + 512],
                                    start=False, stop=True),
                                    r=[("qrx", p), "krx_rope"],
                                    w=[("pool", pb)], tok=True,
                                    reg_r=[("qn", p, g), ("kn", p, w), ("qrx", p), "krx_rope"])
                                if w % 2 == 1:
                                    pb0 = (base + w - 1) % 4
                                    mc = pgi * 4 + s
                                    if w == 1:
                                        b.step("V", lambda e, pb0=pb0, mc=mc: e.reduce_max(
                                            mcol4[:, mc:mc + 1],
                                            pool_t[:, pb0 * 512:pb0 * 512 + 1024],
                                            axis=AX.X),
                                            r=[("pool", pb0), ("pool", pb0 + 1)],
                                            w=[("mcol4", mc)])
                                    else:
                                        b.step("V", lambda e, pb0=pb0: e.reduce_max(
                                            mtmp[:],
                                            pool_t[:, pb0 * 512:pb0 * 512 + 1024],
                                            axis=AX.X),
                                            r=[("pool", pb0), ("pool", pb0 + 1)],
                                            w=["mtmp"])
                                        b.step("V", lambda e, mc=mc: e.tensor_max(
                                            mcol4[:, mc:mc + 1], mcol4[:, mc:mc + 1],
                                            mtmp[:]),
                                            r=[("mcol4", mc), "mtmp"], w=[("mcol4", mc)])

                        def tail(p=p, g=g, pgi=pgi):
                            b.step("V", lambda e: e.tensor_scalar_mul(
                                mneg4[:, pgi * 4:pgi * 4 + 4],
                                mcol4[:, pgi * 4:pgi * 4 + 4], -1.0),
                                r=[("mcol4", pgi * 4 + ss) for ss in range(4)],
                                w=[("mneg4", pgi)])
                            for ss in range(4):
                                b.step("T", lambda e, ss=ss: e.transpose(
                                    ps_sum[0:1, ss * 128:ss * 128 + 128],
                                    mneg4[:, pgi * 4 + ss:pgi * 4 + ss + 1], ident_f[:]),
                                    r=[("mneg4", pgi), d_key(ident_f)], w=[("ps_sum",)])
                            b.step("V", lambda e: e.tensor_copy(
                                qrx[p][64:65, g * 512:g * 512 + 512],
                                ps_sum[0:1, 0:512]),
                                r=[("ps_sum",), ("qrx_rows", p)], w=[("qrx_nm", p, g)])
                        tails.append(tail)
                for tail in tails:
                    tail()
                for p in range(2):
                    h_abs = grp * 2 + p
                    for g in range(2):
                        nch = NCH[g]
                        d0 = DIAG0[g]
                        qsl = slice(g * 512, g * 512 + 512)

                        def emit_ST(c, qsl=qsl, d0=d0, p=p, g=g):
                            bank = c % 2
                            b.step("T", lambda e, p=p, c=c, bank=bank, qsl=qsl: e.matmul(
                                ps_st[bank][:], kn[p][:, c * 128:c * 128 + 128],
                                q_nopeT[p][:, qsl], start=True, stop=False),
                                r=[("kn", p, c // 4), ("qn", p, g)],
                                w=[("st", bank)], tok=False)
                            dd = c - d0
                            last = not (0 <= dd < 4)
                            b.step("T", lambda e, p=p, c=c, bank=bank, last=last, qsl=qsl: e.matmul(
                                ps_st[bank][:], k_ropeX[:, c * 128:c * 128 + 128],
                                qrx[p][0:67, qsl], start=False, stop=last),
                                r=krx_keys + [("qrx", p), ("qrx_nm", p, g), ("qrx_rows", p)],
                                w=[("st", bank)], tok=last,
                                reg_r=([("kn", p, c // 4), ("qn", p, g), ("qrx", p),
                                        ("qrx_nm", p, g), ("qrx_rows", p)] + krx_keys)
                                if last else None)
                            if not last:
                                b.step("T", lambda e, bank=bank, dd=dd: e.matmul(
                                    ps_st[bank][:], ident_bf[:],
                                    tri_sb[:, dd * 512:dd * 512 + 512],
                                    start=False, stop=True),
                                    r=[d_key(ident_bf), d_key(tri_sb)],
                                    w=[("st", bank)], tok=True,
                                    reg_r=[("kn", p, c // 4), ("qn", p, g), ("qrx", p),
                                           ("qrx_nm", p, g), ("qrx_rows", p),
                                           d_key(ident_bf), d_key(tri_sb)] + krx_keys)

                        emit_ST(0)
                        for c in range(nch):
                            bank = c % 2
                            if c + 1 < nch:
                                emit_ST(c + 1)
                            b.step("S", lambda e, c=c, bank=bank: e.activation(
                                PT[bank][:], ps_st[bank][:], AF.Exp, scale=SCALE),
                                r=[("st", bank)], w=[("PT", bank)])
                            b.step("T", lambda e, p=p, c=c, bank=bank, nch=nch: e.matmul(
                                ps_out[:], v_grp[:, c * 256 + p * 128:c * 256 + p * 128 + 128],
                                PT[bank][:], start=(c == 0), stop=(c == nch - 1)),
                                r=[("v", c), ("PT", bank)],
                                w=[("ps_out",)], tok=False)
                            b.step("T", lambda e, c=c, bank=bank, nch=nch: e.matmul(
                                ps_sum[0:1, :], ones_col[:], PT[bank][:],
                                start=(c == 0), stop=(c == nch - 1)),
                                r=["ones_col", ("PT", bank)],
                                w=([("ps_sum",), ("ps_out",)] if c == nch - 1
                                   else [("ps_sum",)]),
                                tok=True,
                                reg_r=["ones_col", ("PT", bank), ("v", c)])
                        b.step("S", lambda e: e.activation(
                            lnrow[:], ps_sum[0:1, :], AF.Ln),
                            r=[("ps_sum",)], w=["lnrow"])
                        b.step("S", lambda e: e.activation(
                            recip_row[:], lnrow[:], AF.Exp, scale=-1.0),
                            r=["lnrow"], w=["recip_row"])
                        b.step("T", lambda e: e.matmul(
                            ps_st[0][:], ones_row_f[:], recip_row[:],
                            start=True, stop=True),
                            r=["ones_row_f", "recip_row"], w=[("st", 0)])
                        b.step("S", lambda e: e.copy(rb_sb[:], ps_st[0][:]),
                               r=[("st", 0)], w=["rb_sb"])
                        b.step("V", lambda e, h_abs=h_abs, g=g: e.tensor_tensor(
                            oT[:, h_abs * NT + g * 512:h_abs * NT + g * 512 + 512],
                            ps_out[:], rb_sb[:], ALU.mult),
                            r=[("ps_out",), "rb_sb"], w=[("oT", h_abs, g)])
            if dbg:
                b.step("Y", lambda e: e.dma_start(out=dbg_t["dbg_oT"][:], in_=oT[:]),
                       r=[("oT", hh, gg) for hh in range(32) for gg in range(2)], dma="dbg")
            b.barrier()
        es_ckv.close()
        es_roB.close()

        oT_keys = [("oT", hh, gg) for hh in range(32) for gg in range(2)]
        with ExitStack() as ph:
            pe = ph.enter_context
            wout_cb = [pe(nc.sbuf_tensor(f"wout{i}", [128, 32 * 128], BF16)) for i in range(4)]
            sa_sb = [pe(nc.sbuf_tensor(f"sa_sb{i}", [128, 512], BF16)) for i in range(4)]
            psC = [pe(nc.psum_tensor(f"psC{i}", [128, 512], F32)) for i in range(8)]

            def wout_dma(ob):
                b.step("G", lambda e, ob=ob: e.dma_start(
                    out=wout_cb[ob % 4][:].rearrange("p (k n) -> p k n", k=32),
                    in_=wout_t[ob]),
                    w=[("wout_cb", ob % 4)], dma=f"wo{ob % 4}")

            wout_dma(0)
            wout_dma(1)
            for og in range(16):
                if 2 * og + 2 < 32:
                    wout_dma(2 * og + 2)
                if 2 * og + 3 < 32:
                    wout_dma(2 * og + 3)
                for K in range(32):
                    for bb in range(2):
                        ob = og * 2 + bb
                        for th in range(2):
                            pb = (og % 2) * 4 + bb * 2 + th
                            b.step("T", lambda e, K=K, ob=ob, th=th, pb=pb: e.matmul(
                                psC[pb][:], wout_cb[ob % 4][:, K * 128:(K + 1) * 128],
                                oT[:, K * NT + th * 512:K * NT + th * 512 + 512],
                                start=(K == 0), stop=(K == 31)),
                                r=[("wout_cb", ob % 4)] + (oT_keys if K == 0 else []),
                                w=[("psC", pb)], tok=(K == 31),
                                reg_r=([("wout_cb", ob % 4)] + oT_keys) if K == 31 else None)
                for bb in range(2):
                    ob = og * 2 + bb
                    for th in range(2):
                        pb = (og % 2) * 4 + bb * 2 + th
                        sb = bb * 2 + th
                        b.step("V", lambda e, pb=pb, sb=sb: e.tensor_copy(
                            sa_sb[sb][:], psC[pb][:]),
                            r=[("psC", pb)], w=[("sa_sb", sb)])
                        b.step("Y", lambda e, ob=ob, th=th, sb=sb: e.dma_start(
                            out=sa_scr[ob * 128:(ob + 1) * 128, th * 512:th * 512 + 512],
                            in_=sa_sb[sb][:]),
                            r=[("sa_sb", sb)], w=[("sa_scr", ob, th)], dma=f"sa{sb}")
            b.barrier()
        es_oT.close()

        with ExitStack() as ph:
            pe = ph.enter_context
            xn_res = pe(nc.sbuf_tensor("xn_res", [128, 32 * NT], BF16))
            w_grp = [pe(nc.sbuf_tensor(f"w_grp{i}", [128, 2 * 4096], BF16)) for i in range(2)]
            qcb_i8 = [pe(nc.sbuf_tensor(f"qcb_i8{i}", [128, 4096], mybir.dt.int8)) for i in range(2)]
            qcb_bf = pe(nc.sbuf_tensor("qcb_bf", [128, 4096], BF16))
            s_ob = [pe(nc.sbuf_tensor(f"s_ob{i}", [128, 4096], BF16)) for i in range(2)]
            sc_row = pe(nc.sbuf_tensor("sc_row", [1, 4096], F32))
            s_sl = [pe(nc.sbuf_tensor(f"s_sl{i}", [128, NT], BF16)) for i in range(2)]
            x_bk = [pe(nc.sbuf_tensor(f"x_bk{i}", [128, NT], F32)) for i in range(2)]
            g8 = [pe(nc.sbuf_tensor(f"g8_{i}", [128, NT], BF16)) for i in range(2)]
            m_tile = [pe(nc.sbuf_tensor(f"m_tile{i}", [128, 512], BF16)) for i in range(2)]
            out_sb = [pe(nc.sbuf_tensor(f"out_sb{i}", [128, NT], F32)) for i in range(2)]
            psD = [pe(nc.psum_tensor(f"psD{i}", [128, 512], F32)) for i in range(4)]
            ps_bc = [pe(nc.psum_tensor(f"ps_bc{i}", [128, 512], F32)) for i in range(2)]
            ps_tr = [pe(nc.psum_tensor(f"ps_tr{i}", [128, 512], BF16)) for i in range(2)]

            cs_keys = [("colsumsT", tc) for tc in range(8)]

            for jc in range(4):
                b.step("Y", lambda e, jc=jc: e.dma_start(
                    out=xn_res[:].rearrange("p (j n) -> p j n", j=32)[:, jc * 8:(jc + 1) * 8, :],
                    in_=xn_scr[:].rearrange("(j p) n -> p j n", p=128)[:, jc * 8:(jc + 1) * 8, :]),
                    r=["xn_scr"], w=[("xn_res", jc)], dma=f"xr{jc}")

            def dequant(gbk):
                buf = gbk % 2
                wg = (gbk // 2) % 2
                b.step("Y", lambda e, gbk=gbk, buf=buf: e.dma_start(
                    out=qcb_i8[buf][:].rearrange("p (j n) -> p j n", j=32),
                    in_=q_tiled[gbk]),
                    w=[("qcb_i8", buf)], dma=f"qc{buf}")
                b.step("G", lambda e, gbk=gbk: e.dma_start(
                    out=sc_row[:], in_=sc_rows_d[gbk:gbk + 1, :]),
                    w=[("sc_row",)], dma="sc")
                for h8 in range(8):
                    pb = h8 % 2
                    b.step("T", lambda e, h8=h8, pb=pb: e.matmul(
                        ps_bc[pb][:], ones_row_f[:],
                        sc_row[:, h8 * 512:(h8 + 1) * 512],
                        start=True, stop=True),
                        r=["ones_row_f", ("sc_row",)], w=[("ps_bc", pb)])
                    b.step("V", lambda e, buf=buf, h8=h8, pb=pb: e.tensor_copy(
                        s_ob[buf][:, h8 * 512:(h8 + 1) * 512], ps_bc[pb][:]),
                        r=[("ps_bc", pb)], w=[("s_ob", buf, h8)])
                b.step("S", lambda e, buf=buf: e.copy(qcb_bf[:], qcb_i8[buf][:]),
                       r=[("qcb_i8", buf)], w=[("qcb_bf",)])
                b.step("V", lambda e, buf=buf, wg=wg, gbk=gbk: e.tensor_tensor(
                    w_grp[wg][:, (gbk % 2) * 4096:(gbk % 2) * 4096 + 4096],
                    qcb_bf[:], s_ob[buf][:], ALU.mult),
                    r=[("qcb_bf",)] + [("s_ob", buf, h8) for h8 in range(8)],
                    w=[("w_grp", wg, gbk % 2)])

            dequant(0)
            dequant(1)
            for g in range(16):
                wg = g % 2
                if g + 1 < 16:
                    dequant(2 * (g + 1))
                    dequant(2 * (g + 1) + 1)
                for i in range(2):
                    gbk = 2 * g + i
                    b.step("Y", lambda e, gbk=gbk, i=i: e.dma_start(
                        out=s_sl[i][:], in_=sa_scr[gbk * 128:(gbk + 1) * 128, :]),
                        w=[("s_sl", i)], dma=f"sl{i}")
                    b.step("Y", lambda e, gbk=gbk, i=i: e.dma_start(
                        out=x_bk[i][:].rearrange("p (tc o) -> p tc o", tc=8),
                        in_=x_blk[gbk]),
                        w=[("x_bk", i)], dma=f"xb{i}")
                    for th in range(2):
                        pb = i * 2 + th
                        b.step("T", lambda e, gbk=gbk, th=th, pb=pb: e.matmul(
                            psD[pb][:], zT_sb[:, gbk * 128:gbk * 128 + 128],
                            colsumsT[0:32, th * 512:th * 512 + 512],
                            start=True, stop=False),
                            r=[d_key(zT_sb)] + cs_keys,
                            w=[("psD", pb)], tok=False)
                for j in range(32):
                    for i in range(2):
                        for th in range(2):
                            pb = i * 2 + th
                            qlast = (j == 31 and i == 1 and th == 1)
                            b.step("T", lambda e, wg=wg, i=i, j=j, th=th, pb=pb: e.matmul(
                                psD[pb][:],
                                w_grp[wg][:, i * 4096 + j * 128:i * 4096 + j * 128 + 128],
                                xn_res[:, j * NT + th * 512:j * NT + th * 512 + 512],
                                start=False, stop=(j == 31)),
                                r=[("w_grp", wg, i), ("xn_res", j // 8)],
                                w=([("psD", q) for q in range(4)] if qlast
                                   else [("psD", pb)]),
                                tok=(th == 1),
                                reg_r=([d_key(zT_sb)] + cs_keys
                                       + [("xn_res", jc) for jc in range(4)]
                                       + [("w_grp", wg, 0), ("w_grp", wg, 1)]) if qlast else None)
                for i in range(2):
                    gbk = 2 * g + i
                    for th in range(2):
                        pb = i * 2 + th
                        b.step("S", lambda e, gbk=gbk, i=i, th=th, pb=pb: e.activation(
                            g8[i][:, th * 512:th * 512 + 512],
                            psD[pb][:], AF.Silu, bias=bias_sb[:, gbk:gbk + 1]),
                            r=[("psD", pb), d_key(bias_sb)], w=[("g8", i, th)])
                for i in range(2):
                    gbk = 2 * g + i
                    for th in range(2):
                        mb = th
                        b.step("V", lambda e, i=i, th=th, mb=mb: e.tensor_tensor(
                            m_tile[mb][:], g8[i][:, th * 512:th * 512 + 512],
                            s_sl[i][:, th * 512:th * 512 + 512], ALU.mult),
                            r=[("g8", i, th), ("s_sl", i)], w=[("m_tile", mb)])
                        for tc4 in range(4):
                            b.step("T", lambda e, mb=mb, tc4=tc4: e.transpose(
                                ps_tr[mb][:, tc4 * 128:tc4 * 128 + 128],
                                m_tile[mb][:, tc4 * 128:tc4 * 128 + 128], ident_bf[:]),
                                r=[("m_tile", mb), d_key(ident_bf)],
                                w=[("ps_tr", mb)], tok=(tc4 == 3),
                                reg_r=[("m_tile", mb), d_key(ident_bf)] if tc4 == 3 else None)
                        b.step("V", lambda e, i=i, th=th, mb=mb: e.tensor_tensor(
                            out_sb[i][:, th * 512:th * 512 + 512],
                            ps_tr[mb][:], x_bk[i][:, th * 512:th * 512 + 512],
                            ALU.add),
                            r=[("ps_tr", mb), ("x_bk", i)], w=[("out_sb", i, th)])
                    b.step("Y", lambda e, gbk=gbk, i=i: e.dma_start(
                        out=out_own.rearrange("(tc p) o -> p tc o", p=128)
                        [:, :, gbk * 128:(gbk + 1) * 128],
                        in_=out_sb[i][:].rearrange("p (tc o) -> p tc o", tc=8)),
                        r=[("out_sb", i, 0), ("out_sb", i, 1)],
                        w=[("out", gbk)], dma=f"ot{i}")
            def fin(e):
                for sm in b.dsems.values():
                    if sm.n > 0:
                        e.wait_ge(sm.h, sm.n)
                return e.nop()
            b.step("Y", fin, tok=False)
            b.flush_pending()

    return nc, dbg_t


def d_key(t):
    return ("sb", t.name if hasattr(t, "name") else id(t))



def kernel(**inputs):
    dbg = int(os.environ.get("KDBG", "0"))
    x = np.asarray(inputs["L0_attn_hidden_states"], np.float32)
    shared = _prep_shared(inputs)
    nc, _ = build_nc(dbg=dbg)

    in_maps = []
    for core in range(8):
        bq, h = divmod(core, 2)
        m = dict(shared)
        m.update(_prep_core(x, bq, h))
        im = {
            "x_full": m["x_full"], "x_own": m["x_own"],
            "cos_k": m["cos_k"], "sin_k": m["sin_k"],
            "cos_q": m["cos_q"], "sin_q": m["sin_q"],
            "maskrows": m["maskrows"], "q_rowsinit": m["q_rowsinit"],
            "consts_f": m["consts_f"], "ident_bf": m["ident_bf"],
            "ident_f": m["ident_f"], "tri": m["tri"], "g_ind": m["g_ind"],
            "wkv_t": m["wkv_t"], "wkr_t": m["wkr_t"], "wqd_t": m["wqd_t"],
            "wqun_t": m["wqun_t"], "wqur_t": m["wqur_t"],
            "wukT_t": m["wukT_t"], "wuvT_t": m["wuvT_t"],
            "wout_t": m["wout_t"], "q_tiled": m["q_tiled"],
            "sc_rows": m["sc_rows"], "zT": m["zT"], "bias_r": m["bias_r"],
            "x_blk_t": m["x_blk_t"],
        }
        in_maps.append(im)

    res = run_bass_kernel_spmd(nc, in_maps, core_ids=list(range(8)))
    out = np.empty((4, SEQ, D), np.float32)
    for core in range(8):
        bq, h = divmod(core, 2)
        out[bq][_own_rows(h)] = res.results[core]["out_own"]
    return out


# version 08 (final)
# speedup vs baseline: 1.9273x; 1.0134x over previous
import os
import math
import numpy as np
import ml_dtypes
from contextlib import ExitStack

import concourse.bass as bass
import concourse.mybir as mybir
from concourse.bass_utils import run_bass_kernel_spmd

F32 = mybir.dt.float32
BF16 = mybir.dt.bfloat16
I32 = mybir.dt.int32
AF = mybir.ActivationFunctionType
AX = mybir.AxisListType
ALU = mybir.AluOpType
BF = ml_dtypes.bfloat16

D = 4096
SEQ = 2048
NT = 1024
H = 32
QL = 1536
KVL = 512
DN = 128
DR = 64
DV = 128
NEG = -1.0e9
SCALE = float(np.float32(1.0) / np.float32(np.sqrt(np.float32(DN + DR))))
EPS = 1.0e-5

NCH = (8, 16)
NW = (2, 4)
DIAG0 = (4, 12)



def _yarn_tables():
    i = np.arange(0, DR, 2, dtype=np.float32)
    freqs = (np.float32(1.0) / (np.float32(10000.0) ** (i / np.float32(DR)))).astype(np.float32)
    wavelengths = (np.float32(2.0 * np.pi) / freqs).astype(np.float32)
    ramp = np.clip((wavelengths / np.float32(4096.0) - np.float32(1.0)) / np.float32(31.0),
                   0.0, 1.0).astype(np.float32)
    scale = (np.float32(1.0) - ramp + ramp * np.float32(40.0)).astype(np.float32)
    return (freqs / scale).astype(np.float32)


def _cos_sin(pos, reps):
    inv = _yarn_tables()
    f = pos.astype(np.float32)[None, :] * inv[:, None]
    c = np.cos(f).astype(np.float32)
    s = np.sin(f).astype(np.float32)
    return np.tile(c, (reps, 1)), np.tile(s, (reps, 1))


def _perm_blocks(h):
    if h == 0:
        return [4, 5, 6, 7, 0, 1, 2, 3, 8, 9, 10, 11, 12, 13, 14, 15]
    return [0, 1, 2, 3, 4, 5, 6, 7, 12, 13, 14, 15, 8, 9, 10, 11]


def _own_rows(h):
    if h == 0:
        return np.r_[0:512, 1536:2048]
    return np.r_[512:1536]


def _maskrows(h):
    m = np.zeros((2, SEQ), np.float32)
    blk = lambda c: slice(c * 128, (c + 1) * 128)
    if h == 0:
        for c in range(0, 4):
            m[0, blk(c)] = NEG
        for c in range(8, 16):
            m[0, blk(c)] = NEG
    else:
        for c in range(8, 16):
            m[0, blk(c)] = NEG
        for c in range(8, 12):
            m[1, blk(c)] = NEG
    return m


def _tri_consts():
    out = np.zeros((128, 4 * 512), np.float32)
    st = np.arange(128)[:, None]
    for d in range(4):
        sq = np.arange(512)[None, :]
        m = np.where(sq < 128 * d, NEG, 0.0)
        inblk = (sq >= 128 * d) & (sq < 128 * (d + 1))
        m = np.where(inblk & (st > (sq - 128 * d)), NEG, m)
        out[:, d * 512:(d + 1) * 512] = m
    return out.astype(BF)


def _tile2(w, kb, mb):
    K, M = w.shape
    t = w.reshape(K // kb, kb, M // mb, mb).transpose(2, 1, 0, 3)
    return np.ascontiguousarray(t)


def _prep_shared(inp):
    d = {}
    d["wkv_t"] = np.ascontiguousarray(
        inp["L0_attn_w_kv_down"].reshape(32, 128, KVL).transpose(1, 0, 2))
    d["wkr_t"] = np.ascontiguousarray(
        inp["L0_attn_w_k_rope"].reshape(32, 128, DR).transpose(1, 0, 2))
    d["wqd_t"] = _tile2(inp["L0_attn_w_q_down"], 128, 128)
    wqun = _tile2(inp["L0_attn_w_q_up_nope"], 128, 256)
    d["wqun_t"] = wqun
    d["wqur_t"] = _tile2(inp["L0_attn_w_q_up_rope"], 128, 128)
    wukT = np.ascontiguousarray(inp["L0_attn_w_uk"].T)
    wuvT = np.ascontiguousarray(inp["L0_attn_w_uv"].T)
    d["wukT_t"] = _tile2(wukT, 128, 256)
    d["wuvT_t"] = _tile2(wuvT, 128, 256)
    d["wout_t"] = _tile2(inp["L0_attn_w_out"], 128, 128)
    q = inp["L0_ffn_qweight"].reshape(4096, 4096)
    qT = q.T
    d["q_tiled"] = _tile2(qT, 128, 128).astype(np.int8)
    sc = inp["L0_ffn_scales"].reshape(32, 128, 32)
    d["sc_rows"] = np.ascontiguousarray(
        sc.transpose(0, 2, 1).reshape(32, 4096))
    d["zT"] = np.ascontiguousarray(
        inp["L0_ffn_scaled_zeros"].reshape(4096, 32).T)
    d["bias_r"] = np.ascontiguousarray(
        inp["L0_ffn_bias"].reshape(32, 128).T)
    d["ident_bf"] = np.eye(128, dtype=BF)
    d["ident_f"] = np.eye(128, dtype=np.float32)
    d["tri"] = _tri_consts()
    g = np.zeros((128, 32, 32), BF)
    for j in range(32):
        g[:, j, j] = BF(1.0)
    d["g_ind"] = g
    cf = np.zeros((128, 8), np.float32)
    cf[:, 0] = EPS
    cf[:, 1] = SCALE
    cf[:, 2] = 0.0
    cf[:, 3] = 1.0
    cf[:, 4] = -1.0
    d["consts_f"] = cf
    qi = np.zeros((3, NT), np.float32)
    qi[1, 0:512] = 1.0
    qi[2, 512:NT] = 1.0
    d["q_rowsinit"] = qi
    return d


def _prep_core(x, b, h):
    d = {}
    perm = _perm_blocks(h)
    rows = np.concatenate([np.arange(c * 128, (c + 1) * 128) for c in perm])
    d["x_full"] = np.ascontiguousarray(x[b][rows])
    own = _own_rows(h)
    xo = np.ascontiguousarray(x[b][own])
    d["x_own"] = xo
    d["x_blk_t"] = np.ascontiguousarray(
        xo.reshape(8, 128, 32, 128).transpose(2, 1, 0, 3))
    pos_k = rows.astype(np.float32)
    ck, sk = _cos_sin(pos_k, 2)
    d["cos_k"], d["sin_k"] = ck, sk
    pos_q = own.astype(np.float32)
    cq, sq = _cos_sin(pos_q, 4)
    d["cos_q"], d["sin_q"] = cq, sq
    d["maskrows"] = _maskrows(h)
    return d



class _Sem:
    def __init__(self, h, name):
        self.h, self.name, self.n = h, name, 0


class _B:

    SEC = {"T": "tensor", "V": "vector", "S": "scalar", "G": "gpsimd", "Y": "sync"}

    def __init__(self, nc, blk, sems, sem_pool):
        self.nc, self.blk = nc, blk
        self.sems = sems
        self.sem_pool = list(sem_pool)
        self.dsems = {}
        self.last_w = {}
        self.readers = {}
        self.waited = {k: {} for k in self.SEC}
        self.pending = {}

    def dsem(self, stream):
        if stream not in self.dsems:
            self.dsems[stream] = _Sem(self.sem_pool.pop(), f"d_{stream}")
        return self.dsems[stream]

    def _need(self, r, w):
        need = {}
        for k in r:
            for s, v in self.last_w.get(k, {}).items():
                if need.get(s, 0) < v:
                    need[s] = v
        for k in w:
            for dd in (self.last_w.get(k, {}), self.readers.get(k, {})):
                for s, v in dd.items():
                    if need.get(s, 0) < v:
                        need[s] = v
        return need

    def step(self, ename, fn, r=(), w=(), tok=True, dma=None, reg_r=None):
        sem = self.dsem(dma) if dma else self.sems[ename]
        need = self._need(r, w)
        wt = self.waited[ename]
        waits = []
        for sm, v in sorted(need.items(), key=lambda kv: kv[0].name):
            if wt.get(sm, 0) < v:
                if sm is self.sems.get(ename) and v > sm.n:
                    raise AssertionError(
                        f"own-sem deadlock {ename} wait {v} > issued {sm.n}")
                waits.append((sm, v))
                wt[sm] = v
        if not tok:
            self.pending.setdefault(ename, []).append((waits, fn, None, 0))
            return None
        step_n = 16 if dma else 1
        sem.n += step_n
        self.pending.setdefault(ename, []).append((waits, fn, sem, step_n))
        t = (sem, sem.n)
        for k in w:
            self.last_w[k] = {sem: sem.n}
            self.readers[k] = {}
        for k in (reg_r if reg_r is not None else r):
            d = self.readers.setdefault(k, {})
            if d.get(sem, 0) < sem.n:
                d[sem] = sem.n
        return t

    def flush_pending(self):
        for ename in list(self.SEC):
            q = self.pending.get(ename)
            if not q:
                continue
            self.pending[ename] = []

            def body(e, q=q):
                for waits, fn, sem, step_n in q:
                    for sm, v in waits:
                        e.wait_ge(sm.h, v)
                    inst = fn(e)
                    if sem is not None:
                        inst.then_inc(sem.h, step_n)

            getattr(self.blk, self.SEC[ename])(body)

    def flush(self, stream, keys):
        sem = self.dsem(stream)
        for k in keys:
            self.last_w[k] = {sem: sem.n}
            self.readers[k] = {}

    def barrier(self):
        self.flush_pending()
        ph = self.sems["ph"]
        target = ph.n + 5
        ph.n = target
        allsems = [self.sems[k] for k in ("T", "V", "S", "G")] + list(self.dsems.values())

        for en in self.SEC:
            def body(e, en=en):
                for s in allsems:
                    if s.n > 0 and self.waited[en].get(s, 0) < s.n:
                        e.wait_ge(s.h, s.n)
                        self.waited[en][s] = s.n
                e.nop().then_inc(ph.h, 1)
                e.wait_ge(ph.h, target)
                self.waited[en][ph] = target
            getattr(self.blk, self.SEC[en])(body)
        self.last_w = {}
        self.readers = {}


def build_nc(dbg=0):
    nc = bass.Bass("TRN2", target_bir_lowering=False)
    dt_in = {}

    def din(name, shape, dtype):
        t = nc.dram_tensor(name, list(shape), dtype, kind="ExternalInput").ap()
        dt_in[name] = t
        return t

    x_full = din("x_full", (SEQ, D), F32)
    x_own = din("x_own", (NT, D), F32)
    x_blk = din("x_blk_t", (32, 128, 8, 128), F32)
    cos_k = din("cos_k", (64, SEQ), F32)
    sin_k = din("sin_k", (64, SEQ), F32)
    cos_q = din("cos_q", (128, NT), F32)
    sin_q = din("sin_q", (128, NT), F32)
    maskrows = din("maskrows", (2, SEQ), F32)
    q_rowsinit = din("q_rowsinit", (3, NT), F32)
    consts_f = din("consts_f", (128, 8), F32)
    ident_bf_d = din("ident_bf", (128, 128), BF16)
    ident_f_d = din("ident_f", (128, 128), F32)
    tri_d = din("tri", (128, 2048), BF16)
    g_ind_d = din("g_ind", (128, 32, 32), BF16)
    wkv_t = din("wkv_t", (128, 32, KVL), F32)
    wkr_t = din("wkr_t", (128, 32, DR), F32)
    wqd_t = din("wqd_t", (12, 128, 32, 128), F32)
    wqun_t = din("wqun_t", (16, 128, 12, 256), F32)
    wqur_t = din("wqur_t", (16, 128, 12, 128), F32)
    wukT_t = din("wukT_t", (16, 128, 4, 256), F32)
    wuvT_t = din("wuvT_t", (16, 128, 4, 256), F32)
    wout_t = din("wout_t", (32, 128, 32, 128), F32)
    q_tiled = din("q_tiled", (32, 128, 32, 128), mybir.dt.int8)
    sc_rows_d = din("sc_rows", (32, 4096), F32)
    zT_d = din("zT", (32, 4096), F32)
    bias_d = din("bias_r", (128, 32), F32)

    out_own = nc.dram_tensor("out_own", [NT, D], F32, kind="ExternalOutput").ap()
    dbg_t = {}
    if dbg:
        for nme, shp, dt in [
            ("dbg_xnT", (128, 32 * NT), BF16), ("dbg_cq", (128, 12 * NT), BF16),
            ("dbg_ckv", (128, 4 * SEQ), BF16), ("dbg_krx", (67, SEQ), BF16),
            ("dbg_oT", (128, 32 * NT), BF16), ("dbg_cs", (32, NT), BF16),
        ]:
            dbg_t[nme] = nc.dram_tensor(nme, list(shp), dt, kind="ExternalOutput").ap()

    xn_scr = nc.dram_tensor("xn_scr", [8, 128, 32, 128], BF16, kind="Internal").ap()
    sa_scr = nc.dram_tensor("sa_scr", [D, NT], BF16, kind="Internal").ap()

    with ExitStack() as top:
        e = top.enter_context
        ident_bf = e(nc.sbuf_tensor("ident_bf_s", [128, 128], BF16))
        ident_f = e(nc.sbuf_tensor("ident_f_s", [128, 128], F32))
        cf_sb = e(nc.sbuf_tensor("consts_f_s", [128, 8], F32))
        ones_col = e(nc.sbuf_tensor("ones_col", [128, 1], BF16))
        ones_row_f = e(nc.sbuf_tensor("ones_row_f", [1, 128], F32))
        ones_row_bf = e(nc.sbuf_tensor("ones_row_bf", [1, 128], BF16))
        colsumsT = e(nc.sbuf_tensor("colsumsT", [32, NT], BF16))
        zT_sb = e(nc.sbuf_tensor("zT_s", [32, 4096], BF16))
        bias_sb = e(nc.sbuf_tensor("bias_s", [128, 32], F32))
        es_roB = top.enter_context(ExitStack())
        tri_sb = es_roB.enter_context(nc.sbuf_tensor("tri_s", [128, 2048], BF16))
        cosq_sb = es_roB.enter_context(nc.sbuf_tensor("cosq", [128, NT], F32))
        sinq_sb = es_roB.enter_context(nc.sbuf_tensor("sinq", [128, NT], F32))
        k_ropeX = es_roB.enter_context(nc.sbuf_tensor("k_ropeX", [67, SEQ], BF16))
        g_ind_sb = es_roB.enter_context(nc.sbuf_tensor("g_ind_s", [128, 32 * 32], BF16))

        nc.const_aps.register(F32, EPS, cf_sb[:, 0:1])
        nc.const_aps.register(F32, SCALE, cf_sb[:, 1:2])
        nc.const_aps.register(F32, 0.0, cf_sb[:, 2:3])
        nc.const_aps.register(F32, 1.0, cf_sb[:, 3:4])
        nc.const_aps.register(F32, -1.0, cf_sb[:, 4:5])

        sems = {}
        for nm in ("T", "V", "S", "G", "Y", "ph"):
            sems[nm] = _Sem(e(nc.semaphore(f"sem_{nm}")), nm)
        sem_pool = [e(nc.semaphore(f"sem_d{i}")) for i in range(48)]
        blk = e(nc.Block())
        b = _B(nc, blk, sems, sem_pool)

        es_ckv = top.enter_context(ExitStack())
        c_kvT = es_ckv.enter_context(nc.sbuf_tensor("c_kvT", [128, 4 * SEQ], BF16))
        c_qT = es_ckv.enter_context(nc.sbuf_tensor("c_qT", [128, 12 * NT], BF16))
        es_kr = top.enter_context(ExitStack())
        k_rope_raw = es_kr.enter_context(
            nc.sbuf_tensor("k_rope_raw", [64, SEQ], F32))
        cosk_sb = es_kr.enter_context(nc.sbuf_tensor("cosk", [64, SEQ], F32))
        sink_sb = es_kr.enter_context(nc.sbuf_tensor("sink", [64, SEQ], F32))

        for dst, src in [(ident_bf, ident_bf_d), (ident_f, ident_f_d),
                         (tri_sb, tri_d), (cf_sb, consts_f),
                         (cosk_sb, cos_k), (sink_sb, sin_k),
                         (cosq_sb, cos_q), (sinq_sb, sin_q),
                         (bias_sb, bias_d)]:
            b.step("Y", lambda e, d=dst, s=src: e.dma_start(out=d[:], in_=s[:]),
                   w=[d_key(dst)], dma="c0")
        b.step("G", lambda e: e.dma_start(out=zT_sb[:], in_=zT_d[:]),
               w=[d_key(zT_sb)], dma="c1")
        b.step("Y", lambda e: e.dma_start(
            out=g_ind_sb[:].rearrange("p (j c) -> p j c", j=32),
            in_=g_ind_d[:]),
            w=[d_key(g_ind_sb)], dma="c0")
        b.flush("c0", [d_key(t) for t in (ident_bf, ident_f, tri_sb, cf_sb,
                                          cosk_sb, sink_sb, cosq_sb, sinq_sb,
                                          bias_sb, g_ind_sb)])
        b.step("V", lambda e: e.memset(ones_col[:], 1.0), w=["ones_col"])
        b.step("V", lambda e: e.memset(ones_row_f[:], 1.0), w=["ones_row_f"])
        b.step("V", lambda e: e.memset(ones_row_bf[:], 1.0), w=["ones_row_bf"])
        b.barrier()

        with ExitStack() as ph:
            pe = ph.enter_context
            wkv_bf = pe(nc.sbuf_tensor("wkv_bf", [128, 32 * 512], BF16))
            wkr_bf = pe(nc.sbuf_tensor("wkr_bf", [128, 32 * 64], BF16))
            xa = [pe(nc.sbuf_tensor(f"xa{i}", [128, D], F32)) for i in range(2)]
            xnb = [pe(nc.sbuf_tensor(f"xnb{i}", [128, D], BF16)) for i in range(2)]
            xnT_tile = [pe(nc.sbuf_tensor(f"xnTt{i}", [128, D], BF16)) for i in range(2)]
            stats = pe(nc.sbuf_tensor("stats", [128, 48], F32))
            aggr = pe(nc.sbuf_tensor("aggr", [128, 2], F32))
            sd = pe(nc.sbuf_tensor("sd", [128, 1], F32))
            rstd = pe(nc.sbuf_tensor("rstd", [128, 1], F32))
            ps_tA = [pe(nc.psum_tensor(f"ps_tA{i}", [128, 1024], BF16)) for i in range(2)]
            ps_kv = pe(nc.psum_tensor("ps_kv", [128, 512], F32))
            ps_kr = pe(nc.psum_tensor("ps_kr", [64, 128], F32))

            b.step("G", lambda e: e.dma_start(
                out=wkv_bf[:].rearrange("p (k m) -> p k m", k=32),
                in_=wkv_t[:]),
                w=["wkv_bf"], dma="wkv")
            b.step("G", lambda e: e.dma_start(
                out=wkr_bf[:].rearrange("p (k m) -> p k m", k=32),
                in_=wkr_t[:]),
                w=["wkr_bf"], dma="wkr")

            def ln_tile(src_ap, t, bi, dst_tile, ot=None):
                b.step("Y", lambda e: e.dma_start(
                    out=xa[bi][:], in_=src_ap[t * 128:(t + 1) * 128, :]),
                    r=[], w=[("xa", bi)], dma=f"xa{bi}")
                for u in range(8):
                    b.step("V", lambda e, u=u: e.bn_stats(
                        stats[:, u * 6:(u + 1) * 6], xa[bi][:, u * 512:(u + 1) * 512]),
                        r=[("xa", bi)], w=[("stats", u)])
                b.step("V", lambda e: e.bn_aggr(aggr[:], stats[:, 0:48]),
                       r=[("stats", u) for u in range(8)], w=["aggr"])
                b.step("S", lambda e: e.activation(sd[:], aggr[:, 1:2], AF.Sqrt, bias=EPS),
                       r=["aggr"], w=["sd"])
                b.step("V", lambda e: e.reciprocal(rstd[:], sd[:]), r=["sd"], w=["rstd"])
                b.step("V", lambda e: e.tensor_scalar(
                    out=xnb[bi][:], in0=xa[bi][:], scalar1=aggr[:, 0:1],
                    scalar2=rstd[:], op0=ALU.subtract, op1=ALU.mult),
                    r=[("xa", bi), "aggr", "rstd"], w=[("xnb", bi)])
                for q in range(4):
                    pbank = q % 2
                    for u in range(8):
                        c = q * 8 + u
                        b.step("T", lambda e, c=c, u=u, pbank=pbank: e.transpose(
                            ps_tA[pbank][:, u * 128:(u + 1) * 128],
                            xnb[bi][:, c * 128:(c + 1) * 128], ident_bf[:]),
                            r=[("xnb", bi), d_key(ident_bf)],
                            w=[("ps_tA", pbank)],
                            tok=(u == 7),
                            reg_r=[("xnb", bi), d_key(ident_bf)] if u == 7 else None)
                    b.step("V", lambda e, q=q, pbank=pbank: e.tensor_copy(
                        dst_tile[:].rearrange("p (c n) -> p c n", c=32)[:, q * 8:(q + 1) * 8, :],
                        ps_tA[pbank][:].rearrange("p (c n) -> p c n", c=8)),
                        r=[("ps_tA", pbank)], w=[("dstq", id(dst_tile), q)])
                if ot is not None:
                    b.step("Y", lambda e, ot=ot, dst_tile=dst_tile: e.dma_start(
                        out=xn_scr[ot],
                        in_=dst_tile[:].rearrange("p (j n) -> p j n", j=32)),
                        r=[("dstq", id(dst_tile), q) for q in range(4)],
                        w=[("xnsc", ot)], dma=f"xs{ot % 2}")

            for t in range(16):
                bi = t % 2
                ot = t - 4 if 4 <= t < 8 else (t - 8 if t >= 12 else None)
                ln_tile(x_full, t, bi, xnT_tile[bi], ot=ot)
                for m in range(4):
                    for K in range(32):
                        b.step("T", lambda e, m=m, K=K, bi=bi: e.matmul(
                            ps_kv[:, m * 128:(m + 1) * 128],
                            wkv_bf[:, K * 512 + m * 128:K * 512 + (m + 1) * 128],
                            xnT_tile[bi][:, K * 128:(K + 1) * 128],
                            start=(K == 0), stop=(K == 31)),
                            r=["wkv_bf", ("dstq", id(xnT_tile[bi]), K // 8)],
                            w=[("ps_kv", m)], tok=(K == 31),
                            reg_r=(["wkv_bf"] + [("dstq", id(xnT_tile[bi]), qq) for qq in range(4)]) if K == 31 else None)
                b.step("V", lambda e, t=t: e.tensor_copy(
                    c_kvT[:].rearrange("p (m n) -> p m n", m=4)[:, :, t * 128:(t + 1) * 128],
                    ps_kv[:].rearrange("p (m n) -> p m n", m=4)),
                    r=[("ps_kv", m) for m in range(4)], w=[("c_kvT", t)])
                for K in range(32):
                    b.step("T", lambda e, K=K, bi=bi: e.matmul(
                        ps_kr[:, :],
                        wkr_bf[:, K * 64:(K + 1) * 64],
                        xnT_tile[bi][:, K * 128:(K + 1) * 128],
                        start=(K == 0), stop=(K == 31)),
                        r=["wkr_bf", ("dstq", id(xnT_tile[bi]), K // 8)],
                        w=[("ps_kr",)], tok=(K == 31),
                        reg_r=(["wkr_bf"] + [("dstq", id(xnT_tile[bi]), qq) for qq in range(4)]) if K == 31 else None)
                b.step("V", lambda e, t=t: e.tensor_copy(
                    k_rope_raw[:, t * 128:(t + 1) * 128], ps_kr[:, :]),
                    r=[("ps_kr",)], w=[("k_rope_raw", t)])
            b.barrier()

        with ExitStack() as ph:
            pe = ph.enter_context
            rotk = pe(nc.sbuf_tensor("rotk", [64, SEQ], F32))
            pA = pe(nc.sbuf_tensor("ropeA", [64, SEQ], F32))
            pB = pe(nc.sbuf_tensor("ropeB", [64, SEQ], F32))
            b.step("V", lambda e: e.tensor_scalar_mul(
                rotk[0:32, :], k_rope_raw[32:64, :], -1.0),
                r=["k_rope_raw"], w=["rotk_a"])
            b.step("V", lambda e: e.tensor_copy(rotk[32:64, :], k_rope_raw[0:32, :]),
                   r=["k_rope_raw"], w=["rotk_b"])
            b.step("V", lambda e: e.tensor_mul(pA[:], k_rope_raw[0:64, :], cosk_sb[:]),
                   r=["k_rope_raw", d_key(cosk_sb)], w=["pA"])
            b.step("V", lambda e: e.tensor_mul(pB[:], rotk[:], sink_sb[:]),
                   r=["rotk_a", "rotk_b", d_key(sink_sb)], w=["pB"])
            b.step("V", lambda e: e.tensor_add(k_ropeX[0:64, :], pA[:], pB[:]),
                   r=["pA", "pB"], w=["krx_rope"])
            b.step("V", lambda e: e.memset(k_ropeX[64:65, :], 1.0), w=["krx_ones"])
            b.step("G", lambda e: e.dma_start(out=k_ropeX[65:67, :], in_=maskrows[:]),
                   w=["krx_mask"], dma="mk")
            if dbg:
                b.step("Y", lambda e: e.dma_start(out=dbg_t["dbg_krx"][:], in_=k_ropeX[:]),
                       r=["krx_rope", "krx_ones", "krx_mask"], dma="dbg")
            b.barrier()
        es_kr.close()

        es_xn = top.enter_context(ExitStack())
        xnT_own = es_xn.enter_context(
            nc.sbuf_tensor("xnT_own", [128, 32 * NT], BF16, side="right"))
        for ot in range(8):
            b.step("Y", lambda e, ot=ot: e.dma_start(
                out=xnT_own[:].rearrange("p (j n) -> p j n", j=32)
                [:, :, ot * 128:(ot + 1) * 128],
                in_=xn_scr[ot]),
                w=[("xnok", ot)], dma=f"xl{ot % 2}")

        with ExitStack() as ph:
            pe = ph.enter_context
            wqd_cb = [pe(nc.sbuf_tensor(f"wqd_cb{i}", [128, 32 * 128], BF16)) for i in range(2)]
            colsq = pe(nc.sbuf_tensor("colsq", [128, 8 * 32], BF16))
            ps_cq = [pe(nc.psum_tensor(f"ps_cq{i}", [128, 512], F32)) for i in range(4)]
            ps_cs = pe(nc.psum_tensor("ps_cs", [128, 32], F32))
            ps_csT = pe(nc.psum_tensor("ps_csT", [32, 128], BF16))

            xn_own_keys = [("xnok", ot) for ot in range(8)]
            for m in range(12):
                b.step("G", lambda e, m=m: e.dma_start(
                    out=wqd_cb[m % 2][:].rearrange("p (k n) -> p k n", k=32),
                    in_=wqd_t[m]),
                    w=[("wqd_cb", m % 2)], dma=f"wqd{m % 2}")
                for t in range(2):
                    pb = (m % 2) * 2 + t
                    for K in range(32):
                        b.step("T", lambda e, m=m, t=t, K=K, pb=pb: e.matmul(
                            ps_cq[pb][:],
                            wqd_cb[m % 2][:, K * 128:(K + 1) * 128],
                            xnT_own[:, K * NT + t * 512:K * NT + t * 512 + 512],
                            start=(K == 0), stop=(K == 31)),
                            r=([("wqd_cb", m % 2)] + (xn_own_keys if K == 0 else [])),
                            w=[("ps_cq", pb)], tok=(K == 31),
                            reg_r=([("wqd_cb", m % 2)] + xn_own_keys) if K == 31 else None)
                    b.step("V", lambda e, m=m, t=t, pb=pb: e.tensor_copy(
                        c_qT[:, m * NT + t * 512:m * NT + t * 512 + 512], ps_cq[pb][:]),
                        r=[("ps_cq", pb)], w=[("c_qT", m, t)])
            for tc in range(8):
                for j in range(32):
                    b.step("T", lambda e, tc=tc, j=j: e.matmul(
                        ps_cs[:], xnT_own[:, j * NT + tc * 128:j * NT + (tc) * 128 + 128],
                        g_ind_sb[:, j * 32:(j + 1) * 32],
                        start=(j == 0), stop=(j == 31)),
                        r=([d_key(g_ind_sb)] + (xn_own_keys if j == 0 else [])),
                        w=[("ps_cs",)], tok=(j == 31),
                        reg_r=([d_key(g_ind_sb)] + xn_own_keys) if j == 31 else None)
                b.step("V", lambda e, tc=tc: e.tensor_copy(
                    colsq[:, tc * 32:(tc + 1) * 32], ps_cs[:]),
                    r=[("ps_cs",)], w=[("colsq", tc)])
                b.step("T", lambda e, tc=tc: e.transpose(
                    ps_csT[:], colsq[:, tc * 32:(tc + 1) * 32], ident_bf[:]),
                    r=[("colsq", tc), d_key(ident_bf)], w=[("ps_csT",)])
                b.step("V", lambda e, tc=tc: e.tensor_copy(
                    colsumsT[0:32, tc * 128:(tc + 1) * 128], ps_csT[:]),
                    r=[("ps_csT",)], w=[("colsumsT", tc)])
            if dbg:
                b.step("Y", lambda e: e.dma_start(out=dbg_t["dbg_xnT"][:], in_=xnT_own[:]),
                       r=xn_own_keys, dma="dbg")
                b.step("Y", lambda e: e.dma_start(out=dbg_t["dbg_cq"][:], in_=c_qT[:]),
                       r=[("c_qT", m, t) for m in range(12) for t in range(2)], dma="dbg")
                b.step("Y", lambda e: e.dma_start(out=dbg_t["dbg_ckv"][:], in_=c_kvT[:]),
                       r=[("c_kvT", t) for t in range(16)], dma="dbg")
                b.step("Y", lambda e: e.dma_start(out=dbg_t["dbg_cs"][:], in_=colsumsT[:]),
                       r=[("colsumsT", tc) for tc in range(8)], dma="dbg")
            b.barrier()
        es_xn.close()

        es_oT = top.enter_context(ExitStack())
        oT = es_oT.enter_context(
            nc.sbuf_tensor("oT", [128, 32 * NT], BF16, side="right"))
        with ExitStack() as ph:
            pe = ph.enter_context
            wqun_bf = [pe(nc.sbuf_tensor(f"wqun{i}", [128, 12 * 256], BF16)) for i in range(2)]
            wqur_bf = [pe(nc.sbuf_tensor(f"wqur{i}", [128, 12 * 128], BF16)) for i in range(2)]
            wukT_bf = [pe(nc.sbuf_tensor(f"wukT{i}", [128, 4 * 256], BF16)) for i in range(2)]
            wuvT_bf = [pe(nc.sbuf_tensor(f"wuvT{i}", [128, 4 * 256], BF16)) for i in range(2)]
            q_nopeT = [pe(nc.sbuf_tensor(f"qnope{i}", [128, NT], BF16)) for i in range(2)]
            qrx = [pe(nc.sbuf_tensor(f"qrx{i}", [67, NT], BF16)) for i in range(2)]
            kn = [pe(nc.sbuf_tensor(f"kn{i}", [128, SEQ], BF16)) for i in range(2)]
            v_grp = pe(nc.sbuf_tensor("v_grp", [128, 16 * 256], BF16))
            rope_tmp = pe(nc.sbuf_tensor("rope_tmp", [128, NT], F32))
            rot = pe(nc.sbuf_tensor("rot", [128, NT], F32))
            prA = pe(nc.sbuf_tensor("prA", [128, NT], F32))
            prB = pe(nc.sbuf_tensor("prB", [128, NT], F32))
            PT = [pe(nc.sbuf_tensor(f"PT{i}", [128, 512], BF16)) for i in range(2)]
            mcol4 = pe(nc.sbuf_tensor("mcol4", [128, 16], F32))
            mtmp = pe(nc.sbuf_tensor("mtmp", [128, 1], F32))
            mneg4 = pe(nc.sbuf_tensor("mneg4", [128, 16], F32))
            recip_row = pe(nc.sbuf_tensor("recip_row", [1, 512], F32))
            lnrow = pe(nc.sbuf_tensor("lnrow", [1, 512], F32))
            rb_sb = pe(nc.sbuf_tensor("rb_sb", [128, 512], F32))
            pool_t = pe(nc.psum_tensor("poolB", [128, 2048], F32))
            pool = [pool_t[:, i * 512:(i + 1) * 512] for i in range(4)]
            ps_st = [pe(nc.psum_tensor(f"ps_stB{i}", [128, 512], F32)) for i in range(2)]
            ps_out = pe(nc.psum_tensor("ps_outB", [128, 512], F32))
            ps_sum = pe(nc.psum_tensor("ps_sumB", [1, 512], F32))

            for p in range(2):
                b.step("G", lambda e, p=p: e.dma_start(out=qrx[p][64:67, :], in_=q_rowsinit[:]),
                       w=[("qrx_rows", p)], dma=f"qi{p}")

            ckv_keys = [("c_kvT", t) for t in range(16)]
            cq_keys = [("c_qT", m, t) for m in range(12) for t in range(2)]
            krx_keys = ["krx_rope", "krx_ones", "krx_mask"]

            def wdma(grp):
                gb = grp % 2
                for (buf, src, ncols) in [(wqun_bf, wqun_t, 256), (wqur_bf, wqur_t, 128),
                                          (wukT_bf, wukT_t, 256), (wuvT_bf, wuvT_t, 256)]:
                    b.step("G", lambda e, buf=buf, src=src, grp=grp: e.dma_start(
                        out=buf[gb][:].rearrange("p (k n) -> p k n", k=src.shape[2]),
                        in_=src[grp]),
                        w=[(id(buf), gb)], dma=f"wg{gb}")
                b.flush(f"wg{gb}", [(id(buf), gb) for buf in
                                    (wqun_bf, wqur_bf, wukT_bf, wuvT_bf)])

            wdma(0)
            pool_i = [0]

            def nxt_pool():
                i = pool_i[0] % 4
                pool_i[0] += 1
                return i

            for grp in range(16):
                gb = grp % 2
                if grp + 1 < 16:
                    wdma(grp + 1)
                for p in range(2):
                    for t in range(2):
                        pb = nxt_pool()
                        for K in range(12):
                            b.step("T", lambda e, p=p, t=t, K=K, pb=pb, gb=gb: e.matmul(
                                pool[pb],
                                wqun_bf[gb][:, K * 256 + p * 128:K * 256 + p * 128 + 128],
                                c_qT[:, K * NT + t * 512:K * NT + t * 512 + 512],
                                start=(K == 0), stop=(K == 11)),
                                r=[(id(wqun_bf), gb), ("c_qT", K, t)],
                                w=[("pool", pb)], tok=(K == 11),
                                reg_r=([(id(wqun_bf), gb)] + cq_keys) if K == 11 else None)
                        b.step("V", lambda e, p=p, t=t, pb=pb: e.tensor_copy(
                            q_nopeT[p][:, t * 512:t * 512 + 512], pool[pb]),
                            r=[("pool", pb)], w=[("qn", p, t)])
                for t in range(2):
                    pb = nxt_pool()
                    for K in range(12):
                        b.step("T", lambda e, t=t, K=K, pb=pb, gb=gb: e.matmul(
                            pool[pb],
                            wqur_bf[gb][:, K * 128:(K + 1) * 128],
                            c_qT[:, K * NT + t * 512:K * NT + t * 512 + 512],
                            start=(K == 0), stop=(K == 11)),
                            r=[(id(wqur_bf), gb), ("c_qT", K, t)],
                            w=[("pool", pb)], tok=(K == 11),
                            reg_r=([(id(wqur_bf), gb)] + cq_keys) if K == 11 else None)
                    b.step("V", lambda e, t=t, pb=pb: e.tensor_copy(
                        rope_tmp[:, t * 512:t * 512 + 512], pool[pb]),
                        r=[("pool", pb)], w=[("rope_tmp", t)])
                rt_keys = [("rope_tmp", 0), ("rope_tmp", 1)]
                for (dsta, srca) in [(0, 32), (32, 0), (64, 96), (96, 64)]:
                    neg = (dsta in (0, 64))
                    b.step("V", lambda e, dsta=dsta, srca=srca, neg=neg:
                           (e.tensor_scalar_mul(rot[dsta:dsta + 32, :],
                                                rope_tmp[srca:srca + 32, :], -1.0)
                            if neg else
                            e.tensor_copy(rot[dsta:dsta + 32, :],
                                          rope_tmp[srca:srca + 32, :])),
                           r=rt_keys, w=[("rot", dsta)])
                rot_keys = [("rot", a) for a in (0, 32, 64, 96)]
                b.step("V", lambda e: e.tensor_mul(prA[:], rope_tmp[:], cosq_sb[:]),
                       r=rt_keys + [d_key(cosq_sb)], w=["prA"])
                b.step("V", lambda e: e.tensor_mul(prB[:], rot[:], sinq_sb[:]),
                       r=rot_keys + [d_key(sinq_sb)], w=["prB"])
                for p in range(2):
                    b.step("V", lambda e, p=p: e.tensor_add(
                        qrx[p][0:64, :], prA[p * 64:p * 64 + 64, :],
                        prB[p * 64:p * 64 + 64, :]),
                        r=["prA", "prB"], w=[("qrx", p)])
                for p in range(2):
                    for s4 in range(4):
                        pb = nxt_pool()
                        for K in range(4):
                            b.step("T", lambda e, p=p, s4=s4, K=K, pb=pb, gb=gb: e.matmul(
                                pool[pb],
                                wukT_bf[gb][:, K * 256 + p * 128:K * 256 + p * 128 + 128],
                                c_kvT[:, K * SEQ + s4 * 512:K * SEQ + s4 * 512 + 512],
                                start=(K == 0), stop=(K == 3)),
                                r=[(id(wukT_bf), gb)] + (ckv_keys if K == 0 else []),
                                w=[("pool", pb)], tok=(K == 3),
                                reg_r=([(id(wukT_bf), gb)] + ckv_keys) if K == 3 else None)
                        b.step("S", lambda e, p=p, s4=s4, pb=pb: e.copy(
                            kn[p][:, s4 * 512:s4 * 512 + 512], pool[pb]),
                            r=[("pool", pb)], w=[("kn", p, s4)])
                for st in range(16):
                    pb = nxt_pool()
                    for K in range(4):
                        b.step("T", lambda e, st=st, K=K, pb=pb, gb=gb: e.matmul(
                            pool[pb][:, 0:256],
                            c_kvT[:, K * SEQ + st * 128:K * SEQ + st * 128 + 128],
                            wuvT_bf[gb][:, K * 256:(K + 1) * 256],
                            start=(K == 0), stop=(K == 3)),
                            r=[(id(wuvT_bf), gb)] + (ckv_keys if K == 0 else []),
                            w=[("pool", pb)], tok=(K == 3),
                            reg_r=([(id(wuvT_bf), gb)] + ckv_keys) if K == 3 else None)
                    b.step("S", lambda e, st=st, pb=pb: e.copy(
                        v_grp[:, st * 256:st * 256 + 256], pool[pb][:, 0:256]),
                        r=[("pool", pb)], w=[("v", st)])

                kn_keys = lambda p: [("kn", p, s4) for s4 in range(4)]
                tails = []
                for p in range(2):
                    for g in range(2):
                        pgi = p * 2 + g
                        for s in range(4):
                            qcol = g * 512 + s * 128
                            nw = NW[g]
                            base = 0 if nw == 4 else (2 * s) % 4
                            for w in range(nw):
                                pb = (base + w) % 4
                                b.step("T", lambda e, p=p, qcol=qcol, w=w, pb=pb: e.matmul(
                                    pool[pb], q_nopeT[p][:, qcol:qcol + 128],
                                    kn[p][:, w * 512:w * 512 + 512],
                                    start=True, stop=False),
                                    r=[("qn", p, g), ("kn", p, w)],
                                    w=[("pool", pb)], tok=False)
                                b.step("T", lambda e, p=p, qcol=qcol, w=w, pb=pb: e.matmul(
                                    pool[pb], qrx[p][0:64, qcol:qcol + 128],
                                    k_ropeX[0:64, w * 512:w * 512 + 512],
                                    start=False, stop=True),
                                    r=[("qrx", p), "krx_rope"],
                                    w=[("pool", pb)], tok=True,
                                    reg_r=[("qn", p, g), ("kn", p, w), ("qrx", p), "krx_rope"])
                                if w % 2 == 1:
                                    pb0 = (base + w - 1) % 4
                                    mc = pgi * 4 + s
                                    if w == 1:
                                        b.step("V", lambda e, pb0=pb0, mc=mc: e.reduce_max(
                                            mcol4[:, mc:mc + 1],
                                            pool_t[:, pb0 * 512:pb0 * 512 + 1024],
                                            axis=AX.X),
                                            r=[("pool", pb0), ("pool", pb0 + 1)],
                                            w=[("mcol4", mc)])
                                    else:
                                        b.step("V", lambda e, pb0=pb0: e.reduce_max(
                                            mtmp[:],
                                            pool_t[:, pb0 * 512:pb0 * 512 + 1024],
                                            axis=AX.X),
                                            r=[("pool", pb0), ("pool", pb0 + 1)],
                                            w=["mtmp"])
                                        b.step("V", lambda e, mc=mc: e.tensor_max(
                                            mcol4[:, mc:mc + 1], mcol4[:, mc:mc + 1],
                                            mtmp[:]),
                                            r=[("mcol4", mc), "mtmp"], w=[("mcol4", mc)])

                        def tail(p=p, g=g, pgi=pgi):
                            b.step("V", lambda e: e.tensor_scalar_mul(
                                mneg4[:, pgi * 4:pgi * 4 + 4],
                                mcol4[:, pgi * 4:pgi * 4 + 4], -1.0),
                                r=[("mcol4", pgi * 4 + ss) for ss in range(4)],
                                w=[("mneg4", pgi)])
                            for ss in range(4):
                                b.step("T", lambda e, ss=ss: e.transpose(
                                    ps_sum[0:1, ss * 128:ss * 128 + 128],
                                    mneg4[:, pgi * 4 + ss:pgi * 4 + ss + 1], ident_f[:]),
                                    r=[("mneg4", pgi), d_key(ident_f)], w=[("ps_sum",)])
                            b.step("V", lambda e: e.tensor_copy(
                                qrx[p][64:65, g * 512:g * 512 + 512],
                                ps_sum[0:1, 0:512]),
                                r=[("ps_sum",), ("qrx_rows", p)], w=[("qrx_nm", p, g)])
                        tails.append(tail)
                for tail in tails:
                    tail()
                for p in range(2):
                    h_abs = grp * 2 + p
                    for g in range(2):
                        nch = NCH[g]
                        d0 = DIAG0[g]
                        qsl = slice(g * 512, g * 512 + 512)

                        def emit_ST(c, qsl=qsl, d0=d0, p=p, g=g):
                            bank = c % 2
                            b.step("T", lambda e, p=p, c=c, bank=bank, qsl=qsl: e.matmul(
                                ps_st[bank][:], kn[p][:, c * 128:c * 128 + 128],
                                q_nopeT[p][:, qsl], start=True, stop=False),
                                r=[("kn", p, c // 4), ("qn", p, g)],
                                w=[("st", bank)], tok=False)
                            dd = c - d0
                            last = not (0 <= dd < 4)
                            b.step("T", lambda e, p=p, c=c, bank=bank, last=last, qsl=qsl: e.matmul(
                                ps_st[bank][:], k_ropeX[:, c * 128:c * 128 + 128],
                                qrx[p][0:67, qsl], start=False, stop=last),
                                r=krx_keys + [("qrx", p), ("qrx_nm", p, g), ("qrx_rows", p)],
                                w=[("st", bank)], tok=last,
                                reg_r=([("kn", p, c // 4), ("qn", p, g), ("qrx", p),
                                        ("qrx_nm", p, g), ("qrx_rows", p)] + krx_keys)
                                if last else None)
                            if not last:
                                b.step("T", lambda e, bank=bank, dd=dd: e.matmul(
                                    ps_st[bank][:], ident_bf[:],
                                    tri_sb[:, dd * 512:dd * 512 + 512],
                                    start=False, stop=True),
                                    r=[d_key(ident_bf), d_key(tri_sb)],
                                    w=[("st", bank)], tok=True,
                                    reg_r=[("kn", p, c // 4), ("qn", p, g), ("qrx", p),
                                           ("qrx_nm", p, g), ("qrx_rows", p),
                                           d_key(ident_bf), d_key(tri_sb)] + krx_keys)

                        emit_ST(0)
                        for c in range(nch):
                            bank = c % 2
                            if c + 1 < nch:
                                emit_ST(c + 1)
                            b.step("S", lambda e, c=c, bank=bank: e.activation(
                                PT[bank][:], ps_st[bank][:], AF.Exp, scale=SCALE),
                                r=[("st", bank)], w=[("PT", bank)])
                            b.step("T", lambda e, p=p, c=c, bank=bank, nch=nch: e.matmul(
                                ps_out[:], v_grp[:, c * 256 + p * 128:c * 256 + p * 128 + 128],
                                PT[bank][:], start=(c == 0), stop=(c == nch - 1)),
                                r=[("v", c), ("PT", bank)],
                                w=[("ps_out",)], tok=False)
                            b.step("T", lambda e, c=c, bank=bank, nch=nch: e.matmul(
                                ps_sum[0:1, :], ones_col[:], PT[bank][:],
                                start=(c == 0), stop=(c == nch - 1)),
                                r=["ones_col", ("PT", bank)],
                                w=([("ps_sum",), ("ps_out",)] if c == nch - 1
                                   else [("ps_sum",)]),
                                tok=True,
                                reg_r=["ones_col", ("PT", bank), ("v", c)])
                        b.step("S", lambda e: e.activation(
                            lnrow[:], ps_sum[0:1, :], AF.Ln),
                            r=[("ps_sum",)], w=["lnrow"])
                        b.step("S", lambda e: e.activation(
                            recip_row[:], lnrow[:], AF.Exp, scale=-1.0),
                            r=["lnrow"], w=["recip_row"])
                        b.step("T", lambda e: e.matmul(
                            ps_st[0][:], ones_row_f[:], recip_row[:],
                            start=True, stop=True),
                            r=["ones_row_f", "recip_row"], w=[("st", 0)])
                        b.step("S", lambda e: e.copy(rb_sb[:], ps_st[0][:]),
                               r=[("st", 0)], w=["rb_sb"])
                        b.step("V", lambda e, h_abs=h_abs, g=g: e.tensor_tensor(
                            oT[:, h_abs * NT + g * 512:h_abs * NT + g * 512 + 512],
                            ps_out[:], rb_sb[:], ALU.mult),
                            r=[("ps_out",), "rb_sb"], w=[("oT", h_abs, g)])
            if dbg:
                b.step("Y", lambda e: e.dma_start(out=dbg_t["dbg_oT"][:], in_=oT[:]),
                       r=[("oT", hh, gg) for hh in range(32) for gg in range(2)], dma="dbg")
            b.barrier()
        es_ckv.close()
        es_roB.close()

        oT_keys = [("oT", hh, gg) for hh in range(32) for gg in range(2)]
        with ExitStack() as ph:
            pe = ph.enter_context
            wout_cb = [pe(nc.sbuf_tensor(f"wout{i}", [128, 32 * 128], BF16)) for i in range(4)]
            sa_sb = [pe(nc.sbuf_tensor(f"sa_sb{i}", [128, 512], BF16)) for i in range(4)]
            psC = [pe(nc.psum_tensor(f"psC{i}", [128, 512], F32)) for i in range(8)]

            def wout_dma(ob):
                b.step("G", lambda e, ob=ob: e.dma_start(
                    out=wout_cb[ob % 4][:].rearrange("p (k n) -> p k n", k=32),
                    in_=wout_t[ob]),
                    w=[("wout_cb", ob % 4)], dma=f"wo{ob % 4}")

            wout_dma(0)
            wout_dma(1)
            for og in range(16):
                if 2 * og + 2 < 32:
                    wout_dma(2 * og + 2)
                if 2 * og + 3 < 32:
                    wout_dma(2 * og + 3)
                for K in range(32):
                    for bb in range(2):
                        ob = og * 2 + bb
                        for th in range(2):
                            pb = (og % 2) * 4 + bb * 2 + th
                            b.step("T", lambda e, K=K, ob=ob, th=th, pb=pb: e.matmul(
                                psC[pb][:], wout_cb[ob % 4][:, K * 128:(K + 1) * 128],
                                oT[:, K * NT + th * 512:K * NT + th * 512 + 512],
                                start=(K == 0), stop=(K == 31)),
                                r=[("wout_cb", ob % 4)] + (oT_keys if K == 0 else []),
                                w=[("psC", pb)], tok=(K == 31),
                                reg_r=([("wout_cb", ob % 4)] + oT_keys) if K == 31 else None)
                for bb in range(2):
                    ob = og * 2 + bb
                    for th in range(2):
                        pb = (og % 2) * 4 + bb * 2 + th
                        sb = bb * 2 + th
                        b.step("V", lambda e, pb=pb, sb=sb: e.tensor_copy(
                            sa_sb[sb][:], psC[pb][:]),
                            r=[("psC", pb)], w=[("sa_sb", sb)])
                        b.step("Y", lambda e, ob=ob, th=th, sb=sb: e.dma_start(
                            out=sa_scr[ob * 128:(ob + 1) * 128, th * 512:th * 512 + 512],
                            in_=sa_sb[sb][:]),
                            r=[("sa_sb", sb)], w=[("sa_scr", ob, th)], dma=f"sa{sb}")
            b.barrier()
        es_oT.close()

        with ExitStack() as ph:
            pe = ph.enter_context
            xn_res = pe(nc.sbuf_tensor("xn_res", [128, 32 * NT], BF16))
            w_grp = [pe(nc.sbuf_tensor(f"w_grp{i}", [128, 2 * 4096], BF16)) for i in range(2)]
            qcb_i8 = [pe(nc.sbuf_tensor(f"qcb_i8{i}", [128, 4096], mybir.dt.int8)) for i in range(2)]
            qcb_bf = pe(nc.sbuf_tensor("qcb_bf", [128, 4096], BF16))
            s_ob = [pe(nc.sbuf_tensor(f"s_ob{i}", [128, 4096], BF16)) for i in range(2)]
            sc_row = pe(nc.sbuf_tensor("sc_row", [1, 4096], F32))
            s_sl = [pe(nc.sbuf_tensor(f"s_sl{i}", [128, NT], BF16)) for i in range(2)]
            x_bk = [pe(nc.sbuf_tensor(f"x_bk{i}", [128, NT], F32)) for i in range(2)]
            g8 = [pe(nc.sbuf_tensor(f"g8_{i}", [128, NT], BF16)) for i in range(2)]
            m_tile = [pe(nc.sbuf_tensor(f"m_tile{i}", [128, 512], BF16)) for i in range(2)]
            out_sb = [pe(nc.sbuf_tensor(f"out_sb{i}", [128, NT], F32)) for i in range(2)]
            psD = [pe(nc.psum_tensor(f"psD{i}", [128, 512], F32)) for i in range(4)]
            ps_bc = [pe(nc.psum_tensor(f"ps_bc{i}", [128, 512], F32)) for i in range(2)]
            ps_tr = [pe(nc.psum_tensor(f"ps_tr{i}", [128, 512], BF16)) for i in range(2)]

            cs_keys = [("colsumsT", tc) for tc in range(8)]

            for ot in range(8):
                b.step("Y", lambda e, ot=ot: e.dma_start(
                    out=xn_res[:].rearrange("p (j n) -> p j n", j=32)
                    [:, :, ot * 128:(ot + 1) * 128],
                    in_=xn_scr[ot]),
                    w=[("xn_res", ot)], dma=f"xr{ot % 4}")

            def dequant(gbk):
                buf = gbk % 2
                wg = (gbk // 2) % 2
                b.step("Y", lambda e, gbk=gbk, buf=buf: e.dma_start(
                    out=qcb_i8[buf][:].rearrange("p (j n) -> p j n", j=32),
                    in_=q_tiled[gbk]),
                    w=[("qcb_i8", buf)], dma=f"qc{buf}")
                b.step("G", lambda e, gbk=gbk: e.dma_start(
                    out=sc_row[:], in_=sc_rows_d[gbk:gbk + 1, :]),
                    w=[("sc_row",)], dma="sc")
                for h8 in range(8):
                    pb = h8 % 2
                    b.step("T", lambda e, h8=h8, pb=pb: e.matmul(
                        ps_bc[pb][:], ones_row_f[:],
                        sc_row[:, h8 * 512:(h8 + 1) * 512],
                        start=True, stop=True),
                        r=["ones_row_f", ("sc_row",)], w=[("ps_bc", pb)])
                    b.step("V", lambda e, buf=buf, h8=h8, pb=pb: e.tensor_copy(
                        s_ob[buf][:, h8 * 512:(h8 + 1) * 512], ps_bc[pb][:]),
                        r=[("ps_bc", pb)], w=[("s_ob", buf, h8)])
                b.step("S", lambda e, buf=buf: e.copy(qcb_bf[:], qcb_i8[buf][:]),
                       r=[("qcb_i8", buf)], w=[("qcb_bf",)])
                b.step("V", lambda e, buf=buf, wg=wg, gbk=gbk: e.tensor_tensor(
                    w_grp[wg][:, (gbk % 2) * 4096:(gbk % 2) * 4096 + 4096],
                    qcb_bf[:], s_ob[buf][:], ALU.mult),
                    r=[("qcb_bf",)] + [("s_ob", buf, h8) for h8 in range(8)],
                    w=[("w_grp", wg, gbk % 2)])

            dequant(0)
            dequant(1)
            for g in range(16):
                wg = g % 2
                if g + 1 < 16:
                    dequant(2 * (g + 1))
                    dequant(2 * (g + 1) + 1)
                for i in range(2):
                    gbk = 2 * g + i
                    b.step("Y", lambda e, gbk=gbk, i=i: e.dma_start(
                        out=s_sl[i][:], in_=sa_scr[gbk * 128:(gbk + 1) * 128, :]),
                        w=[("s_sl", i)], dma=f"sl{i}")
                    b.step("Y", lambda e, gbk=gbk, i=i: e.dma_start(
                        out=x_bk[i][:].rearrange("p (tc o) -> p tc o", tc=8),
                        in_=x_blk[gbk]),
                        w=[("x_bk", i)], dma=f"xb{i}")
                    for th in range(2):
                        pb = i * 2 + th
                        b.step("T", lambda e, gbk=gbk, th=th, pb=pb: e.matmul(
                            psD[pb][:], zT_sb[:, gbk * 128:gbk * 128 + 128],
                            colsumsT[0:32, th * 512:th * 512 + 512],
                            start=True, stop=False),
                            r=[d_key(zT_sb)] + cs_keys,
                            w=[("psD", pb)], tok=False)
                for j in range(32):
                    for i in range(2):
                        for th in range(2):
                            pb = i * 2 + th
                            qlast = (j == 31 and i == 1 and th == 1)
                            b.step("T", lambda e, wg=wg, i=i, j=j, th=th, pb=pb: e.matmul(
                                psD[pb][:],
                                w_grp[wg][:, i * 4096 + j * 128:i * 4096 + j * 128 + 128],
                                xn_res[:, j * NT + th * 512:j * NT + th * 512 + 512],
                                start=False, stop=(j == 31)),
                                r=[("w_grp", wg, i)]
                                + ([("xn_res", ot) for ot in range(8)] if j == 0 else []),
                                w=([("psD", q) for q in range(4)] if qlast
                                   else [("psD", pb)]),
                                tok=(th == 1),
                                reg_r=([d_key(zT_sb)] + cs_keys
                                       + [("xn_res", ot) for ot in range(8)]
                                       + [("w_grp", wg, 0), ("w_grp", wg, 1)]) if qlast else None)
                for i in range(2):
                    gbk = 2 * g + i
                    for th in range(2):
                        pb = i * 2 + th
                        b.step("S", lambda e, gbk=gbk, i=i, th=th, pb=pb: e.activation(
                            g8[i][:, th * 512:th * 512 + 512],
                            psD[pb][:], AF.Silu, bias=bias_sb[:, gbk:gbk + 1]),
                            r=[("psD", pb), d_key(bias_sb)], w=[("g8", i, th)])
                for i in range(2):
                    gbk = 2 * g + i
                    for th in range(2):
                        mb = th
                        b.step("V", lambda e, i=i, th=th, mb=mb: e.tensor_tensor(
                            m_tile[mb][:], g8[i][:, th * 512:th * 512 + 512],
                            s_sl[i][:, th * 512:th * 512 + 512], ALU.mult),
                            r=[("g8", i, th), ("s_sl", i)], w=[("m_tile", mb)])
                        for tc4 in range(4):
                            b.step("T", lambda e, mb=mb, tc4=tc4: e.transpose(
                                ps_tr[mb][:, tc4 * 128:tc4 * 128 + 128],
                                m_tile[mb][:, tc4 * 128:tc4 * 128 + 128], ident_bf[:]),
                                r=[("m_tile", mb), d_key(ident_bf)],
                                w=[("ps_tr", mb)], tok=(tc4 == 3),
                                reg_r=[("m_tile", mb), d_key(ident_bf)] if tc4 == 3 else None)
                        b.step("V", lambda e, i=i, th=th, mb=mb: e.tensor_tensor(
                            out_sb[i][:, th * 512:th * 512 + 512],
                            ps_tr[mb][:], x_bk[i][:, th * 512:th * 512 + 512],
                            ALU.add),
                            r=[("ps_tr", mb), ("x_bk", i)], w=[("out_sb", i, th)])
                    b.step("Y", lambda e, gbk=gbk, i=i: e.dma_start(
                        out=out_own.rearrange("(tc p) o -> p tc o", p=128)
                        [:, :, gbk * 128:(gbk + 1) * 128],
                        in_=out_sb[i][:].rearrange("p (tc o) -> p tc o", tc=8)),
                        r=[("out_sb", i, 0), ("out_sb", i, 1)],
                        w=[("out", gbk)], dma=f"ot{i}")
            def fin(e):
                for sm in b.dsems.values():
                    if sm.n > 0:
                        e.wait_ge(sm.h, sm.n)
                return e.nop()
            b.step("Y", fin, tok=False)
            b.flush_pending()

    return nc, dbg_t


def d_key(t):
    return ("sb", t.name if hasattr(t, "name") else id(t))



def kernel(**inputs):
    dbg = int(os.environ.get("KDBG", "0"))
    x = np.asarray(inputs["L0_attn_hidden_states"], np.float32)
    shared = _prep_shared(inputs)
    nc, _ = build_nc(dbg=dbg)

    in_maps = []
    for core in range(8):
        bq, h = divmod(core, 2)
        m = dict(shared)
        m.update(_prep_core(x, bq, h))
        im = {
            "x_full": m["x_full"], "x_own": m["x_own"],
            "cos_k": m["cos_k"], "sin_k": m["sin_k"],
            "cos_q": m["cos_q"], "sin_q": m["sin_q"],
            "maskrows": m["maskrows"], "q_rowsinit": m["q_rowsinit"],
            "consts_f": m["consts_f"], "ident_bf": m["ident_bf"],
            "ident_f": m["ident_f"], "tri": m["tri"], "g_ind": m["g_ind"],
            "wkv_t": m["wkv_t"], "wkr_t": m["wkr_t"], "wqd_t": m["wqd_t"],
            "wqun_t": m["wqun_t"], "wqur_t": m["wqur_t"],
            "wukT_t": m["wukT_t"], "wuvT_t": m["wuvT_t"],
            "wout_t": m["wout_t"], "q_tiled": m["q_tiled"],
            "sc_rows": m["sc_rows"], "zT": m["zT"], "bias_r": m["bias_r"],
            "x_blk_t": m["x_blk_t"],
        }
        in_maps.append(im)

    res = run_bass_kernel_spmd(nc, in_maps, core_ids=list(range(8)))
    out = np.empty((4, SEQ, D), np.float32)
    for core in range(8):
        bq, h = divmod(core, 2)
        out[bq][_own_rows(h)] = res.results[core]["out_own"]
    return out
